# version 03
# baseline (speedup 1.0000x reference)
import math
import numpy as np
import concourse.bass as bass
import concourse.mybir as mybir

F32 = mybir.dt.float32
BF16 = mybir.dt.bfloat16
I32 = mybir.dt.int32
AF = mybir.ActivationFunctionType
ALU = mybir.AluOpType
AX = mybir.AxisListType

D = 1024
NCHUNK = 8


class Ctx:

    def __init__(self, nc, st, uid=""):
        self.nc = nc
        self.st = st
        self.uid = uid
        self.blk = None
        self.sems = {}
        self.bufs = {}
        self.drams = {}

    def sem(self, name):
        if name not in self.sems:
            h = self.st.enter_context(self.nc.semaphore(name))
            self.sems[name] = [h, 0]
        return self.sems[name][0]

    def inc(self, instr, name, n=1):
        h = self.sem(name)
        instr.then_inc(h, n)
        self.sems[name][1] += n
        return self.sems[name][1]

    def val(self, name):
        self.sem(name)
        return self.sems[name][1]

    def wait(self, eng, name, v=None):
        h = self.sem(name)
        eng.wait_ge(h, self.sems[name][1] if v is None else v)

    def seq(self, eng, instr, name=None):
        name = name or f"seq_{eng.engine}"
        v = self.inc(instr, name)
        eng.wait_ge(self.sem(name), v)
        return instr

    def sbuf(self, name, shape, dtype):
        t = self.st.enter_context(
            self.nc.sbuf_tensor(name + self.uid, shape, dtype))
        self.bufs[name] = t
        return t

    def psum(self, name, shape, dtype=F32):
        t = self.st.enter_context(
            self.nc.psum_tensor(name + self.uid, shape, dtype))
        self.bufs[name] = t
        return t

    def dram_in(self, name, shape, dtype):
        cache = getattr(self.nc, "_dram_cache", None)
        if cache is None:
            cache = self.nc._dram_cache = {}
        if name not in cache:
            cache[name] = self.nc.dram_tensor(
                name, shape, dtype, kind="ExternalInput").ap()
        self.drams[name] = cache[name]
        return cache[name]

    def dram_out(self, name, shape, dtype):
        ap = self.nc.dram_tensor(name, shape, dtype, kind="ExternalOutput").ap()
        self.drams[name] = ap
        return ap

    def dram_tmp(self, name, shape, dtype):
        ap = self.nc.dram_tensor(name + self.uid, shape, dtype,
                                 kind="Internal").ap()
        self.drams[name] = ap
        return ap


KONST = [1e-5, 1.0, 0.5, 2.0, 1e-12, 1e-8, 1.0 / 127.0, 0.0001,
         float(np.float32(1.0 / math.e)), float(np.float32(0.5 - math.log(4.0 / 3.0))),
         -1e9, 1e4, -1e4]
KIDX = {v: i for i, v in enumerate(KONST)}


def const_inputs():
    return {
        "c_ident": np.eye(128, dtype=np.float32),
        "c_ones_col": np.ones((128, 1), dtype=np.float32),
        "c_ones_row": np.ones((1, 512), dtype=np.float32),
        "c_konst": np.tile(np.asarray(KONST, np.float32)[None, :], (128, 1)),
        "c_sel8": np.concatenate([np.tile((np.arange(8) == e).astype(np.float32)[:, None],
                                          (1, 128)) for e in range(8)], axis=1),
        "c_sel32": np.concatenate([np.tile((np.arange(32) == j).astype(np.float32)[:, None],
                                           (1, 128)) for j in range(32)], axis=1),
        "c_bones": _bones(),
        "c_mlneg": -np.kron(np.eye(4, dtype=np.float32),
                            np.tril(np.ones((32, 32), np.float32), -1)),
        "c_mu32": np.triu(np.ones((32, 32), np.float32)),
    }


def _bones():
    b = np.zeros((128, 128), np.float32)
    for c in range(8):
        for h2 in range(2):
            b[h2 * 64:(h2 + 1) * 64, c * 16 + 2 * c + h2] = 0.0
    b[:] = 0.0
    for c in range(8):
        b[0:64, c * 16 + 2 * c] = 1.0
        b[64:128, c * 16 + 2 * c + 1] = 1.0
    return b


def kap(ctx, value, p=128):
    return ctx.bufs["konst"][0:p, KIDX[value]:KIDX[value] + 1]


CONST_TABLE = [
    ("ident_bf", "c_ident", [128, 128], BF16),
    ("ident_f32", "c_ident", [128, 128], F32),
    ("ones_col", "c_ones_col", [128, 1], F32),
    ("ones_col_bf", "c_ones_col", [128, 1], BF16),
    ("ones_row", "c_ones_row", [1, 512], F32),
    ("ones_row_bf", "c_ones_row", [1, 512], BF16),
    ("konst", "c_konst", [128, len(KONST)], F32),
    ("sel8", "c_sel8", [8, 1024], F32),
    ("sel32", "c_sel32", [32, 32 * 128], F32),
    ("bones", "c_bones", [128, 128], F32),
    ("mlneg", "c_mlneg", [128, 128], BF16),
    ("mu32", "c_mu32", [32, 32], BF16),
]


def const_inputs_for(skip=()):
    ci = const_inputs()
    keys = {key for nm, key, _, _ in CONST_TABLE if nm not in skip}
    return {k: v for k, v in ci.items() if k in keys}


def declare_consts(ctx, skip=()):
    hbm = {}
    loads = []
    for nm, key, shape, dt in CONST_TABLE:
        if nm in skip:
            continue
        if key not in hbm:
            hbm[key] = ctx.dram_in(key, shape, F32)
        loads.append((ctx.sbuf(nm, shape, dt), hbm[key]))

    @ctx.blk.gpsimd
    def _(g):
        for sb, hb in loads:
            ctx.inc(g.dma_start(out=sb[:], in_=hb), "c_const", 16)


def emit_xn(ctx, x3, xn3, NTOK, tag, eps=1e-5):
    blk = ctx.blk
    TT = 512
    nt = NTOK // TT

    xt = ctx.sbuf(f"{tag}_xt", [128, 2, NCHUNK, TT], F32)
    xnb = ctx.sbuf(f"{tag}_xnb", [128, 2, NCHUNK, TT], BF16)
    sq = ctx.sbuf(f"{tag}_sq", [128, TT], F32)
    rows = ctx.sbuf(f"{tag}_rows", [1, 2, TT], F32)
    rrow = ctx.sbuf(f"{tag}_rrow", [1, 2, TT], F32)
    t1 = ctx.sbuf(f"{tag}_t1", [128, TT], F32)
    ps_s = ctx.psum(f"{tag}_ps_s", [1, 2, TT], F32)
    ps_b = ctx.psum(f"{tag}_ps_b", [128, 2, TT], F32)
    ones_col = ctx.bufs["ones_col"]
    ones_row = ctx.bufs["ones_row"]

    s_x = [f"{tag}_sx0", f"{tag}_sx1"]
    s_sq = f"{tag}_ssq"
    s_st = f"{tag}_sst"
    s_rowcp = f"{tag}_srowcp"
    s_row = f"{tag}_srow"
    s_bc = f"{tag}_sbc"
    s_t1 = f"{tag}_st1"
    s_xn = f"{tag}_sxn"
    s_out = [f"{tag}_sout0", f"{tag}_sout1"]

    @blk.sync
    def _(s):
        for i in range(nt):
            pb = i % 2
            if i >= 2:
                s.wait_ge(ctx.sem(s_t1), (i - 1) * NCHUNK)
            ctx.inc(s.dma_start(out=xt[:, pb], in_=x3[:, :, i * TT:(i + 1) * TT]),
                    s_x[pb], 16)

    @blk.scalar
    def _(sc):
        ctx.wait(sc, "c_const")
        for i in range(nt):
            pb = i % 2
            sc.wait_ge(ctx.sem(s_x[pb]), 16 * (i // 2 + 1))
            for c in range(NCHUNK):
                ic = i * NCHUNK + c
                if ic > 0:
                    sc.wait_ge(ctx.sem(s_st), 2 * ic)
                ctx.inc(sc.activation(sq[:], xt[:, pb, c], AF.Square), s_sq)
            sc.wait_ge(ctx.sem(s_rowcp), 2 * (i + 1))
            a1 = sc.activation(rrow[:1, 0], rows[:1, 1], AF.Ln, bias=kap(ctx, eps, 1))
            ctx.seq(sc, a1, f"{tag}_scseq")
            ctx.inc(sc.activation(rrow[:1, 0], rrow[:1, 0], AF.Exp, scale=-0.5),
                    f"{tag}_sln")

    @blk.tensor
    def _(t):
        ctx.wait(t, "c_const")
        for i in range(nt):
            pb = i % 2
            t.wait_ge(ctx.sem(s_x[pb]), 16 * (i // 2 + 1))
            if i >= 1:
                t.wait_ge(ctx.sem(s_rowcp), 2 * i)
            for c in range(NCHUNK):
                ic = i * NCHUNK + c
                t.wait_ge(ctx.sem(s_sq), ic + 1)
                t.matmul(ps_s[:1, 0], ones_col[:], xt[:, pb, c],
                         start=(c == 0), stop=(c == NCHUNK - 1))
                mm = t.matmul(ps_s[:1, 1], ones_col[:], sq[:],
                              start=(c == 0), stop=(c == NCHUNK - 1))
                ctx.inc(mm, s_st, 2)
            t.wait_ge(ctx.sem(s_row), i + 1)
            if i >= 1:
                t.wait_ge(ctx.sem(s_xn), i * NCHUNK)
            t.matmul(ps_b[:, 0], ones_row[:1, 0:128], rrow[:1, 0], start=True, stop=True)
            mm = t.matmul(ps_b[:, 1], ones_row[:1, 0:128], rrow[:1, 1], start=True, stop=True)
            ctx.inc(mm, s_bc)

    @blk.vector
    def _(v):
        for i in range(nt):
            pb = i % 2
            v.wait_ge(ctx.sem(s_st), 2 * NCHUNK * (i + 1))
            if i >= 1:
                v.wait_ge(ctx.sem(s_bc), i)
            ctx.seq(v, v.tensor_copy(rows[:1, :], ps_s[:1, :]))
            ctx.inc(v.engine_nop(), s_rowcp)
            ctx.seq(v, v.tensor_scalar_mul(rows[:1, 0], rows[:1, 0], 1.0 / D))
            ctx.seq(v, v.scalar_tensor_tensor(rrow[:1, 1], rows[:1, 0], 1.0,
                                              rows[:1, 0], ALU.mult, ALU.mult))
            ctx.seq(v, v.scalar_tensor_tensor(rows[:1, 1], rows[:1, 1], 1.0 / D,
                                              rrow[:1, 1], ALU.mult, ALU.subtract))
            ctx.inc(v.engine_nop(), s_rowcp)
            v.wait_ge(ctx.sem(f"{tag}_sln"), i + 1)
            ctx.inc(v.scalar_tensor_tensor(rrow[:1, 1], rows[:1, 0], -1.0,
                                           rrow[:1, 0], ALU.mult, ALU.mult), s_row)
            v.wait_ge(ctx.sem(s_bc), i + 1)
            for c in range(NCHUNK):
                ic = i * NCHUNK + c
                if i >= 2 and c == 0:
                    v.wait_ge(ctx.sem(s_out[pb]), 16 * NCHUNK * (i // 2))
                if ic > 0:
                    v.wait_ge(ctx.sem(s_xn), ic)
                vt = ctx.inc(v.tensor_tensor(t1[:], xt[:, pb, c], ps_b[:, 0], ALU.mult), s_t1)
                v.wait_ge(ctx.sem(s_t1), vt)
                ctx.inc(v.tensor_tensor(xnb[:, pb, c], t1[:], ps_b[:, 1], ALU.add), s_xn)

    @blk.gpsimd
    def _(g):
        for i in range(nt):
            pb = i % 2
            for c in range(NCHUNK):
                ic = i * NCHUNK + c
                g.wait_ge(ctx.sem(s_xn), ic + 1)
                ctx.inc(g.dma_start(out=xn3[:, c, i * TT:(i + 1) * TT],
                                    in_=xnb[:, pb, c]), s_out[pb], 16)

    return dict(done=[(s, ctx.val(s)) for s in s_out])


def emit_w_load_direct(ctx, g, wt, w_hbm, sem):
    return ctx.inc(g.dma_start(out=wt.ap(), in_=w_hbm.rearrange("(a p) f -> p a f", p=128)),
                   sem, 16)


def emit_wT_load(ctx, tag, wt, w_hbm, DO, DI, do0=0, ndo=None, di0=0, ndi=None,
                 gates=(), bufs=None, pst=None, prev_ev=None):
    blk = ctx.blk
    ndo = ndo or DO
    ndi = ndi or DI
    nob = ndo // 128
    na = ndi // 128
    if bufs is not None and "stg" in bufs:
        stg = bufs["stg"]
    else:
        stg = ctx.sbuf(f"{tag}_stg", [128, 2, ndi], BF16)
        if bufs is not None:
            bufs["stg"] = stg
    if pst is not None:
        ps_t = pst
    else:
        ps_t = ctx.psum(f"{tag}_pst", [128, 8, 128], BF16)
    ident = ctx.bufs["ident_bf"]
    s_stg = [f"{tag}_sstg0", f"{tag}_sstg1"]
    s_tp = f"{tag}_stp"
    s_ev = f"{tag}_sev"
    ngrp = (na + 7) // 8

    @blk.gpsimd
    def _(g):
        for sem, val in gates:
            g.wait_ge(ctx.sem(sem), val)
        for ob in range(nob):
            pb = ob % 2
            if ob >= 2:
                g.wait_ge(ctx.sem(s_tp), (ob - 1) * ngrp)
            ctx.inc(g.dma_start(out=stg[:, pb],
                                in_=w_hbm[do0 + ob * 128: do0 + (ob + 1) * 128,
                                          di0:di0 + ndi]),
                    s_stg[pb], 16)

    @blk.tensor
    def _(t):
        ctx.wait(t, "c_const")
        for ob in range(nob):
            pb = ob % 2
            t.wait_ge(ctx.sem(s_stg[pb]), 16 * (ob // 2 + 1))
            for grp in range(ngrp):
                gi = ob * ngrp + grp
                if gi > 0:
                    t.wait_ge(ctx.sem(s_ev), gi)
                elif prev_ev is not None:
                    t.wait_ge(ctx.sem(prev_ev[0]), prev_ev[1])
                a0 = grp * 8
                g_n = min(8, na - a0)
                for j in range(g_n):
                    mm = t.transpose(ps_t[:, j, :],
                                     stg[:, pb, (a0 + j) * 128:(a0 + j + 1) * 128],
                                     ident[:])
                ctx.inc(mm, s_tp)

    @blk.vector
    def _(v):
        for ob in range(nob):
            for grp in range(ngrp):
                gi = ob * ngrp + grp
                v.wait_ge(ctx.sem(s_tp), gi + 1)
                a0 = grp * 8
                g_n = min(8, na - a0)
                ctx.inc(v.tensor_copy(wt[:, a0:a0 + g_n, ob * 128:(ob + 1) * 128],
                                      ps_t[:, 0:g_n, :]), s_ev)

    return (s_ev, ctx.val(s_ev))


def emit_l0ffn(ctx, tag, xn3, x3, wg_h, wu_h, wd_h, NTOK, gates_acc=()):
    blk = ctx.blk
    TT = 512
    nt = NTOK // TT
    HID = 3072
    HH = HID // 2
    nhc = HH // 128
    S = 1.0 / 127.0

    wtg = ctx.sbuf(f"{tag}_wtg", [128, 8, HH], BF16)
    wtu = ctx.sbuf(f"{tag}_wtu", [128, 8, HH], BF16)
    wtd = ctx.sbuf(f"{tag}_wtd", [128, nhc, D], BF16)
    xnt = ctx.sbuf(f"{tag}_xnt", [128, 2, NCHUNK, TT], BF16)
    hbuf = ctx.sbuf(f"{tag}_h", [128, 2, nhc, TT], BF16)
    sil = ctx.sbuf(f"{tag}_sil", [128, 2, TT], BF16)
    ot = ctx.sbuf(f"{tag}_ot", [128, 2, TT], F32)
    ps_g = ctx.psum(f"{tag}_psg", [128, 2, TT], F32)
    ps_u = ctx.psum(f"{tag}_psu", [128, 2, TT], F32)
    ps_o = ctx.psum(f"{tag}_pso", [128, 2, TT], F32)

    s_xn = [f"{tag}_sxn0", f"{tag}_sxn1"]
    s_g = f"{tag}_sg"
    s_u = f"{tag}_su"
    s_sil = f"{tag}_ssil"
    s_h = f"{tag}_sh"
    s_dn = f"{tag}_sdn"
    s_ot = f"{tag}_sot"
    s_oacc = [f"{tag}_soa0", f"{tag}_soa1"]
    lbufs_g, lbufs_u, lbufs_d = {}, {}, {}
    pst = ctx.psum(f"{tag}_pst", [128, 8, 128], BF16)
    prev = None

    for r in range(2):
        if r == 0:
            gates = []
        else:
            gates = [(s_u, nhc * nt), (s_dn, NCHUNK * nt),
                     (ld_g[0], ld_g[1]), (ld_u[0], ld_u[1]), (ld_d[0], ld_d[1])]
        ld_g = emit_wT_load(ctx, f"{tag}_lg{r}", wtg, wg_h, HID, D, do0=r * HH, ndo=HH,
                            gates=gates, bufs=lbufs_g, pst=pst, prev_ev=prev)
        ld_u = emit_wT_load(ctx, f"{tag}_lu{r}", wtu, wu_h, HID, D, do0=r * HH, ndo=HH,
                            gates=gates, bufs=lbufs_u, pst=pst, prev_ev=ld_g)
        ld_d = emit_wT_load(ctx, f"{tag}_ld{r}", wtd, wd_h, D, HID, di0=r * HH, ndi=HH,
                            gates=gates, bufs=lbufs_d, pst=pst, prev_ev=ld_u)
        prev = ld_d
        base_u = nhc * nt * r
        base_dn = NCHUNK * nt * r
        base_sil = nhc * nt * r
        base_h = nhc * nt * r
        base_ot = NCHUNK * nt * r

        @blk.sync
        def _(s, r=r):
            for i in range(nt):
                pb = i % 2
                if r > 0 or i >= 2:
                    gidx = r * nt + i - 2
                    s.wait_ge(ctx.sem(s_u), nhc * (gidx + 1))
                ctx.inc(s.dma_start(out=xnt[:, pb], in_=xn3[:, :, i * TT:(i + 1) * TT]),
                        s_xn[pb], 16)

        @blk.tensor
        def _(t, r=r, ld_g=ld_g, ld_u=ld_u, ld_d=ld_d, base_u=base_u, base_dn=base_dn,
              base_sil=base_sil, base_h=base_h, base_ot=base_ot):
            t.wait_ge(ctx.sem(ld_g[0]), ld_g[1])
            t.wait_ge(ctx.sem(ld_u[0]), ld_u[1])
            t.wait_ge(ctx.sem(ld_d[0]), ld_d[1])
            for i in range(nt):
                pb = i % 2
                gi = r * nt + i
                t.wait_ge(ctx.sem(s_xn[pb]), 16 * (gi // 2 + 1))
                for hc in range(nhc):
                    ih = (r * nt + i) * nhc + hc
                    hp = ih % 2
                    if ih >= 2:
                        t.wait_ge(ctx.sem(s_sil), ih - 1)
                        t.wait_ge(ctx.sem(s_h), ih - 1)
                    for a in range(NCHUNK):
                        mm = t.matmul(ps_g[:, hp], wtg[:, a, hc * 128:(hc + 1) * 128],
                                      xnt[:, pb, a], start=(a == 0), stop=(a == NCHUNK - 1))
                    ctx.inc(mm, s_g)
                    for a in range(NCHUNK):
                        mm = t.matmul(ps_u[:, hp], wtu[:, a, hc * 128:(hc + 1) * 128],
                                      xnt[:, pb, a], start=(a == 0), stop=(a == NCHUNK - 1))
                    ctx.inc(mm, s_u)
                t.wait_ge(ctx.sem(s_h), nhc * (r * nt + i + 1))
                for dc in range(NCHUNK):
                    idn = (r * nt + i) * NCHUNK + dc
                    dp = idn % 2
                    if idn >= 2:
                        t.wait_ge(ctx.sem(s_ot), idn - 1)
                    for a in range(nhc):
                        mm = t.matmul(ps_o[:, dp], wtd[:, a, dc * 128:(dc + 1) * 128],
                                      hbuf[:, pb, a], start=(a == 0), stop=(a == nhc - 1))
                    ctx.inc(mm, s_dn)

        @blk.scalar
        def _(sc, r=r):
            ctx.wait(sc, "c_const")
            for i in range(nt):
                for hc in range(nhc):
                    ih = (r * nt + i) * nhc + hc
                    hp = ih % 2
                    sc.wait_ge(ctx.sem(s_g), ih + 1)
                    if ih >= 2:
                        sc.wait_ge(ctx.sem(s_h), ih - 1)
                    ctx.inc(sc.activation(sil[:, hp], ps_g[:, hp], AF.Silu, scale=S), s_sil)

        @blk.vector
        def _(v, r=r):
            for i in range(nt):
                pb = i % 2
                for hc in range(nhc):
                    ih = (r * nt + i) * nhc + hc
                    hp = ih % 2
                    v.wait_ge(ctx.sem(s_u), ih + 1)
                    v.wait_ge(ctx.sem(s_sil), ih + 1)
                    gi = r * nt + i
                    if gi >= 2 and hc == 0:
                        v.wait_ge(ctx.sem(s_dn), NCHUNK * (gi - 1))
                    ctx.inc(v.scalar_tensor_tensor(hbuf[:, pb, hc], ps_u[:, hp], kap(ctx, S),
                                                   sil[:, hp], ALU.mult, ALU.mult), s_h)
                for dc in range(NCHUNK):
                    idn = (r * nt + i) * NCHUNK + dc
                    dp = idn % 2
                    v.wait_ge(ctx.sem(s_dn), idn + 1)
                    if idn >= 2:
                        v.wait_ge(ctx.sem(s_oacc[dp]), 16 * (idn // 2))
                    ctx.inc(v.tensor_scalar_mul(ot[:, dp], ps_o[:, dp], kap(ctx, S)), s_ot)

        @blk.gpsimd
        def _(g, r=r):
            for sem, val in gates_acc:
                g.wait_ge(ctx.sem(sem), val)
            for i in range(nt):
                for dc in range(NCHUNK):
                    idn = (r * nt + i) * NCHUNK + dc
                    dp = idn % 2
                    g.wait_ge(ctx.sem(s_ot), idn + 1)
                    ctx.inc(g.dma_start(out=x3[:, dc, i * TT:(i + 1) * TT], in_=ot[:, dp],
                                        accum_op=ALU.add), s_oacc[dp], 16)

    return dict(done=[(s, ctx.val(s)) for s in s_oacc])


def emit_wT_to_dram(ctx, tag, w_hbm, DO, DI, pst=None, prev_ev=None, bufs=None,
                    sem_tag=None):
    blk = ctx.blk
    na = DI // 128
    nob = DO // 128
    wtd = ctx.dram_tmp(f"{tag}_wT", [128, na, DO], BF16)
    tag = sem_tag or tag
    if bufs is not None and "stg" in bufs:
        stg = bufs["stg"][:, :, 0:DI] if bufs["stg"].shape[2] != DI else bufs["stg"]
        wtmp = bufs["wtmp"][:, :, 0:na, :] if bufs["wtmp"].shape[2] != na else bufs["wtmp"]
        prev_tp = bufs.get("prev_tp")
        prev_wo = bufs.get("prev_wo")
    else:
        wtmp = ctx.sbuf(f"{tag}_wtmp", [128, 2, na, 128], BF16)
        stg = ctx.sbuf(f"{tag}_stg", [128, 2, DI], BF16)
        prev_tp = prev_wo = None
        if bufs is not None:
            bufs["stg"], bufs["wtmp"] = stg, wtmp
    if pst is None:
        pst = ctx.psum(f"{tag}_pst", [128, 8, 128], BF16)
    ident = ctx.bufs["ident_bf"]
    s_stg = [f"{tag}_sstg0", f"{tag}_sstg1"]
    s_tp = f"{tag}_stp"
    s_ev = f"{tag}_sev"
    s_wo = [f"{tag}_swo0", f"{tag}_swo1"]
    ngrp = (na + 7) // 8
    b_stg = [ctx.val(s_stg[0]), ctx.val(s_stg[1])]
    b_tp = ctx.val(s_tp)
    b_ev = ctx.val(s_ev)
    b_wo = [ctx.val(s_wo[0]), ctx.val(s_wo[1])]

    nbs = stg.shape[1]
    nbw = wtmp.shape[1]

    @blk.gpsimd
    def _(g):
        if prev_tp is not None:
            g.wait_ge(ctx.sem(prev_tp[0]), prev_tp[1])
        for ob in range(nob):
            pb = ob % nbs
            if ob >= nbs:
                g.wait_ge(ctx.sem(s_tp), b_tp + (ob - nbs + 1) * ngrp)
            ctx.inc(g.dma_start(out=stg[:, pb],
                                in_=w_hbm[ob * 128:(ob + 1) * 128, :]), s_stg[pb], 16)

    @blk.tensor
    def _(t):
        ctx.wait(t, "c_const")
        for ob in range(nob):
            pb = ob % nbs
            t.wait_ge(ctx.sem(s_stg[pb]), b_stg[pb] + 16 * (ob // nbs + 1))
            for grp in range(ngrp):
                gi = ob * ngrp + grp
                if gi > 0:
                    t.wait_ge(ctx.sem(s_ev), b_ev + gi)
                elif prev_ev is not None:
                    t.wait_ge(ctx.sem(prev_ev[0]), prev_ev[1])
                a0 = grp * 8
                g_n = min(8, na - a0)
                for j in range(g_n):
                    mm = t.transpose(pst[:, j, :],
                                     stg[:, pb, (a0 + j) * 128:(a0 + j + 1) * 128],
                                     ident[:])
                ctx.inc(mm, s_tp)

    @blk.vector
    def _(v):
        if prev_wo is not None:
            for sem, val in prev_wo:
                v.wait_ge(ctx.sem(sem), val)
        for ob in range(nob):
            pw = ob % nbw
            if ob >= nbw:
                v.wait_ge(ctx.sem(s_wo[pw]), b_wo[pw] + 16 * ((ob - nbw) // nbw + 1))
            for grp in range(ngrp):
                gi = ob * ngrp + grp
                v.wait_ge(ctx.sem(s_tp), b_tp + gi + 1)
                a0 = grp * 8
                g_n = min(8, na - a0)
                ctx.inc(v.tensor_copy(wtmp[:, pw, a0:a0 + g_n, :], pst[:, 0:g_n, :]), s_ev)

    @blk.sync
    def _(s):
        for ob in range(nob):
            pw = ob % nbw
            s.wait_ge(ctx.sem(s_ev), b_ev + (ob + 1) * ngrp)
            ctx.inc(s.dma_start(out=wtd[:, :, ob * 128:(ob + 1) * 128],
                                in_=wtmp[:, pw]), s_wo[pw], 16)

    gate = [(s_wo[0], ctx.val(s_wo[0])), (s_wo[1], ctx.val(s_wo[1]))]
    if bufs is not None:
        bufs["prev_tp"] = (s_tp, ctx.val(s_tp))
        bufs["prev_wo"] = gate
    return wtd, gate, (s_ev, ctx.val(s_ev))


def emit_colvec_load(ctx, tag, v_hbm, N, pst32=None, prev=None, row_buf=None):
    blk = ctx.blk
    ncol = (N + 127) // 128
    if row_buf is not None:
        row = row_buf[:1, 0:N] if hasattr(row_buf, "shape") and len(row_buf.shape) == 2 and hasattr(row_buf, "ap") else row_buf
    else:
        row = ctx.sbuf(f"{tag}_row", [1, N], F32)
    col = ctx.sbuf(f"{tag}_col", [min(128, N), ncol], F32)
    if pst32 is None:
        pst32 = ctx.psum(f"{tag}_pst32", [128, ncol], F32)
    ident32 = ctx.bufs["ident_f32"]
    s_r = f"{tag}_sr"
    s_t = f"{tag}_st"
    s_c = f"{tag}_sc"

    @blk.gpsimd
    def _(g):
        if prev is not None and row_buf is not None:
            g.wait_ge(ctx.sem(prev[0]), prev[1])
        ctx.inc(g.dma_start(out=row[:], in_=v_hbm), s_r, 16)

    @blk.tensor
    def _(t):
        ctx.wait(t, "c_const")
        if prev is not None:
            t.wait_ge(ctx.sem(prev[0]), prev[1])
        t.wait_ge(ctx.sem(s_r), ctx.val(s_r))
        for c in range(ncol):
            n0 = c * 128
            nn = min(128, N - n0)
            mm = t.transpose(pst32[0:nn, c:c + 1], row[:1, n0:n0 + nn],
                             ident32[0:1, 0:1])
        ctx.inc(mm, s_t)

    @blk.vector
    def _(v):
        v.wait_ge(ctx.sem(s_t), ctx.val(s_t))
        nn = min(128, N)
        ctx.inc(v.tensor_copy(col[0:nn, :], pst32[0:nn, 0:ncol]), s_c)

    return col, (s_c, ctx.val(s_c))


class GemmStream:
    def __init__(self, ctx, tag, weights, DI, DO, nkt, ev_sems, *, do_block=512,
                 TT=512, banks=2, psums=None, wbufs=None):
        self.ctx, self.tag = ctx, tag
        self.weights, self.DI, self.DO, self.nkt = weights, DI, DO, nkt
        self.ev_sems = ev_sems
        self.do_block, self.TT, self.banks = do_block, TT, banks
        self.na = DI // 128
        self.nblk = DO // do_block
        self.ncb = do_block // 128
        self.npg = (DO // 128) * nkt
        self.sched = [(jb * self.ncb + c, k) for jb in range(self.nblk)
                      for c in range(self.ncb) for k in range(nkt)]
        self.base = 0
        self.jbase = 0
        self.psums = []
        self.s_mm = []
        self.wbufs = []
        self.s_wl = []
        for wi, (wn, kind, srcw) in enumerate(weights):
            if psums is not None:
                self.psums.append(psums[wi])
            else:
                self.psums.append(ctx.psum(f"{tag}_{wn}_ps", [128, banks, TT], F32))
            self.s_mm.append(f"{tag}_{wn}_smm")
            if kind == "dram":
                if wbufs is not None:
                    self.wbufs.append(wbufs[wi])
                else:
                    self.wbufs.append(ctx.sbuf(f"{tag}_{wn}_wb",
                                               [128, 2, self.na, do_block], BF16))
                self.s_wl.append([f"{tag}_{wn}_swl0", f"{tag}_{wn}_swl1"])
            else:
                self.wbufs.append(None)
                self.s_wl.append(None)

    def emit(self, xt_fn, gates=(), dma_gates=(), gate_fn=None, wsrcs=None):
        ctx, blk = self.ctx, self.ctx.blk
        weights, banks = self.weights, self.banks
        na, nkt, ncb, nblk = self.na, self.nkt, self.ncb, self.nblk
        base, jbase = self.base, self.jbase
        last = len(weights) - 1
        s_mm, s_wl, psums, wbufs = self.s_mm, self.s_wl, self.psums, self.wbufs
        do_block = self.do_block
        if wsrcs is not None:
            weights = [(wn, kind, wsrcs[wi]) for wi, (wn, kind, _s) in enumerate(weights)]

        @blk.sync
        def _(s):
            for sem, val in dma_gates:
                s.wait_ge(ctx.sem(sem), val)
            for jb in range(nblk):
                jg = jbase + jb
                for wi, (wn, kind, srcw) in enumerate(weights):
                    if kind != "dram":
                        continue
                    nb = wbufs[wi].shape[1]
                    pb = jg % nb
                    if jg >= nb:
                        s.wait_ge(ctx.sem(s_mm[last]), (jg - nb + 1) * ncb * nkt)
                    ctx.inc(s.dma_start(out=wbufs[wi][:, pb],
                                        in_=srcw[:, :, jb * do_block:(jb + 1) * do_block]),
                            s_wl[wi][pb], 16)

        @blk.tensor
        def _(t):
            for sem, val in gates:
                t.wait_ge(ctx.sem(sem), val)
            n = base
            for jb in range(nblk):
                jg = jbase + jb
                for wi, (wn, kind, srcw) in enumerate(weights):
                    if kind == "dram":
                        nb = wbufs[wi].shape[1]
                        t.wait_ge(ctx.sem(s_wl[wi][jg % nb]), 16 * (jg // nb + 1))
                for c in range(ncb):
                    for k in range(nkt):
                        if gate_fn is not None:
                            gate_fn(t, k)
                        for wi, (wn, kind, srcw) in enumerate(weights):
                            if n >= banks:
                                t.wait_ge(ctx.sem(self.ev_sems[wi]), n - banks + 1)
                            wsrc = (wbufs[wi][:, jg % wbufs[wi].shape[1]]
                                    if kind == "dram" else srcw)
                            coff = (jb * ncb + c) * 128 if kind == "sbuf" else c * 128
                            for a in range(na):
                                mm = t.matmul(psums[wi][:, n % banks],
                                              wsrc[:, a, coff:coff + 128],
                                              xt_fn(a, k), start=(a == 0),
                                              stop=(a == na - 1))
                            ctx.inc(mm, s_mm[wi])
                        n += 1

        self.base += self.npg
        self.jbase += nblk
        return dict(base=base)


def emit_l2ffn(ctx, tag, xn3, x3, w1h, s1h, w2h, s2h, h1h, sh1h, h2h, sh2h,
               w3h, s3h, NTOK, gates_acc=()):
    blk = ctx.blk
    PT = 1024
    nkt = PT // 512
    npc = NTOK // PT
    DH = 4096
    nhc = DH // 128

    pst = ctx.psum(f"{tag}_pst", [128, 8, 128], BF16)
    wA = {}
    prev = None
    ab = {"stg": ctx.sbuf(f"{tag}_Astg", [128, 1, DH], BF16),
          "wtmp": ctx.sbuf(f"{tag}_Awtmp", [128, 1, DH // 128, 128], BF16)}
    for nm, wh, DO_, DI_ in (("w1", w1h, DH, D), ("w2", w2h, DH, D),
                             ("h1", h1h, DH, DH), ("h2", h2h, DH, DH),
                             ("w3", w3h, D, DH)):
        wT, gate, ev = emit_wT_to_dram(ctx, f"{tag}_{nm}", wh, DO_, DI_, pst=pst,
                                       prev_ev=prev, bufs=ab)
        wA[nm] = (wT, gate)
        prev = ev
    pst32 = ctx.psum(f"{tag}_pst32", [128, 32], F32)
    scols = {}
    prevc = None
    srow = ctx.sbuf(f"{tag}_srow", [1, DH], F32)
    for nm, sh, N_ in (("s1", s1h, DH), ("s2", s2h, DH), ("sh1", sh1h, DH),
                       ("sh2", sh2h, DH), ("s3", s3h, D)):
        col, cg = emit_colvec_load(ctx, f"{tag}_{nm}", sh, N_, pst32=pst32, prev=prevc,
                                   row_buf=srow)
        scols[nm] = col
        prevc = cg
    s_scales = prevc

    h0 = ctx.sbuf(f"{tag}_h0", [128, nhc, PT], BF16)
    h1g = ctx.sbuf(f"{tag}_h1g", [128, nhc, PT], BF16)
    xnt = ctx.sbuf(f"{tag}_xnt", [128, NCHUNK, PT], BF16)
    sil = ctx.sbuf(f"{tag}_sil", [128, 2, 512], BF16)
    ot = ctx.sbuf(f"{tag}_ot", [128, 1, 512], F32)
    psA = [ctx.psum(f"{tag}_psA", [128, 2, 512], F32)]
    wbB = [ctx.sbuf(f"{tag}_wbB", [128, 2, nhc, 128], BF16)]

    wbg1 = [ctx.sbuf(f"{tag}_wbg1a", [128, 1, 8, 128], BF16),
            ctx.sbuf(f"{tag}_wbg1b", [128, 1, 8, 128], BF16)]
    g1 = GemmStream(ctx, f"{tag}_g1", [("w1", "dram", wA["w1"][0]),
                                       ("w2", "dram", wA["w2"][0])],
                    D, DH, nkt, [f"{tag}_ev1a", f"{tag}_ev1b"], do_block=128,
                    wbufs=wbg1)
    g23 = GemmStream(ctx, f"{tag}_g23", [("h", "dram", None)], DH, DH, nkt,
                     [f"{tag}_ev23"], do_block=128, psums=psA, wbufs=wbB)
    g4 = GemmStream(ctx, f"{tag}_g4", [("w3", "dram", wA["w3"][0])], DH, D, nkt,
                    [f"{tag}_ev4"], do_block=128, psums=psA, wbufs=wbB)

    s_xnt = f"{tag}_sxnt"
    s_oacc = [f"{tag}_soa0", f"{tag}_soa1"]
    npg = 64
    npg4 = 16

    for p in range(npc):
        @blk.sync
        def _(s, p=p):
            if p > 0:
                s.wait_ge(ctx.sem(g1.s_mm[1]), npg * p)
            ctx.inc(s.dma_start(out=xnt.ap(), in_=xn3[:, :, p * PT:(p + 1) * PT]),
                    s_xnt, 16)

        gts = [(s_xnt, 16 * (p + 1)), (s_scales[0], s_scales[1])]
        if p == 0:
            gts += [wA["w1"][1][0], wA["w1"][1][1], wA["w2"][1][0], wA["w2"][1][1]]
        g1.emit(lambda a, k: xnt[:, a, k * 512:(k + 1) * 512], gates=gts)

        @blk.scalar
        def _(sc, p=p):
            ctx.wait(sc, "c_const")
            base = p * npg
            for nl, (j, k) in enumerate(g1.sched):
                n = base + nl
                sc.wait_ge(ctx.sem(g1.s_mm[0]), n + 1)
                if n >= 2:
                    sc.wait_ge(ctx.sem(f"{tag}_ev1b"), n - 1)
                ctx.inc(sc.activation(sil[:, n % 2], g1.psums[0][:, n % 2], AF.Silu,
                                      scale=scols["s1"][:, j:j + 1]), f"{tag}_ev1a")

        @blk.vector
        def _(v, p=p):
            base = p * npg
            for nl, (j, k) in enumerate(g1.sched):
                n = base + nl
                v.wait_ge(ctx.sem(g1.s_mm[1]), n + 1)
                v.wait_ge(ctx.sem(f"{tag}_ev1a"), n + 1)
                if p > 0 and nl == 0:
                    v.wait_ge(ctx.sem(g23.s_mm[0]), (2 * p - 1) * npg)
                    v.wait_ge(ctx.sem(g4.s_mm[0]), npg4 * p)
                tt = v.scalar_tensor_tensor(h0[:, j, k * 512:(k + 1) * 512],
                                            g1.psums[1][:, n % 2],
                                            scols["s2"][:, j:j + 1],
                                            sil[:, n % 2], ALU.mult, ALU.mult)
                ctx.inc(tt, f"{tag}_ev1b")

        g2gates = [(f"{tag}_ev1b", (p + 1) * npg)]
        if p > 0:
            g2gates.append((f"{tag}_ev4", npg4 * p))
        g23.emit(lambda a, k: h0[:, a, k * 512:(k + 1) * 512],
                 gates=g2gates, wsrcs=[wA["h1"][0]],
                 dma_gates=([wA["h1"][1][0], wA["h1"][1][1]] if p == 0 else
                            [(g4.s_mm[0], npg4 * p)]))

        @blk.scalar
        def _(sc, p=p):
            base = 2 * p * npg
            for nl, (j, k) in enumerate(g23.sched):
                n = base + nl
                sc.wait_ge(ctx.sem(g23.s_mm[0]), n + 1)
                ctx.inc(sc.activation(h1g[:, j, k * 512:(k + 1) * 512],
                                      g23.psums[0][:, n % 2], AF.Gelu,
                                      scale=scols["sh1"][:, j:j + 1]), f"{tag}_ev23")

        g23.emit(lambda a, k: h1g[:, a, k * 512:(k + 1) * 512],
                 gates=[(f"{tag}_ev23", (2 * p + 1) * npg)], wsrcs=[wA["h2"][0]],
                 dma_gates=([wA["h2"][1][0], wA["h2"][1][1]] if p == 0 else []))

        @blk.scalar
        def _(sc, p=p):
            base = (2 * p + 1) * npg
            for nl, (j, k) in enumerate(g23.sched):
                n = base + nl
                sc.wait_ge(ctx.sem(g23.s_mm[0]), n + 1)
                ctx.inc(sc.activation(h0[:, j, k * 512:(k + 1) * 512],
                                      g23.psums[0][:, n % 2], AF.Gelu,
                                      scale=scols["sh2"][:, j:j + 1]), f"{tag}_ev23")

        g4.emit(lambda a, k: h0[:, a, k * 512:(k + 1) * 512],
                gates=[(f"{tag}_ev23", (2 * p + 2) * npg)],
                dma_gates=[(g23.s_mm[0], (2 * p + 2) * npg)] +
                           ([wA["w3"][1][0], wA["w3"][1][1]] if p == 0 else []))

        @blk.vector
        def _(v, p=p):
            base = p * npg4
            for nl, (j, k) in enumerate(g4.sched):
                n = base + nl
                v.wait_ge(ctx.sem(g4.s_mm[0]), n + 1)
                if n >= 1:
                    v.wait_ge(ctx.sem(s_oacc[0]), 16 * n)
                ctx.inc(v.tensor_scalar_mul(ot[:, 0], g4.psums[0][:, n % 2],
                                            scols["s3"][:, j:j + 1]), f"{tag}_ev4")

        @blk.gpsimd
        def _(g, p=p):
            for sem, val in gates_acc:
                g.wait_ge(ctx.sem(sem), val)
            base = p * npg4
            for nl, (j, k) in enumerate(g4.sched):
                n = base + nl
                g.wait_ge(ctx.sem(f"{tag}_ev4"), n + 1)
                ctx.inc(g.dma_start(out=x3[:, j, p * PT + k * 512: p * PT + (k + 1) * 512],
                                    in_=ot[:, 0], accum_op=ALU.add), s_oacc[0], 16)

    return dict(done=[(s, ctx.val(s)) for s in s_oacc])






def emit_l1ffn(ctx, tag, xn3, x3, wg_h, wu_h, wd_h, wr_h, temp_h, NTOK, gates_acc=()):
    blk = ctx.blk
    TT = 512
    GT = 2048
    ngr = NTOK // GT
    ntg = GT // TT
    ntile = GT // 128
    E = 8
    GC = float(np.float32(1.0 / math.e))
    GL = float(np.float32(0.5 - math.log(4.0 / 3.0)))

    wgt = ctx.sbuf(f"{tag}_wgt", [128, NCHUNK, D], BF16)
    wut = ctx.sbuf(f"{tag}_wut", [128, NCHUNK, D], BF16)
    wdt = ctx.sbuf(f"{tag}_wdt", [128, NCHUNK, D], BF16)
    wrt = ctx.sbuf(f"{tag}_wrt", [128, NCHUNK, E], BF16)
    xng = ctx.sbuf(f"{tag}_xng", [128, NCHUNK, GT], BF16)
    y_acc = ctx.sbuf(f"{tag}_yacc", [128, NCHUNK, GT], F32)
    rows = ctx.sbuf(f"{tag}_rows", [8, 4, GT], F32)
    wrow = ctx.sbuf(f"{tag}_wrow", [8, GT], F32)
    wfb = ctx.sbuf(f"{tag}_wfb", [128, ntile, 16], F32)
    tmA = ctx.sbuf(f"{tag}_tmA", [128, ntile, 8], F32)
    tmc = ctx.sbuf(f"{tag}_tmc", [128, 6], F32)
    rt = ctx.sbuf(f"{tag}_rt", [1, 2], F32)
    rtcol = ctx.sbuf(f"{tag}_rtcol", [8, 1], F32)
    wb_sb = ctx.sbuf(f"{tag}_wb", [128, TT], F32)
    sil = ctx.sbuf(f"{tag}_sil", [128, 2, TT], BF16)
    h0 = ctx.sbuf(f"{tag}_h0", [128, 2, TT], BF16)
    hw = ctx.sbuf(f"{tag}_hw", [128, NCHUNK, TT], BF16)
    ps_g = ctx.psum(f"{tag}_psg", [128, 2, TT], F32)
    ps_u = ctx.psum(f"{tag}_psu", [128, 2, TT], F32)
    ps_d = ctx.psum(f"{tag}_psd", [128, 2, TT], F32)
    ps_w = ctx.psum(f"{tag}_psw", [128, 2, TT], F32)
    ps_t = ps_d[:, 0, 0:16]
    ident32 = ctx.bufs["ident_f32"]
    ones_row = ctx.bufs["ones_row"]

    s_wr = f"{tag}_swr"
    s_temp = f"{tag}_stemp"
    s_xng = f"{tag}_sxng"
    s_rmm = f"{tag}_srmm"
    s_inh = f"{tag}_sinh"
    s_rows = f"{tag}_srows"
    s_rtp = f"{tag}_srtp"
    s_wfb = f"{tag}_swfb"
    s_tm = f"{tag}_stm"
    s_btp = f"{tag}_sbtp"
    s_wrow = f"{tag}_swrow"
    s_wl = [f"{tag}_swlg", f"{tag}_swlu", f"{tag}_swld"]
    s_gmm = f"{tag}_sgmm"
    s_umm = f"{tag}_summ"
    s_dmm = f"{tag}_sdmm"
    s_sil = f"{tag}_ssil"
    s_h0 = f"{tag}_sh0"
    s_hw = f"{tag}_shw"
    s_wbm = f"{tag}_swbm"
    s_wbe = f"{tag}_swbe"
    s_yacc = f"{tag}_syacc"
    s_yout = f"{tag}_syout"

    @blk.gpsimd
    def _(g):
        ctx.inc(g.dma_start(out=wrt.ap(), in_=wr_h.rearrange("(a p) f -> p a f", p=128)),
                s_wr, 16)
        ctx.inc(g.dma_start(out=rt[:1, 0:1], in_=temp_h), s_temp, 16)

    s_rtc = f"{tag}_srtc"

    @blk.vector
    def _(v):
        ctx.wait(v, s_temp, 16)
        ctx.inc(v.reciprocal(rt[:1, 1:2], rt[:1, 0:1]), s_rtc)

    C = dict(rmm=0, inh=0, rows=0, rtp=0, wfb=0, tm=0, btp=0, wrow=0,
             wl=0, xng=0, gmm=0, umm=0, dmm=0, sil=0, h0=0, hw=0, wbm=0,
             wbe=0, yacc=0, yout=0)

    first = [True]
    for g in range(ngr):
        @blk.sync
        def _(s, g=g):
            if g > 0:
                s.wait_ge(ctx.sem(s_gmm), C["gmm"])
                s.wait_ge(ctx.sem(s_rmm), C["rmm"])
            ctx.inc(s.dma_start(out=xng.ap(), in_=xn3[:, :, g * GT:(g + 1) * GT]),
                    s_xng, 16)

        @blk.gpsimd
        def _(gp, g=g):
            for e in range(E):
                idx = g * E + e
                if idx > 0:
                    gp.wait_ge(ctx.sem(s_gmm), 32 * idx)
                    gp.wait_ge(ctx.sem(s_umm), 32 * idx)
                    gp.wait_ge(ctx.sem(s_dmm), 32 * idx)
                ctx.inc(gp.dma_start(out=wgt.ap(),
                                     in_=wg_h[e].rearrange("(a p) f -> p a f", p=128)),
                        s_wl[0], 16)
                ctx.inc(gp.dma_start(out=wut.ap(),
                                     in_=wu_h[e].rearrange("(a p) f -> p a f", p=128)),
                        s_wl[1], 16)
                ctx.inc(gp.dma_start(out=wdt.ap(),
                                     in_=wd_h[e].rearrange("(a p) f -> p a f", p=128)),
                        s_wl[2], 16)

        @blk.tensor
        def _(t, g=g):
            if first[0]:
                ctx.wait(t, "c_const")
                ctx.wait(t, s_wr, 16)
                t.wait_ge(ctx.sem(s_rtc), 1)
                mm = t.matmul(ps_w[0:8, 0, 0:1], ones_row[:1, 0:8], rt[:1, 1:2],
                              start=True, stop=True)
                ctx.inc(mm, s_rmm)
                C["rmm"] += 1
                first[0] = False
            t.wait_ge(ctx.sem(s_xng), 16 * (g + 1))
            if g > 0:
                t.wait_ge(ctx.sem(s_sil), C["gmm"])
            for k in range(ntg):
                if C["rmm"] >= 2 + 1:
                    t.wait_ge(ctx.sem(s_inh), C["inh"] - 1)
                for a in range(NCHUNK):
                    mm = t.matmul(ps_g[0:8, C["rmm"] % 2, :], wrt[:, a, :],
                                  xng[:, a, k * TT:(k + 1) * TT],
                                  start=(a == 0), stop=(a == NCHUNK - 1))
                ctx.inc(mm, s_rmm)
                C["rmm"] += 1
                C["inh"] += 1
            if g > 0:
                t.wait_ge(ctx.sem(s_yacc), C["dmm"])
            for i in range(ntile):
                t.wait_ge(ctx.sem(s_rows), C["rows"] + 4)
                if C["rtp"] >= 1:
                    t.wait_ge(ctx.sem(s_wfb), C["wfb"] - 1 + 1)
                t.transpose(ps_t[:, 0:8], rows[0:8, 2, i * 128:(i + 1) * 128],
                            ident32[0:8, 0:8])
                mm = t.transpose(ps_t[:, 8:16], rows[0:8, 3, i * 128:(i + 1) * 128],
                                 ident32[0:8, 0:8])
                ctx.inc(mm, s_rtp)
                C["rtp"] += 1
                C["wfb"] += 1
            C["rows"] += 4
            if g > 0:
                t.wait_ge(ctx.sem(s_wbe), C["wbm"])
            for i in range(ntile):
                t.wait_ge(ctx.sem(s_tm), C["tm"] + i + 1)
                if C["btp"] >= 1:
                    t.wait_ge(ctx.sem(s_wrow), C["wrow"] - 1 + 1)
                mm = t.transpose(ps_w[0:8, 0, 0:128], tmA[:, i, :], ident32[:, 0:128])
                ctx.inc(mm, s_btp)
                C["btp"] += 1
                C["wrow"] += 1
            C["tm"] += ntile
            for e in range(E):
                idx = g * E + e
                for w_i in range(3):
                    t.wait_ge(ctx.sem(s_wl[w_i]), 16 * (idx + 1))
                for k in range(ntg):
                    it = idx * ntg + k
                    t.wait_ge(ctx.sem(s_wrow), (g + 1) * ntile)
                    if it >= 2:
                        t.wait_ge(ctx.sem(s_wbe), it - 1)
                    mm = t.matmul(ps_w[:, it % 2, :],
                                  ctx.bufs["sel8"][0:8, e * 128:(e + 1) * 128],
                                  wrow[0:8, k * TT:(k + 1) * TT],
                                  start=True, stop=True)
                    ctx.inc(mm, s_wbm)
                    C["wbm"] += 1
                    for hc in range(NCHUNK):
                        hcg = C["gmm"]
                        if hcg >= 2:
                            t.wait_ge(ctx.sem(s_sil), hcg - 1)
                            t.wait_ge(ctx.sem(s_h0), hcg - 1)
                        for a in range(NCHUNK):
                            mm = t.matmul(ps_g[:, hcg % 2, :],
                                          wgt[:, a, hc * 128:(hc + 1) * 128],
                                          xng[:, a, k * TT:(k + 1) * TT],
                                          start=(a == 0), stop=(a == NCHUNK - 1))
                        ctx.inc(mm, s_gmm)
                        for a in range(NCHUNK):
                            mm = t.matmul(ps_u[:, hcg % 2, :],
                                          wut[:, a, hc * 128:(hc + 1) * 128],
                                          xng[:, a, k * TT:(k + 1) * TT],
                                          start=(a == 0), stop=(a == NCHUNK - 1))
                        ctx.inc(mm, s_umm)
                        C["gmm"] += 1
                        C["umm"] += 1
                    t.wait_ge(ctx.sem(s_hw), C["gmm"])
                    for dc in range(NCHUNK):
                        dcg = C["dmm"]
                        if dcg >= 2:
                            t.wait_ge(ctx.sem(s_yacc), dcg - 1)
                        for a in range(NCHUNK):
                            mm = t.matmul(ps_d[:, dcg % 2, :],
                                          wdt[:, a, dc * 128:(dc + 1) * 128],
                                          hw[:, a, :], start=(a == 0),
                                          stop=(a == NCHUNK - 1))
                        ctx.inc(mm, s_dmm)
                        C["dmm"] += 1

        @blk.scalar
        def _(sc, g=g):
            if g == 0:
                ctx.wait(sc, "c_const")
                sc.wait_ge(ctx.sem(s_rtc), 2)
            base_rmm = 1 + g * (ntg + 0)
            for k in range(ntg):
                sc.wait_ge(ctx.sem(s_rmm), base_rmm + k + 1)
                ctx.inc(sc.activation(rows[:, 0, k * TT:(k + 1) * TT],
                                      ps_g[0:8, (base_rmm + k) % 2, :],
                                      AF.Sigmoid, scale=rtcol[:, 0:1]), s_inh)
            sc.wait_ge(ctx.sem(s_inh), (g + 1) * ntg)
            a1 = sc.activation(rows[:, 1, :], rows[:, 0, :], AF.Abs, scale=-1.0,
                               bias=kap(ctx, GC, 8))
            ctx.seq(sc, a1)
            ctx.inc(sc.activation(rows[:, 2, :], rows[:, 1, :], AF.Exp, scale=-10.0),
                    s_rows)
            a3 = sc.activation(rows[:, 3, :], rows[:, 1, :], AF.Exp,
                               scale=float(np.float32(-1.0 / 0.3)))
            ctx.inc(a3, s_rows)
            for e in range(E):
                for k in range(ntg):
                    for hc in range(NCHUNK):
                        hcg = C["sil"]
                        sc.wait_ge(ctx.sem(s_gmm), hcg + 1)
                        if hcg >= 2:
                            sc.wait_ge(ctx.sem(s_h0), hcg - 1)
                        ctx.inc(sc.activation(sil[:, hcg % 2], ps_g[:, hcg % 2],
                                              AF.Silu), s_sil)
                        C["sil"] += 1

        @blk.vector
        def _(v, g=g):
            if g == 0:
                v.wait_ge(ctx.sem(s_rtc), 1)
                v.wait_ge(ctx.sem(s_rmm), 1)
                ctx.inc(v.tensor_copy(rtcol[:], ps_w[0:8, 0, 0:1]), s_rtc)
            v.wait_ge(ctx.sem(s_rows), g * 4 + 2)
            ctx.seq(v, v.tensor_scalar(wrow[:, 0:GT], rows[:, 0, :], kap(ctx, GL, 8),
                                       None, ALU.is_ge))
            ctx.seq(v, v.tensor_tensor(rows[:, 2, :], rows[:, 2, :], wrow[:, 0:GT],
                                       ALU.mult))
            ctx.seq(v, v.tensor_scalar(wrow[:, 0:GT], rows[:, 0, :], kap(ctx, 0.5, 8),
                                       None, ALU.is_le))
            ctx.inc(v.tensor_tensor(rows[:, 2, :], rows[:, 2, :], wrow[:, 0:GT],
                                    ALU.mult), s_rows, 2)
            for i in range(ntile):
                v.wait_ge(ctx.sem(s_rtp), g * ntile + i + 1)
                vv = ctx.inc(v.tensor_copy(wfb[:, i, :], ps_t[:]), s_wfb)
                v.wait_ge(ctx.sem(s_wfb), vv)
                w_tm = wfb[:, i, 0:8]
                fb_tm = wfb[:, i, 8:16]
                A = tmA[:, i, :]
                ctx.seq(v, v.reduce_max(tmc[:, 0:1], fb_tm, axis=AX.X))
                ctx.seq(v, v.tensor_scalar(A, fb_tm, tmc[:, 0:1], None, ALU.is_equal))
                ctx.seq(v, v.scalar_tensor_tensor(A, A, kap(ctx, -1e9), fb_tm,
                                                  ALU.mult, ALU.add))
                ctx.seq(v, v.reduce_max(tmc[:, 1:2], A, axis=AX.X))
                ctx.seq(v, v.tensor_scalar(A, fb_tm, tmc[:, 1:2], None, ALU.is_ge))
                ctx.seq(v, v.tensor_tensor(A, fb_tm, A, ALU.mult))
                ctx.seq(v, v.reduce_sum(tmc[:, 2:3], A, axis=AX.X))
                ctx.seq(v, v.tensor_scalar(tmc[:, 2:3], tmc[:, 2:3], kap(ctx, 1e-8),
                                           None, ALU.max))
                ctx.seq(v, v.reciprocal(tmc[:, 3:4], tmc[:, 2:3]))
                ctx.seq(v, v.tensor_scalar(A, A, tmc[:, 3:4], None, ALU.mult))
                ctx.seq(v, v.reduce_sum(tmc[:, 0:1], w_tm, axis=AX.X))
                ctx.seq(v, v.tensor_scalar(tmc[:, 1:2], tmc[:, 0:1], kap(ctx, 1e-8),
                                           None, ALU.is_lt))
                ctx.seq(v, v.tensor_sub(A, A, w_tm))
                ctx.seq(v, v.scalar_tensor_tensor(A, A, tmc[:, 1:2], w_tm,
                                                  ALU.mult, ALU.add))
                ctx.seq(v, v.reduce_sum(tmc[:, 4:5], A, axis=AX.X))
                ctx.seq(v, v.tensor_scalar(tmc[:, 4:5], tmc[:, 4:5], kap(ctx, 1e-8),
                                           None, ALU.max))
                ctx.seq(v, v.reciprocal(tmc[:, 5:6], tmc[:, 4:5]))
                ctx.inc(v.tensor_scalar(A, A, tmc[:, 5:6], None, ALU.mult), s_tm)
            for i in range(ntile):
                v.wait_ge(ctx.sem(s_btp), g * ntile + i + 1)
                ctx.inc(v.tensor_copy(wrow[:, i * 128:(i + 1) * 128],
                                      ps_w[0:8, 0, 0:128]), s_wrow)
            for e in range(E):
                idx = g * E + e
                for k in range(ntg):
                    it = idx * ntg + k
                    v.wait_ge(ctx.sem(s_wbm), it + 1)
                    if it >= 1:
                        v.wait_ge(ctx.sem(s_hw), it * NCHUNK)
                        v.wait_ge(ctx.sem(s_dmm), it * NCHUNK)
                    ctx.inc(v.tensor_copy(wb_sb[:], ps_w[:, it % 2, :]), s_wbe)
                    for hc in range(NCHUNK):
                        hcg = C["h0"]
                        v.wait_ge(ctx.sem(s_umm), hcg + 1)
                        v.wait_ge(ctx.sem(s_sil), hcg + 1)
                        if hcg >= 1:
                            v.wait_ge(ctx.sem(s_hw), hcg)
                        vv = ctx.inc(v.scalar_tensor_tensor(h0[:, hcg % 2],
                                                            ps_u[:, hcg % 2], 1.0,
                                                            sil[:, hcg % 2],
                                                            ALU.mult, ALU.mult), s_h0)
                        v.wait_ge(ctx.sem(s_h0), vv)
                        ctx.inc(v.tensor_tensor(hw[:, hc, :], h0[:, hcg % 2],
                                                wb_sb[:], ALU.mult), s_hw)
                        C["h0"] += 1
                        C["hw"] += 1
                    for dc in range(NCHUNK):
                        dcg = C["yacc"]
                        v.wait_ge(ctx.sem(s_dmm), dcg + 1)
                        ys = y_acc[:, dc, k * TT:(k + 1) * TT]
                        if e == 0:
                            if g > 0:
                                v.wait_ge(ctx.sem(s_yout), 16 * NCHUNK * g)
                            ctx.inc(v.tensor_copy(ys, ps_d[:, dcg % 2, :]), s_yacc)
                        else:
                            v.wait_ge(ctx.sem(s_yacc), dcg - 32 + 1)
                            ctx.inc(v.tensor_tensor(ys, ys, ps_d[:, dcg % 2, :],
                                                    ALU.add), s_yacc)
                        C["yacc"] += 1

        @blk.gpsimd
        def _(g_, g=g):
            for sem, val in gates_acc:
                g_.wait_ge(ctx.sem(sem), val)
            g_.wait_ge(ctx.sem(s_yacc), (g + 1) * E * ntg * NCHUNK)
            for dc in range(NCHUNK):
                ctx.inc(g_.dma_start(out=x3[:, dc, g * GT:(g + 1) * GT],
                                     in_=y_acc[:, dc, :], accum_op=ALU.add), s_yout, 16)

    return dict(done=[(s_yout, ctx.val(s_yout))])


def emit_l3ffn(ctx, tag, xn3, x3, gamma_h, beta_h, wr_h, gu_h, dn_h, NTOK,
               gates_acc=()):
    blk = ctx.blk
    TT = 512
    GT = 2048
    ngr = NTOK // GT
    ntg = GT // TT
    ntile = GT // 128
    E = 8
    DFF = 2048

    ps_g = ctx.psum(f"{tag}_psg", [128, 2, TT], F32)
    ps_u = ctx.psum(f"{tag}_psu", [128, 2, TT], F32)
    ps_d = ctx.psum(f"{tag}_psd", [128, 2, TT], F32)
    ps_w = ctx.psum(f"{tag}_psw", [128, 2, TT], F32)
    pst = ps_g[:, 0, :].bitcast(BF16).rearrange("p (a f) -> p a f", f=128)
    ab = {"stg": ctx.sbuf(f"{tag}_Astg", [128, 1, DFF], BF16),
          "wtmp": ctx.sbuf(f"{tag}_Awtmp", [128, 1, 16, 128], BF16)}
    guT = []
    dnT = []
    prev = None
    for e in range(E):
        wT, gate, ev = emit_wT_to_dram(ctx, f"{tag}_gu{e}", gu_h[e], 2 * DFF, D,
                                       pst=pst, prev_ev=prev, bufs=ab,
                                       sem_tag=f"{tag}_wA")
        guT.append((wT, gate))
        prev = ev
        wT, gate, ev = emit_wT_to_dram(ctx, f"{tag}_dn{e}", dn_h[e], D, DFF,
                                       pst=pst, prev_ev=prev, bufs=ab,
                                       sem_tag=f"{tag}_wA")
        dnT.append((wT, gate))
        prev = ev
    wA_gate = []
    for lst in (guT, dnT):
        for wT, gate in lst:
            wA_gate += gate
    pst32 = ps_u[:, 0, 0:8]
    rows = ctx.sbuf(f"{tag}_rows", [1, 2, 2, 512], F32)
    srow = rows[0:1, 0].rearrange("p a f -> p (a f)")
    gcol, cg1 = emit_colvec_load(ctx, f"{tag}_gam", gamma_h, D, pst32=pst32,
                                 row_buf=srow)
    bcol, cg2 = emit_colvec_load(ctx, f"{tag}_bet", beta_h, D, pst32=pst32,
                                 prev=cg1, row_buf=srow)
    s_scales = cg2

    wrstg = ab["stg"][0:8, 0, 0:D]
    wrtT = ctx.sbuf(f"{tag}_wrt", [128, NCHUNK, E], BF16)
    s_wrs = f"{tag}_swrs"
    s_wrt = f"{tag}_swrt"
    s_wrte = f"{tag}_swrte"

    @blk.gpsimd
    def _(g):
        g.wait_ge(ctx.sem(prev[0]), prev[1])
        ctx.inc(g.dma_start(out=wrstg, in_=wr_h), s_wrs, 16)

    @blk.tensor
    def _(t):
        ctx.wait(t, "c_const")
        ctx.wait(t, s_wrs, 16)
        t.wait_ge(ctx.sem(prev[0]), prev[1])
        for a in range(NCHUNK):
            mm = t.transpose(pst[:, a, 0:8],
                             wrstg[0:8, a * 128:(a + 1) * 128], ctx.bufs["ident_bf"][0:8, 0:8])
        ctx.inc(mm, s_wrt)

    @blk.vector
    def _(v):
        v.wait_ge(ctx.sem(s_wrt), 1)
        ctx.inc(v.tensor_copy(wrtT.ap(), pst[:, :, 0:8]), s_wrte)

    xng = ctx.sbuf(f"{tag}_xng", [128, NCHUNK, GT], BF16)
    y_acc = ctx.sbuf(f"{tag}_yacc", [128, NCHUNK, GT], F32)
    wga = ctx.sbuf(f"{tag}_wga", [128, NCHUNK, 1024], BF16)
    wua = ctx.sbuf(f"{tag}_wua", [128, NCHUNK, 1024], BF16)
    wdn = ctx.sbuf(f"{tag}_wdn", [128, NCHUNK, 1024], BF16)
    xe = ctx.sbuf(f"{tag}_xe", [128, NCHUNK, TT], BF16)
    hbf = ctx.sbuf(f"{tag}_hbf", [128, NCHUNK, TT], BF16)
    sil = ctx.sbuf(f"{tag}_sil", [128, 2, TT], BF16)
    t1 = ctx.sbuf(f"{tag}_t1", [128, TT], F32)
    sq = t1
    rrow = ctx.sbuf(f"{tag}_rrow", [1, 2, 2, TT], F32)
    grow = ctx.sbuf(f"{tag}_grow", [8, GT], F32)
    gtm = ctx.sbuf(f"{tag}_gtm", [128, ntile, 8], F32)
    tmc = ctx.sbuf(f"{tag}_tmc", [128, 6], F32)
    ps_s = ps_w[0:1, :, :]
    ps_b = ps_d
    ones_col = ctx.bufs["ones_col"]
    ones_col_bf = ctx.bufs["ones_col_bf"]
    ones_row = ctx.bufs["ones_row"]
    ident32 = ctx.bufs["ident_f32"]
    sel8 = ctx.bufs["sel8"]

    s_xng = f"{tag}_sxng"
    s_sq = f"{tag}_ssq"
    s_st = f"{tag}_sst"
    s_rowcp = f"{tag}_srowcp"
    s_ln = f"{tag}_sln"
    s_row = f"{tag}_srow"
    s_bc = f"{tag}_sbc"
    s_xf = f"{tag}_sxf"
    s_rmm = f"{tag}_srmm"
    s_prob = f"{tag}_sprob"
    s_rtp = f"{tag}_srtp"
    s_gtm = f"{tag}_sgtm"
    s_tm = f"{tag}_stm"
    s_btp = f"{tag}_sbtp"
    s_grow = f"{tag}_sgrow"
    s_wl = [f"{tag}_swla", f"{tag}_swlb", f"{tag}_swlc"]
    s_wbm = f"{tag}_swbm"
    s_xe = f"{tag}_sxe"
    s_gmm = f"{tag}_sgmm"
    s_umm = f"{tag}_summ"
    s_dmm = f"{tag}_sdmm"
    s_sil = f"{tag}_ssil"
    s_h = f"{tag}_sh"
    s_yacc = f"{tag}_syacc"
    s_yout = f"{tag}_syout"

    C = dict(sq=0, st=0, rowcp=0, ln=0, row=0, bc=0, xf=0, rmm=0, prob=0, rtp=0,
             gtm=0, tm=0, btp=0, grow=0, wl=0, wbm=0, xe=0, gmm=0, umm=0, dmm=0,
             sil=0, h=0, yacc=0)

    for g in range(ngr):
        @blk.sync
        def _(s, g=g):
            if g > 0:
                s.wait_ge(ctx.sem(s_gmm), C["gmm"])
            ctx.inc(s.dma_start(out=xng.ap(), in_=xn3[:, :, g * GT:(g + 1) * GT]),
                    s_xng, 16)

        @blk.gpsimd
        def _(gp, g=g):
            for e in range(E):
                idx = g * E + e
                for r in range(2):
                    ir = idx * 2 + r
                    if ir > 0:
                        gp.wait_ge(ctx.sem(s_gmm), 32 * ir)
                        gp.wait_ge(ctx.sem(s_umm), 32 * ir)
                        gp.wait_ge(ctx.sem(s_dmm), 32 * ir)
                    if ir == 0:
                        for sem, val in wA_gate:
                            gp.wait_ge(ctx.sem(sem), val)
                    ctx.inc(gp.dma_start(out=wga.ap(),
                                         in_=guT[e][0][:, :, r * 1024:(r + 1) * 1024]),
                            s_wl[0], 16)
                    ctx.inc(gp.dma_start(out=wua.ap(),
                                         in_=guT[e][0][:, :, DFF + r * 1024:DFF + (r + 1) * 1024]),
                            s_wl[1], 16)
                    ctx.inc(gp.dma_start(out=wdn.ap(),
                                         in_=dnT[e][0][:, r * NCHUNK:(r + 1) * NCHUNK, :]),
                            s_wl[2], 16)

        @blk.scalar
        def _(sc, g=g):
            if g == 0:
                ctx.wait(sc, "c_const")
            for k in range(ntg):
                kb = g * ntg + k
                sc.wait_ge(ctx.sem(s_xng), 16 * (g + 1))
                sc.wait_ge(ctx.sem(s_xf), kb * NCHUNK)
                for c in range(NCHUNK):
                    ic = C["sq"]
                    if ic > 0:
                        sc.wait_ge(ctx.sem(s_st), 2 * ic)
                    ctx.inc(sc.activation(sq[:], xng[:, c, k * TT:(k + 1) * TT],
                                          AF.Square), s_sq)
                    C["sq"] += 1
                kb = g * ntg + k
                sc.wait_ge(ctx.sem(s_rowcp), 2 * (kb + 1))
                a1 = sc.activation(rrow[:1, kb % 2, 0], rows[:1, kb % 2, 1], AF.Ln,
                                   bias=kap(ctx, 1e-5, 1))
                ctx.seq(sc, a1, f"{tag}_scseq")
                ctx.inc(sc.activation(rrow[:1, kb % 2, 0], rrow[:1, kb % 2, 0],
                                      AF.Exp, scale=-0.5), s_ln)
            for i in range(ntile):
                sc.wait_ge(ctx.sem(s_gtm), C["gtm"] + i + 1)
                ctx.inc(sc.activation(gtm[:, i, :], gtm[:, i, :], AF.Exp,
                                      bias=tmc[:, 0:1], scale=1.0), s_prob)
            C["gtm"] += ntile
            for e in range(E):
                for r in range(2):
                    for k in range(ntg):
                        for hc in range(NCHUNK):
                            hcg = C["sil"]
                            sc.wait_ge(ctx.sem(s_gmm), hcg + 1)
                            if hcg >= 2:
                                sc.wait_ge(ctx.sem(s_h), hcg - 1)
                            ctx.inc(sc.activation(sil[:, hcg % 2], ps_g[:, hcg % 2],
                                                  AF.Silu), s_sil)
                            C["sil"] += 1

        @blk.tensor
        def _(t, g=g):
            if g == 0:
                ctx.wait(t, "c_const")
                t.wait_ge(ctx.sem(s_wrte), 1)
                t.wait_ge(ctx.sem(s_scales[0]), s_scales[1])
            t.wait_ge(ctx.sem(s_xng), 16 * (g + 1))
            if g > 0:
                t.wait_ge(ctx.sem(s_xe), C["xe"])
                t.wait_ge(ctx.sem(s_yacc), C["dmm"])
            for k in range(ntg):
                kb = g * ntg + k
                for c in range(NCHUNK):
                    ic = kb * NCHUNK + c
                    t.wait_ge(ctx.sem(s_sq), ic + 1)
                    if c == 0 and kb > 0:
                        t.wait_ge(ctx.sem(s_rowcp), 2 * kb)
                    t.matmul(ps_s[:1, 0], ones_col_bf[:], xng[:, c, k * TT:(k + 1) * TT],
                             start=(c == 0), stop=(c == NCHUNK - 1))
                    mm = t.matmul(ps_s[:1, 1], ones_col[:], sq[:],
                                  start=(c == 0), stop=(c == NCHUNK - 1))
                    ctx.inc(mm, s_st, 2)
                    C["st"] += 2
                t.wait_ge(ctx.sem(s_row), kb + 1)
                if kb > 0:
                    t.wait_ge(ctx.sem(s_xf), C["xf"])
                t.matmul(ps_b[:, 0, :], ones_row[:1, 0:128],
                         rrow[:1, kb % 2, 0], start=True, stop=True)
                mm = t.matmul(ps_b[:, 1, :], ones_row[:1, 0:128],
                              rrow[:1, kb % 2, 1], start=True, stop=True)
                ctx.inc(mm, s_bc)
                C["bc"] += 1
                C["xf"] += NCHUNK
            for k in range(ntg):
                t.wait_ge(ctx.sem(s_xf), (g + 1) * ntg * NCHUNK)
                if C["rmm"] >= 2:
                    t.wait_ge(ctx.sem(s_grow), C["rmm"] - 1)
                for a in range(NCHUNK):
                    mm = t.matmul(ps_w[0:8, C["rmm"] % 2, :], wrtT[:, a, :],
                                  xng[:, a, k * TT:(k + 1) * TT],
                                  start=(a == 0), stop=(a == NCHUNK - 1))
                ctx.inc(mm, s_rmm)
                C["rmm"] += 1
            for i in range(ntile):
                t.wait_ge(ctx.sem(s_grow), C["grow"] + i // 4 + 1)
                if C["rtp"] >= 1:
                    t.wait_ge(ctx.sem(s_gtm), C["rtp"])
                mm = t.transpose(pst32[:, 0:8], grow[0:8, i * 128:(i + 1) * 128],
                                 ident32[0:8, 0:8])
                ctx.inc(mm, s_rtp)
                C["rtp"] += 1
            C["grow"] += ntg
            for i in range(ntile):
                t.wait_ge(ctx.sem(s_tm), C["tm"] + i + 1)
                if C["btp"] >= 1:
                    t.wait_ge(ctx.sem(f"{tag}_sgrow2"), C["btp"])
                mm = t.transpose(ps_w[0:8, 0, 0:128], gtm[:, i, :], ident32[:, 0:128])
                ctx.inc(mm, s_btp)
                C["btp"] += 1
            C["tm"] += ntile
            for e in range(E):
                idx = g * E + e
                for r in range(2):
                    ir = idx * 2 + r
                    for w_i in range(3):
                        t.wait_ge(ctx.sem(s_wl[w_i]), 16 * (ir + 1))
                    for k in range(ntg):
                        it = ir * ntg + k
                        t.wait_ge(ctx.sem(f"{tag}_sgrow2"), (g + 1) * ntile)
                        iw = C["wbm"]
                        if iw >= 2:
                            t.wait_ge(ctx.sem(s_xe), (iw - 1) * NCHUNK)
                        mm = t.matmul(ps_w[:, iw % 2, :],
                                      sel8[0:8, e * 128:(e + 1) * 128],
                                      grow[0:8, k * TT:(k + 1) * TT],
                                      start=True, stop=True)
                        ctx.inc(mm, s_wbm)
                        C["wbm"] += 1
                        for hc in range(NCHUNK):
                            hcg = C["gmm"]
                            t.wait_ge(ctx.sem(s_xe), (iw + 1) * NCHUNK)
                            if hcg >= 2:
                                t.wait_ge(ctx.sem(s_sil), hcg - 1)
                                t.wait_ge(ctx.sem(s_h), hcg - 1)
                            for a in range(NCHUNK):
                                mm = t.matmul(ps_g[:, hcg % 2, :],
                                              wga[:, a, hc * 128:(hc + 1) * 128],
                                              xe[:, a, :],
                                              start=(a == 0), stop=(a == NCHUNK - 1))
                            ctx.inc(mm, s_gmm)
                            for a in range(NCHUNK):
                                mm2 = t.matmul(ps_u[:, hcg % 2, :],
                                               wua[:, a, hc * 128:(hc + 1) * 128],
                                               xe[:, a, :],
                                               start=(a == 0), stop=(a == NCHUNK - 1))
                            ctx.inc(mm2, s_umm)
                            C["gmm"] += 1
                            C["umm"] += 1
                        t.wait_ge(ctx.sem(s_h), C["gmm"])
                        for dc in range(NCHUNK):
                            dcg = C["dmm"]
                            if dcg >= 2:
                                t.wait_ge(ctx.sem(s_yacc), dcg - 1)
                            for a in range(NCHUNK):
                                mm = t.matmul(ps_d[:, dcg % 2, :],
                                              wdn[:, a, dc * 128:(dc + 1) * 128],
                                              hbf[:, a, :], start=(a == 0),
                                              stop=(a == NCHUNK - 1))
                            ctx.inc(mm, s_dmm)
                            C["dmm"] += 1

        @blk.vector
        def _(v, g=g):
            for k in range(ntg):
                kb = g * ntg + k
                pb = kb % 2
                v.wait_ge(ctx.sem(s_st), (kb + 1) * NCHUNK * 2)
                if kb >= 2:
                    v.wait_ge(ctx.sem(s_bc), kb - 1)
                    v.wait_ge(ctx.sem(s_ln), kb - 1)
                ctx.seq(v, v.tensor_copy(rows[:1, pb], ps_s[:1, :]))
                ctx.inc(v.engine_nop(), s_rowcp)
                ctx.seq(v, v.tensor_scalar_mul(rows[:1, pb, 0], rows[:1, pb, 0], 1.0 / D))
                ctx.seq(v, v.scalar_tensor_tensor(rrow[:1, pb, 1], rows[:1, pb, 0], 1.0,
                                                  rows[:1, pb, 0], ALU.mult, ALU.mult))
                ctx.seq(v, v.scalar_tensor_tensor(rows[:1, pb, 1], rows[:1, pb, 1],
                                                  1.0 / D, rrow[:1, pb, 1],
                                                  ALU.mult, ALU.subtract))
                ctx.inc(v.engine_nop(), s_rowcp)
                v.wait_ge(ctx.sem(s_ln), kb + 1)
                ctx.inc(v.scalar_tensor_tensor(rrow[:1, pb, 1], rows[:1, pb, 0], -1.0,
                                               rrow[:1, pb, 0], ALU.mult, ALU.mult),
                        s_row)
                v.wait_ge(ctx.sem(s_bc), kb + 1)
                for c in range(NCHUNK):
                    if kb * NCHUNK + c > 0:
                        v.wait_ge(ctx.sem(s_xf), kb * NCHUNK + c)
                    vv = ctx.inc(v.tensor_tensor(t1[:], xng[:, c, k * TT:(k + 1) * TT],
                                                 ps_b[:, 0, :], ALU.mult), f"{tag}_st1")
                    v.wait_ge(ctx.sem(f"{tag}_st1"), vv)
                    vv = ctx.inc(v.tensor_tensor(t1[:], t1[:], ps_b[:, 1, :],
                                                 ALU.add), f"{tag}_st1")
                    v.wait_ge(ctx.sem(f"{tag}_st1"), vv)
                    ctx.inc(v.tensor_scalar(xng[:, c, k * TT:(k + 1) * TT], t1[:],
                                            gcol[:, c:c + 1], bcol[:, c:c + 1],
                                            ALU.mult, ALU.add), s_xf)
            for k in range(ntg):
                kb = g * ntg + k
                v.wait_ge(ctx.sem(s_rmm), kb + 1)
                vv = ctx.inc(v.tensor_scalar(grow[0:8, k * TT:(k + 1) * TT],
                                             ps_w[0:8, kb % 2, :], kap(ctx, 1e4, 8),
                                             kap(ctx, -1e4, 8), ALU.min, ALU.max),
                             s_grow)
                v.wait_ge(ctx.sem(s_grow), vv)
            for i in range(ntile):
                v.wait_ge(ctx.sem(s_rtp), g * ntile + i + 1)
                A = gtm[:, i, :]
                vv = ctx.inc(v.tensor_copy(A, pst32[:, 0:8]), s_gtm0 := f"{tag}_sgtm0")
                v.wait_ge(ctx.sem(s_gtm0), vv)
                ctx.seq(v, v.reduce_max(tmc[:, 0:1], A, axis=AX.X, negate=True))
                ctx.inc(v.engine_nop(), s_gtm)
                v.wait_ge(ctx.sem(s_prob), g * ntile + i + 1)
                ctx.seq(v, v.reduce_sum(tmc[:, 1:2], A, axis=AX.X))
                ctx.seq(v, v.tensor_scalar(tmc[:, 1:2], tmc[:, 1:2], kap(ctx, 1e-8),
                                           None, ALU.add))
                ctx.seq(v, v.reciprocal(tmc[:, 2:3], tmc[:, 1:2]))
                ctx.seq(v, v.tensor_scalar(A, A, tmc[:, 2:3], kap(ctx, 1e-8),
                                           ALU.mult, ALU.max))
                ctx.seq(v, v.tensor_scalar(A, A, kap(ctx, 1.0), None, ALU.min))
                ctx.seq(v, v.reduce_max(tmc[:, 0:1], A, axis=AX.X))
                ctx.seq(v, v.tensor_scalar(sq[:, 0:8], A, tmc[:, 0:1], None, ALU.is_equal))
                ctx.seq(v, v.scalar_tensor_tensor(sq[:, 0:8], sq[:, 0:8], kap(ctx, -1e9),
                                                  A, ALU.mult, ALU.add))
                ctx.seq(v, v.reduce_max(tmc[:, 1:2], sq[:, 0:8], axis=AX.X))
                ctx.seq(v, v.tensor_scalar(sq[:, 0:8], A, tmc[:, 1:2], None, ALU.is_ge))
                ctx.inc(v.tensor_tensor(A, A, sq[:, 0:8], ALU.mult), s_tm)
            for i in range(ntile):
                v.wait_ge(ctx.sem(s_btp), g * ntile + i + 1)
                ctx.inc(v.tensor_copy(grow[:, i * 128:(i + 1) * 128],
                                      ps_w[0:8, 0, 0:128]), f"{tag}_sgrow2")
            for e in range(E):
                idx = g * E + e
                for r in range(2):
                    for k in range(ntg):
                        iw = (idx * 2 + r) * ntg + k
                        v.wait_ge(ctx.sem(s_wbm), iw + 1)
                        for c in range(NCHUNK):
                            if C["xe"] >= NCHUNK and c == 0:
                                v.wait_ge(ctx.sem(s_umm), iw * NCHUNK)
                            ctx.inc(v.tensor_tensor(xe[:, c, :],
                                                    xng[:, c, k * TT:(k + 1) * TT],
                                                    ps_w[:, iw % 2, :], ALU.mult), s_xe)
                            C["xe"] += 1
                        for hc in range(NCHUNK):
                            hcg = C["h"]
                            v.wait_ge(ctx.sem(s_umm), hcg + 1)
                            v.wait_ge(ctx.sem(s_sil), hcg + 1)
                            if hcg >= 1:
                                v.wait_ge(ctx.sem(s_dmm), (hcg // NCHUNK) * NCHUNK)
                            ctx.inc(v.scalar_tensor_tensor(hbf[:, hc, :],
                                                           ps_u[:, hcg % 2], 1.0,
                                                           sil[:, hcg % 2],
                                                           ALU.mult, ALU.mult), s_h)
                            C["h"] += 1
                        for dc in range(NCHUNK):
                            dcg = C["yacc"]
                            v.wait_ge(ctx.sem(s_dmm), dcg + 1)
                            ys = y_acc[:, dc, k * TT:(k + 1) * TT]
                            if e == 0 and r == 0:
                                if g > 0:
                                    v.wait_ge(ctx.sem(s_yout), 16 * NCHUNK * g)
                                ctx.inc(v.tensor_copy(ys, ps_d[:, dcg % 2, :]), s_yacc)
                            else:
                                v.wait_ge(ctx.sem(s_yacc), dcg - 32 + 1)
                                ctx.inc(v.tensor_tensor(ys, ys, ps_d[:, dcg % 2, :],
                                                        ALU.add), s_yacc)
                            C["yacc"] += 1

        @blk.gpsimd
        def _(g_, g=g):
            for sem, val in gates_acc:
                g_.wait_ge(ctx.sem(sem), val)
            g_.wait_ge(ctx.sem(s_yacc), (g + 1) * E * 2 * ntg * NCHUNK)
            for dc in range(NCHUNK):
                ctx.inc(g_.dma_start(out=x3[:, dc, g * GT:(g + 1) * GT],
                                     in_=y_acc[:, dc, :], accum_op=ALU.add), s_yout, 16)

    return dict(done=[(s_yout, ctx.val(s_yout))])




def emit_l2attn(ctx, tag, xn3, x3, wqkv_h, wo_h, NTOK, gates_acc=()):
    blk = ctx.blk
    TT = 512
    ST = 2048
    nb = NTOK // ST
    ntt = ST // TT
    nkc = ST // 128
    H = 16
    NPV = H * ntt * nkc

    wslab = ctx.sbuf(f"{tag}_wslab", [128, NCHUNK, H * 64], BF16)
    woT = ctx.sbuf(f"{tag}_wo", [128, NCHUNK, D], BF16)
    xnt = ctx.sbuf(f"{tag}_xnt", [128, 2, NCHUNK, TT], BF16)
    q_fm = ctx.sbuf(f"{tag}_qfm", [128, NCHUNK, ST], BF16)
    k_fm = ctx.sbuf(f"{tag}_kfm", [128, NCHUNK, ST], BF16)
    v65 = ctx.sbuf(f"{tag}_v65", [128, nkc, H, 65], BF16)
    o_fm = ctx.sbuf(f"{tag}_ofm", [128, NCHUNK, ST], BF16)
    vtmp = ctx.sbuf(f"{tag}_vtmp", [128, TT], BF16)
    p_sb = ctx.sbuf(f"{tag}_psb", [128, 2, TT], BF16)
    o_raw = ctx.sbuf(f"{tag}_oraw", [64, TT], F32)
    rrow = ctx.sbuf(f"{tag}_rrow", [1, TT], F32)
    ot = ctx.sbuf(f"{tag}_ot", [128, 2, TT], F32)
    ps_A = ctx.psum(f"{tag}_psA", [128, 2, TT], F32)
    ps_o = ctx.psum(f"{tag}_pso", [128, 2, TT], F32)
    ps_b = ctx.psum(f"{tag}_psb2", [128, TT], F32)
    ps_t = ctx.psum(f"{tag}_pst", [128, 4, 128], BF16)
    ident = ctx.bufs["ident_bf"]
    ones_row = ctx.bufs["ones_row"]

    s_w = f"{tag}_sw"
    s_xn = [f"{tag}_sxn0", f"{tag}_sxn1"]
    s_pj = f"{tag}_spj"
    s_pA = f"{tag}_spA"
    s_vev = f"{tag}_svev"
    s_vt = f"{tag}_svt"
    s_ve = f"{tag}_sve"
    s_vm = f"{tag}_svm"
    s_qk = f"{tag}_sqk"
    s_st = f"{tag}_sst"
    s_ex = f"{tag}_sex"
    s_pv = f"{tag}_spv"
    s_rc = f"{tag}_src"
    s_bc = f"{tag}_sbc"
    s_on = f"{tag}_son"
    s_wo = f"{tag}_swo"
    s_ot = f"{tag}_sot"
    s_oacc = [f"{tag}_soacc0", f"{tag}_soacc1"]

    C = dict(pj=0, pA=0, vev=0, vevv=0, vt=0, ve=0, qk=0, st=0, ex=0, pv=0, rc=0,
             bc=0, on=0, wo=0, ot=0, oacc=0)

    s_ws = f"{tag}_sws"

    @blk.gpsimd
    def _(g):
        ctx.inc(g.dma_start(out=woT.ap(),
                            in_=wo_h.rearrange("(c p) f -> p c f", p=128)), s_w, 16)

    for b in range(nb):
        @blk.gpsimd
        def _(g, b=b):
            for a in range(3):
                sl = b * 3 + a
                if sl > 0:
                    g.wait_ge(ctx.sem(s_pj), 32 * sl)
                for h in range(H):
                    ctx.inc(g.dma_start(out=wslab[:, :, h * 64:(h + 1) * 64],
                                        in_=wqkv_h[h, a].rearrange("(c p) n -> p c n",
                                                                   p=128)), s_ws, 16)

        @blk.sync
        def _(s, b=b):
            for a in range(3):
                for kt in range(ntt):
                    i = (b * 3 + a) * ntt + kt
                    pb = i % 2
                    if i >= 2:
                        s.wait_ge(ctx.sem(s_pj), (i - 1) * 8)
                    ctx.inc(s.dma_start(out=xnt[:, pb],
                                        in_=xn3[:, :, (b * ntt + kt) * TT:
                                                (b * ntt + kt + 1) * TT]),
                            s_xn[pb], 16)

        @blk.tensor
        def _(t, b=b):
            if b == 0:
                ctx.wait(t, "c_const")
                ctx.wait(t, s_w, 16)
            if b > 0:
                t.wait_ge(ctx.sem(s_ot), 32 * b)
            for a in range(3):
                t.wait_ge(ctx.sem(s_ws), 16 * 16 * (b * 3 + a + 1))
                for kt in range(ntt):
                    i = (b * 3 + a) * ntt + kt
                    pb = i % 2
                    t.wait_ge(ctx.sem(s_xn[pb]), 16 * (i // 2 + 1))
                    for pr in range(H // 2):
                        gi = C["pj"]
                        gl = gi - 96 * b
                        if gl >= 2:
                            pa = 2 if (gl - 2) >= 64 else 0
                            if pa == 2:
                                t.wait_ge(ctx.sem(s_vev), 32 * b + (gl - 2) - 64 + 1)
                            else:
                                t.wait_ge(ctx.sem(s_qk), 64 * b + (gl - 2) + 1)
                        for c in range(NCHUNK):
                            mm = t.matmul(ps_A[:, gi % 2, :],
                                          wslab[:, c, pr * 128:(pr + 1) * 128],
                                          xnt[:, pb, c], start=(c == 0),
                                          stop=(c == NCHUNK - 1))
                        ctx.inc(mm, s_pj)
                        C["pj"] += 1
                        if a == 2:
                            t.wait_ge(ctx.sem(s_vev), C["vev"] + 1)
                            t.wait_ge(ctx.sem(s_ve), C["vt"])
                            for j in range(4):
                                mm = t.transpose(ps_t[:, j, :],
                                                 vtmp[:, j * 128:(j + 1) * 128],
                                                 ident[:])
                            ctx.inc(mm, s_vt)
                            C["vt"] += 1
                            C["vev"] += 1
            t.wait_ge(ctx.sem(s_vm), b + 1)
            t.wait_ge(ctx.sem(s_qk), (b + 1) * ntt * 16)
            t.wait_ge(ctx.sem(s_ve), (b + 1) * ntt * 8)
            for h in range(H):
                hb = (h % 2) * 64
                hc = h // 2
                for tqb in range(ntt):
                    io = C["pv"] // nkc
                    if io >= 2:
                        t.wait_ge(ctx.sem(s_on), io - 1)
                    for tk in range(nkc):
                        ist = C["st"]
                        if ist - b * NPV >= 2:
                            t.wait_ge(ctx.sem(s_ex), ist - 1)
                        mm = t.matmul(ps_A[:, ist % 2, :],
                                      k_fm[hb:hb + 64, hc, tk * 128:(tk + 1) * 128],
                                      q_fm[hb:hb + 64, hc, tqb * TT:(tqb + 1) * TT],
                                      start=True, stop=True,
                                      tile_position=(hb, 0))
                        ctx.inc(mm, s_st)
                        C["st"] += 1
                        t.wait_ge(ctx.sem(s_ex), C["st"])
                        mm = t.matmul(ps_o[0:65, io % 2, :],
                                      v65[:, tk, h, :], p_sb[:, ist % 2, :],
                                      start=(tk == 0), stop=(tk == nkc - 1))
                        ctx.inc(mm, s_pv)
                        C["pv"] += 1
                    t.wait_ge(ctx.sem(s_rc), 2 * io + 2)
                    if io >= 1:
                        t.wait_ge(ctx.sem(s_on), io)
                    mm = t.matmul(ps_b[0:64, :], ones_row[:1, 0:64], rrow[:1, :],
                                  start=True, stop=True)
                    ctx.inc(mm, s_bc)
                    C["bc"] += 1
            t.wait_ge(ctx.sem(s_on), (b + 1) * H * ntt)
            for kt in range(ntt):
                for dc in range(NCHUNK):
                    iw = C["wo"]
                    if iw - 32 * b >= 2:
                        t.wait_ge(ctx.sem(s_ot), iw - 1)
                    for c in range(NCHUNK):
                        mm = t.matmul(ps_A[:, iw % 2, :],
                                      woT[:, c, dc * 128:(dc + 1) * 128],
                                      o_fm[:, c, kt * TT:(kt + 1) * TT],
                                      start=(c == 0), stop=(c == NCHUNK - 1))
                    ctx.inc(mm, s_wo)
                    C["wo"] += 1

        @blk.scalar
        def _(sc, b=b):
            if b == 0:
                ctx.wait(sc, "c_const")
            for n in range(NPV):
                ie = C["ex"]
                sc.wait_ge(ctx.sem(s_st), ie + 1)
                if ie >= 2:
                    sc.wait_ge(ctx.sem(s_pv), ie - 1)
                e = sc.activation(p_sb[:, ie % 2, :], ps_A[:, ie % 2, :],
                                  AF.Exp, scale=0.125)
                ctx.inc(e, s_ex)
                C["ex"] += 1

        @blk.vector
        def _(v, b=b):
            if b > 0:
                v.wait_ge(ctx.sem(s_pv), b * NPV)
            ctx.inc(v.memset(v65[:, :, :, 64:65], 1.0), s_vm)
            for a in range(3):
                for kt in range(ntt):
                    for pr in range(H // 2):
                        gj = 96 * b + a * 32 + kt * 8 + pr
                        v.wait_ge(ctx.sem(s_pj), gj + 1)
                        if a == 2:
                            if C["vevv"] > 0:
                                v.wait_ge(ctx.sem(s_vt), C["vevv"])
                            C["vevv"] += 1
                            ctx.inc(v.tensor_copy(vtmp[:], ps_A[:, gj % 2, :]), s_vev)
                            v.wait_ge(ctx.sem(s_vt), C["ve"] + 1)
                            if b > 0 and C["ve"] % (ntt * 8) == 0:
                                v.wait_ge(ctx.sem(s_pv), b * NPV)
                            ctx.inc(v.tensor_copy(
                                v65[:, kt * 4:(kt + 1) * 4, 2 * pr:2 * pr + 2, 0:64],
                                ps_t.ap().rearrange("p a (h n) -> p a h n", h=2)),
                                s_ve)
                            C["ve"] += 1
                        else:
                            if b > 0 and C["qk"] % (ntt * 16) == 0:
                                v.wait_ge(ctx.sem(s_st), b * NPV)
                            dst = q_fm if a == 0 else k_fm
                            ctx.inc(v.tensor_copy(dst[:, pr, kt * TT:(kt + 1) * TT],
                                                  ps_A[:, gj % 2, :]), s_qk)
                            C["qk"] += 1
            for h in range(H):
                hb = (h % 2) * 64
                hc = h // 2
                for tqb in range(ntt):
                    io = C["on"]
                    v.wait_ge(ctx.sem(s_pv), (io + 1) * nkc)
                    if io >= 1:
                        v.wait_ge(ctx.sem(s_on), io)
                    if b > 0 and io == b * H * ntt:
                        v.wait_ge(ctx.sem(s_wo), b * 32)
                    vv = ctx.inc(v.tensor_copy(o_raw[:], ps_o[0:64, io % 2, :]), s_rc)
                    v.wait_ge(ctx.sem(s_rc), vv)
                    if io >= 1:
                        v.wait_ge(ctx.sem(s_bc), io)
                    ctx.inc(v.reciprocal(rrow[:1, :], ps_o[64:65, io % 2, :]), s_rc)
                    v.wait_ge(ctx.sem(s_bc), io + 1)
                    ctx.inc(v.tensor_tensor(o_fm[hb:hb + 64, hc, tqb * TT:(tqb + 1) * TT],
                                            o_raw[:], ps_b[0:64, :], ALU.mult), s_on)
                    C["on"] += 1
            for kt in range(ntt):
                for dc in range(NCHUNK):
                    it = C["ot"]
                    v.wait_ge(ctx.sem(s_wo), it + 1)
                    if it >= 2:
                        v.wait_ge(ctx.sem(s_oacc[it % 2]), 16 * (it // 2))
                    ctx.inc(v.tensor_copy(ot[:, it % 2, :], ps_A[:, it % 2, :]), s_ot)
                    C["ot"] += 1

        @blk.gpsimd
        def _(g, b=b):
            for sem, val in gates_acc:
                g.wait_ge(ctx.sem(sem), val)
            for kt in range(ntt):
                for dc in range(NCHUNK):
                    it = C["oacc"]
                    g.wait_ge(ctx.sem(s_ot), it + 1)
                    ctx.inc(g.dma_start(out=x3[:, dc, (b * ntt + kt) * TT:
                                               (b * ntt + kt + 1) * TT],
                                        in_=ot[:, it % 2, :], accum_op=ALU.add),
                            s_oacc[it % 2], 16)
                    C["oacc"] += 1

    return dict(done=[(s, ctx.val(s)) for s in s_oacc])


def emit_l1attn(ctx, tag, xn3, x3, ck_h, alog_h, wbc_h, bbc_h, dcoef_h,
                wdt_h, bdt_h, wdtp_h, bdtp_h, NTOK, gates_acc=()):
    blk = ctx.blk
    ST = 2048
    HT = 1024
    nb = NTOK // ST
    N16 = 16

    wbcT = ctx.sbuf(f"{tag}_wbc", [128, NCHUNK, 32], BF16)
    wdtT = ctx.sbuf(f"{tag}_wdt", [128, NCHUNK, 64], BF16)
    wdtpT = ctx.sbuf(f"{tag}_wdtp", [64, D], BF16)
    A_sb = ctx.sbuf(f"{tag}_A", [128, NCHUNK, N16], F32)
    xnh = ctx.sbuf(f"{tag}_xnh", [128, NCHUNK, HT + 2], BF16)
    u_bf = ctx.sbuf(f"{tag}_u", [128, NCHUNK, HT], BF16)
    dt_f = ctx.sbuf(f"{tag}_dt", [128, NCHUNK, HT], F32)
    xdt = ctx.sbuf(f"{tag}_xdt", [128, NCHUNK, HT], F32)
    y_acc = ctx.sbuf(f"{tag}_y", [128, NCHUNK, HT], F32)
    bc_sb = ctx.sbuf(f"{tag}_bc", [32, HT], F32)
    dt1_bf = ctx.sbuf(f"{tag}_dt1", [64, HT], BF16)
    Bb_sb = ctx.sbuf(f"{tag}_Bb", [128, HT], F32)
    Cb_sb = ctx.sbuf(f"{tag}_Cb", [128, HT], F32)
    alf = ctx.sbuf(f"{tag}_alf", [128, HT], F32)
    bet = ctx.sbuf(f"{tag}_bet", [128, HT], F32)
    hbuf = ctx.sbuf(f"{tag}_hb", [128, HT], F32)
    scr2 = ctx.sbuf(f"{tag}_scr2", [128, HT], F32)
    scr = ctx.sbuf(f"{tag}_scr", [128, HT], F32)
    sp1 = ctx.sbuf(f"{tag}_sp1", [128, 512], F32)
    carry = ctx.sbuf(f"{tag}_carry", [128, NCHUNK, N16], F32)
    otf = ctx.sbuf(f"{tag}_otf", [128, 2, HT], F32)
    ps_A = ctx.psum(f"{tag}_psA", [128, 2, 512], F32)
    ps_B = ctx.psum(f"{tag}_psB", [128, 2, 512], F32)
    sel32 = ctx.bufs["sel32"]

    pst32 = ps_B[:, 0, 0:32]
    srow = bc_sb[0:1, 0:HT]
    k_c, cgk = emit_colvec_load(ctx, f"{tag}_ck0", ck_h[0], D, pst32=pst32, row_buf=srow)
    k_c1, cgk1 = emit_colvec_load(ctx, f"{tag}_ck1", ck_h[1], D, pst32=pst32,
                                  prev=cgk, row_buf=srow)
    k_c2, cgk2 = emit_colvec_load(ctx, f"{tag}_ck2", ck_h[2], D, pst32=pst32,
                                  prev=cgk1, row_buf=srow)
    bbc_c, cgb = emit_colvec_load(ctx, f"{tag}_bbc", bbc_h, 32, pst32=pst32,
                                  prev=cgk2, row_buf=srow)
    bdt_c, cgd = emit_colvec_load(ctx, f"{tag}_bdt", bdt_h, 64, pst32=pst32,
                                  prev=cgb, row_buf=srow)
    bdtp_c, cgp = emit_colvec_load(ctx, f"{tag}_bdtp", bdtp_h, D, pst32=pst32,
                                   prev=cgd, row_buf=srow)
    dco_c, cgD = emit_colvec_load(ctx, f"{tag}_dco", dcoef_h, D, pst32=pst32,
                                  prev=cgp, row_buf=srow)
    s_cols = cgD

    s_w = f"{tag}_sw"
    s_A = f"{tag}_sA"
    s_xn = f"{tag}_sxn"
    s_xm = f"{tag}_sxm"
    s_cv = f"{tag}_scv"
    s_u = f"{tag}_su"
    s_bcm = f"{tag}_sbcm"
    s_bce = f"{tag}_sbce"
    s_d1m = f"{tag}_sd1m"
    s_d1e = f"{tag}_sd1e"
    s_d2m = f"{tag}_sd2m"
    s_sp3 = f"{tag}_ssp3"
    s_sp = f"{tag}_ssp"
    s_xd = f"{tag}_sxd"
    s_bb = f"{tag}_sbb"
    s_be = f"{tag}_sbe"
    s_al = f"{tag}_sal"
    s_bt = f"{tag}_sbt"
    s_h = f"{tag}_sh"
    s_cy = f"{tag}_scy"
    s_ym = f"{tag}_sym"
    s_ya = f"{tag}_sya"
    s_fin = f"{tag}_sfin"
    s_oacc = [f"{tag}_soa0", f"{tag}_soa1"]

    @blk.gpsimd
    def _(g):
        ctx.inc(g.dma_start(out=wbcT.ap(),
                            in_=wbc_h.rearrange("(c p) f -> p c f", p=128)), s_w, 16)
        ctx.inc(g.dma_start(out=wdtT.ap(),
                            in_=wdt_h.rearrange("(c p) f -> p c f", p=128)), s_w, 16)
        ctx.inc(g.dma_start(out=wdtpT.ap(), in_=wdtp_h), s_w, 16)
        ctx.inc(g.dma_start(out=A_sb.ap(),
                            in_=alog_h.rearrange("(c p) n -> p c n", p=128)), s_w, 16)

    @blk.scalar
    def _(sc):
        ctx.wait(sc, "c_const")
        ctx.wait(sc, s_w, 64)
        ctx.inc(sc.activation(A_sb.ap(), A_sb.ap(), AF.Exp), s_A)

    @blk.vector
    def _(v):
        v.wait_ge(ctx.sem(s_A), 1)
        ctx.inc(v.tensor_scalar_mul(A_sb.ap(), A_sb.ap(), -1.0), s_A)

    for b in range(nb):
        for hh in range(2):
            ih = b * 2 + hh
            t0 = b * ST + hh * HT
            J0 = ih * NCHUNK * N16

            @blk.sync
            def _(s, ih=ih, t0=t0, hh=hh):
                if ih > 0:
                    s.wait_ge(ctx.sem(s_fin), ih * NCHUNK)
                lo = t0 - 1
                hi = t0 + HT + 1
                lo_c = max(lo, b * ST)
                hi_c = min(hi, b * ST + ST)
                ctx.inc(s.dma_start(out=xnh[:, :, lo_c - lo: lo_c - lo + (hi_c - lo_c)],
                                    in_=xn3[:, :, lo_c:hi_c]), s_xn, 16)

            @blk.vector
            def _(v, ih=ih, hh=hh):
                if ih > 0:
                    v.wait_ge(ctx.sem(s_fin), ih * NCHUNK)
                if hh == 0:
                    ctx.inc(v.memset(xnh[:, :, 0:1], 0.0), s_xm)
                else:
                    ctx.inc(v.memset(xnh[:, :, HT + 1:HT + 2], 0.0), s_xm)
                v.wait_ge(ctx.sem(s_cols[0]), s_cols[1])
                v.wait_ge(ctx.sem(s_xn), 16 * (ih + 1))
                v.wait_ge(ctx.sem(s_xm), ih + 1)
                for c in range(NCHUNK):
                    iu = ih * NCHUNK + c
                    if iu > 0:
                        v.wait_ge(ctx.sem(s_u), iu)
                    vv = ctx.inc(v.tensor_scalar(scr[:], xnh[:, c, 0:HT],
                                                 k_c[:, c:c + 1], None, ALU.mult), s_cv)
                    v.wait_ge(ctx.sem(s_cv), vv)
                    vv = ctx.inc(v.scalar_tensor_tensor(scr[:], xnh[:, c, 1:HT + 1],
                                                        k_c1[:, c:c + 1], scr[:],
                                                        ALU.mult, ALU.add), s_cv)
                    v.wait_ge(ctx.sem(s_cv), vv)
                    ctx.inc(v.scalar_tensor_tensor(scr[:], xnh[:, c, 2:HT + 2],
                                                   k_c2[:, c:c + 1], scr[:],
                                                   ALU.mult, ALU.add), s_cv)
                for k in range(2):
                    ib = ih * 2 + k
                    v.wait_ge(ctx.sem(s_bcm), ib + 1)
                    ctx.inc(v.tensor_scalar(bc_sb[0:32, k * 512:(k + 1) * 512],
                                            ps_B[0:32, ib % 2, :], bbc_c[0:32, 0:1],
                                            None, ALU.add), s_bce)
                for k in range(2):
                    i1 = ih * 2 + k
                    v.wait_ge(ctx.sem(s_d1m), i1 + 1)
                    ctx.inc(v.tensor_scalar(dt1_bf[:, k * 512:(k + 1) * 512],
                                            ps_A[0:64, i1 % 2, :], bdt_c[0:64, 0:1],
                                            None, ALU.add), s_d1e)
                for dc in range(NCHUNK):
                    for k in range(2):
                        i2 = ih * 16 + dc * 2 + k
                        if ih > 0 and dc == 0 and k == 0:
                            v.wait_ge(ctx.sem(s_al), ih * 128)
                        v.wait_ge(ctx.sem(s_sp3), i2 + 1)
                        vv = ctx.inc(v.tensor_scalar(scr[:, 0:512], ps_A[:, i2 % 2, :],
                                                     bdtp_c[:, dc:dc + 1], 0.0,
                                                     ALU.add, ALU.max), f"{tag}_srl")
                        v.wait_ge(ctx.sem(f"{tag}_srl"), vv)
                        ctx.inc(v.tensor_add(dt_f[:, dc, k * 512:(k + 1) * 512],
                                             scr[:, 0:512], sp1[:]), s_sp)
                for c in range(NCHUNK):
                    if ih > 0 and c == 0:
                        v.wait_ge(ctx.sem(s_bt), J0)
                    v.wait_ge(ctx.sem(s_sp), (ih + 1) * 16)
                    ctx.inc(v.tensor_tensor(xdt[:, c, :], xnh[:, c, 1:HT + 1],
                                            dt_f[:, c, :], ALU.mult), s_xd)
                for n in range(N16):
                    jn = ih * N16 + n
                    v.wait_ge(ctx.sem(s_bb), jn + 1)
                    if jn > 0:
                        v.wait_ge(ctx.sem(s_bt), jn * NCHUNK)
                        v.wait_ge(ctx.sem(s_ym), jn * NCHUNK)
                    vv = ctx.inc(v.tensor_copy(Bb_sb.ap(), ps_B[:, :, :]
                                               .rearrange("p a f -> p (a f)")), s_be)
                    v.wait_ge(ctx.sem(s_be), vv)
                    ctx.inc(v.tensor_copy(Cb_sb.ap(), ps_A[:, :, :]
                                          .rearrange("p a f -> p (a f)")), s_be)
                    for c in range(NCHUNK):
                        j = J0 + n * NCHUNK + c
                        v.wait_ge(ctx.sem(s_al), j + 1)
                        v.wait_ge(ctx.sem(s_bt), j + 1)
                        if j > 0:
                            v.wait_ge(ctx.sem(s_ym), j)
                            v.wait_ge(ctx.sem(s_cy), j)
                        init = 0.0 if hh == 0 else carry[:, c, n:n + 1]
                        if hh == 1:
                            v.wait_ge(ctx.sem(s_cy), J0 - NCHUNK * N16 + n * NCHUNK + c + 1)
                        vv = ctx.inc(v.tensor_tensor_scan(hbuf[:], alf[:], bet[:],
                                                          init, ALU.mult, ALU.add), s_h)
                        v.wait_ge(ctx.sem(s_h), vv)
                        ctx.inc(v.tensor_copy(carry[:, c, n:n + 1], hbuf[:, HT - 1:HT]),
                                s_cy)
                        v.wait_ge(ctx.sem(s_ym), j + 1)
                        if ih > 0 and n == 0:
                            v.wait_ge(ctx.sem(s_fin), ih * NCHUNK)
                        if n == 0:
                            ctx.inc(v.tensor_copy(y_acc[:, c, :], scr2[:]), s_ya)
                        else:
                            ctx.inc(v.tensor_tensor(y_acc[:, c, :], y_acc[:, c, :],
                                                    scr2[:], ALU.add), s_ya)
                for c in range(NCHUNK):
                    v.wait_ge(ctx.sem(s_ya), J0 + N16 * NCHUNK - (NCHUNK - c - 1))
                    if ih >= 1 or c >= 2:
                        v.wait_ge(ctx.sem(s_oacc[c % 2]), 16 * (ih * 4 + c // 2))
                    ctx.inc(v.scalar_tensor_tensor(otf[:, c % 2, :], xnh[:, c, 1:HT + 1],
                                                   dco_c[:, c:c + 1], y_acc[:, c, :],
                                                   ALU.mult, ALU.add), s_fin)

            @blk.scalar
            def _(sc, ih=ih):
                for c in range(NCHUNK):
                    iu = ih * NCHUNK + c
                    sc.wait_ge(ctx.sem(s_cv), 3 * (iu + 1))
                    if ih > 0 and c == 0:
                        sc.wait_ge(ctx.sem(s_bcm), ih * 2)
                        sc.wait_ge(ctx.sem(s_d1m), ih * 2)
                    ctx.inc(sc.activation(u_bf[:, c, :], scr[:], AF.Silu), s_u)
                for dc in range(NCHUNK):
                    for k in range(2):
                        i2 = ih * 16 + dc * 2 + k
                        sc.wait_ge(ctx.sem(s_d2m), i2 + 1)
                        if i2 > 0:
                            sc.wait_ge(ctx.sem(s_sp), i2)
                        a1 = sc.activation(sp1[:], ps_A[:, i2 % 2, :], AF.Abs,
                                           bias=bdtp_c[:, dc:dc + 1])
                        ctx.seq(sc, a1, f"{tag}_spseq")
                        a2 = sc.activation(sp1[:], sp1[:], AF.Exp, scale=-1.0)
                        ctx.seq(sc, a2, f"{tag}_spseq")
                        ctx.inc(sc.activation(sp1[:], sp1[:], AF.Ln, bias=kap(ctx, 1.0)),
                                s_sp3)
                for n in range(N16):
                    for c in range(NCHUNK):
                        j = ih * N16 * NCHUNK + n * NCHUNK + c
                        if c == 0 and n == 0:
                            sc.wait_ge(ctx.sem(s_sp), (ih + 1) * 16)
                        if j > 0:
                            sc.wait_ge(ctx.sem(s_h), j)
                        ctx.inc(sc.activation(alf[:], dt_f[:, c, :], AF.Exp,
                                              scale=A_sb[:, c, n:n + 1]), s_al)

            @blk.gpsimd
            def _(gp, ih=ih):
                for n in range(N16):
                    jn = ih * N16 + n
                    gp.wait_ge(ctx.sem(s_be), 2 * (jn + 1))
                    for c in range(NCHUNK):
                        j = ih * N16 * NCHUNK + n * NCHUNK + c
                        gp.wait_ge(ctx.sem(s_xd), ih * NCHUNK + c + 1)
                        if j > 0:
                            gp.wait_ge(ctx.sem(s_h), j)
                        ctx.inc(gp.tensor_tensor(bet[:], xdt[:, c, :], Bb_sb[:],
                                                 ALU.mult), s_bt)
                        gp.wait_ge(ctx.sem(s_h), j + 1)
                        if j > 0:
                            gp.wait_ge(ctx.sem(s_ya), j)
                        ctx.inc(gp.tensor_tensor(scr2[:], hbuf[:], Cb_sb[:],
                                                 ALU.mult), s_ym)

            @blk.tensor
            def _(t, ih=ih):
                if ih == 0:
                    ctx.wait(t, "c_const")
                    ctx.wait(t, s_w, 64)
                    t.wait_ge(ctx.sem(s_A), 2)
                    t.wait_ge(ctx.sem(s_cols[0]), s_cols[1])
                t.wait_ge(ctx.sem(s_u), (ih + 1) * NCHUNK)
                if ih > 0:
                    t.wait_ge(ctx.sem(s_be), ih * 2 * N16)
                for k in range(2):
                    ib = ih * 2 + k
                    if ib >= 2:
                        t.wait_ge(ctx.sem(s_bce), ib - 1)
                    for a in range(NCHUNK):
                        mm = t.matmul(ps_B[0:32, ib % 2, :], wbcT[:, a, :],
                                      u_bf[:, a, k * 512:(k + 1) * 512],
                                      start=(a == 0), stop=(a == NCHUNK - 1))
                    ctx.inc(mm, s_bcm)
                for k in range(2):
                    i1 = ih * 2 + k
                    if i1 >= 2:
                        t.wait_ge(ctx.sem(s_d1e), i1 - 1)
                    for a in range(NCHUNK):
                        mm = t.matmul(ps_A[0:64, i1 % 2, :], wdtT[:, a, :],
                                      u_bf[:, a, k * 512:(k + 1) * 512],
                                      start=(a == 0), stop=(a == NCHUNK - 1))
                    ctx.inc(mm, s_d1m)
                t.wait_ge(ctx.sem(s_d1e), (ih + 1) * 2)
                for dc in range(NCHUNK):
                    for k in range(2):
                        i2 = ih * 16 + dc * 2 + k
                        if i2 >= 2:
                            t.wait_ge(ctx.sem(s_sp), i2 - 1)
                        mm = t.matmul(ps_A[:, i2 % 2, :],
                                      wdtpT[:, dc * 128:(dc + 1) * 128],
                                      dt1_bf[:, k * 512:(k + 1) * 512],
                                      start=True, stop=True)
                        ctx.inc(mm, s_d2m)
                t.wait_ge(ctx.sem(s_bce), (ih + 1) * 2)
                t.wait_ge(ctx.sem(s_sp), (ih + 1) * 16)
                for n in range(N16):
                    jn = ih * N16 + n
                    if jn > 0:
                        t.wait_ge(ctx.sem(s_be), jn * 2)
                    for k in range(2):
                        t.matmul(ps_B[:, k, :], sel32[:, n * 128:(n + 1) * 128],
                                 bc_sb[0:32, k * 512:(k + 1) * 512],
                                 start=True, stop=True)
                        mm2 = t.matmul(ps_A[:, k, :],
                                       sel32[:, (16 + n) * 128:(17 + n) * 128],
                                       bc_sb[0:32, k * 512:(k + 1) * 512],
                                       start=True, stop=True)
                    ctx.inc(mm2, s_bb)

            @blk.gpsimd
            def _(g_, ih=ih):
                for sem, val in gates_acc:
                    g_.wait_ge(ctx.sem(sem), val)
                for c in range(NCHUNK):
                    g_.wait_ge(ctx.sem(s_fin), ih * NCHUNK + c + 1)
                    ctx.inc(g_.dma_start(out=x3[:, c, t0:t0 + HT],
                                         in_=otf[:, c % 2, :], accum_op=ALU.add),
                            s_oacc[c % 2], 16)

    return dict(done=[(s, ctx.val(s)) for s in s_oacc])


class Stages:

    ENGS = ("sync", "gpsimd", "tensor", "vector", "scalar")

    def __init__(self, ctx, tag):
        self.ctx = ctx
        self.bar = f"{tag}_bar"
        self.bard = {"sync": f"{tag}_bardS", "gpsimd": f"{tag}_bardG"}
        ctx.sem(self.bar)
        for s in self.bard.values():
            ctx.sem(s)

    def stage(self, **engfns):
        ctx = self.ctx
        blk = ctx.blk
        target = ctx.val(self.bar)
        targetd = {k: ctx.val(s) for k, s in self.bard.items()}
        for ename in self.ENGS:
            sec = getattr(blk, ename)
            fn = engfns.get(ename)

            def body(e, fn=fn, ename=ename):
                e.wait_ge(ctx.sem(self.bar), target)
                for k, s in self.bard.items():
                    if targetd[k]:
                        e.wait_ge(ctx.sem(s), targetd[k])
                outs = []

                def fin(instr, n=1):
                    outs.append((instr, n))
                if fn is not None:
                    fn(e, fin)
                got_c = False
                for instr, n in outs:
                    ctx.inc(instr, self.bard[ename] if n == 16 else self.bar, n)
                    if n != 16:
                        got_c = True
                if not got_c:
                    ctx.inc(e.nop() if ename in ("sync", "tensor", "gpsimd", "scalar")
                            else e.engine_nop(), self.bar, 1)
            sec(body)


def emit_l0attn(ctx, tag, xn3, x3, W, NTOK, gates_acc=(), phases=6):
    blk = ctx.blk
    ST = 2048
    TT = 512
    nb = NTOK // ST
    ntt = ST // TT
    H = 16
    M = 256
    EPS_K = 1e-4
    NORM = float(np.float32(64.0 ** (-0.25)))
    HNN = float(np.float32(0.5 * (NORM * NORM)))

    ps8 = ctx.psum(f"{tag}_ps8", [128, 6, TT], F32)
    pstA = ps8[:, 0, :].bitcast(BF16).rearrange("p (a f) -> p a f", f=128)
    ab = {"stg": ctx.sbuf(f"{tag}_Astg", [128, 1, 4096], BF16),
          "wtmp": ctx.sbuf(f"{tag}_Awtmp", [128, 1, 32, 128], BF16)}
    wT = {}
    prev = None
    for nm, DO_, DI_ in (("wq", D, D), ("wk", D, D), ("wv", D, D), ("wo", D, D),
                         ("w1", 4096, D), ("w2", D, 4096)):
        wTd, gate, ev = emit_wT_to_dram(ctx, f"{tag}_{nm}", W[nm], DO_, DI_,
                                        pst=pstA, prev_ev=prev, bufs=ab,
                                        sem_tag=f"{tag}_wA")
        wT[nm] = (wTd, gate)
        prev = ev
    ps_dg = ctx.psum(f"{tag}_psdg", [128, 2, TT], F32)
    pst32 = ps_dg[:, 0, 0:32]
    a1t = ctx.sbuf(f"{tag}_a1t", [128, NCHUNK, TT], F32)
    srow_t = a1t[0:1, :, :].rearrange("p a f -> p (a f)")
    cols = {}
    prevc = None
    for nm, N_ in (("bq", D), ("bk", D), ("bv", D), ("bo", D), ("b2", D),
                   ("g2", D), ("be2", D), ("g3", D), ("be3", D), ("b1", 4096)):
        col, cg = emit_colvec_load(ctx, f"{tag}_{nm}", W[nm], N_, pst32=pst32,
                                   prev=prevc, row_buf=srow_t)
        cols[nm] = col
        prevc = cg
    s_cols = prevc

    projdup = ctx.sbuf(f"{tag}_projd", [128, M], BF16)
    pstg = ctx.sbuf(f"{tag}_pstg", [128, 2, 64], BF16)
    s_prj = f"{tag}_sprj"

    @blk.gpsimd
    def _(g):
        ctx.inc(g.dma_start(out=pstg.ap(),
                            in_=W["proj"].rearrange("(a p) n -> p a n", p=128)),
                f"{tag}_sprjd", 16)

    @blk.tensor
    def _(t):
        ctx.wait(t, "c_const")
        ctx.wait(t, f"{tag}_sprjd", 16)
        t.wait_ge(ctx.sem(prev[0]), prev[1])
        for a in range(2):
            t.transpose(pstA[0:64, a, :], pstg[:, a, :], ctx.bufs["ident_bf"][:])
            mm = t.transpose(pstA[64:128, a, :], pstg[:, a, :],
                             ctx.bufs["ident_bf"][:])
        ctx.inc(mm, s_prj)

    @blk.vector
    def _(v):
        v.wait_ge(ctx.sem(s_prj), 1)
        ctx.inc(v.tensor_scalar_mul(projdup.ap().rearrange("p (a m) -> p a m", a=2),
                                    pstA[:, 0:2, :], NORM), s_prj)

    k_fm = ctx.sbuf(f"{tag}_kfm", [128, NCHUNK, ST], BF16)
    v65 = ctx.sbuf(f"{tag}_v65", [128, 16, H, 65], BF16)
    wsl = ctx.sbuf(f"{tag}_wsl", [128, 2, NCHUNK, D], BF16)
    xnt = ctx.sbuf(f"{tag}_xnt", [128, NCHUNK, TT], BF16)
    stg8 = ctx.sbuf(f"{tag}_stg8", [128, NCHUNK, TT], BF16)
    ekb = ctx.sbuf(f"{tag}_ekb", [128, 4, M], BF16)
    eqT = ctx.sbuf(f"{tag}_eqT", [128, 8, 128], BF16)
    t1s = ctx.sbuf(f"{tag}_t1s", [128, TT], F32)
    o_tm = ctx.sbuf(f"{tag}_otm", [128, 4, D], BF16)
    ofm = ctx.sbuf(f"{tag}_ofm", [128, NCHUNK, TT], BF16)
    x2b = k_fm
    hb = stg8
    ctx_sb = ctx.sbuf(f"{tag}_ctxs", [128, H, 2, 65], BF16)
    epsrow = ctx.sbuf(f"{tag}_epsr", [1, H, 65], BF16)
    epsq = ctx.sbuf(f"{tag}_epsq", [1, H, 65], BF16)
    svrow = ctx.sbuf(f"{tag}_svr", [1, H * 65], F32)
    stabn = ctx.sbuf(f"{tag}_stabn", [128, H], F32)
    rmax = ctx.sbuf(f"{tag}_rmax", [128, H], F32)
    mtmp = ctx.sbuf(f"{tag}_mtmp", [128, 4], F32)
    bias3 = ctx.sbuf(f"{tag}_bias3", [128, 4], F32)
    diagk = ctx.sbuf(f"{tag}_diagk", [128, 16, H], F32)
    bias2 = ctx.sbuf(f"{tag}_bias2", [128, 16, H], F32)
    diagq = ctx.sbuf(f"{tag}_diagq", [128, 4, H], F32)
    rows = t1s[0:1, 0:TT].rearrange("p (a f) -> p a f", a=1)[0:1, 0, :] if False         else svrow[0:1, 0:1024].rearrange("p (a f) -> p a f", a=2)
    rrow = o_tm.ap().bitcast(F32)[0:1, 0:2, :]
    a2h = ctx.dram_tmp(f"{tag}_a2h", [128, NCHUNK, NTOK], F32)
    psTv = ps8[:, 4, :].bitcast(BF16).rearrange("p (a f) -> p a f", f=128)
    psT23 = ps8[:, 2, :].bitcast(BF16).rearrange("p (a f) -> p a f", f=128)
    ps_cx = ps8[:, 4:6, :].rearrange("p a f -> p (a f)").rearrange(
        "p (t f) -> p t f", f=128)
    bones = ctx.bufs["bones"]
    ident = ctx.bufs["ident_bf"]
    ident32 = ctx.bufs["ident_f32"]
    ones_row = ctx.bufs["ones_row"]
    ones_col = ctx.bufs["ones_col"]
    ones_col_bf = ctx.bufs["ones_col_bf"]
    onesrow_bf = ctx.bufs["ones_row_bf"]

    S = Stages(ctx, tag)
    WA = ([g for _nm in ("wq", "wk", "wv", "wo", "w1", "w2") for g in wT[_nm][1]] +
          [(s_cols[0], s_cols[1]), (s_prj, 2), ("c_const", ctx.val("c_const"))] +
          list(gates_acc))

    def gate_stage(e, fin):
        for sem, val in WA:
            e.wait_ge(ctx.sem(sem), val)
    S.stage(tensor=gate_stage, vector=gate_stage, scalar=gate_stage,
            sync=gate_stage, gpsimd=gate_stage)

    def proj_tile(wslice, bcol, dst_writer, kt):
        for lo in (0, 4):
            def mmw(t, fin, lo=lo, wslice=wslice):
                for dcn in range(lo, lo + 4):
                    for a in range(NCHUNK):
                        mm = t.matmul(ps8[:, dcn - lo, :],
                                      wslice[:, a, dcn * 128:(dcn + 1) * 128],
                                      xnt[:, a, :], start=(a == 0),
                                      stop=(a == NCHUNK - 1))
                fin(mm)

            def evw(v, fin, lo=lo, bcol=bcol, dst_writer=dst_writer, kt=kt):
                for dcn in range(lo, lo + 4):
                    e = v.tensor_scalar(dst_writer(dcn), ps8[:, dcn - lo, :],
                                        bcol[:, dcn:dcn + 1], None, ALU.add)
                fin(e)
            S.stage(tensor=mmw)
            S.stage(vector=evw)

    def diag_tile(src_fm_getter, dst, sub0):
        for c in range(NCHUNK):
            def sqf(sc, fin, c=c):
                fin(sc.activation(t1s[:], src_fm_getter(c), AF.Square))

            def dgm(t, fin, c=c):
                fin(t.matmul(ps_dg[0:16, 0, :], bones[:, c * 16:(c + 1) * 16],
                             t1s[:], start=(c == 0), stop=(c == NCHUNK - 1)))
            S.stage(scalar=sqf)
            S.stage(tensor=dgm)

        def dgr(v, fin):
            fin(v.tensor_scalar_mul(t1s[0:16, :], ps_dg[0:16, 0, :], HNN))
        S.stage(vector=dgr)
        for j in range(4):
            def dgt1(t, fin, j=j):
                fin(t.transpose(ps8[:, 0, 0:16], t1s[0:16, j * 128:(j + 1) * 128],
                                ident32[0:16, 0:16]))

            def dge1(v, fin, j=j):
                fin(v.tensor_copy(dst[:, sub0 + j, :], ps8[:, 0, 0:16]))
            S.stage(tensor=dgt1)
            S.stage(vector=dge1)

    def ln_affine(gname, bname, out_writer):
        for c in range(NCHUNK):
            def sqf(sc, fin, c=c):
                fin(sc.activation(t1s[:], a1t[:, c, :], AF.Square))

            def stm(t, fin, c=c):
                t.matmul(ps_dg[0:1, 0, :], ones_col[:], a1t[:, c, :],
                         start=(c == 0), stop=(c == NCHUNK - 1))
                fin(t.matmul(ps_dg[0:1, 1, :], ones_col[:], t1s[:],
                             start=(c == 0), stop=(c == NCHUNK - 1)))
            S.stage(scalar=sqf)
            S.stage(tensor=stm)

        def rowm(v, fin):
            ctx.seq(v, v.tensor_copy(rows[:1, :], ps_dg[0:1, 0:2, :]))
            ctx.seq(v, v.tensor_scalar_mul(rows[:1, 0], rows[:1, 0], 1.0 / D))
            ctx.seq(v, v.scalar_tensor_tensor(rrow[:1, 1], rows[:1, 0], 1.0,
                                              rows[:1, 0], ALU.mult, ALU.mult))
            fin(v.scalar_tensor_tensor(rows[:1, 1], rows[:1, 1], 1.0 / D,
                                       rrow[:1, 1], ALU.mult, ALU.subtract))
        S.stage(vector=rowm)

        def lnr(sc, fin):
            a1 = sc.activation(rrow[:1, 0], rows[:1, 1], AF.Ln, bias=kap(ctx, 1e-5, 1))
            ctx.seq(sc, a1, f"{tag}_scs")
            fin(sc.activation(rrow[:1, 0], rrow[:1, 0], AF.Exp, scale=-0.5))
        S.stage(scalar=lnr)

        def mrr(v, fin):
            fin(v.scalar_tensor_tensor(rrow[:1, 1], rows[:1, 0], -1.0, rrow[:1, 0],
                                       ALU.mult, ALU.mult))
        S.stage(vector=mrr)

        def bcm(t, fin):
            t.matmul(ps_dg[:, 0, :], ones_row[:1, 0:128], rrow[:1, 0],
                     start=True, stop=True)
            fin(t.matmul(ps_dg[:, 1, :], ones_row[:1, 0:128], rrow[:1, 1],
                         start=True, stop=True))
        S.stage(tensor=bcm)

        def fini(v, fin):
            for c in range(NCHUNK):
                ctx.seq(v, v.tensor_tensor(t1s[:], a1t[:, c, :], ps_dg[:, 0, :],
                                           ALU.mult))
                ctx.seq(v, v.tensor_tensor(t1s[:], t1s[:], ps_dg[:, 1, :], ALU.add))
                e = out_writer(v, c)
                if c < NCHUNK - 1:
                    ctx.seq(v, e)
            fin(e)
        S.stage(vector=fini)

    for b in range(nb):
        tb = b * ST

        def ld_kv(g, fin):
            fin(g.dma_start(out=wsl[:, 0], in_=wT["wk"][0]), 16)
            fin(g.dma_start(out=wsl[:, 1], in_=wT["wv"][0]), 16)
        S.stage(gpsimd=ld_kv)

        for kt in range(ntt):
            def ldx(s, fin, kt=kt):
                fin(s.dma_start(out=xnt.ap(),
                                in_=xn3[:, :, tb + kt * TT: tb + (kt + 1) * TT]), 16)
            S.stage(sync=ldx)

            proj_tile(wsl[:, 0], cols["bk"],
                      lambda dcn, kt=kt: k_fm[:, dcn, kt * TT:(kt + 1) * TT], kt)
            proj_tile(wsl[:, 1], cols["bv"],
                      lambda dcn: stg8[:, dcn, :], kt)

            for j in range(4):
                def vtw(t, fin, j=j):
                    for c2 in range(2):
                        for s4 in range(4):
                            mm = t.transpose(psTv[:, c2 * 4 + s4, :],
                                             stg8[:, 2 * j + c2,
                                                  s4 * 128:(s4 + 1) * 128],
                                             ident[:])
                    fin(mm)

                def vte(v, fin, j=j, kt=kt):
                    for c2 in range(2):
                        c = 2 * j + c2
                        e = v.tensor_copy(
                            v65[:, kt * 4:(kt + 1) * 4, 2 * c:2 * c + 2, 0:64],
                            psTv[:, c2 * 4:(c2 + 1) * 4, :]
                            .rearrange("p a (h n) -> p a h n", h=2))
                    fin(e)
                S.stage(tensor=vtw)
                S.stage(vector=vte)

            diag_tile(lambda c, kt=kt: k_fm[:, c, kt * TT:(kt + 1) * TT],
                      diagk, kt * 4)

        def vmset(v, fin, b=b):
            if b == 0:
                fin(v.memset(v65[:, :, :, 64:65], 1.0))
        S.stage(vector=vmset)

        if phases < 2:
            continue
        def rmx0(v, fin):
            fin(v.memset(rmax[:], -1e30))
        S.stage(vector=rmx0)
        for kt in range(16):
            for qq in range(4):
                def xpk(t, fin, kt=kt, qq=qq):
                    for hj in range(4):
                        h = qq * 4 + hj
                        hb_ = (h % 2) * 64
                        mm = t.matmul(ps8[:, hj, 0:M],
                                      k_fm[hb_:hb_ + 64, h // 2,
                                           kt * 128:(kt + 1) * 128],
                                      projdup[hb_:hb_ + 64, :], start=True,
                                      stop=True)
                    fin(mm)

                def rmx(v, fin, kt=kt, qq=qq):
                    for hj in range(4):
                        h = qq * 4 + hj
                        ctx.seq(v, v.reduce_max(mtmp[:, 0:1], ps8[:, hj, 0:M],
                                                axis=AX.X))
                        e = v.tensor_tensor(rmax[:, h:h + 1], rmax[:, h:h + 1],
                                            mtmp[:, 0:1], ALU.max)
                        if hj < 3:
                            ctx.seq(v, e)
                    fin(e)
                S.stage(tensor=xpk)
                S.stage(vector=rmx)

        if phases == 12:
            continue

        def stb1(t, fin):
            fin(t.transpose(ps8[0:16, 0, 0:128], rmax[:], ident32[:, 0:128]))
        S.stage(tensor=stb1)

        def stb2(v, fin):
            ctx.seq(v, v.reduce_max(mtmp[0:16, 1:2], ps8[0:16, 0, 0:128], axis=AX.X))
            fin(v.tensor_copy(t1s[0:16, 0:1], mtmp[0:16, 1:2]))
        S.stage(vector=stb2)

        def stb3(t, fin):
            fin(t.transpose(ps8[0:1, 1, 0:16], t1s[0:16, 0:1], ident32[0:16, 0:16]))
        S.stage(tensor=stb3)

        def stb4(v, fin):
            fin(v.tensor_copy(t1s[0:1, 0:16], ps8[0:1, 1, 0:16]))
        S.stage(vector=stb4)

        def stb5(t, fin):
            fin(t.matmul(ps8[:, 0, 0:16], ones_row[:1, 0:128], t1s[0:1, 0:16],
                         start=True, stop=True))
        S.stage(tensor=stb5)

        def stb6(v, fin):
            ctx.seq(v, v.tensor_copy(stabn[:], ps8[:, 0, 0:16]))
            for i in range(16):
                e = v.scalar_tensor_tensor(bias2[:, i, :], diagk[:, i, :], -1.0,
                                           stabn[:], ALU.mult, ALU.subtract)
            fin(e)
        S.stage(vector=stb6)

        if phases == 13:
            continue
        def svm(t, fin):
            for kt in range(16):
                va = v65[:, kt, :, :].rearrange("p h n -> p (h n)")
                t.matmul(ps8[0:1, 0, 0:512], ones_col_bf[:], va[:, 0:512],
                         start=(kt == 0), stop=(kt == 15))
                t.matmul(ps8[0:1, 1, 0:512], ones_col_bf[:], va[:, 512:1024],
                         start=(kt == 0), stop=(kt == 15))
                mm = t.matmul(ps8[0:1, 2, 0:16], ones_col_bf[:], va[:, 1024:1040],
                              start=(kt == 0), stop=(kt == 15))
            fin(mm)
        S.stage(tensor=svm)

        def sve(v, fin):
            ctx.seq(v, v.tensor_scalar_mul(svrow[0:1, 0:512], ps8[0:1, 0, 0:512],
                                           EPS_K))
            ctx.seq(v, v.tensor_scalar_mul(svrow[0:1, 512:1024], ps8[0:1, 1, 0:512],
                                           EPS_K))
            ctx.seq(v, v.tensor_scalar_mul(svrow[0:1, 1024:1040], ps8[0:1, 2, 0:16],
                                           EPS_K))
            fin(v.tensor_copy(epsrow.ap().rearrange("p h n -> p (h n)"),
                              svrow[0:1, 0:1040]))
        S.stage(vector=sve)

        if phases < 3:
            continue
        for pp in range(8):
            for kt in range(16):
                def xpk2(t, fin, kt=kt, pp=pp):
                    for hj in range(2):
                        h = pp * 2 + hj
                        hb_ = (h % 2) * 64
                        mm = t.matmul(ps8[:, hj, 0:M],
                                      k_fm[hb_:hb_ + 64, h // 2,
                                           kt * 128:(kt + 1) * 128],
                                      projdup[hb_:hb_ + 64, :], start=True,
                                      stop=True)
                    fin(mm)

                def ek2(sc, fin, kt=kt, pp=pp):
                    for hj in range(2):
                        h = pp * 2 + hj
                        e = sc.activation(ekb[:, hj, :], ps8[:, hj, 0:M],
                                          AF.Exp, bias=bias2[:, kt, h:h + 1])
                    fin(e)

                def cxm(t, fin, kt=kt, pp=pp):
                    last = None
                    for hj in range(2):
                        h = pp * 2 + hj
                        for mc in range(2):
                            bk_ = 2 + hj * 2 + mc
                            if kt == 0:
                                t.matmul(ps8[:, bk_, 0:65], onesrow_bf[:1, 0:128],
                                         epsrow[0:1, h, :], start=True, stop=False)
                            last = t.matmul(ps8[:, bk_, 0:65],
                                            ekb[:, hj, mc * 128:(mc + 1) * 128],
                                            v65[:, kt, h, :], start=False,
                                            stop=(kt == 15))
                    fin(last)
                S.stage(tensor=xpk2)
                S.stage(scalar=ek2)
                S.stage(tensor=cxm)

            def cxe(v, fin, pp=pp):
                for hj in range(2):
                    for mc in range(2):
                        e = v.tensor_copy(ctx_sb[:, pp * 2 + hj, mc, :],
                                          ps8[:, 2 + hj * 2 + mc, 0:65])
                fin(e)
            S.stage(vector=cxe)

        for qq in range(4):
            def eqm(t, fin, qq=qq):
                for hj in range(4):
                    h = qq * 4 + hj
                    for mc in range(2):
                        mm = t.matmul(ps8[0:1, hj, 0:65], ones_col_bf[:],
                                      ctx_sb[:, h, mc, :], start=(mc == 0),
                                      stop=(mc == 1))
                fin(mm)

            def eqe(v, fin, qq=qq):
                for hj in range(4):
                    h = qq * 4 + hj
                    e = v.tensor_scalar_mul(epsq[0:1, h, :], ps8[0:1, hj, 0:65],
                                            EPS_K)
                fin(e)
            S.stage(tensor=eqm)
            S.stage(vector=eqe)

        if phases < 4:
            continue
        def ld_qo(g, fin):
            fin(g.dma_start(out=wsl[:, 0], in_=wT["wq"][0]), 16)
            fin(g.dma_start(out=wsl[:, 1], in_=wT["wo"][0]), 16)
        S.stage(gpsimd=ld_qo)

        for kt in range(ntt):
            def ldx2(s, fin, kt=kt):
                fin(s.dma_start(out=xnt.ap(),
                                in_=xn3[:, :, tb + kt * TT: tb + (kt + 1) * TT]), 16)
            S.stage(sync=ldx2)
            proj_tile(wsl[:, 0], cols["bq"], lambda dcn: stg8[:, dcn, :], kt)
            diag_tile(lambda c: stg8[:, c, :], diagq, 0)

            for sub in range(4):
                for qq in range(4):
                    def xpq(t, fin, sub=sub, qq=qq):
                        for hj in range(4):
                            h = qq * 4 + hj
                            hb_ = (h % 2) * 64
                            mm = t.matmul(ps8[:, hj, 0:M],
                                          stg8[hb_:hb_ + 64, h // 2,
                                               sub * 128:(sub + 1) * 128],
                                          projdup[hb_:hb_ + 64, :], start=True,
                                          stop=True)
                        fin(mm)

                    def qbias(v, fin, sub=sub, qq=qq):
                        for hj in range(4):
                            h = qq * 4 + hj
                            ctx.seq(v, v.reduce_max(
                                mtmp[:, 0:1], ps8[:, hj, 0:M],
                                axis=AX.X, negate=True))
                            e = v.tensor_sub(bias3[:, hj:hj + 1], mtmp[:, 0:1],
                                             diagq[:, sub, h:h + 1])
                            if hj < 3:
                                ctx.seq(v, e)
                        fin(e)

                    def eqx(sc, fin, sub=sub, qq=qq):
                        for hj in range(4):
                            e = sc.activation(ekb[:, hj, :], ps8[:, hj, 0:M],
                                              AF.Exp, bias=bias3[:, hj:hj + 1])
                        fin(e)

                    def eqt(t, fin, qq=qq):
                        for hj in range(4):
                            for mc in range(2):
                                mm = t.transpose(psT23[:, hj * 2 + mc, :],
                                                 ekb[:, hj, mc * 128:(mc + 1) * 128],
                                                 ident[:])
                        fin(mm)

                    def eqte(v, fin):
                        fin(v.tensor_copy(eqT.ap(), psT23[:, 0:8, :]))
                    S.stage(tensor=xpq)
                    S.stage(vector=qbias)
                    S.stage(scalar=eqx)
                    S.stage(tensor=eqt)
                    S.stage(vector=eqte)

                    def umm(t, fin, qq=qq):
                        last = None
                        for hj in range(4):
                            h = qq * 4 + hj
                            t.matmul(ps8[:, hj, 0:65], onesrow_bf[:1, 0:128],
                                     epsq[0:1, h, :], start=True, stop=False)
                            for mc in range(2):
                                last = t.matmul(ps8[:, hj, 0:65],
                                                eqT[:, hj * 2 + mc, :],
                                                ctx_sb[:, h, mc, :], start=False,
                                                stop=(mc == 1))
                        fin(last)

                    def uev(v, fin, sub=sub, qq=qq):
                        for hj in range(4):
                            h = qq * 4 + hj
                            ctx.seq(v, v.reciprocal(mtmp[:, 1:2],
                                                    ps8[:, hj, 64:65]))
                            e = v.tensor_scalar(o_tm[:, sub, h * 64:(h + 1) * 64],
                                                ps8[:, hj, 0:64],
                                                mtmp[:, 1:2], None, ALU.mult)
                            if hj < 3:
                                ctx.seq(v, e)
                        fin(e)
                    S.stage(tensor=umm)
                    S.stage(vector=uev)

            for j in range(4):
                def otw(t, fin, j=j):
                    for c2 in range(2):
                        for s4 in range(4):
                            mm = t.transpose(psTv[:, c2 * 4 + s4, :],
                                             o_tm[:, s4, (2 * j + c2) * 128:
                                                  (2 * j + c2 + 1) * 128],
                                             ident[:])
                    fin(mm)

                def ote(v, fin, j=j):
                    for c2 in range(2):
                        e = v.tensor_copy(
                            ofm[:, 2 * j + c2, :].rearrange("p (a f) -> p a f", f=128),
                            psTv[:, c2 * 4:(c2 + 1) * 4, :])
                    fin(e)
                S.stage(tensor=otw)
                S.stage(vector=ote)

            for lo in (0, 4):
                def womm(t, fin, lo=lo):
                    for dcn in range(lo, lo + 4):
                        for a in range(NCHUNK):
                            mm = t.matmul(ps8[:, dcn - lo, :],
                                          wsl[:, 1, a, dcn * 128:(dcn + 1) * 128],
                                          ofm[:, a, :], start=(a == 0),
                                          stop=(a == NCHUNK - 1))
                    fin(mm)

                def a1ev(v, fin, lo=lo):
                    for dcn in range(lo, lo + 4):
                        e = v.scalar_tensor_tensor(a1t[:, dcn, :], ps8[:, dcn - lo, :],
                                                   cols["bo"][:, dcn:dcn + 1],
                                                   xnt[:, dcn, :], ALU.add, ALU.add)
                    fin(e)
                S.stage(tensor=womm)
                S.stage(vector=a1ev)

            def x2w(v, c, kt=kt):
                e = v.tensor_scalar(x2b[:, c, kt * TT:(kt + 1) * TT], t1s[:],
                                    cols["g2"][:, c:c + 1], cols["be2"][:, c:c + 1],
                                    ALU.mult, ALU.add)
                ctx.seq(v, e)
                return v.tensor_scalar(a1t[:, c, :], t1s[:], cols["g2"][:, c:c + 1],
                                       cols["be2"][:, c:c + 1], ALU.mult, ALU.add)
            ln_affine("g2", "be2", x2w)

            def a2w(s, fin, kt=kt):
                fin(s.dma_start(out=a2h[:, :, tb + kt * TT: tb + (kt + 1) * TT],
                                in_=a1t.ap()), 16)
            S.stage(sync=a2w)

        if phases < 5:
            continue
        for r in range(4):
            def ldw12(g, fin, r=r):
                fin(g.dma_start(out=wsl[:, 0],
                                in_=wT["w1"][0][:, :, r * 1024:(r + 1) * 1024]), 16)
                fin(g.dma_start(out=wsl[:, 1],
                                in_=wT["w2"][0][:, r * NCHUNK:(r + 1) * NCHUNK, :]),
                    16)
            S.stage(gpsimd=ldw12)

            for kt in range(ntt):
                for lo in (0, 4):
                    def hmm(t, fin, lo=lo, kt=kt):
                        for hcn in range(lo, lo + 4):
                            for a in range(NCHUNK):
                                mm = t.matmul(ps8[:, hcn - lo, :],
                                              wsl[:, 0, a, hcn * 128:(hcn + 1) * 128],
                                              x2b[:, a, kt * TT:(kt + 1) * TT],
                                              start=(a == 0), stop=(a == NCHUNK - 1))
                        fin(mm)

                    def hev(v, fin, lo=lo, r=r):
                        for hcn in range(lo, lo + 4):
                            e = v.tensor_scalar(hb[:, hcn, :], ps8[:, hcn - lo, :],
                                                cols["b1"][:, r * 8 + hcn:
                                                           r * 8 + hcn + 1],
                                                0.0, ALU.add, ALU.max)
                        fin(e)
                    S.stage(tensor=hmm)
                    S.stage(vector=hev)
                for lo in (0, 4):
                    def f2mm(t, fin, lo=lo):
                        for dcn in range(lo, lo + 4):
                            for a in range(NCHUNK):
                                mm = t.matmul(ps8[:, dcn - lo, :],
                                              wsl[:, 1, a, dcn * 128:(dcn + 1) * 128],
                                              hb[:, a, :], start=(a == 0),
                                              stop=(a == NCHUNK - 1))
                        fin(mm)

                    def f2ev(v, fin, lo=lo, r=r):
                        for dcn in range(lo, lo + 4):
                            if r == 3:
                                e = v.tensor_scalar(a1t[:, dcn, :], ps8[:, dcn - lo, :],
                                                    cols["b2"][:, dcn:dcn + 1], None,
                                                    ALU.add)
                            else:
                                e = v.tensor_copy(a1t[:, dcn, :], ps8[:, dcn - lo, :])
                        fin(e)
                    S.stage(tensor=f2mm)
                    S.stage(vector=f2ev)

                def fac(g, fin, kt=kt):
                    fin(g.dma_start(out=a2h[:, :, tb + kt * TT: tb + (kt + 1) * TT],
                                    in_=a1t.ap(), accum_op=ALU.add), 16)
                S.stage(gpsimd=fac)

        if phases < 6:
            continue
        for kt in range(ntt):
            def lda2(s, fin, kt=kt):
                fin(s.dma_start(out=a1t.ap(),
                                in_=a2h[:, :, tb + kt * TT: tb + (kt + 1) * TT]), 16)
            S.stage(sync=lda2)

            def l3w(v, c):
                return v.tensor_scalar(a1t[:, c, :], t1s[:], cols["g3"][:, c:c + 1],
                                       cols["be3"][:, c:c + 1], ALU.mult, ALU.add)
            ln_affine("g3", "be3", l3w)

            def l3acc(g, fin, kt=kt):
                fin(g.dma_start(out=x3[:, :, tb + kt * TT: tb + (kt + 1) * TT],
                                in_=a1t.ap(), accum_op=ALU.add), 16)
            S.stage(gpsimd=l3acc)

    return dict(done=[(S.bar, ctx.val(S.bar))] +
                [(s, ctx.val(s)) for s in S.bard.values()])


def emit_l3attn(ctx, tag, xn3, x3, W, NTOK, gates_acc=(), SEQ=2048):
    blk = ctx.blk
    TT = 512
    NB = NTOK // SEQ
    ntt = SEQ // TT
    NTILE = SEQ // 128
    H = 4

    ps = ctx.psum(f"{tag}_ps", [128, 8, TT], F32)
    psT = ps[:, 0, :].bitcast(BF16).rearrange("p (a f) -> p a f", f=128)

    qn = ctx.dram_tmp(f"{tag}_qn", [128, NCHUNK, NTOK], BF16)
    kn = ctx.dram_tmp(f"{tag}_kn", [128, NCHUNK, NTOK], BF16)
    vp = ctx.dram_tmp(f"{tag}_vp", [128, NCHUNK, NTOK], BF16)
    gdr = ctx.dram_tmp(f"{tag}_gdr", [128, NTOK // 128, 20], F32)

    wsl3 = ctx.sbuf(f"{tag}_wsl3", [128, 3, NCHUNK, D], BF16)
    wcb = ctx.sbuf(f"{tag}_wcb", [128, NCHUNK, 16], BF16)
    ccol = ctx.sbuf(f"{tag}_ccol", [128, NCHUNK, 12], F32)
    bghc = ctx.sbuf(f"{tag}_bgh", [16, 1], F32)
    nwr = ctx.sbuf(f"{tag}_nwr", [1, 256], F32)
    nw_bc = ctx.sbuf(f"{tag}_nwbc", [128, 256], BF16)
    xnt3 = ctx.sbuf(f"{tag}_xnt", [128, NCHUNK, TT], BF16)
    wbuf = ctx.sbuf(f"{tag}_wbuf", [128, NCHUNK, TT + 3], BF16)
    cbuf = ctx.sbuf(f"{tag}_cbuf", [128, NCHUNK, TT], F32)
    obf = ctx.sbuf(f"{tag}_obf", [128, NCHUNK, TT], BF16)
    rnr = ctx.sbuf(f"{tag}_rnr", [1, H, TT], F32)
    car = ctx.sbuf(f"{tag}_car", [128, 3, NCHUNK, 3], BF16)
    gt32w = ctx.sbuf(f"{tag}_gt32w", [16, TT], F32)
    gtt = ctx.sbuf(f"{tag}_gtt", [128, 4, 16], F32)
    gsc = ctx.sbuf(f"{tag}_gsc", [128, 4, 4], F32)
    gden = ctx.sbuf(f"{tag}_gden", [128, 4, 4], F32)
    gout = ctx.sbuf(f"{tag}_gout", [128, 4, 20], F32)
    qkv_b = cbuf.ap().bitcast(BF16).rearrange("p a f -> p (a f)").rearrange(
        "p (c f) -> p c f", c=4)
    v_b = obf.ap().rearrange("p a f -> p (a f)").rearrange(
        "p (c f) -> p c f", c=2)
    Qf = qkv_b[:, 0:2, 0:SEQ]
    Kf = qkv_b[:, 2:4, 0:SEQ]
    Vf = v_b[:, 0:2, 0:SEQ]
    O_fm = ctx.sbuf(f"{tag}_Ofm", [128, NCHUNK, SEQ], BF16)
    gt128 = ctx.sbuf(f"{tag}_gt128", [128, NTILE, 20], F32)
    g32 = ctx.sbuf(f"{tag}_g32", [32, NTILE, 4, 20], F32)
    ktmb = ctx.sbuf(f"{tag}_ktmb", [128, 2, 128], BF16)
    vtmb = ctx.sbuf(f"{tag}_vtmb", [128, 2, 128], BF16)
    vwb = ctx.sbuf(f"{tag}_vwb", [128, 2, 128], BF16)
    vbb = ctx.sbuf(f"{tag}_vbb", [128, 2, 2, 128], BF16)
    kbb = ctx.sbuf(f"{tag}_kbb", [128, 2, 2, 128], BF16)
    utm = ctx.sbuf(f"{tag}_utm", [128, 2, 2, 128], BF16)
    wmf = ctx.sbuf(f"{tag}_wmf", [128, 2, 2, 128], BF16)
    mf = ctx.sbuf(f"{tag}_mf", [128, 2, 12, 128], BF16)
    t128 = ctx.sbuf(f"{tag}_t128", [128, 128], F32)
    S32 = ctx.sbuf(f"{tag}_S32", [128, 2, 2, 256], F32)
    Sbf = ctx.sbuf(f"{tag}_Sbf", [128, 2, 2, 256], BF16)
    u2b = ctx.sbuf(f"{tag}_u2b", [32, 2, 256], BF16)
    k32b = ctx.sbuf(f"{tag}_k32b", [32, 256], BF16)
    atb = ctx.sbuf(f"{tag}_atb", [32, 32], BF16)
    scr32 = ctx.sbuf(f"{tag}_scr32", [32, 256], F32)
    ssq = ctx.sbuf(f"{tag}_ssq", [32, 2], F32)
    o32T = ctx.sbuf(f"{tag}_o32T", [32, 4, 256], BF16)

    ident = ctx.bufs["ident_bf"]
    ident32 = ctx.bufs["ident_f32"]
    ones_row = ctx.bufs["ones_row"]
    ones_col = ctx.bufs["ones_col"]
    mlneg = ctx.bufs["mlneg"]
    mu32 = ctx.bufs["mu32"]

    S = Stages(ctx, tag)
    WA = [("c_const", ctx.val("c_const"))] + list(gates_acc)

    def gate_stage(e, fin):
        for sem, val in WA:
            e.wait_ge(ctx.sem(sem), val)
    S.stage(tensor=gate_stage, vector=gate_stage, scalar=gate_stage,
            sync=gate_stage, gpsimd=gate_stage)

    def ldw(g, fin):
        for i, nm in enumerate(("wq", "wk", "wv")):
            fin(g.dma_start(out=wsl3[:, i], in_=W[nm]), 16)
        fin(g.dma_start(out=wcb.ap(), in_=W["wcat"]), 16)

    def ldw2(s, fin):
        fin(s.dma_start(out=ccol.ap(), in_=W["conv"]), 16)
        fin(s.dma_start(out=bghc.ap(), in_=W["bgh"]), 16)
        fin(s.dma_start(out=nwr.ap(), in_=W["normw"]), 16)
    S.stage(gpsimd=ldw, sync=ldw2)

    def nw1(t, fin):
        fin(t.matmul(ps[:, 0, 0:256], ones_row[:1, 0:128], nwr[0:1, :],
                     start=True, stop=True))
    S.stage(tensor=nw1)

    def nw2(v, fin):
        fin(v.tensor_copy(nw_bc.ap(), ps[:, 0, 0:256]))
    S.stage(vector=nw2)

    for b in range(NB):
        def cz(v, fin):
            fin(v.memset(car.ap(), 0.0))
        S.stage(vector=cz)

        for kt in range(ntt):
            tb = b * SEQ + kt * TT

            def ldx(s, fin, tb=tb):
                fin(s.dma_start(out=xnt3.ap(), in_=xn3[:, :, tb:tb + TT]), 16)
            S.stage(sync=ldx)

            for pi in range(3):
                for lo in (0, 4):
                    def pm(t, fin, pi=pi, lo=lo):
                        for dcn in range(lo, lo + 4):
                            for a in range(NCHUNK):
                                mm = t.matmul(ps[:, dcn - lo, :],
                                              wsl3[:, pi, a,
                                                   dcn * 128:(dcn + 1) * 128],
                                              xnt3[:, a, :], start=(a == 0),
                                              stop=(a == NCHUNK - 1))
                        fin(mm)

                    def pe(v, fin, lo=lo):
                        for dcn in range(lo, lo + 4):
                            e = v.tensor_copy(wbuf[:, dcn, 3:3 + TT],
                                              ps[:, dcn - lo, :])
                        fin(e)
                    S.stage(tensor=pm)
                    S.stage(vector=pe)

                def cv(v, fin, pi=pi):
                    ctx.seq(v, v.tensor_copy(wbuf[:, :, 0:3], car[:, pi]))
                    for c in range(NCHUNK):
                        ctx.seq(v, v.tensor_scalar(
                            cbuf[:, c, :], wbuf[:, c, 0:TT],
                            ccol[:, c, 4 * pi:4 * pi + 1], None, ALU.mult))
                        for j in range(1, 4):
                            ctx.seq(v, v.scalar_tensor_tensor(
                                cbuf[:, c, :], wbuf[:, c, j:j + TT],
                                ccol[:, c, 4 * pi + j:4 * pi + j + 1],
                                cbuf[:, c, :], ALU.mult, ALU.add))
                    fin(v.tensor_copy(car[:, pi], wbuf[:, :, TT:TT + 3]))
                S.stage(vector=cv)

                if pi < 2:
                    def sl(sc, fin, pi=pi):
                        for c in range(NCHUNK):
                            ctx.seq(sc, sc.activation(wbuf[:, c, 3:3 + TT],
                                                      cbuf[:, c, :], AF.Silu))
                            e = sc.activation(cbuf[:, c, :],
                                              wbuf[:, c, 3:3 + TT], AF.Square)
                            if c < NCHUNK - 1:
                                ctx.seq(sc, e)
                        fin(e)
                    S.stage(scalar=sl)

                    def nq2(t, fin):
                        for h in range(H):
                            for d in range(2):
                                mm = t.matmul(ps[0:1, h, :], ones_col[:],
                                              cbuf[:, 2 * h + d, :],
                                              start=(d == 0), stop=(d == 1))
                        fin(mm)
                    S.stage(tensor=nq2)

                    def nq3(sc, fin):
                        for h in range(H):
                            ctx.seq(sc, sc.activation(rnr[0:1, h, :],
                                                      ps[0:1, h, :], AF.Ln,
                                                      bias=kap(ctx, 1e-12, 1)))
                            e = sc.activation(rnr[0:1, h, :], rnr[0:1, h, :],
                                              AF.Exp, scale=-0.5)
                            if h < H - 1:
                                ctx.seq(sc, e)
                        fin(e)
                    S.stage(scalar=nq3)

                    def nq4(t, fin):
                        for h in range(H):
                            mm = t.matmul(ps[:, 4 + h, :], ones_row[:1, 0:128],
                                          rnr[0:1, h, :], start=True, stop=True)
                        fin(mm)
                    S.stage(tensor=nq4)

                    def nq5(v, fin):
                        for h in range(H):
                            for d in range(2):
                                e = v.tensor_tensor(obf[:, 2 * h + d, :],
                                                    wbuf[:, 2 * h + d, 3:3 + TT],
                                                    ps[:, 4 + h, :], ALU.mult)
                        fin(e)
                    S.stage(vector=nq5)

                    def st1(s, fin, pi=pi, tb=tb):
                        dst = qn if pi == 0 else kn
                        fin(s.dma_start(out=dst[:, :, tb:tb + TT],
                                        in_=obf.ap()), 16)
                    S.stage(sync=st1)
                else:
                    def sl2(sc, fin):
                        for c in range(NCHUNK):
                            e = sc.activation(wbuf[:, c, 3:3 + TT],
                                              cbuf[:, c, :], AF.Silu)
                        fin(e)
                    S.stage(scalar=sl2)

                    def st2(s, fin, tb=tb):
                        fin(s.dma_start(out=vp[:, :, tb:tb + TT],
                                        in_=wbuf[:, :, 3:3 + TT]), 16)
                    S.stage(sync=st2)

            def gm(t, fin):
                for a in range(NCHUNK):
                    mm = t.matmul(ps[0:16, 0, :], wcb[:, a, :], xnt3[:, a, :],
                                  start=(a == 0), stop=(a == NCHUNK - 1))
                fin(mm)
            S.stage(tensor=gm)

            def ge(v, fin):
                fin(v.tensor_scalar(gt32w.ap(), ps[0:16, 0, :],
                                    bghc[0:16, 0:1], None, ALU.add))
            S.stage(vector=ge)

            def gt(t, fin):
                for st in range(4):
                    mm = t.transpose(ps[:, 1, st * 16:(st + 1) * 16],
                                     gt32w[0:16, st * 128:(st + 1) * 128],
                                     ident32[0:16, 0:16])
                fin(mm)
            S.stage(tensor=gt)

            def gw1(v, fin):
                fin(v.tensor_copy(gtt.ap(),
                                  ps[:, 1, 0:64].rearrange("p (a f) -> p a f",
                                                           f=16)))
            S.stage(vector=gw1)

            def gw2(sc, fin):
                for st in range(4):
                    e = sc.activation(gtt[:, st, 0:8], gtt[:, st, 0:8],
                                      AF.Sigmoid)
                fin(e)
            S.stage(scalar=gw2)

            def gw3(v, fin):
                for st in range(4):
                    ctx.seq(v, v.tensor_tensor(gsc[:, st], gtt[:, st, 8:12],
                                               gtt[:, st, 12:16], ALU.max))
                    ctx.seq(v, v.tensor_scalar(gsc[:, st], gsc[:, st], 0.0,
                                               None, ALU.max))
                    ctx.seq(v, v.tensor_sub(gtt[:, st, 8:12], gtt[:, st, 8:12],
                                            gsc[:, st]))
                    e = v.tensor_sub(gtt[:, st, 12:16], gtt[:, st, 12:16],
                                     gsc[:, st])
                    if st < 3:
                        ctx.seq(v, e)
                fin(e)
            S.stage(vector=gw3)

            def gw4(sc, fin):
                for st in range(4):
                    ctx.seq(sc, sc.activation(gtt[:, st, 8:12], gtt[:, st, 8:12],
                                              AF.Exp))
                    ctx.seq(sc, sc.activation(gtt[:, st, 12:16],
                                              gtt[:, st, 12:16], AF.Exp))
                    e = sc.activation(gsc[:, st], gsc[:, st], AF.Exp, scale=-1.0)
                    if st < 3:
                        ctx.seq(sc, e)
                fin(e)
            S.stage(scalar=gw4)

            def gw5(v, fin):
                for st in range(4):
                    ctx.seq(v, v.tensor_tensor(gden[:, st], gtt[:, st, 8:12],
                                               gtt[:, st, 12:16], ALU.add))
                    ctx.seq(v, v.tensor_tensor(gden[:, st], gden[:, st],
                                               gsc[:, st], ALU.add))
                    ctx.seq(v, v.reciprocal(gden[:, st], gden[:, st]))
                    ctx.seq(v, v.tensor_copy(gout[:, st, 0:8], gtt[:, st, 0:8]))
                    ctx.seq(v, v.tensor_tensor(gout[:, st, 8:12],
                                               gtt[:, st, 8:12], gden[:, st],
                                               ALU.mult))
                    ctx.seq(v, v.tensor_tensor(gout[:, st, 12:16],
                                               gtt[:, st, 12:16], gden[:, st],
                                               ALU.mult))
                    e = v.tensor_tensor(gout[:, st, 16:20], gsc[:, st],
                                        gden[:, st], ALU.mult)
                    if st < 3:
                        ctx.seq(v, e)
                fin(e)
            S.stage(vector=gw5)

            def gd(s, fin, tb=tb):
                fin(s.dma_start(out=gdr[:, tb // 128:tb // 128 + 4, :],
                                in_=gout.ap()), 16)
            S.stage(sync=gd)

    def ldwo(g, fin):
        fin(g.dma_start(out=wsl3[:, 0], in_=W["wo"]), 16)
    S.stage(gpsimd=ldwo)

    for b in range(NB):
        for h in range(H):
            def bl(s, fin, b=b, h=h):
                fin(s.dma_start(out=Qf, in_=qn[:, 2 * h:2 * h + 2,
                                             b * SEQ:(b + 1) * SEQ]), 16)
                fin(s.dma_start(out=Kf, in_=kn[:, 2 * h:2 * h + 2,
                                             b * SEQ:(b + 1) * SEQ]), 16)
                fin(s.dma_start(out=Vf, in_=vp[:, 2 * h:2 * h + 2,
                                             b * SEQ:(b + 1) * SEQ]), 16)
                fin(s.dma_start(out=gt128.ap(),
                                in_=gdr[:, b * NTILE:(b + 1) * NTILE, :]), 16)
                for qq in range(4):
                    fin(s.dma_start(
                        out=g32[:, :, qq, :],
                        in_=gdr[32 * qq:32 * (qq + 1),
                                b * NTILE:(b + 1) * NTILE, :]), 16)

            def bl2(v, fin):
                ctx.seq(v, v.memset(S32.ap(), 0.0))
                fin(v.memset(Sbf.ap(), 0.0))
            S.stage(sync=bl, vector=bl2)

            for it in range(NTILE):
                tok0 = it * 128

                def t1(t, fin, tok0=tok0):
                    for d in range(2):
                        t.transpose(psT[:, d, :], Kf[:, d, tok0:tok0 + 128],
                                    ident[:])
                        mm = t.transpose(psT[:, 2 + d, :],
                                         Vf[:, d, tok0:tok0 + 128], ident[:])
                    fin(mm)
                S.stage(tensor=t1)

                def t2(v, fin, it=it, h=h):
                    ctx.seq(v, v.tensor_copy(ktmb.ap(), psT[:, 0:2, :]))
                    ctx.seq(v, v.tensor_copy(vtmb.ap(), psT[:, 2:4, :]))
                    ctx.seq(v, v.tensor_scalar(vwb.ap(), vtmb.ap(),
                                               gt128[:, it, 16 + h:17 + h],
                                               None, ALU.mult))
                    for s in range(2):
                        bc = gt128[:, it, 4 * s + h:4 * s + h + 1]
                        ctx.seq(v, v.tensor_scalar(vbb[:, s], vtmb.ap(), bc,
                                                   None, ALU.mult))
                        e = v.tensor_scalar(kbb[:, s], ktmb.ap(), bc,
                                            None, ALU.mult)
                        if s < 1:
                            ctx.seq(v, e)
                    fin(e)
                S.stage(vector=t2)

                def t3(t, fin, tok0=tok0):
                    for d in range(2):
                        mm = t.matmul(ps[:, 1, 0:128], Kf[:, d, tok0:tok0 + 128],
                                      Kf[:, d, tok0:tok0 + 128],
                                      start=(d == 0), stop=(d == 1))
                    fin(mm)
                S.stage(tensor=t3)

                def t4(v, fin, it=it, h=h):
                    for s in range(2):
                        bc = gt128[:, it, 4 * s + h:4 * s + h + 1]
                        ctx.seq(v, v.tensor_scalar(t128.ap(), ps[:, 1, 0:128],
                                                   bc, None, ALU.mult))
                        e = v.tensor_tensor(mf[:, s, 0, :], t128.ap(),
                                            mlneg[:], ALU.mult)
                        if s < 1:
                            ctx.seq(v, e)
                    fin(e)
                S.stage(vector=t4)

                def t5(t, fin):
                    for s in range(2):
                        mm = t.transpose(psT[:, s, :], mf[:, s, 0, :], ident[:])
                    fin(mm)
                S.stage(tensor=t5)

                def t6(v, fin):
                    for s in range(2):
                        e = v.tensor_copy(mf[:, s, 1, :], psT[:, s, :])
                    fin(e)
                S.stage(vector=t6)

                for r in range(3):
                    def sq(t, fin, r=r):
                        for s in range(2):
                            t.matmul(ps[:, 1 + 2 * s, 0:128], mf[:, s, 2 * r + 1, :],
                                     mf[:, s, 2 * r, :], start=True, stop=True)
                            mm = t.matmul(ps[:, 2 + 2 * s, 0:128],
                                          mf[:, s, 2 * r, :],
                                          mf[:, s, 2 * r + 1, :],
                                          start=True, stop=True)
                        fin(mm)

                    def sqe(v, fin, r=r):
                        for s in range(2):
                            v.tensor_copy(mf[:, s, 2 * r + 2, :],
                                          ps[:, 1 + 2 * s, 0:128])
                            e = v.tensor_copy(mf[:, s, 2 * r + 3, :],
                                              ps[:, 2 + 2 * s, 0:128])
                        fin(e)
                    S.stage(tensor=sq)
                    S.stage(vector=sqe)

                def m16(t, fin):
                    for s in range(2):
                        mm = t.matmul(ps[:, 1 + s, 0:128], mf[:, s, 7, :],
                                      mf[:, s, 6, :], start=True, stop=True)
                    fin(mm)
                S.stage(tensor=m16)

                def m16e(v, fin):
                    for s in range(2):
                        e = v.tensor_copy(mf[:, s, 8, :], ps[:, 1 + s, 0:128])
                    fin(e)
                S.stage(vector=m16e)

                qsteps = [
                    ([(None, None), (None, 1), (None, 3), (2, 1)], 9),
                    ([(None, 9), (4, 9)], 10),
                    ([(None, 10), (6, 10)], 9),
                    ([(None, 9), (8, 9)], 11),
                ]
                for terms, dst in qsteps:
                    def qmm(t, fin, terms=terms):
                        for s in range(2):
                            for i, (lh, rh) in enumerate(terms):
                                lhs = ident[:] if lh is None else mf[:, s, lh, :]
                                rhs = ident[:] if rh is None else mf[:, s, rh, :]
                                mm = t.matmul(ps[:, 1 + s, 0:128], lhs, rhs,
                                              start=(i == 0),
                                              stop=(i == len(terms) - 1))
                        fin(mm)

                    def qev(v, fin, dst=dst):
                        for s in range(2):
                            e = v.tensor_copy(mf[:, s, dst, :],
                                              ps[:, 1 + s, 0:128])
                        fin(e)
                    S.stage(tensor=qmm)
                    S.stage(vector=qev)

                def u1(t, fin):
                    for s in range(2):
                        t.matmul(ps[:, 1 + s, 0:256], mf[:, s, 11, :],
                                 vbb[:, s].rearrange("p a f -> p (a f)"),
                                 start=True, stop=True)
                        mm = t.matmul(ps[:, 3 + s, 0:256], mf[:, s, 11, :],
                                      kbb[:, s].rearrange("p a f -> p (a f)"),
                                      start=True, stop=True)
                    fin(mm)
                S.stage(tensor=u1)

                def u2(v, fin):
                    for s in range(2):
                        v.tensor_copy(utm[:, s],
                                      ps[:, 1 + s, 0:256].rearrange(
                                          "p (a f) -> p a f", f=128))
                        e = v.tensor_scalar_mul(kbb[:, s],
                                                ps[:, 3 + s, 0:256].rearrange(
                                                    "p (a f) -> p a f", f=128),
                                                -1.0)
                    fin(e)
                S.stage(vector=u2)

                def u3(t, fin):
                    for s in range(2):
                        for d in range(2):
                            mm = t.transpose(psT[:, 2 * s + d, :],
                                             kbb[:, s, d, :], ident[:])
                    fin(mm)
                S.stage(tensor=u3)

                def u4(v, fin):
                    for s in range(2):
                        e = v.tensor_copy(wmf[:, s], psT[:, 2 * s:2 * s + 2, :])
                    fin(e)
                S.stage(vector=u4)

                for ci in range(4):
                    j32 = it * 4 + ci

                    def ca(t, fin, tok0=tok0, ci=ci):
                        for s in range(2):
                            t.matmul(ps[0:32, 1 + s, 0:256],
                                     ident[:, 32 * ci:32 * ci + 32],
                                     utm[:, s].rearrange("p a f -> p (a f)"),
                                     start=True, stop=False)
                            for d in range(2):
                                t.matmul(ps[0:32, 1 + s, 0:256],
                                         wmf[:, s, d, 32 * ci:32 * ci + 32],
                                         Sbf[:, s, d, :], start=False,
                                         stop=(d == 1))
                        t.matmul(ps[0:32, 3, 0:256],
                                 ident[:, 32 * ci:32 * ci + 32],
                                 ktmb.ap().rearrange("p a f -> p (a f)"),
                                 start=True, stop=True)
                        for d in range(2):
                            mm = t.matmul(ps[0:32, 4, 0:32],
                                          Kf[:, d, tok0 + 32 * ci:
                                             tok0 + 32 * ci + 32],
                                          Qf[:, d, tok0 + 32 * ci:
                                             tok0 + 32 * ci + 32],
                                          start=(d == 0), stop=(d == 1))
                        fin(mm)
                    S.stage(tensor=ca)

                    def cb(v, fin, it=it, ci=ci, h=h):
                        for s in range(2):
                            ctx.seq(v, v.tensor_scalar(
                                u2b[:, s, :], ps[0:32, 1 + s, 0:256],
                                g32[0:32, it, ci, 8 + 4 * s + h:9 + 4 * s + h],
                                None, ALU.mult))
                        ctx.seq(v, v.tensor_copy(k32b.ap(), ps[0:32, 3, 0:256]))
                        fin(v.tensor_tensor(atb.ap(), ps[0:32, 4, 0:32],
                                            mu32[:], ALU.mult))
                    S.stage(vector=cb)

                    def cc(t, fin, tok0=tok0, ci=ci):
                        first = True
                        for s in range(2):
                            for d in range(2):
                                t.matmul(ps[0:32, 5, 0:256],
                                         Qf[:, d, tok0 + 32 * ci:
                                            tok0 + 32 * ci + 32],
                                         Sbf[:, s, d, :], start=first,
                                         stop=False)
                                first = False
                        for s in range(2):
                            t.matmul(ps[0:32, 5, 0:256], atb[:], u2b[:, s, :],
                                     start=False, stop=False)
                        mm = t.matmul(ps[0:32, 5, 0:256],
                                      ident[:, 32 * ci:32 * ci + 32],
                                      vwb.ap().rearrange("p a f -> p (a f)"),
                                      start=False, stop=True)
                        fin(mm)
                    S.stage(tensor=cc)

                    def cd(sc, fin):
                        ctx.seq(sc, sc.activation(scr32.ap(),
                                                  ps[0:32, 5, 0:256], AF.Square,
                                                  accum_out=ssq[:, 0:1]))
                        ctx.seq(sc, sc.activation(ssq[:, 1:2], ssq[:, 0:1],
                                                  AF.Ln, scale=1.0 / 256.0,
                                                  bias=kap(ctx, 1e-5, 32)))
                        fin(sc.activation(ssq[:, 1:2], ssq[:, 1:2], AF.Exp,
                                          scale=-0.5))

                    def ce(t, fin):
                        for s in range(2):
                            for d1 in range(2):
                                mm = t.matmul(ps[:, 1 + 2 * s + d1, 0:256],
                                              k32b[:, 128 * d1:128 * (d1 + 1)],
                                              u2b[:, s, :], start=True,
                                              stop=True)
                        fin(mm)
                    S.stage(scalar=cd, tensor=ce)

                    def cf(v, fin, ci=ci):
                        ctx.seq(v, v.tensor_tensor(scr32.ap(),
                                                   ps[0:32, 5, 0:256],
                                                   nw_bc[0:32, :], ALU.mult))
                        ctx.seq(v, v.tensor_scalar(o32T[:, ci, :], scr32.ap(),
                                                   ssq[0:32, 1:2], None,
                                                   ALU.mult))
                        for s in range(2):
                            ctx.seq(v, v.tensor_tensor(
                                S32[:, s], S32[:, s],
                                ps[:, 1 + 2 * s:3 + 2 * s, 0:256], ALU.add))
                            e = v.tensor_copy(Sbf[:, s], S32[:, s])
                            if s < 1:
                                ctx.seq(v, e)
                        fin(e)
                    S.stage(vector=cf)

                def cg(t, fin):
                    for ci in range(4):
                        for d in range(2):
                            mm = t.transpose(psT[:, ci * 2 + d, 0:32],
                                             o32T[:, ci, 128 * d:128 * (d + 1)],
                                             ident[0:32, 0:32])
                    fin(mm)
                S.stage(tensor=cg)

                def ch(v, fin, tok0=tok0, h=h):
                    for ci in range(4):
                        for d in range(2):
                            e = v.tensor_copy(O_fm[:, 2 * h + d,
                                                   tok0 + 32 * ci:
                                                   tok0 + 32 * ci + 32],
                                              psT[:, ci * 2 + d, 0:32])
                    fin(e)
                S.stage(vector=ch)

        for kt in range(ntt):
            for lo in (0, 4):
                def cm(t, fin, kt=kt, lo=lo):
                    for dcn in range(lo, lo + 4):
                        for a in range(NCHUNK):
                            mm = t.matmul(ps[:, dcn - lo, :],
                                          wsl3[:, 0, a, dcn * 128:(dcn + 1) * 128],
                                          O_fm[:, a, kt * TT:(kt + 1) * TT],
                                          start=(a == 0), stop=(a == NCHUNK - 1))
                    fin(mm)

                def cev(v, fin, lo=lo):
                    for dcn in range(lo, lo + 4):
                        e = v.tensor_copy(cbuf[:, dcn, :], ps[:, dcn - lo, :])
                    fin(e)
                S.stage(tensor=cm)
                S.stage(vector=cev)

            def cdma(g, fin, b=b, kt=kt):
                fin(g.dma_start(out=x3[:, :, b * SEQ + kt * TT:
                                       b * SEQ + (kt + 1) * TT],
                                in_=cbuf.ap(), accum_op=ALU.add), 16)
            S.stage(gpsimd=cdma)

    return dict(done=[(S.bar, ctx.val(S.bar))] +
                [(s, ctx.val(s)) for s in S.bard.values()])


def emit_xadd(ctx, tag, xn3, x3, NTOK, gates_acc=()):
    TT = 512
    nt = NTOK // TT
    xb = ctx.sbuf(f"{tag}_xb", [128, NCHUNK, TT], BF16)
    xf = ctx.sbuf(f"{tag}_xf", [128, NCHUNK, TT], F32)
    S = Stages(ctx, tag)

    def gate_stage(e, fin):
        for sem, val in gates_acc:
            e.wait_ge(ctx.sem(sem), val)
    S.stage(sync=gate_stage, vector=gate_stage, gpsimd=gate_stage)
    for kt in range(nt):
        def ld(s, fin, kt=kt):
            fin(s.dma_start(out=xb.ap(), in_=xn3[:, :, kt * TT:(kt + 1) * TT]),
                16)
        S.stage(sync=ld)

        def cp(v, fin):
            fin(v.tensor_copy(xf.ap(), xb.ap()))
        S.stage(vector=cp)

        def ac(g, fin, kt=kt):
            fin(g.dma_start(out=x3[:, :, kt * TT:(kt + 1) * TT], in_=xf.ap(),
                            accum_op=ALU.add), 16)
        S.stage(gpsimd=ac)
    return dict(done=[(S.bar, ctx.val(S.bar))] +
                [(s, ctx.val(s)) for s in S.bard.values()])


import ml_dtypes
from contextlib import ExitStack

NTOK_K = 4096
SEQ_K = 2048
USE_FUSED = True
_ALLC = ("ident_bf", "ident_f32", "ones_col", "ones_col_bf", "ones_row",
         "ones_row_bf", "konst", "sel8", "sel32", "bones", "mlneg", "mu32")
_KEEP = {
    "xn": ("ones_col", "ones_row", "konst"),
    "l0attn": ("bones", "ident_bf", "ident_f32", "ones_col", "ones_col_bf",
               "ones_row", "ones_row_bf", "konst"),
    "l0ffn": ("ident_bf", "konst"),
    "l1attn": ("sel32", "ident_f32", "konst"),
    "l1ffn": ("ident_f32", "ones_row", "sel8", "konst"),
    "l2attn": ("ident_bf", "ones_row", "konst"),
    "l2ffn": ("ident_bf", "ident_f32", "konst"),
    "l3attn": ("ident_bf", "ident_f32", "ones_col", "ones_row", "konst",
               "mlneg", "mu32"),
    "l3ffn": ("ident_bf", "ident_f32", "ones_col", "ones_col_bf", "ones_row",
              "sel8", "konst"),
    "xadd": (),
}
_SKIP = {k: tuple(c for c in _ALLC if c not in v) for k, v in _KEEP.items()}


def _fm(x):
    return np.ascontiguousarray(x.T.reshape(8, 128, -1).transpose(1, 0, 2))


def _unfm(a):
    return np.asarray(a).transpose(1, 0, 2).reshape(1024, -1).T


def _wTh(w):
    return np.ascontiguousarray(
        w.reshape(-1, 128, w.shape[1]).transpose(1, 0, 2))


def _finish(ctx, blk, r):
    @blk.vector
    def _(v):
        for s, val in r["done"]:
            ctx.wait(v, s, val)


_W_SHAPES = {
    "l0attn": [("wq", [1024, 1024], F32), ("wk", [1024, 1024], F32),
               ("wv", [1024, 1024], F32), ("wo", [1024, 1024], F32),
               ("w1", [4096, 1024], F32), ("w2", [1024, 4096], F32),
               ("proj", [256, 64], F32),
               ("bq", [1, 1024], F32), ("bk", [1, 1024], F32),
               ("bv", [1, 1024], F32), ("bo", [1, 1024], F32),
               ("b1", [1, 4096], F32), ("b2", [1, 1024], F32),
               ("g2", [1, 1024], F32), ("be2", [1, 1024], F32),
               ("g3", [1, 1024], F32), ("be3", [1, 1024], F32)],
    "l0ffn": [("wg", [3072, 1024], I32), ("wu", [3072, 1024], I32),
              ("wd", [1024, 3072], I32)],
    "l1attn": [("ck", [3, 1, 1024], F32), ("alog", [1024, 16], F32),
               ("wbc", [1024, 32], F32), ("bbc", [1, 32], F32),
               ("dco", [1, 1024], F32), ("wdt", [1024, 64], F32),
               ("bdt", [1, 64], F32), ("wdtp", [64, 1024], F32),
               ("bdtp", [1, 1024], F32)],
    "l1ffn": [("wg", [8, 1024, 1024], F32), ("wu", [8, 1024, 1024], F32),
              ("wd", [8, 1024, 1024], F32), ("wr", [1024, 8], F32),
              ("temp", [1, 1], F32)],
    "l2attn": [("wqkv", [16, 3, 1024, 64], F32), ("wo", [1024, 1024], F32)],
    "l2ffn": [("w1", [4096, 1024], F32), ("w2", [4096, 1024], F32),
              ("h1", [4096, 4096], F32), ("h2", [4096, 4096], F32),
              ("w3", [1024, 4096], F32), ("s1", [1, 4096], F32),
              ("s2", [1, 4096], F32), ("sh1", [1, 4096], F32),
              ("sh2", [1, 4096], F32), ("s3", [1, 1024], F32)],
    "l3attn": [("wq", [128, 8, 1024], F32), ("wk", [128, 8, 1024], F32),
               ("wv", [128, 8, 1024], F32), ("wo", [128, 8, 1024], F32),
               ("wcat", [128, 8, 16], F32), ("conv", [128, 8, 12], F32),
               ("bgh", [16, 1], F32), ("normw", [1, 256], F32)],
    "l3ffn": [("gam", [1, 1024], F32), ("bet", [1, 1024], F32),
              ("wr", [8, 1024], F32), ("gu", [8, 4096, 1024], F32),
              ("dn", [8, 1024, 2048], F32)],
}


def _emit_block(ctx, bnm, xn3, x3, ga):
    declare_consts(ctx, skip=_SKIP[bnm])
    W = {nm: ctx.dram_in(f"{bnm}_{nm}", sh, dt)
         for nm, sh, dt in _W_SHAPES[bnm]}
    if bnm == "l0attn":
        return emit_l0attn(ctx, "a0", xn3, x3, W, NTOK_K, gates_acc=ga)
    if bnm == "l0ffn":
        return emit_l0ffn(ctx, "f0", xn3, x3, W["wg"], W["wu"], W["wd"],
                          NTOK_K, gates_acc=ga)
    if bnm == "l1attn":
        return emit_l1attn(ctx, "m0", xn3, x3, W["ck"], W["alog"], W["wbc"],
                           W["bbc"], W["dco"], W["wdt"], W["bdt"], W["wdtp"],
                           W["bdtp"], NTOK_K, gates_acc=ga)
    if bnm == "l1ffn":
        return emit_l1ffn(ctx, "m1", xn3, x3, W["wg"], W["wu"], W["wd"],
                          W["wr"], W["temp"], NTOK_K, gates_acc=ga)
    if bnm == "l2attn":
        return emit_l2attn(ctx, "a2", xn3, x3, W["wqkv"], W["wo"], NTOK_K,
                           gates_acc=ga)
    if bnm == "l2ffn":
        return emit_l2ffn(ctx, "t2", xn3, x3, W["w1"], W["s1"], W["w2"],
                          W["s2"], W["h1"], W["sh1"], W["h2"], W["sh2"],
                          W["w3"], W["s3"], NTOK_K, gates_acc=ga)
    if bnm == "l3attn":
        return emit_l3attn(ctx, "l3a", xn3, x3, W, NTOK_K, gates_acc=ga,
                           SEQ=SEQ_K)
    if bnm == "l3ffn":
        return emit_l3ffn(ctx, "t3", xn3, x3, W["gam"], W["bet"], W["wr"],
                          W["gu"], W["dn"], NTOK_K, gates_acc=ga)
    raise KeyError(bnm)


_LAYER_BLOCKS = [("l0attn", "l0ffn"), ("l1attn", "l1ffn"),
                 ("l2attn", "l2ffn"), ("l3attn", "l3ffn")]


def _build_fused():
    nc = bass.Bass("TRN2", target_bir_lowering=False)
    xin = nc.dram_tensor("xin", [128, NCHUNK, NTOK_K], F32,
                         kind="ExternalInput").ap()
    x3 = nc.dram_tensor("x", [128, NCHUNK, NTOK_K], F32,
                        kind="ExternalOutput").ap()
    xn3 = nc.dram_tensor("xn_state", [128, NCHUNK, NTOK_K], BF16,
                         kind="Internal").ap()
    uid = [0]

    def pass_(emitfn):
        uid[0] += 1
        with nc.reset_on_exit():
            st = ExitStack()
            ctx = Ctx(nc, st, uid=f"@{uid[0]}")
            with st:
                with nc.Block() as blk:
                    ctx.blk = blk
                    r = emitfn(ctx, blk)
                    if r is not None:
                        _finish(ctx, blk, r)

    def p_init(ctx, blk):
        @blk.sync
        def _(s):
            ctx.inc(s.dma_start(out=x3, in_=xin), "xi", 16)
            ctx.wait(s, "xi", 16)
        return None
    pass_(p_init)

    for li, blocks in enumerate(_LAYER_BLOCKS):
        def p_xn(ctx, blk):
            declare_consts(ctx, skip=_SKIP["xn"])
            return emit_xn(ctx, x3, xn3, NTOK_K, "xn0")
        pass_(p_xn)
        for bnm in blocks:
            pass_(lambda ctx, blk, bnm=bnm: _emit_block(ctx, bnm, xn3, x3, ()))
        if li == 3:
            pass_(lambda ctx, blk: emit_xadd(ctx, "xa", xn3, x3, NTOK_K))
    return nc


def _build_xn():
    nc = bass.Bass("TRN2", target_bir_lowering=False)
    st = ExitStack()
    ctx = Ctx(nc, st)
    x3 = ctx.dram_in("x", [128, NCHUNK, NTOK_K], F32)
    xn3 = ctx.dram_out("xn", [128, NCHUNK, NTOK_K], BF16)
    with st:
        with nc.Block() as blk:
            ctx.blk = blk
            declare_consts(ctx, skip=_SKIP["xn"])
            r = emit_xn(ctx, x3, xn3, NTOK_K, "xn0")
            _finish(ctx, blk, r)
    return nc


def _resid_build(emitfn):
    nc = bass.Bass("TRN2", target_bir_lowering=False)
    st = ExitStack()
    ctx = Ctx(nc, st)
    xn3 = ctx.dram_in("xn", [128, NCHUNK, NTOK_K], BF16)
    x0 = ctx.dram_in("x0", [128, NCHUNK, NTOK_K], F32)
    x3 = ctx.dram_out("x", [128, NCHUNK, NTOK_K], F32)
    with st:
        with nc.Block() as blk:
            ctx.blk = blk

            @blk.sync
            def _(s):
                ctx.inc(s.dma_start(out=x3, in_=x0), "s_xinit", 16)

            r = emitfn(ctx, xn3, x3, [("s_xinit", 16)])
            _finish(ctx, blk, r)
    return nc


def _build_block(bnm):
    return _resid_build(
        lambda ctx, xn3, x3, ga: _emit_block(ctx, bnm, xn3, x3, ga))


def _build_xadd():
    return _resid_build(
        lambda ctx, xn3, x3, ga: emit_xadd(ctx, "xa", xn3, x3, NTOK_K,
                                           gates_acc=ga))


_PROGS = {}


def _prog(name, builder):
    if name not in _PROGS:
        _PROGS[name] = builder()
    return _PROGS[name]


def _weight_maps(inp):
    r1 = lambda v: np.ascontiguousarray(np.asarray(v, np.float32))[None]
    wm = {}
    wm["l0attn"] = {
        "wq": inp["L0_attn_Wq"], "wk": inp["L0_attn_Wk"],
        "wv": inp["L0_attn_Wv"], "wo": inp["L0_attn_Wo"],
        "w1": inp["L0_attn_W1"], "w2": inp["L0_attn_W2"],
        "proj": inp["L0_attn_proj"],
        "bq": r1(inp["L0_attn_bq"]), "bk": r1(inp["L0_attn_bk"]),
        "bv": r1(inp["L0_attn_bv"]), "bo": r1(inp["L0_attn_bo"]),
        "b1": r1(inp["L0_attn_b1"]), "b2": r1(inp["L0_attn_b2"]),
        "g2": r1(inp["L0_attn_g2"]), "be2": r1(inp["L0_attn_be2"]),
        "g3": r1(inp["L0_attn_g3"]), "be3": r1(inp["L0_attn_be3"]),
    }
    wm["l0ffn"] = {
        "wg": np.asarray(inp["L0_ffn_w_gate"], np.int32),
        "wu": np.asarray(inp["L0_ffn_w_up"], np.int32),
        "wd": np.asarray(inp["L0_ffn_w_down"], np.int32),
    }
    wm["l1attn"] = {
        "ck": inp["L1_attn_conv_kernel"], "alog": inp["L1_attn_A_log"],
        "wbc": inp["L1_attn_W_bc"], "bbc": r1(inp["L1_attn_b_bc"]),
        "dco": r1(inp["L1_attn_Dcoeff"]), "wdt": inp["L1_attn_W_dt"],
        "bdt": r1(inp["L1_attn_b_dt"]), "wdtp": inp["L1_attn_W_dtp"],
        "bdtp": r1(inp["L1_attn_b_dtp"]),
    }
    wm["l1ffn"] = {
        "wg": inp["L1_ffn_Wg"], "wu": inp["L1_ffn_Wu"],
        "wd": inp["L1_ffn_Wd"], "wr": inp["L1_ffn_Wr"],
        "temp": r1(inp["L1_ffn_temperature"]),
    }
    wm["l2attn"] = {"wqkv": inp["L2_attn_w_qkv"], "wo": inp["L2_attn_w_o"]}
    wm["l2ffn"] = {
        "w1": inp["L2_ffn_w1_w"], "s1": r1(inp["L2_ffn_w1_s"]),
        "w2": inp["L2_ffn_w2_w"], "s2": r1(inp["L2_ffn_w2_s"]),
        "h1": inp["L2_ffn_h1_w"], "sh1": r1(inp["L2_ffn_h1_s"]),
        "h2": inp["L2_ffn_h2_w"], "sh2": r1(inp["L2_ffn_h2_s"]),
        "w3": inp["L2_ffn_w3_w"], "s3": r1(inp["L2_ffn_w3_s"]),
    }
    f32 = np.float32
    wm["l3attn"] = {
        "wq": _wTh(np.asarray(inp["L3_attn_Wq"], f32)),
        "wk": _wTh(np.asarray(inp["L3_attn_Wk"], f32)),
        "wv": _wTh(np.asarray(inp["L3_attn_Wv"], f32)),
        "wo": _wTh(np.asarray(inp["L3_attn_Wo"], f32)),
        "wcat": _wTh(np.concatenate(
            [np.asarray(inp["L3_attn_Wb"], f32),
             np.asarray(inp["L3_attn_Wg"], f32),
             np.asarray(inp["L3_attn_Wh"], f32)], 1)),
        "conv": _wTh(np.concatenate(
            [np.asarray(inp["L3_attn_conv_q"], f32),
             np.asarray(inp["L3_attn_conv_k"], f32),
             np.asarray(inp["L3_attn_conv_v"], f32)], 1)),
        "bgh": np.concatenate(
            [np.zeros(8, f32), np.asarray(inp["L3_attn_bg"], f32),
             np.asarray(inp["L3_attn_bh"], f32)]).astype(f32)[:, None],
        "normw": np.asarray(inp["L3_attn_norm_w"], f32)[None, :],
    }
    wm["l3ffn"] = {
        "gam": r1(inp["L3_ffn_ln_gamma"]), "bet": r1(inp["L3_ffn_ln_beta"]),
        "wr": inp["L3_ffn_router_w"], "gu": inp["L3_ffn_gate_up_w"],
        "dn": inp["L3_ffn_down_w"],
    }
    return wm


def kernel(**inp):
    from concourse.bass_utils import run_bass_kernel_spmd
    cores = list(range(8))
    x_full = np.asarray(inp["L0_attn_video"], np.float32)
    x_state = [_fm(np.ascontiguousarray(
        x_full[2 * i:2 * i + 2].reshape(NTOK_K, 1024))) for i in range(8)]
    wm = _weight_maps(inp)

    if USE_FUSED:
        allconsts = const_inputs()
        winp = {f"{b}_{k}": v for b, m in wm.items() for k, v in m.items()}
        nc = _prog("fused", _build_fused)
        res = run_bass_kernel_spmd(
            nc, [{**allconsts, **winp, "xin": x_state[i]} for i in cores],
            core_ids=cores)
        x_state = [res.results[i]["x"] for i in cores]
    else:
        cmaps = {k: const_inputs_for(v) for k, v in _SKIP.items()}
        for li, (anm, fnm) in enumerate(_LAYER_BLOCKS):
            nc = _prog("xn", _build_xn)
            res = run_bass_kernel_spmd(
                nc, [{**cmaps["xn"], "x": x_state[i]} for i in cores],
                core_ids=cores)
            xn_state = [res.results[i]["xn"] for i in cores]
            for bnm in (anm, fnm):
                pin = {f"{bnm}_{k}": v for k, v in wm[bnm].items()}
                nc = _prog(bnm, lambda bnm=bnm: _build_block(bnm))
                res = run_bass_kernel_spmd(
                    nc, [{**cmaps[bnm], **pin, "xn": xn_state[i],
                          "x0": x_state[i]} for i in cores], core_ids=cores)
                x_state = [res.results[i]["x"] for i in cores]
            if li == 3:
                nc = _prog("xadd", _build_xadd)
                res = run_bass_kernel_spmd(
                    nc, [{"xn": xn_state[i], "x0": x_state[i]}
                         for i in cores], core_ids=cores)
                x_state = [res.results[i]["x"] for i in cores]

    out = np.concatenate(
        [_unfm(np.asarray(x_state[i], np.float32)).reshape(2, SEQ_K, 1024)
         for i in cores], 0)
    return np.ascontiguousarray(out.astype(np.float32))


# version 04 (final)
# speedup vs baseline: 1.0277x; 1.0277x over previous
import math
import numpy as np
import concourse.bass as bass
import concourse.mybir as mybir

F32 = mybir.dt.float32
BF16 = mybir.dt.bfloat16
I32 = mybir.dt.int32
AF = mybir.ActivationFunctionType
ALU = mybir.AluOpType
AX = mybir.AxisListType

D = 1024
NCHUNK = 8


class Ctx:

    def __init__(self, nc, st, uid=""):
        self.nc = nc
        self.st = st
        self.uid = uid
        self.blk = None
        self.sems = {}
        self.bufs = {}
        self.drams = {}

    def sem(self, name):
        if name not in self.sems:
            h = self.st.enter_context(self.nc.semaphore(name))
            self.sems[name] = [h, 0]
        return self.sems[name][0]

    def inc(self, instr, name, n=1):
        h = self.sem(name)
        instr.then_inc(h, n)
        self.sems[name][1] += n
        return self.sems[name][1]

    def val(self, name):
        self.sem(name)
        return self.sems[name][1]

    def wait(self, eng, name, v=None):
        h = self.sem(name)
        eng.wait_ge(h, self.sems[name][1] if v is None else v)

    def seq(self, eng, instr, name=None):
        name = name or f"seq_{eng.engine}"
        v = self.inc(instr, name)
        eng.wait_ge(self.sem(name), v)
        return instr

    def sbuf(self, name, shape, dtype):
        t = self.st.enter_context(
            self.nc.sbuf_tensor(name + self.uid, shape, dtype))
        self.bufs[name] = t
        return t

    def psum(self, name, shape, dtype=F32):
        t = self.st.enter_context(
            self.nc.psum_tensor(name + self.uid, shape, dtype))
        self.bufs[name] = t
        return t

    def dram_in(self, name, shape, dtype):
        cache = getattr(self.nc, "_dram_cache", None)
        if cache is None:
            cache = self.nc._dram_cache = {}
        if name not in cache:
            cache[name] = self.nc.dram_tensor(
                name, shape, dtype, kind="ExternalInput").ap()
        self.drams[name] = cache[name]
        return cache[name]

    def dram_out(self, name, shape, dtype):
        ap = self.nc.dram_tensor(name, shape, dtype, kind="ExternalOutput").ap()
        self.drams[name] = ap
        return ap

    def dram_tmp(self, name, shape, dtype):
        ap = self.nc.dram_tensor(name + self.uid, shape, dtype,
                                 kind="Internal").ap()
        self.drams[name] = ap
        return ap


KONST = [1e-5, 1.0, 0.5, 2.0, 1e-12, 1e-8, 1.0 / 127.0, 0.0001,
         float(np.float32(1.0 / math.e)), float(np.float32(0.5 - math.log(4.0 / 3.0))),
         -1e9, 1e4, -1e4]
KIDX = {v: i for i, v in enumerate(KONST)}


def const_inputs():
    return {
        "c_ident": np.eye(128, dtype=np.float32),
        "c_ones_col": np.ones((128, 1), dtype=np.float32),
        "c_ones_row": np.ones((1, 512), dtype=np.float32),
        "c_konst": np.tile(np.asarray(KONST, np.float32)[None, :], (128, 1)),
        "c_sel8": np.concatenate([np.tile((np.arange(8) == e).astype(np.float32)[:, None],
                                          (1, 128)) for e in range(8)], axis=1),
        "c_sel32": np.concatenate([np.tile((np.arange(32) == j).astype(np.float32)[:, None],
                                           (1, 128)) for j in range(32)], axis=1),
        "c_bones": _bones(),
        "c_mlneg": -np.kron(np.eye(4, dtype=np.float32),
                            np.tril(np.ones((32, 32), np.float32), -1)),
        "c_mu32": np.triu(np.ones((32, 32), np.float32)),
    }


def _bones():
    b = np.zeros((128, 128), np.float32)
    for c in range(8):
        for h2 in range(2):
            b[h2 * 64:(h2 + 1) * 64, c * 16 + 2 * c + h2] = 0.0
    b[:] = 0.0
    for c in range(8):
        b[0:64, c * 16 + 2 * c] = 1.0
        b[64:128, c * 16 + 2 * c + 1] = 1.0
    return b


def kap(ctx, value, p=128):
    return ctx.bufs["konst"][0:p, KIDX[value]:KIDX[value] + 1]


CONST_TABLE = [
    ("ident_bf", "c_ident", [128, 128], BF16),
    ("ident_f32", "c_ident", [128, 128], F32),
    ("ones_col", "c_ones_col", [128, 1], F32),
    ("ones_col_bf", "c_ones_col", [128, 1], BF16),
    ("ones_row", "c_ones_row", [1, 512], F32),
    ("ones_row_bf", "c_ones_row", [1, 512], BF16),
    ("konst", "c_konst", [128, len(KONST)], F32),
    ("sel8", "c_sel8", [8, 1024], F32),
    ("sel32", "c_sel32", [32, 32 * 128], F32),
    ("bones", "c_bones", [128, 128], F32),
    ("mlneg", "c_mlneg", [128, 128], BF16),
    ("mu32", "c_mu32", [32, 32], BF16),
]


def const_inputs_for(skip=()):
    ci = const_inputs()
    keys = {key for nm, key, _, _ in CONST_TABLE if nm not in skip}
    return {k: v for k, v in ci.items() if k in keys}


def declare_consts(ctx, skip=()):
    hbm = {}
    loads = []
    for nm, key, shape, dt in CONST_TABLE:
        if nm in skip:
            continue
        if key not in hbm:
            hbm[key] = ctx.dram_in(key, shape, F32)
        loads.append((ctx.sbuf(nm, shape, dt), hbm[key]))

    @ctx.blk.gpsimd
    def _(g):
        for sb, hb in loads:
            ctx.inc(g.dma_start(out=sb[:], in_=hb), "c_const", 16)


def emit_xn(ctx, x3, xn3, NTOK, tag, eps=1e-5):
    blk = ctx.blk
    TT = 512
    nt = NTOK // TT

    xt = ctx.sbuf(f"{tag}_xt", [128, 2, NCHUNK, TT], F32)
    xnb = ctx.sbuf(f"{tag}_xnb", [128, 2, NCHUNK, TT], BF16)
    sq = ctx.sbuf(f"{tag}_sq", [128, TT], F32)
    rows = ctx.sbuf(f"{tag}_rows", [1, 2, TT], F32)
    rrow = ctx.sbuf(f"{tag}_rrow", [1, 2, TT], F32)
    t1 = ctx.sbuf(f"{tag}_t1", [128, TT], F32)
    ps_s = ctx.psum(f"{tag}_ps_s", [1, 2, TT], F32)
    ps_b = ctx.psum(f"{tag}_ps_b", [128, 2, TT], F32)
    ones_col = ctx.bufs["ones_col"]
    ones_row = ctx.bufs["ones_row"]

    s_x = [f"{tag}_sx0", f"{tag}_sx1"]
    s_sq = f"{tag}_ssq"
    s_st = f"{tag}_sst"
    s_rowcp = f"{tag}_srowcp"
    s_row = f"{tag}_srow"
    s_bc = f"{tag}_sbc"
    s_t1 = f"{tag}_st1"
    s_xn = f"{tag}_sxn"
    s_out = [f"{tag}_sout0", f"{tag}_sout1"]

    @blk.sync
    def _(s):
        for i in range(nt):
            pb = i % 2
            if i >= 2:
                s.wait_ge(ctx.sem(s_t1), (i - 1) * NCHUNK)
            ctx.inc(s.dma_start(out=xt[:, pb], in_=x3[:, :, i * TT:(i + 1) * TT]),
                    s_x[pb], 16)

    @blk.scalar
    def _(sc):
        ctx.wait(sc, "c_const")
        for i in range(nt):
            pb = i % 2
            sc.wait_ge(ctx.sem(s_x[pb]), 16 * (i // 2 + 1))
            for c in range(NCHUNK):
                ic = i * NCHUNK + c
                if ic > 0:
                    sc.wait_ge(ctx.sem(s_st), 2 * ic)
                ctx.inc(sc.activation(sq[:], xt[:, pb, c], AF.Square), s_sq)
            sc.wait_ge(ctx.sem(s_rowcp), 2 * (i + 1))
            a1 = sc.activation(rrow[:1, 0], rows[:1, 1], AF.Ln, bias=kap(ctx, eps, 1))
            ctx.seq(sc, a1, f"{tag}_scseq")
            ctx.inc(sc.activation(rrow[:1, 0], rrow[:1, 0], AF.Exp, scale=-0.5),
                    f"{tag}_sln")

    @blk.tensor
    def _(t):
        ctx.wait(t, "c_const")
        for i in range(nt):
            pb = i % 2
            t.wait_ge(ctx.sem(s_x[pb]), 16 * (i // 2 + 1))
            if i >= 1:
                t.wait_ge(ctx.sem(s_rowcp), 2 * i)
            for c in range(NCHUNK):
                ic = i * NCHUNK + c
                t.wait_ge(ctx.sem(s_sq), ic + 1)
                t.matmul(ps_s[:1, 0], ones_col[:], xt[:, pb, c],
                         start=(c == 0), stop=(c == NCHUNK - 1))
                mm = t.matmul(ps_s[:1, 1], ones_col[:], sq[:],
                              start=(c == 0), stop=(c == NCHUNK - 1))
                ctx.inc(mm, s_st, 2)
            t.wait_ge(ctx.sem(s_row), i + 1)
            if i >= 1:
                t.wait_ge(ctx.sem(s_xn), i * NCHUNK)
            t.matmul(ps_b[:, 0], ones_row[:1, 0:128], rrow[:1, 0], start=True, stop=True)
            mm = t.matmul(ps_b[:, 1], ones_row[:1, 0:128], rrow[:1, 1], start=True, stop=True)
            ctx.inc(mm, s_bc)

    @blk.vector
    def _(v):
        for i in range(nt):
            pb = i % 2
            v.wait_ge(ctx.sem(s_st), 2 * NCHUNK * (i + 1))
            if i >= 1:
                v.wait_ge(ctx.sem(s_bc), i)
            ctx.seq(v, v.tensor_copy(rows[:1, :], ps_s[:1, :]))
            ctx.inc(v.engine_nop(), s_rowcp)
            ctx.seq(v, v.tensor_scalar_mul(rows[:1, 0], rows[:1, 0], 1.0 / D))
            ctx.seq(v, v.scalar_tensor_tensor(rrow[:1, 1], rows[:1, 0], 1.0,
                                              rows[:1, 0], ALU.mult, ALU.mult))
            ctx.seq(v, v.scalar_tensor_tensor(rows[:1, 1], rows[:1, 1], 1.0 / D,
                                              rrow[:1, 1], ALU.mult, ALU.subtract))
            ctx.inc(v.engine_nop(), s_rowcp)
            v.wait_ge(ctx.sem(f"{tag}_sln"), i + 1)
            ctx.inc(v.scalar_tensor_tensor(rrow[:1, 1], rows[:1, 0], -1.0,
                                           rrow[:1, 0], ALU.mult, ALU.mult), s_row)
            v.wait_ge(ctx.sem(s_bc), i + 1)
            for c in range(NCHUNK):
                ic = i * NCHUNK + c
                if i >= 2 and c == 0:
                    v.wait_ge(ctx.sem(s_out[pb]), 16 * NCHUNK * (i // 2))
                if ic > 0:
                    v.wait_ge(ctx.sem(s_xn), ic)
                vt = ctx.inc(v.tensor_tensor(t1[:], xt[:, pb, c], ps_b[:, 0], ALU.mult), s_t1)
                v.wait_ge(ctx.sem(s_t1), vt)
                ctx.inc(v.tensor_tensor(xnb[:, pb, c], t1[:], ps_b[:, 1], ALU.add), s_xn)

    @blk.gpsimd
    def _(g):
        for i in range(nt):
            pb = i % 2
            for c in range(NCHUNK):
                ic = i * NCHUNK + c
                g.wait_ge(ctx.sem(s_xn), ic + 1)
                ctx.inc(g.dma_start(out=xn3[:, c, i * TT:(i + 1) * TT],
                                    in_=xnb[:, pb, c]), s_out[pb], 16)

    return dict(done=[(s, ctx.val(s)) for s in s_out])


def emit_w_load_direct(ctx, g, wt, w_hbm, sem):
    return ctx.inc(g.dma_start(out=wt.ap(), in_=w_hbm.rearrange("(a p) f -> p a f", p=128)),
                   sem, 16)


def emit_wT_load(ctx, tag, wt, w_hbm, DO, DI, do0=0, ndo=None, di0=0, ndi=None,
                 gates=(), bufs=None, pst=None, prev_ev=None):
    blk = ctx.blk
    ndo = ndo or DO
    ndi = ndi or DI
    nob = ndo // 128
    na = ndi // 128
    if bufs is not None and "stg" in bufs:
        stg = bufs["stg"]
    else:
        stg = ctx.sbuf(f"{tag}_stg", [128, 2, ndi], BF16)
        if bufs is not None:
            bufs["stg"] = stg
    if pst is not None:
        ps_t = pst
    else:
        ps_t = ctx.psum(f"{tag}_pst", [128, 8, 128], BF16)
    ident = ctx.bufs["ident_bf"]
    s_stg = [f"{tag}_sstg0", f"{tag}_sstg1"]
    s_tp = f"{tag}_stp"
    s_ev = f"{tag}_sev"
    ngrp = (na + 7) // 8

    @blk.gpsimd
    def _(g):
        for sem, val in gates:
            g.wait_ge(ctx.sem(sem), val)
        for ob in range(nob):
            pb = ob % 2
            if ob >= 2:
                g.wait_ge(ctx.sem(s_tp), (ob - 1) * ngrp)
            ctx.inc(g.dma_start(out=stg[:, pb],
                                in_=w_hbm[do0 + ob * 128: do0 + (ob + 1) * 128,
                                          di0:di0 + ndi]),
                    s_stg[pb], 16)

    @blk.tensor
    def _(t):
        ctx.wait(t, "c_const")
        for ob in range(nob):
            pb = ob % 2
            t.wait_ge(ctx.sem(s_stg[pb]), 16 * (ob // 2 + 1))
            for grp in range(ngrp):
                gi = ob * ngrp + grp
                if gi > 0:
                    t.wait_ge(ctx.sem(s_ev), gi)
                elif prev_ev is not None:
                    t.wait_ge(ctx.sem(prev_ev[0]), prev_ev[1])
                a0 = grp * 8
                g_n = min(8, na - a0)
                for j in range(g_n):
                    mm = t.transpose(ps_t[:, j, :],
                                     stg[:, pb, (a0 + j) * 128:(a0 + j + 1) * 128],
                                     ident[:])
                ctx.inc(mm, s_tp)

    @blk.vector
    def _(v):
        for ob in range(nob):
            for grp in range(ngrp):
                gi = ob * ngrp + grp
                v.wait_ge(ctx.sem(s_tp), gi + 1)
                a0 = grp * 8
                g_n = min(8, na - a0)
                ctx.inc(v.tensor_copy(wt[:, a0:a0 + g_n, ob * 128:(ob + 1) * 128],
                                      ps_t[:, 0:g_n, :]), s_ev)

    return (s_ev, ctx.val(s_ev))


def emit_l0ffn(ctx, tag, xn3, x3, wg_h, wu_h, wd_h, NTOK, gates_acc=()):
    blk = ctx.blk
    TT = 512
    nt = NTOK // TT
    HID = 3072
    HH = HID // 2
    nhc = HH // 128
    S = 1.0 / 127.0

    wtg = ctx.sbuf(f"{tag}_wtg", [128, 8, HH], BF16)
    wtu = ctx.sbuf(f"{tag}_wtu", [128, 8, HH], BF16)
    wtd = ctx.sbuf(f"{tag}_wtd", [128, nhc, D], BF16)
    xnt = ctx.sbuf(f"{tag}_xnt", [128, 2, NCHUNK, TT], BF16)
    hbuf = ctx.sbuf(f"{tag}_h", [128, 2, nhc, TT], BF16)
    sil = ctx.sbuf(f"{tag}_sil", [128, 2, TT], BF16)
    ot = ctx.sbuf(f"{tag}_ot", [128, 2, TT], F32)
    ps_g = ctx.psum(f"{tag}_psg", [128, 2, TT], F32)
    ps_u = ctx.psum(f"{tag}_psu", [128, 2, TT], F32)
    ps_o = ctx.psum(f"{tag}_pso", [128, 2, TT], F32)

    s_xn = [f"{tag}_sxn0", f"{tag}_sxn1"]
    s_g = f"{tag}_sg"
    s_u = f"{tag}_su"
    s_sil = f"{tag}_ssil"
    s_h = f"{tag}_sh"
    s_dn = f"{tag}_sdn"
    s_ot = f"{tag}_sot"
    s_oacc = [f"{tag}_soa0", f"{tag}_soa1"]
    lbufs_g, lbufs_u, lbufs_d = {}, {}, {}
    pst = ctx.psum(f"{tag}_pst", [128, 8, 128], BF16)
    prev = None

    for r in range(2):
        if r == 0:
            gates = []
        else:
            gates = [(s_u, nhc * nt), (s_dn, NCHUNK * nt),
                     (ld_g[0], ld_g[1]), (ld_u[0], ld_u[1]), (ld_d[0], ld_d[1])]
        ld_g = emit_wT_load(ctx, f"{tag}_lg{r}", wtg, wg_h, HID, D, do0=r * HH, ndo=HH,
                            gates=gates, bufs=lbufs_g, pst=pst, prev_ev=prev)
        ld_u = emit_wT_load(ctx, f"{tag}_lu{r}", wtu, wu_h, HID, D, do0=r * HH, ndo=HH,
                            gates=gates, bufs=lbufs_u, pst=pst, prev_ev=ld_g)
        ld_d = emit_wT_load(ctx, f"{tag}_ld{r}", wtd, wd_h, D, HID, di0=r * HH, ndi=HH,
                            gates=gates, bufs=lbufs_d, pst=pst, prev_ev=ld_u)
        prev = ld_d
        base_u = nhc * nt * r
        base_dn = NCHUNK * nt * r
        base_sil = nhc * nt * r
        base_h = nhc * nt * r
        base_ot = NCHUNK * nt * r

        @blk.sync
        def _(s, r=r):
            for i in range(nt):
                pb = i % 2
                if r > 0 or i >= 2:
                    gidx = r * nt + i - 2
                    s.wait_ge(ctx.sem(s_u), nhc * (gidx + 1))
                ctx.inc(s.dma_start(out=xnt[:, pb], in_=xn3[:, :, i * TT:(i + 1) * TT]),
                        s_xn[pb], 16)

        @blk.tensor
        def _(t, r=r, ld_g=ld_g, ld_u=ld_u, ld_d=ld_d, base_u=base_u, base_dn=base_dn,
              base_sil=base_sil, base_h=base_h, base_ot=base_ot):
            t.wait_ge(ctx.sem(ld_g[0]), ld_g[1])
            t.wait_ge(ctx.sem(ld_u[0]), ld_u[1])
            t.wait_ge(ctx.sem(ld_d[0]), ld_d[1])
            for i in range(nt):
                pb = i % 2
                gi = r * nt + i
                t.wait_ge(ctx.sem(s_xn[pb]), 16 * (gi // 2 + 1))
                for hc in range(nhc):
                    ih = (r * nt + i) * nhc + hc
                    hp = ih % 2
                    if ih >= 2:
                        t.wait_ge(ctx.sem(s_sil), ih - 1)
                        t.wait_ge(ctx.sem(s_h), ih - 1)
                    for a in range(NCHUNK):
                        mm = t.matmul(ps_g[:, hp], wtg[:, a, hc * 128:(hc + 1) * 128],
                                      xnt[:, pb, a], start=(a == 0), stop=(a == NCHUNK - 1))
                    ctx.inc(mm, s_g)
                    for a in range(NCHUNK):
                        mm = t.matmul(ps_u[:, hp], wtu[:, a, hc * 128:(hc + 1) * 128],
                                      xnt[:, pb, a], start=(a == 0), stop=(a == NCHUNK - 1))
                    ctx.inc(mm, s_u)
                t.wait_ge(ctx.sem(s_h), nhc * (r * nt + i + 1))
                for dc in range(NCHUNK):
                    idn = (r * nt + i) * NCHUNK + dc
                    dp = idn % 2
                    if idn >= 2:
                        t.wait_ge(ctx.sem(s_ot), idn - 1)
                    for a in range(nhc):
                        mm = t.matmul(ps_o[:, dp], wtd[:, a, dc * 128:(dc + 1) * 128],
                                      hbuf[:, pb, a], start=(a == 0), stop=(a == nhc - 1))
                    ctx.inc(mm, s_dn)

        @blk.scalar
        def _(sc, r=r):
            ctx.wait(sc, "c_const")
            for i in range(nt):
                for hc in range(nhc):
                    ih = (r * nt + i) * nhc + hc
                    hp = ih % 2
                    sc.wait_ge(ctx.sem(s_g), ih + 1)
                    if ih >= 2:
                        sc.wait_ge(ctx.sem(s_h), ih - 1)
                    ctx.inc(sc.activation(sil[:, hp], ps_g[:, hp], AF.Silu, scale=S), s_sil)

        @blk.vector
        def _(v, r=r):
            for i in range(nt):
                pb = i % 2
                for hc in range(nhc):
                    ih = (r * nt + i) * nhc + hc
                    hp = ih % 2
                    v.wait_ge(ctx.sem(s_u), ih + 1)
                    v.wait_ge(ctx.sem(s_sil), ih + 1)
                    gi = r * nt + i
                    if gi >= 2 and hc == 0:
                        v.wait_ge(ctx.sem(s_dn), NCHUNK * (gi - 1))
                    ctx.inc(v.scalar_tensor_tensor(hbuf[:, pb, hc], ps_u[:, hp], kap(ctx, S),
                                                   sil[:, hp], ALU.mult, ALU.mult), s_h)
                for dc in range(NCHUNK):
                    idn = (r * nt + i) * NCHUNK + dc
                    dp = idn % 2
                    v.wait_ge(ctx.sem(s_dn), idn + 1)
                    if idn >= 2:
                        v.wait_ge(ctx.sem(s_oacc[dp]), 16 * (idn // 2))
                    ctx.inc(v.tensor_scalar_mul(ot[:, dp], ps_o[:, dp], kap(ctx, S)), s_ot)

        @blk.gpsimd
        def _(g, r=r):
            for sem, val in gates_acc:
                g.wait_ge(ctx.sem(sem), val)
            for i in range(nt):
                for dc in range(NCHUNK):
                    idn = (r * nt + i) * NCHUNK + dc
                    dp = idn % 2
                    g.wait_ge(ctx.sem(s_ot), idn + 1)
                    ctx.inc(g.dma_start(out=x3[:, dc, i * TT:(i + 1) * TT], in_=ot[:, dp],
                                        accum_op=ALU.add), s_oacc[dp], 16)

    return dict(done=[(s, ctx.val(s)) for s in s_oacc])


def emit_wT_to_dram(ctx, tag, w_hbm, DO, DI, pst=None, prev_ev=None, bufs=None,
                    sem_tag=None):
    blk = ctx.blk
    na = DI // 128
    nob = DO // 128
    wtd = ctx.dram_tmp(f"{tag}_wT", [128, na, DO], BF16)
    tag = sem_tag or tag
    if bufs is not None and "stg" in bufs:
        stg = bufs["stg"][:, :, 0:DI] if bufs["stg"].shape[2] != DI else bufs["stg"]
        wtmp = bufs["wtmp"][:, :, 0:na, :] if bufs["wtmp"].shape[2] != na else bufs["wtmp"]
        prev_tp = bufs.get("prev_tp")
        prev_wo = bufs.get("prev_wo")
    else:
        wtmp = ctx.sbuf(f"{tag}_wtmp", [128, 2, na, 128], BF16)
        stg = ctx.sbuf(f"{tag}_stg", [128, 2, DI], BF16)
        prev_tp = prev_wo = None
        if bufs is not None:
            bufs["stg"], bufs["wtmp"] = stg, wtmp
    if pst is None:
        pst = ctx.psum(f"{tag}_pst", [128, 8, 128], BF16)
    ident = ctx.bufs["ident_bf"]
    s_stg = [f"{tag}_sstg0", f"{tag}_sstg1"]
    s_tp = f"{tag}_stp"
    s_ev = f"{tag}_sev"
    s_wo = [f"{tag}_swo0", f"{tag}_swo1"]
    ngrp = (na + 7) // 8
    b_stg = [ctx.val(s_stg[0]), ctx.val(s_stg[1])]
    b_tp = ctx.val(s_tp)
    b_ev = ctx.val(s_ev)
    b_wo = [ctx.val(s_wo[0]), ctx.val(s_wo[1])]

    nbs = stg.shape[1]
    nbw = wtmp.shape[1]

    @blk.gpsimd
    def _(g):
        if prev_tp is not None:
            g.wait_ge(ctx.sem(prev_tp[0]), prev_tp[1])
        for ob in range(nob):
            pb = ob % nbs
            if ob >= nbs:
                g.wait_ge(ctx.sem(s_tp), b_tp + (ob - nbs + 1) * ngrp)
            ctx.inc(g.dma_start(out=stg[:, pb],
                                in_=w_hbm[ob * 128:(ob + 1) * 128, :]), s_stg[pb], 16)

    @blk.tensor
    def _(t):
        ctx.wait(t, "c_const")
        for ob in range(nob):
            pb = ob % nbs
            t.wait_ge(ctx.sem(s_stg[pb]), b_stg[pb] + 16 * (ob // nbs + 1))
            for grp in range(ngrp):
                gi = ob * ngrp + grp
                if gi > 0:
                    t.wait_ge(ctx.sem(s_ev), b_ev + gi)
                elif prev_ev is not None:
                    t.wait_ge(ctx.sem(prev_ev[0]), prev_ev[1])
                a0 = grp * 8
                g_n = min(8, na - a0)
                for j in range(g_n):
                    mm = t.transpose(pst[:, j, :],
                                     stg[:, pb, (a0 + j) * 128:(a0 + j + 1) * 128],
                                     ident[:])
                ctx.inc(mm, s_tp)

    @blk.vector
    def _(v):
        if prev_wo is not None:
            for sem, val in prev_wo:
                v.wait_ge(ctx.sem(sem), val)
        for ob in range(nob):
            pw = ob % nbw
            if ob >= nbw:
                v.wait_ge(ctx.sem(s_wo[pw]), b_wo[pw] + 16 * ((ob - nbw) // nbw + 1))
            for grp in range(ngrp):
                gi = ob * ngrp + grp
                v.wait_ge(ctx.sem(s_tp), b_tp + gi + 1)
                a0 = grp * 8
                g_n = min(8, na - a0)
                ctx.inc(v.tensor_copy(wtmp[:, pw, a0:a0 + g_n, :], pst[:, 0:g_n, :]), s_ev)

    @blk.sync
    def _(s):
        for ob in range(nob):
            pw = ob % nbw
            s.wait_ge(ctx.sem(s_ev), b_ev + (ob + 1) * ngrp)
            ctx.inc(s.dma_start(out=wtd[:, :, ob * 128:(ob + 1) * 128],
                                in_=wtmp[:, pw]), s_wo[pw], 16)

    gate = [(s_wo[0], ctx.val(s_wo[0])), (s_wo[1], ctx.val(s_wo[1]))]
    if bufs is not None:
        bufs["prev_tp"] = (s_tp, ctx.val(s_tp))
        bufs["prev_wo"] = gate
    return wtd, gate, (s_ev, ctx.val(s_ev))


def emit_colvec_load(ctx, tag, v_hbm, N, pst32=None, prev=None, row_buf=None):
    blk = ctx.blk
    ncol = (N + 127) // 128
    if row_buf is not None:
        row = row_buf[:1, 0:N] if hasattr(row_buf, "shape") and len(row_buf.shape) == 2 and hasattr(row_buf, "ap") else row_buf
    else:
        row = ctx.sbuf(f"{tag}_row", [1, N], F32)
    col = ctx.sbuf(f"{tag}_col", [min(128, N), ncol], F32)
    if pst32 is None:
        pst32 = ctx.psum(f"{tag}_pst32", [128, ncol], F32)
    ident32 = ctx.bufs["ident_f32"]
    s_r = f"{tag}_sr"
    s_t = f"{tag}_st"
    s_c = f"{tag}_sc"

    @blk.gpsimd
    def _(g):
        if prev is not None and row_buf is not None:
            g.wait_ge(ctx.sem(prev[0]), prev[1])
        ctx.inc(g.dma_start(out=row[:], in_=v_hbm), s_r, 16)

    @blk.tensor
    def _(t):
        ctx.wait(t, "c_const")
        if prev is not None:
            t.wait_ge(ctx.sem(prev[0]), prev[1])
        t.wait_ge(ctx.sem(s_r), ctx.val(s_r))
        for c in range(ncol):
            n0 = c * 128
            nn = min(128, N - n0)
            mm = t.transpose(pst32[0:nn, c:c + 1], row[:1, n0:n0 + nn],
                             ident32[0:1, 0:1])
        ctx.inc(mm, s_t)

    @blk.vector
    def _(v):
        v.wait_ge(ctx.sem(s_t), ctx.val(s_t))
        nn = min(128, N)
        ctx.inc(v.tensor_copy(col[0:nn, :], pst32[0:nn, 0:ncol]), s_c)

    return col, (s_c, ctx.val(s_c))


class GemmStream:
    def __init__(self, ctx, tag, weights, DI, DO, nkt, ev_sems, *, do_block=512,
                 TT=512, banks=2, psums=None, wbufs=None):
        self.ctx, self.tag = ctx, tag
        self.weights, self.DI, self.DO, self.nkt = weights, DI, DO, nkt
        self.ev_sems = ev_sems
        self.do_block, self.TT, self.banks = do_block, TT, banks
        self.na = DI // 128
        self.nblk = DO // do_block
        self.ncb = do_block // 128
        self.npg = (DO // 128) * nkt
        self.sched = [(jb * self.ncb + c, k) for jb in range(self.nblk)
                      for c in range(self.ncb) for k in range(nkt)]
        self.base = 0
        self.jbase = 0
        self.psums = []
        self.s_mm = []
        self.wbufs = []
        self.s_wl = []
        for wi, (wn, kind, srcw) in enumerate(weights):
            if psums is not None:
                self.psums.append(psums[wi])
            else:
                self.psums.append(ctx.psum(f"{tag}_{wn}_ps", [128, banks, TT], F32))
            self.s_mm.append(f"{tag}_{wn}_smm")
            if kind == "dram":
                if wbufs is not None:
                    self.wbufs.append(wbufs[wi])
                else:
                    self.wbufs.append(ctx.sbuf(f"{tag}_{wn}_wb",
                                               [128, 2, self.na, do_block], BF16))
                self.s_wl.append([f"{tag}_{wn}_swl0", f"{tag}_{wn}_swl1"])
            else:
                self.wbufs.append(None)
                self.s_wl.append(None)

    def emit(self, xt_fn, gates=(), dma_gates=(), gate_fn=None, wsrcs=None):
        ctx, blk = self.ctx, self.ctx.blk
        weights, banks = self.weights, self.banks
        na, nkt, ncb, nblk = self.na, self.nkt, self.ncb, self.nblk
        base, jbase = self.base, self.jbase
        last = len(weights) - 1
        s_mm, s_wl, psums, wbufs = self.s_mm, self.s_wl, self.psums, self.wbufs
        do_block = self.do_block
        if wsrcs is not None:
            weights = [(wn, kind, wsrcs[wi]) for wi, (wn, kind, _s) in enumerate(weights)]

        @blk.sync
        def _(s):
            for sem, val in dma_gates:
                s.wait_ge(ctx.sem(sem), val)
            for jb in range(nblk):
                jg = jbase + jb
                for wi, (wn, kind, srcw) in enumerate(weights):
                    if kind != "dram":
                        continue
                    nb = wbufs[wi].shape[1]
                    pb = jg % nb
                    if jg >= nb:
                        s.wait_ge(ctx.sem(s_mm[last]), (jg - nb + 1) * ncb * nkt)
                    ctx.inc(s.dma_start(out=wbufs[wi][:, pb],
                                        in_=srcw[:, :, jb * do_block:(jb + 1) * do_block]),
                            s_wl[wi][pb], 16)

        @blk.tensor
        def _(t):
            for sem, val in gates:
                t.wait_ge(ctx.sem(sem), val)
            n = base
            for jb in range(nblk):
                jg = jbase + jb
                for wi, (wn, kind, srcw) in enumerate(weights):
                    if kind == "dram":
                        nb = wbufs[wi].shape[1]
                        t.wait_ge(ctx.sem(s_wl[wi][jg % nb]), 16 * (jg // nb + 1))
                for c in range(ncb):
                    for k in range(nkt):
                        if gate_fn is not None:
                            gate_fn(t, k)
                        for wi, (wn, kind, srcw) in enumerate(weights):
                            if n >= banks:
                                t.wait_ge(ctx.sem(self.ev_sems[wi]), n - banks + 1)
                            wsrc = (wbufs[wi][:, jg % wbufs[wi].shape[1]]
                                    if kind == "dram" else srcw)
                            coff = (jb * ncb + c) * 128 if kind == "sbuf" else c * 128
                            for a in range(na):
                                mm = t.matmul(psums[wi][:, n % banks],
                                              wsrc[:, a, coff:coff + 128],
                                              xt_fn(a, k), start=(a == 0),
                                              stop=(a == na - 1))
                            ctx.inc(mm, s_mm[wi])
                        n += 1

        self.base += self.npg
        self.jbase += nblk
        return dict(base=base)


def emit_l2ffn(ctx, tag, xn3, x3, w1h, s1h, w2h, s2h, h1h, sh1h, h2h, sh2h,
               w3h, s3h, NTOK, gates_acc=()):
    blk = ctx.blk
    PT = 1024
    nkt = PT // 512
    npc = NTOK // PT
    DH = 4096
    nhc = DH // 128

    pst = ctx.psum(f"{tag}_pst", [128, 8, 128], BF16)
    wA = {}
    prev = None
    ab = {"stg": ctx.sbuf(f"{tag}_Astg", [128, 1, DH], BF16),
          "wtmp": ctx.sbuf(f"{tag}_Awtmp", [128, 1, DH // 128, 128], BF16)}
    for nm, wh, DO_, DI_ in (("w1", w1h, DH, D), ("w2", w2h, DH, D),
                             ("h1", h1h, DH, DH), ("h2", h2h, DH, DH),
                             ("w3", w3h, D, DH)):
        wT, gate, ev = emit_wT_to_dram(ctx, f"{tag}_{nm}", wh, DO_, DI_, pst=pst,
                                       prev_ev=prev, bufs=ab)
        wA[nm] = (wT, gate)
        prev = ev
    pst32 = ctx.psum(f"{tag}_pst32", [128, 32], F32)
    scols = {}
    prevc = None
    srow = ctx.sbuf(f"{tag}_srow", [1, DH], F32)
    for nm, sh, N_ in (("s1", s1h, DH), ("s2", s2h, DH), ("sh1", sh1h, DH),
                       ("sh2", sh2h, DH), ("s3", s3h, D)):
        col, cg = emit_colvec_load(ctx, f"{tag}_{nm}", sh, N_, pst32=pst32, prev=prevc,
                                   row_buf=srow)
        scols[nm] = col
        prevc = cg
    s_scales = prevc

    h0 = ctx.sbuf(f"{tag}_h0", [128, nhc, PT], BF16)
    h1g = ctx.sbuf(f"{tag}_h1g", [128, nhc, PT], BF16)
    xnt = ctx.sbuf(f"{tag}_xnt", [128, NCHUNK, PT], BF16)
    sil = ctx.sbuf(f"{tag}_sil", [128, 2, 512], BF16)
    ot = ctx.sbuf(f"{tag}_ot", [128, 1, 512], F32)
    psA = [ctx.psum(f"{tag}_psA", [128, 2, 512], F32)]
    wbB = [ctx.sbuf(f"{tag}_wbB", [128, 2, nhc, 128], BF16)]

    wbg1 = [ctx.sbuf(f"{tag}_wbg1a", [128, 1, 8, 128], BF16),
            ctx.sbuf(f"{tag}_wbg1b", [128, 1, 8, 128], BF16)]
    g1 = GemmStream(ctx, f"{tag}_g1", [("w1", "dram", wA["w1"][0]),
                                       ("w2", "dram", wA["w2"][0])],
                    D, DH, nkt, [f"{tag}_ev1a", f"{tag}_ev1b"], do_block=128,
                    wbufs=wbg1)
    g23 = GemmStream(ctx, f"{tag}_g23", [("h", "dram", None)], DH, DH, nkt,
                     [f"{tag}_ev23"], do_block=128, psums=psA, wbufs=wbB)
    g4 = GemmStream(ctx, f"{tag}_g4", [("w3", "dram", wA["w3"][0])], DH, D, nkt,
                    [f"{tag}_ev4"], do_block=128, psums=psA, wbufs=wbB)

    s_xnt = f"{tag}_sxnt"
    s_oacc = [f"{tag}_soa0", f"{tag}_soa1"]
    npg = 64
    npg4 = 16

    for p in range(npc):
        @blk.sync
        def _(s, p=p):
            if p > 0:
                s.wait_ge(ctx.sem(g1.s_mm[1]), npg * p)
            ctx.inc(s.dma_start(out=xnt.ap(), in_=xn3[:, :, p * PT:(p + 1) * PT]),
                    s_xnt, 16)

        gts = [(s_xnt, 16 * (p + 1)), (s_scales[0], s_scales[1])]
        if p == 0:
            gts += [wA["w1"][1][0], wA["w1"][1][1], wA["w2"][1][0], wA["w2"][1][1]]
        g1.emit(lambda a, k: xnt[:, a, k * 512:(k + 1) * 512], gates=gts)

        @blk.scalar
        def _(sc, p=p):
            ctx.wait(sc, "c_const")
            base = p * npg
            for nl, (j, k) in enumerate(g1.sched):
                n = base + nl
                sc.wait_ge(ctx.sem(g1.s_mm[0]), n + 1)
                if n >= 2:
                    sc.wait_ge(ctx.sem(f"{tag}_ev1b"), n - 1)
                ctx.inc(sc.activation(sil[:, n % 2], g1.psums[0][:, n % 2], AF.Silu,
                                      scale=scols["s1"][:, j:j + 1]), f"{tag}_ev1a")

        @blk.vector
        def _(v, p=p):
            base = p * npg
            for nl, (j, k) in enumerate(g1.sched):
                n = base + nl
                v.wait_ge(ctx.sem(g1.s_mm[1]), n + 1)
                v.wait_ge(ctx.sem(f"{tag}_ev1a"), n + 1)
                if p > 0 and nl == 0:
                    v.wait_ge(ctx.sem(g23.s_mm[0]), (2 * p - 1) * npg)
                    v.wait_ge(ctx.sem(g4.s_mm[0]), npg4 * p)
                tt = v.scalar_tensor_tensor(h0[:, j, k * 512:(k + 1) * 512],
                                            g1.psums[1][:, n % 2],
                                            scols["s2"][:, j:j + 1],
                                            sil[:, n % 2], ALU.mult, ALU.mult)
                ctx.inc(tt, f"{tag}_ev1b")

        g2gates = [(f"{tag}_ev1b", (p + 1) * npg)]
        if p > 0:
            g2gates.append((f"{tag}_ev4", npg4 * p))
        g23.emit(lambda a, k: h0[:, a, k * 512:(k + 1) * 512],
                 gates=g2gates, wsrcs=[wA["h1"][0]],
                 dma_gates=([wA["h1"][1][0], wA["h1"][1][1]] if p == 0 else
                            [(g4.s_mm[0], npg4 * p)]))

        @blk.scalar
        def _(sc, p=p):
            base = 2 * p * npg
            for nl, (j, k) in enumerate(g23.sched):
                n = base + nl
                sc.wait_ge(ctx.sem(g23.s_mm[0]), n + 1)
                ctx.inc(sc.activation(h1g[:, j, k * 512:(k + 1) * 512],
                                      g23.psums[0][:, n % 2], AF.Gelu,
                                      scale=scols["sh1"][:, j:j + 1]), f"{tag}_ev23")

        g23.emit(lambda a, k: h1g[:, a, k * 512:(k + 1) * 512],
                 gates=[(f"{tag}_ev23", (2 * p + 1) * npg)], wsrcs=[wA["h2"][0]],
                 dma_gates=([wA["h2"][1][0], wA["h2"][1][1]] if p == 0 else []))

        @blk.scalar
        def _(sc, p=p):
            base = (2 * p + 1) * npg
            for nl, (j, k) in enumerate(g23.sched):
                n = base + nl
                sc.wait_ge(ctx.sem(g23.s_mm[0]), n + 1)
                ctx.inc(sc.activation(h0[:, j, k * 512:(k + 1) * 512],
                                      g23.psums[0][:, n % 2], AF.Gelu,
                                      scale=scols["sh2"][:, j:j + 1]), f"{tag}_ev23")

        g4.emit(lambda a, k: h0[:, a, k * 512:(k + 1) * 512],
                gates=[(f"{tag}_ev23", (2 * p + 2) * npg)],
                dma_gates=[(g23.s_mm[0], (2 * p + 2) * npg)] +
                           ([wA["w3"][1][0], wA["w3"][1][1]] if p == 0 else []))

        @blk.vector
        def _(v, p=p):
            base = p * npg4
            for nl, (j, k) in enumerate(g4.sched):
                n = base + nl
                v.wait_ge(ctx.sem(g4.s_mm[0]), n + 1)
                if n >= 1:
                    v.wait_ge(ctx.sem(s_oacc[0]), 16 * n)
                ctx.inc(v.tensor_scalar_mul(ot[:, 0], g4.psums[0][:, n % 2],
                                            scols["s3"][:, j:j + 1]), f"{tag}_ev4")

        @blk.gpsimd
        def _(g, p=p):
            for sem, val in gates_acc:
                g.wait_ge(ctx.sem(sem), val)
            base = p * npg4
            for nl, (j, k) in enumerate(g4.sched):
                n = base + nl
                g.wait_ge(ctx.sem(f"{tag}_ev4"), n + 1)
                ctx.inc(g.dma_start(out=x3[:, j, p * PT + k * 512: p * PT + (k + 1) * 512],
                                    in_=ot[:, 0], accum_op=ALU.add), s_oacc[0], 16)

    return dict(done=[(s, ctx.val(s)) for s in s_oacc])






def emit_l1ffn(ctx, tag, xn3, x3, wg_h, wu_h, wd_h, wr_h, temp_h, NTOK, gates_acc=()):
    blk = ctx.blk
    TT = 512
    GT = 2048
    ngr = NTOK // GT
    ntg = GT // TT
    ntile = GT // 128
    E = 8
    GC = float(np.float32(1.0 / math.e))
    GL = float(np.float32(0.5 - math.log(4.0 / 3.0)))

    wgt = ctx.sbuf(f"{tag}_wgt", [128, NCHUNK, D], BF16)
    wut = ctx.sbuf(f"{tag}_wut", [128, NCHUNK, D], BF16)
    wdt = ctx.sbuf(f"{tag}_wdt", [128, NCHUNK, D], BF16)
    wrt = ctx.sbuf(f"{tag}_wrt", [128, NCHUNK, E], BF16)
    xng = ctx.sbuf(f"{tag}_xng", [128, NCHUNK, GT], BF16)
    y_acc = ctx.sbuf(f"{tag}_yacc", [128, NCHUNK, GT], F32)
    rows = ctx.sbuf(f"{tag}_rows", [8, 4, GT], F32)
    wrow = ctx.sbuf(f"{tag}_wrow", [8, GT], F32)
    wfb = ctx.sbuf(f"{tag}_wfb", [128, ntile, 16], F32)
    tmA = ctx.sbuf(f"{tag}_tmA", [128, ntile, 8], F32)
    tmc = ctx.sbuf(f"{tag}_tmc", [128, 6], F32)
    rt = ctx.sbuf(f"{tag}_rt", [1, 2], F32)
    rtcol = ctx.sbuf(f"{tag}_rtcol", [8, 1], F32)
    wb_sb = ctx.sbuf(f"{tag}_wb", [128, TT], F32)
    sil = ctx.sbuf(f"{tag}_sil", [128, 2, TT], BF16)
    h0 = ctx.sbuf(f"{tag}_h0", [128, 2, TT], BF16)
    hw = ctx.sbuf(f"{tag}_hw", [128, NCHUNK, TT], BF16)
    ps_g = ctx.psum(f"{tag}_psg", [128, 2, TT], F32)
    ps_u = ctx.psum(f"{tag}_psu", [128, 2, TT], F32)
    ps_d = ctx.psum(f"{tag}_psd", [128, 2, TT], F32)
    ps_w = ctx.psum(f"{tag}_psw", [128, 2, TT], F32)
    ps_t = ps_d[:, 0, 0:16]
    ident32 = ctx.bufs["ident_f32"]
    ones_row = ctx.bufs["ones_row"]

    s_wr = f"{tag}_swr"
    s_temp = f"{tag}_stemp"
    s_xng = f"{tag}_sxng"
    s_rmm = f"{tag}_srmm"
    s_inh = f"{tag}_sinh"
    s_rows = f"{tag}_srows"
    s_rtp = f"{tag}_srtp"
    s_wfb = f"{tag}_swfb"
    s_tm = f"{tag}_stm"
    s_btp = f"{tag}_sbtp"
    s_wrow = f"{tag}_swrow"
    s_wl = [f"{tag}_swlg", f"{tag}_swlu", f"{tag}_swld"]
    s_gmm = f"{tag}_sgmm"
    s_umm = f"{tag}_summ"
    s_dmm = f"{tag}_sdmm"
    s_sil = f"{tag}_ssil"
    s_h0 = f"{tag}_sh0"
    s_hw = f"{tag}_shw"
    s_wbm = f"{tag}_swbm"
    s_wbe = f"{tag}_swbe"
    s_yacc = f"{tag}_syacc"
    s_yout = f"{tag}_syout"

    @blk.gpsimd
    def _(g):
        ctx.inc(g.dma_start(out=wrt.ap(), in_=wr_h.rearrange("(a p) f -> p a f", p=128)),
                s_wr, 16)
        ctx.inc(g.dma_start(out=rt[:1, 0:1], in_=temp_h), s_temp, 16)

    s_rtc = f"{tag}_srtc"

    @blk.vector
    def _(v):
        ctx.wait(v, s_temp, 16)
        ctx.inc(v.reciprocal(rt[:1, 1:2], rt[:1, 0:1]), s_rtc)

    C = dict(rmm=0, inh=0, rows=0, rtp=0, wfb=0, tm=0, btp=0, wrow=0,
             wl=0, xng=0, gmm=0, umm=0, dmm=0, sil=0, h0=0, hw=0, wbm=0,
             wbe=0, yacc=0, yout=0)

    first = [True]
    for g in range(ngr):
        @blk.sync
        def _(s, g=g):
            if g > 0:
                s.wait_ge(ctx.sem(s_gmm), C["gmm"])
                s.wait_ge(ctx.sem(s_rmm), C["rmm"])
            ctx.inc(s.dma_start(out=xng.ap(), in_=xn3[:, :, g * GT:(g + 1) * GT]),
                    s_xng, 16)

        @blk.gpsimd
        def _(gp, g=g):
            for e in range(E):
                idx = g * E + e
                if idx > 0:
                    gp.wait_ge(ctx.sem(s_gmm), 32 * idx)
                    gp.wait_ge(ctx.sem(s_umm), 32 * idx)
                    gp.wait_ge(ctx.sem(s_dmm), 32 * idx)
                ctx.inc(gp.dma_start(out=wgt.ap(),
                                     in_=wg_h[e].rearrange("(a p) f -> p a f", p=128)),
                        s_wl[0], 16)
                ctx.inc(gp.dma_start(out=wut.ap(),
                                     in_=wu_h[e].rearrange("(a p) f -> p a f", p=128)),
                        s_wl[1], 16)
                ctx.inc(gp.dma_start(out=wdt.ap(),
                                     in_=wd_h[e].rearrange("(a p) f -> p a f", p=128)),
                        s_wl[2], 16)

        @blk.tensor
        def _(t, g=g):
            if first[0]:
                ctx.wait(t, "c_const")
                ctx.wait(t, s_wr, 16)
                t.wait_ge(ctx.sem(s_rtc), 1)
                mm = t.matmul(ps_w[0:8, 0, 0:1], ones_row[:1, 0:8], rt[:1, 1:2],
                              start=True, stop=True)
                ctx.inc(mm, s_rmm)
                C["rmm"] += 1
                first[0] = False
            t.wait_ge(ctx.sem(s_xng), 16 * (g + 1))
            if g > 0:
                t.wait_ge(ctx.sem(s_sil), C["gmm"])
            for k in range(ntg):
                if C["rmm"] >= 2 + 1:
                    t.wait_ge(ctx.sem(s_inh), C["inh"] - 1)
                for a in range(NCHUNK):
                    mm = t.matmul(ps_g[0:8, C["rmm"] % 2, :], wrt[:, a, :],
                                  xng[:, a, k * TT:(k + 1) * TT],
                                  start=(a == 0), stop=(a == NCHUNK - 1))
                ctx.inc(mm, s_rmm)
                C["rmm"] += 1
                C["inh"] += 1
            if g > 0:
                t.wait_ge(ctx.sem(s_yacc), C["dmm"])
            for i in range(ntile):
                t.wait_ge(ctx.sem(s_rows), C["rows"] + 4)
                if C["rtp"] >= 1:
                    t.wait_ge(ctx.sem(s_wfb), C["wfb"] - 1 + 1)
                t.transpose(ps_t[:, 0:8], rows[0:8, 2, i * 128:(i + 1) * 128],
                            ident32[0:8, 0:8])
                mm = t.transpose(ps_t[:, 8:16], rows[0:8, 3, i * 128:(i + 1) * 128],
                                 ident32[0:8, 0:8])
                ctx.inc(mm, s_rtp)
                C["rtp"] += 1
                C["wfb"] += 1
            C["rows"] += 4
            if g > 0:
                t.wait_ge(ctx.sem(s_wbe), C["wbm"])
            for i in range(ntile):
                t.wait_ge(ctx.sem(s_tm), C["tm"] + i + 1)
                if C["btp"] >= 1:
                    t.wait_ge(ctx.sem(s_wrow), C["wrow"] - 1 + 1)
                mm = t.transpose(ps_w[0:8, 0, 0:128], tmA[:, i, :], ident32[:, 0:128])
                ctx.inc(mm, s_btp)
                C["btp"] += 1
                C["wrow"] += 1
            C["tm"] += ntile
            for e in range(E):
                idx = g * E + e
                for w_i in range(3):
                    t.wait_ge(ctx.sem(s_wl[w_i]), 16 * (idx + 1))
                for k in range(ntg):
                    it = idx * ntg + k
                    t.wait_ge(ctx.sem(s_wrow), (g + 1) * ntile)
                    if it >= 2:
                        t.wait_ge(ctx.sem(s_wbe), it - 1)
                    mm = t.matmul(ps_w[:, it % 2, :],
                                  ctx.bufs["sel8"][0:8, e * 128:(e + 1) * 128],
                                  wrow[0:8, k * TT:(k + 1) * TT],
                                  start=True, stop=True)
                    ctx.inc(mm, s_wbm)
                    C["wbm"] += 1
                    for hc in range(NCHUNK):
                        hcg = C["gmm"]
                        if hcg >= 2:
                            t.wait_ge(ctx.sem(s_sil), hcg - 1)
                            t.wait_ge(ctx.sem(s_h0), hcg - 1)
                        for a in range(NCHUNK):
                            mm = t.matmul(ps_g[:, hcg % 2, :],
                                          wgt[:, a, hc * 128:(hc + 1) * 128],
                                          xng[:, a, k * TT:(k + 1) * TT],
                                          start=(a == 0), stop=(a == NCHUNK - 1))
                        ctx.inc(mm, s_gmm)
                        for a in range(NCHUNK):
                            mm = t.matmul(ps_u[:, hcg % 2, :],
                                          wut[:, a, hc * 128:(hc + 1) * 128],
                                          xng[:, a, k * TT:(k + 1) * TT],
                                          start=(a == 0), stop=(a == NCHUNK - 1))
                        ctx.inc(mm, s_umm)
                        C["gmm"] += 1
                        C["umm"] += 1
                    t.wait_ge(ctx.sem(s_hw), C["gmm"])
                    for dc in range(NCHUNK):
                        dcg = C["dmm"]
                        if dcg >= 2:
                            t.wait_ge(ctx.sem(s_yacc), dcg - 1)
                        for a in range(NCHUNK):
                            mm = t.matmul(ps_d[:, dcg % 2, :],
                                          wdt[:, a, dc * 128:(dc + 1) * 128],
                                          hw[:, a, :], start=(a == 0),
                                          stop=(a == NCHUNK - 1))
                        ctx.inc(mm, s_dmm)
                        C["dmm"] += 1

        @blk.scalar
        def _(sc, g=g):
            if g == 0:
                ctx.wait(sc, "c_const")
                sc.wait_ge(ctx.sem(s_rtc), 2)
            base_rmm = 1 + g * (ntg + 0)
            for k in range(ntg):
                sc.wait_ge(ctx.sem(s_rmm), base_rmm + k + 1)
                ctx.inc(sc.activation(rows[:, 0, k * TT:(k + 1) * TT],
                                      ps_g[0:8, (base_rmm + k) % 2, :],
                                      AF.Sigmoid, scale=rtcol[:, 0:1]), s_inh)
            sc.wait_ge(ctx.sem(s_inh), (g + 1) * ntg)
            a1 = sc.activation(rows[:, 1, :], rows[:, 0, :], AF.Abs, scale=-1.0,
                               bias=kap(ctx, GC, 8))
            ctx.seq(sc, a1)
            ctx.inc(sc.activation(rows[:, 2, :], rows[:, 1, :], AF.Exp, scale=-10.0),
                    s_rows)
            a3 = sc.activation(rows[:, 3, :], rows[:, 1, :], AF.Exp,
                               scale=float(np.float32(-1.0 / 0.3)))
            ctx.inc(a3, s_rows)
            for e in range(E):
                for k in range(ntg):
                    for hc in range(NCHUNK):
                        hcg = C["sil"]
                        sc.wait_ge(ctx.sem(s_gmm), hcg + 1)
                        if hcg >= 2:
                            sc.wait_ge(ctx.sem(s_h0), hcg - 1)
                        ctx.inc(sc.activation(sil[:, hcg % 2], ps_g[:, hcg % 2],
                                              AF.Silu), s_sil)
                        C["sil"] += 1

        @blk.vector
        def _(v, g=g):
            if g == 0:
                v.wait_ge(ctx.sem(s_rtc), 1)
                v.wait_ge(ctx.sem(s_rmm), 1)
                ctx.inc(v.tensor_copy(rtcol[:], ps_w[0:8, 0, 0:1]), s_rtc)
            v.wait_ge(ctx.sem(s_rows), g * 4 + 2)
            ctx.seq(v, v.tensor_scalar(wrow[:, 0:GT], rows[:, 0, :], kap(ctx, GL, 8),
                                       None, ALU.is_ge))
            ctx.seq(v, v.tensor_tensor(rows[:, 2, :], rows[:, 2, :], wrow[:, 0:GT],
                                       ALU.mult))
            ctx.seq(v, v.tensor_scalar(wrow[:, 0:GT], rows[:, 0, :], kap(ctx, 0.5, 8),
                                       None, ALU.is_le))
            ctx.inc(v.tensor_tensor(rows[:, 2, :], rows[:, 2, :], wrow[:, 0:GT],
                                    ALU.mult), s_rows, 2)
            for i in range(ntile):
                v.wait_ge(ctx.sem(s_rtp), g * ntile + i + 1)
                vv = ctx.inc(v.tensor_copy(wfb[:, i, :], ps_t[:]), s_wfb)
                v.wait_ge(ctx.sem(s_wfb), vv)
                w_tm = wfb[:, i, 0:8]
                fb_tm = wfb[:, i, 8:16]
                A = tmA[:, i, :]
                ctx.seq(v, v.reduce_max(tmc[:, 0:1], fb_tm, axis=AX.X))
                ctx.seq(v, v.tensor_scalar(A, fb_tm, tmc[:, 0:1], None, ALU.is_equal))
                ctx.seq(v, v.scalar_tensor_tensor(A, A, kap(ctx, -1e9), fb_tm,
                                                  ALU.mult, ALU.add))
                ctx.seq(v, v.reduce_max(tmc[:, 1:2], A, axis=AX.X))
                ctx.seq(v, v.tensor_scalar(A, fb_tm, tmc[:, 1:2], None, ALU.is_ge))
                ctx.seq(v, v.tensor_tensor(A, fb_tm, A, ALU.mult))
                ctx.seq(v, v.reduce_sum(tmc[:, 2:3], A, axis=AX.X))
                ctx.seq(v, v.tensor_scalar(tmc[:, 2:3], tmc[:, 2:3], kap(ctx, 1e-8),
                                           None, ALU.max))
                ctx.seq(v, v.reciprocal(tmc[:, 3:4], tmc[:, 2:3]))
                ctx.seq(v, v.tensor_scalar(A, A, tmc[:, 3:4], None, ALU.mult))
                ctx.seq(v, v.reduce_sum(tmc[:, 0:1], w_tm, axis=AX.X))
                ctx.seq(v, v.tensor_scalar(tmc[:, 1:2], tmc[:, 0:1], kap(ctx, 1e-8),
                                           None, ALU.is_lt))
                ctx.seq(v, v.tensor_sub(A, A, w_tm))
                ctx.seq(v, v.scalar_tensor_tensor(A, A, tmc[:, 1:2], w_tm,
                                                  ALU.mult, ALU.add))
                ctx.seq(v, v.reduce_sum(tmc[:, 4:5], A, axis=AX.X))
                ctx.seq(v, v.tensor_scalar(tmc[:, 4:5], tmc[:, 4:5], kap(ctx, 1e-8),
                                           None, ALU.max))
                ctx.seq(v, v.reciprocal(tmc[:, 5:6], tmc[:, 4:5]))
                ctx.inc(v.tensor_scalar(A, A, tmc[:, 5:6], None, ALU.mult), s_tm)
            for i in range(ntile):
                v.wait_ge(ctx.sem(s_btp), g * ntile + i + 1)
                ctx.inc(v.tensor_copy(wrow[:, i * 128:(i + 1) * 128],
                                      ps_w[0:8, 0, 0:128]), s_wrow)
            for e in range(E):
                idx = g * E + e
                for k in range(ntg):
                    it = idx * ntg + k
                    v.wait_ge(ctx.sem(s_wbm), it + 1)
                    if it >= 1:
                        v.wait_ge(ctx.sem(s_hw), it * NCHUNK)
                        v.wait_ge(ctx.sem(s_dmm), it * NCHUNK)
                    ctx.inc(v.tensor_copy(wb_sb[:], ps_w[:, it % 2, :]), s_wbe)
                    for hc in range(NCHUNK):
                        hcg = C["h0"]
                        v.wait_ge(ctx.sem(s_umm), hcg + 1)
                        v.wait_ge(ctx.sem(s_sil), hcg + 1)
                        if hcg >= 1:
                            v.wait_ge(ctx.sem(s_hw), hcg)
                        vv = ctx.inc(v.scalar_tensor_tensor(h0[:, hcg % 2],
                                                            ps_u[:, hcg % 2], 1.0,
                                                            sil[:, hcg % 2],
                                                            ALU.mult, ALU.mult), s_h0)
                        v.wait_ge(ctx.sem(s_h0), vv)
                        ctx.inc(v.tensor_tensor(hw[:, hc, :], h0[:, hcg % 2],
                                                wb_sb[:], ALU.mult), s_hw)
                        C["h0"] += 1
                        C["hw"] += 1
                    for dc in range(NCHUNK):
                        dcg = C["yacc"]
                        v.wait_ge(ctx.sem(s_dmm), dcg + 1)
                        ys = y_acc[:, dc, k * TT:(k + 1) * TT]
                        if e == 0:
                            if g > 0:
                                v.wait_ge(ctx.sem(s_yout), 16 * NCHUNK * g)
                            ctx.inc(v.tensor_copy(ys, ps_d[:, dcg % 2, :]), s_yacc)
                        else:
                            v.wait_ge(ctx.sem(s_yacc), dcg - 32 + 1)
                            ctx.inc(v.tensor_tensor(ys, ys, ps_d[:, dcg % 2, :],
                                                    ALU.add), s_yacc)
                        C["yacc"] += 1

        @blk.gpsimd
        def _(g_, g=g):
            for sem, val in gates_acc:
                g_.wait_ge(ctx.sem(sem), val)
            g_.wait_ge(ctx.sem(s_yacc), (g + 1) * E * ntg * NCHUNK)
            for dc in range(NCHUNK):
                ctx.inc(g_.dma_start(out=x3[:, dc, g * GT:(g + 1) * GT],
                                     in_=y_acc[:, dc, :], accum_op=ALU.add), s_yout, 16)

    return dict(done=[(s_yout, ctx.val(s_yout))])


def emit_l3ffn(ctx, tag, xn3, x3, gamma_h, beta_h, wr_h, gu_h, dn_h, NTOK,
               gates_acc=()):
    blk = ctx.blk
    TT = 512
    GT = 2048
    ngr = NTOK // GT
    ntg = GT // TT
    ntile = GT // 128
    E = 8
    DFF = 2048

    ps_g = ctx.psum(f"{tag}_psg", [128, 2, TT], F32)
    ps_u = ctx.psum(f"{tag}_psu", [128, 2, TT], F32)
    ps_d = ctx.psum(f"{tag}_psd", [128, 2, TT], F32)
    ps_w = ctx.psum(f"{tag}_psw", [128, 2, TT], F32)
    pst = ps_g[:, 0, :].bitcast(BF16).rearrange("p (a f) -> p a f", f=128)
    ab = {"stg": ctx.sbuf(f"{tag}_Astg", [128, 1, DFF], BF16),
          "wtmp": ctx.sbuf(f"{tag}_Awtmp", [128, 1, 16, 128], BF16)}
    guT = []
    dnT = []
    prev = None
    for e in range(E):
        wT, gate, ev = emit_wT_to_dram(ctx, f"{tag}_gu{e}", gu_h[e], 2 * DFF, D,
                                       pst=pst, prev_ev=prev, bufs=ab,
                                       sem_tag=f"{tag}_wA")
        guT.append((wT, gate))
        prev = ev
        wT, gate, ev = emit_wT_to_dram(ctx, f"{tag}_dn{e}", dn_h[e], D, DFF,
                                       pst=pst, prev_ev=prev, bufs=ab,
                                       sem_tag=f"{tag}_wA")
        dnT.append((wT, gate))
        prev = ev
    wA_gate = []
    for lst in (guT, dnT):
        for wT, gate in lst:
            wA_gate += gate
    pst32 = ps_u[:, 0, 0:8]
    rows = ctx.sbuf(f"{tag}_rows", [1, 2, 2, 512], F32)
    srow = rows[0:1, 0].rearrange("p a f -> p (a f)")
    gcol, cg1 = emit_colvec_load(ctx, f"{tag}_gam", gamma_h, D, pst32=pst32,
                                 row_buf=srow)
    bcol, cg2 = emit_colvec_load(ctx, f"{tag}_bet", beta_h, D, pst32=pst32,
                                 prev=cg1, row_buf=srow)
    s_scales = cg2

    wrstg = ab["stg"][0:8, 0, 0:D]
    wrtT = ctx.sbuf(f"{tag}_wrt", [128, NCHUNK, E], BF16)
    s_wrs = f"{tag}_swrs"
    s_wrt = f"{tag}_swrt"
    s_wrte = f"{tag}_swrte"

    @blk.gpsimd
    def _(g):
        g.wait_ge(ctx.sem(prev[0]), prev[1])
        ctx.inc(g.dma_start(out=wrstg, in_=wr_h), s_wrs, 16)

    @blk.tensor
    def _(t):
        ctx.wait(t, "c_const")
        ctx.wait(t, s_wrs, 16)
        t.wait_ge(ctx.sem(prev[0]), prev[1])
        for a in range(NCHUNK):
            mm = t.transpose(pst[:, a, 0:8],
                             wrstg[0:8, a * 128:(a + 1) * 128], ctx.bufs["ident_bf"][0:8, 0:8])
        ctx.inc(mm, s_wrt)

    @blk.vector
    def _(v):
        v.wait_ge(ctx.sem(s_wrt), 1)
        ctx.inc(v.tensor_copy(wrtT.ap(), pst[:, :, 0:8]), s_wrte)

    xng = ctx.sbuf(f"{tag}_xng", [128, NCHUNK, GT], BF16)
    y_acc = ctx.sbuf(f"{tag}_yacc", [128, NCHUNK, GT], F32)
    wga = ctx.sbuf(f"{tag}_wga", [128, NCHUNK, 1024], BF16)
    wua = ctx.sbuf(f"{tag}_wua", [128, NCHUNK, 1024], BF16)
    wdn = ctx.sbuf(f"{tag}_wdn", [128, NCHUNK, 1024], BF16)
    xe = ctx.sbuf(f"{tag}_xe", [128, NCHUNK, TT], BF16)
    hbf = ctx.sbuf(f"{tag}_hbf", [128, NCHUNK, TT], BF16)
    sil = ctx.sbuf(f"{tag}_sil", [128, 2, TT], BF16)
    t1 = ctx.sbuf(f"{tag}_t1", [128, TT], F32)
    sq = t1
    rrow = ctx.sbuf(f"{tag}_rrow", [1, 2, 2, TT], F32)
    grow = ctx.sbuf(f"{tag}_grow", [8, GT], F32)
    gtm = ctx.sbuf(f"{tag}_gtm", [128, ntile, 8], F32)
    tmc = ctx.sbuf(f"{tag}_tmc", [128, 6], F32)
    ps_s = ps_w[0:1, :, :]
    ps_b = ps_d
    ones_col = ctx.bufs["ones_col"]
    ones_col_bf = ctx.bufs["ones_col_bf"]
    ones_row = ctx.bufs["ones_row"]
    ident32 = ctx.bufs["ident_f32"]
    sel8 = ctx.bufs["sel8"]

    s_xng = f"{tag}_sxng"
    s_sq = f"{tag}_ssq"
    s_st = f"{tag}_sst"
    s_rowcp = f"{tag}_srowcp"
    s_ln = f"{tag}_sln"
    s_row = f"{tag}_srow"
    s_bc = f"{tag}_sbc"
    s_xf = f"{tag}_sxf"
    s_rmm = f"{tag}_srmm"
    s_prob = f"{tag}_sprob"
    s_rtp = f"{tag}_srtp"
    s_gtm = f"{tag}_sgtm"
    s_tm = f"{tag}_stm"
    s_btp = f"{tag}_sbtp"
    s_grow = f"{tag}_sgrow"
    s_wl = [f"{tag}_swla", f"{tag}_swlb", f"{tag}_swlc"]
    s_wbm = f"{tag}_swbm"
    s_xe = f"{tag}_sxe"
    s_gmm = f"{tag}_sgmm"
    s_umm = f"{tag}_summ"
    s_dmm = f"{tag}_sdmm"
    s_sil = f"{tag}_ssil"
    s_h = f"{tag}_sh"
    s_yacc = f"{tag}_syacc"
    s_yout = f"{tag}_syout"

    C = dict(sq=0, st=0, rowcp=0, ln=0, row=0, bc=0, xf=0, rmm=0, prob=0, rtp=0,
             gtm=0, tm=0, btp=0, grow=0, wl=0, wbm=0, xe=0, gmm=0, umm=0, dmm=0,
             sil=0, h=0, yacc=0)

    for g in range(ngr):
        @blk.sync
        def _(s, g=g):
            if g > 0:
                s.wait_ge(ctx.sem(s_gmm), C["gmm"])
            ctx.inc(s.dma_start(out=xng.ap(), in_=xn3[:, :, g * GT:(g + 1) * GT]),
                    s_xng, 16)

        @blk.gpsimd
        def _(gp, g=g):
            for e in range(E):
                idx = g * E + e
                for r in range(2):
                    ir = idx * 2 + r
                    if ir > 0:
                        gp.wait_ge(ctx.sem(s_gmm), 32 * ir)
                        gp.wait_ge(ctx.sem(s_umm), 32 * ir)
                        gp.wait_ge(ctx.sem(s_dmm), 32 * ir)
                    if ir == 0:
                        for sem, val in wA_gate:
                            gp.wait_ge(ctx.sem(sem), val)
                    ctx.inc(gp.dma_start(out=wga.ap(),
                                         in_=guT[e][0][:, :, r * 1024:(r + 1) * 1024]),
                            s_wl[0], 16)
                    ctx.inc(gp.dma_start(out=wua.ap(),
                                         in_=guT[e][0][:, :, DFF + r * 1024:DFF + (r + 1) * 1024]),
                            s_wl[1], 16)
                    ctx.inc(gp.dma_start(out=wdn.ap(),
                                         in_=dnT[e][0][:, r * NCHUNK:(r + 1) * NCHUNK, :]),
                            s_wl[2], 16)

        @blk.scalar
        def _(sc, g=g):
            if g == 0:
                ctx.wait(sc, "c_const")
            for k in range(ntg):
                kb = g * ntg + k
                sc.wait_ge(ctx.sem(s_xng), 16 * (g + 1))
                sc.wait_ge(ctx.sem(s_xf), kb * NCHUNK)
                for c in range(NCHUNK):
                    ic = C["sq"]
                    if ic > 0:
                        sc.wait_ge(ctx.sem(s_st), 2 * ic)
                    ctx.inc(sc.activation(sq[:], xng[:, c, k * TT:(k + 1) * TT],
                                          AF.Square), s_sq)
                    C["sq"] += 1
                kb = g * ntg + k
                sc.wait_ge(ctx.sem(s_rowcp), 2 * (kb + 1))
                a1 = sc.activation(rrow[:1, kb % 2, 0], rows[:1, kb % 2, 1], AF.Ln,
                                   bias=kap(ctx, 1e-5, 1))
                ctx.seq(sc, a1, f"{tag}_scseq")
                ctx.inc(sc.activation(rrow[:1, kb % 2, 0], rrow[:1, kb % 2, 0],
                                      AF.Exp, scale=-0.5), s_ln)
            for i in range(ntile):
                sc.wait_ge(ctx.sem(s_gtm), C["gtm"] + i + 1)
                ctx.inc(sc.activation(gtm[:, i, :], gtm[:, i, :], AF.Exp,
                                      bias=tmc[:, 0:1], scale=1.0), s_prob)
            C["gtm"] += ntile
            for e in range(E):
                for r in range(2):
                    for k in range(ntg):
                        for hc in range(NCHUNK):
                            hcg = C["sil"]
                            sc.wait_ge(ctx.sem(s_gmm), hcg + 1)
                            if hcg >= 2:
                                sc.wait_ge(ctx.sem(s_h), hcg - 1)
                            ctx.inc(sc.activation(sil[:, hcg % 2], ps_g[:, hcg % 2],
                                                  AF.Silu), s_sil)
                            C["sil"] += 1

        @blk.tensor
        def _(t, g=g):
            if g == 0:
                ctx.wait(t, "c_const")
                t.wait_ge(ctx.sem(s_wrte), 1)
                t.wait_ge(ctx.sem(s_scales[0]), s_scales[1])
            t.wait_ge(ctx.sem(s_xng), 16 * (g + 1))
            if g > 0:
                t.wait_ge(ctx.sem(s_xe), C["xe"])
                t.wait_ge(ctx.sem(s_yacc), C["dmm"])
            for k in range(ntg):
                kb = g * ntg + k
                for c in range(NCHUNK):
                    ic = kb * NCHUNK + c
                    t.wait_ge(ctx.sem(s_sq), ic + 1)
                    if c == 0 and kb > 0:
                        t.wait_ge(ctx.sem(s_rowcp), 2 * kb)
                    t.matmul(ps_s[:1, 0], ones_col_bf[:], xng[:, c, k * TT:(k + 1) * TT],
                             start=(c == 0), stop=(c == NCHUNK - 1))
                    mm = t.matmul(ps_s[:1, 1], ones_col[:], sq[:],
                                  start=(c == 0), stop=(c == NCHUNK - 1))
                    ctx.inc(mm, s_st, 2)
                    C["st"] += 2
                t.wait_ge(ctx.sem(s_row), kb + 1)
                if kb > 0:
                    t.wait_ge(ctx.sem(s_xf), C["xf"])
                t.matmul(ps_b[:, 0, :], ones_row[:1, 0:128],
                         rrow[:1, kb % 2, 0], start=True, stop=True)
                mm = t.matmul(ps_b[:, 1, :], ones_row[:1, 0:128],
                              rrow[:1, kb % 2, 1], start=True, stop=True)
                ctx.inc(mm, s_bc)
                C["bc"] += 1
                C["xf"] += NCHUNK
            for k in range(ntg):
                t.wait_ge(ctx.sem(s_xf), (g + 1) * ntg * NCHUNK)
                if C["rmm"] >= 2:
                    t.wait_ge(ctx.sem(s_grow), C["rmm"] - 1)
                for a in range(NCHUNK):
                    mm = t.matmul(ps_w[0:8, C["rmm"] % 2, :], wrtT[:, a, :],
                                  xng[:, a, k * TT:(k + 1) * TT],
                                  start=(a == 0), stop=(a == NCHUNK - 1))
                ctx.inc(mm, s_rmm)
                C["rmm"] += 1
            for i in range(ntile):
                t.wait_ge(ctx.sem(s_grow), C["grow"] + i // 4 + 1)
                if C["rtp"] >= 1:
                    t.wait_ge(ctx.sem(s_gtm), C["rtp"])
                mm = t.transpose(pst32[:, 0:8], grow[0:8, i * 128:(i + 1) * 128],
                                 ident32[0:8, 0:8])
                ctx.inc(mm, s_rtp)
                C["rtp"] += 1
            C["grow"] += ntg
            for i in range(ntile):
                t.wait_ge(ctx.sem(s_tm), C["tm"] + i + 1)
                if C["btp"] >= 1:
                    t.wait_ge(ctx.sem(f"{tag}_sgrow2"), C["btp"])
                mm = t.transpose(ps_w[0:8, 0, 0:128], gtm[:, i, :], ident32[:, 0:128])
                ctx.inc(mm, s_btp)
                C["btp"] += 1
            C["tm"] += ntile
            for e in range(E):
                idx = g * E + e
                for r in range(2):
                    ir = idx * 2 + r
                    for w_i in range(3):
                        t.wait_ge(ctx.sem(s_wl[w_i]), 16 * (ir + 1))
                    for k in range(ntg):
                        it = ir * ntg + k
                        t.wait_ge(ctx.sem(f"{tag}_sgrow2"), (g + 1) * ntile)
                        iw = C["wbm"]
                        if iw >= 2:
                            t.wait_ge(ctx.sem(s_xe), (iw - 1) * NCHUNK)
                        mm = t.matmul(ps_w[:, iw % 2, :],
                                      sel8[0:8, e * 128:(e + 1) * 128],
                                      grow[0:8, k * TT:(k + 1) * TT],
                                      start=True, stop=True)
                        ctx.inc(mm, s_wbm)
                        C["wbm"] += 1
                        for hc in range(NCHUNK):
                            hcg = C["gmm"]
                            t.wait_ge(ctx.sem(s_xe), (iw + 1) * NCHUNK)
                            if hcg >= 2:
                                t.wait_ge(ctx.sem(s_sil), hcg - 1)
                                t.wait_ge(ctx.sem(s_h), hcg - 1)
                            for a in range(NCHUNK):
                                mm = t.matmul(ps_g[:, hcg % 2, :],
                                              wga[:, a, hc * 128:(hc + 1) * 128],
                                              xe[:, a, :],
                                              start=(a == 0), stop=(a == NCHUNK - 1))
                            ctx.inc(mm, s_gmm)
                            for a in range(NCHUNK):
                                mm2 = t.matmul(ps_u[:, hcg % 2, :],
                                               wua[:, a, hc * 128:(hc + 1) * 128],
                                               xe[:, a, :],
                                               start=(a == 0), stop=(a == NCHUNK - 1))
                            ctx.inc(mm2, s_umm)
                            C["gmm"] += 1
                            C["umm"] += 1
                        t.wait_ge(ctx.sem(s_h), C["gmm"])
                        for dc in range(NCHUNK):
                            dcg = C["dmm"]
                            if dcg >= 2:
                                t.wait_ge(ctx.sem(s_yacc), dcg - 1)
                            for a in range(NCHUNK):
                                mm = t.matmul(ps_d[:, dcg % 2, :],
                                              wdn[:, a, dc * 128:(dc + 1) * 128],
                                              hbf[:, a, :], start=(a == 0),
                                              stop=(a == NCHUNK - 1))
                            ctx.inc(mm, s_dmm)
                            C["dmm"] += 1

        @blk.vector
        def _(v, g=g):
            for k in range(ntg):
                kb = g * ntg + k
                pb = kb % 2
                v.wait_ge(ctx.sem(s_st), (kb + 1) * NCHUNK * 2)
                if kb >= 2:
                    v.wait_ge(ctx.sem(s_bc), kb - 1)
                    v.wait_ge(ctx.sem(s_ln), kb - 1)
                ctx.seq(v, v.tensor_copy(rows[:1, pb], ps_s[:1, :]))
                ctx.inc(v.engine_nop(), s_rowcp)
                ctx.seq(v, v.tensor_scalar_mul(rows[:1, pb, 0], rows[:1, pb, 0], 1.0 / D))
                ctx.seq(v, v.scalar_tensor_tensor(rrow[:1, pb, 1], rows[:1, pb, 0], 1.0,
                                                  rows[:1, pb, 0], ALU.mult, ALU.mult))
                ctx.seq(v, v.scalar_tensor_tensor(rows[:1, pb, 1], rows[:1, pb, 1],
                                                  1.0 / D, rrow[:1, pb, 1],
                                                  ALU.mult, ALU.subtract))
                ctx.inc(v.engine_nop(), s_rowcp)
                v.wait_ge(ctx.sem(s_ln), kb + 1)
                ctx.inc(v.scalar_tensor_tensor(rrow[:1, pb, 1], rows[:1, pb, 0], -1.0,
                                               rrow[:1, pb, 0], ALU.mult, ALU.mult),
                        s_row)
                v.wait_ge(ctx.sem(s_bc), kb + 1)
                for c in range(NCHUNK):
                    if kb * NCHUNK + c > 0:
                        v.wait_ge(ctx.sem(s_xf), kb * NCHUNK + c)
                    vv = ctx.inc(v.tensor_tensor(t1[:], xng[:, c, k * TT:(k + 1) * TT],
                                                 ps_b[:, 0, :], ALU.mult), f"{tag}_st1")
                    v.wait_ge(ctx.sem(f"{tag}_st1"), vv)
                    vv = ctx.inc(v.tensor_tensor(t1[:], t1[:], ps_b[:, 1, :],
                                                 ALU.add), f"{tag}_st1")
                    v.wait_ge(ctx.sem(f"{tag}_st1"), vv)
                    ctx.inc(v.tensor_scalar(xng[:, c, k * TT:(k + 1) * TT], t1[:],
                                            gcol[:, c:c + 1], bcol[:, c:c + 1],
                                            ALU.mult, ALU.add), s_xf)
            for k in range(ntg):
                kb = g * ntg + k
                v.wait_ge(ctx.sem(s_rmm), kb + 1)
                vv = ctx.inc(v.tensor_scalar(grow[0:8, k * TT:(k + 1) * TT],
                                             ps_w[0:8, kb % 2, :], kap(ctx, 1e4, 8),
                                             kap(ctx, -1e4, 8), ALU.min, ALU.max),
                             s_grow)
                v.wait_ge(ctx.sem(s_grow), vv)
            for i in range(ntile):
                v.wait_ge(ctx.sem(s_rtp), g * ntile + i + 1)
                A = gtm[:, i, :]
                vv = ctx.inc(v.tensor_copy(A, pst32[:, 0:8]), s_gtm0 := f"{tag}_sgtm0")
                v.wait_ge(ctx.sem(s_gtm0), vv)
                ctx.seq(v, v.reduce_max(tmc[:, 0:1], A, axis=AX.X, negate=True))
                ctx.inc(v.engine_nop(), s_gtm)
                v.wait_ge(ctx.sem(s_prob), g * ntile + i + 1)
                ctx.seq(v, v.reduce_sum(tmc[:, 1:2], A, axis=AX.X))
                ctx.seq(v, v.tensor_scalar(tmc[:, 1:2], tmc[:, 1:2], kap(ctx, 1e-8),
                                           None, ALU.add))
                ctx.seq(v, v.reciprocal(tmc[:, 2:3], tmc[:, 1:2]))
                ctx.seq(v, v.tensor_scalar(A, A, tmc[:, 2:3], kap(ctx, 1e-8),
                                           ALU.mult, ALU.max))
                ctx.seq(v, v.tensor_scalar(A, A, kap(ctx, 1.0), None, ALU.min))
                ctx.seq(v, v.reduce_max(tmc[:, 0:1], A, axis=AX.X))
                ctx.seq(v, v.tensor_scalar(sq[:, 0:8], A, tmc[:, 0:1], None, ALU.is_equal))
                ctx.seq(v, v.scalar_tensor_tensor(sq[:, 0:8], sq[:, 0:8], kap(ctx, -1e9),
                                                  A, ALU.mult, ALU.add))
                ctx.seq(v, v.reduce_max(tmc[:, 1:2], sq[:, 0:8], axis=AX.X))
                ctx.seq(v, v.tensor_scalar(sq[:, 0:8], A, tmc[:, 1:2], None, ALU.is_ge))
                ctx.inc(v.tensor_tensor(A, A, sq[:, 0:8], ALU.mult), s_tm)
            for i in range(ntile):
                v.wait_ge(ctx.sem(s_btp), g * ntile + i + 1)
                ctx.inc(v.tensor_copy(grow[:, i * 128:(i + 1) * 128],
                                      ps_w[0:8, 0, 0:128]), f"{tag}_sgrow2")
            for e in range(E):
                idx = g * E + e
                for r in range(2):
                    for k in range(ntg):
                        iw = (idx * 2 + r) * ntg + k
                        v.wait_ge(ctx.sem(s_wbm), iw + 1)
                        for c in range(NCHUNK):
                            if C["xe"] >= NCHUNK and c == 0:
                                v.wait_ge(ctx.sem(s_umm), iw * NCHUNK)
                            ctx.inc(v.tensor_tensor(xe[:, c, :],
                                                    xng[:, c, k * TT:(k + 1) * TT],
                                                    ps_w[:, iw % 2, :], ALU.mult), s_xe)
                            C["xe"] += 1
                        for hc in range(NCHUNK):
                            hcg = C["h"]
                            v.wait_ge(ctx.sem(s_umm), hcg + 1)
                            v.wait_ge(ctx.sem(s_sil), hcg + 1)
                            if hcg >= 1:
                                v.wait_ge(ctx.sem(s_dmm), (hcg // NCHUNK) * NCHUNK)
                            ctx.inc(v.scalar_tensor_tensor(hbf[:, hc, :],
                                                           ps_u[:, hcg % 2], 1.0,
                                                           sil[:, hcg % 2],
                                                           ALU.mult, ALU.mult), s_h)
                            C["h"] += 1
                        for dc in range(NCHUNK):
                            dcg = C["yacc"]
                            v.wait_ge(ctx.sem(s_dmm), dcg + 1)
                            ys = y_acc[:, dc, k * TT:(k + 1) * TT]
                            if e == 0 and r == 0:
                                if g > 0:
                                    v.wait_ge(ctx.sem(s_yout), 16 * NCHUNK * g)
                                ctx.inc(v.tensor_copy(ys, ps_d[:, dcg % 2, :]), s_yacc)
                            else:
                                v.wait_ge(ctx.sem(s_yacc), dcg - 32 + 1)
                                ctx.inc(v.tensor_tensor(ys, ys, ps_d[:, dcg % 2, :],
                                                        ALU.add), s_yacc)
                            C["yacc"] += 1

        @blk.gpsimd
        def _(g_, g=g):
            for sem, val in gates_acc:
                g_.wait_ge(ctx.sem(sem), val)
            g_.wait_ge(ctx.sem(s_yacc), (g + 1) * E * 2 * ntg * NCHUNK)
            for dc in range(NCHUNK):
                ctx.inc(g_.dma_start(out=x3[:, dc, g * GT:(g + 1) * GT],
                                     in_=y_acc[:, dc, :], accum_op=ALU.add), s_yout, 16)

    return dict(done=[(s_yout, ctx.val(s_yout))])




def emit_l2attn(ctx, tag, xn3, x3, wqkv_h, wo_h, NTOK, gates_acc=()):
    blk = ctx.blk
    TT = 512
    ST = 2048
    nb = NTOK // ST
    ntt = ST // TT
    nkc = ST // 128
    H = 16
    NPV = H * ntt * nkc

    wslab = ctx.sbuf(f"{tag}_wslab", [128, NCHUNK, H * 64], BF16)
    woT = ctx.sbuf(f"{tag}_wo", [128, NCHUNK, D], BF16)
    xnt = ctx.sbuf(f"{tag}_xnt", [128, 2, NCHUNK, TT], BF16)
    q_fm = ctx.sbuf(f"{tag}_qfm", [128, NCHUNK, ST], BF16)
    k_fm = ctx.sbuf(f"{tag}_kfm", [128, NCHUNK, ST], BF16)
    v65 = ctx.sbuf(f"{tag}_v65", [128, nkc, H, 65], BF16)
    o_fm = ctx.sbuf(f"{tag}_ofm", [128, NCHUNK, ST], BF16)
    vtmp = ctx.sbuf(f"{tag}_vtmp", [128, TT], BF16)
    p_sb = ctx.sbuf(f"{tag}_psb", [128, 2, TT], BF16)
    o_raw = ctx.sbuf(f"{tag}_oraw", [64, TT], F32)
    rrow = ctx.sbuf(f"{tag}_rrow", [1, TT], F32)
    ot = ctx.sbuf(f"{tag}_ot", [128, 2, TT], F32)
    ps_A = ctx.psum(f"{tag}_psA", [128, 2, TT], F32)
    ps_o = ctx.psum(f"{tag}_pso", [128, 2, TT], F32)
    ps_b = ctx.psum(f"{tag}_psb2", [128, TT], F32)
    ps_t = ctx.psum(f"{tag}_pst", [128, 4, 128], BF16)
    ident = ctx.bufs["ident_bf"]
    ones_row = ctx.bufs["ones_row"]

    s_w = f"{tag}_sw"
    s_xn = [f"{tag}_sxn0", f"{tag}_sxn1"]
    s_pj = f"{tag}_spj"
    s_pA = f"{tag}_spA"
    s_vev = f"{tag}_svev"
    s_vt = f"{tag}_svt"
    s_ve = f"{tag}_sve"
    s_vm = f"{tag}_svm"
    s_qk = f"{tag}_sqk"
    s_st = f"{tag}_sst"
    s_ex = f"{tag}_sex"
    s_pv = f"{tag}_spv"
    s_rc = f"{tag}_src"
    s_bc = f"{tag}_sbc"
    s_on = f"{tag}_son"
    s_wo = f"{tag}_swo"
    s_ot = f"{tag}_sot"
    s_oacc = [f"{tag}_soacc0", f"{tag}_soacc1"]

    C = dict(pj=0, pA=0, vev=0, vevv=0, vt=0, ve=0, qk=0, st=0, ex=0, pv=0, rc=0,
             bc=0, on=0, wo=0, ot=0, oacc=0)

    s_ws = f"{tag}_sws"

    @blk.gpsimd
    def _(g):
        ctx.inc(g.dma_start(out=woT.ap(),
                            in_=wo_h.rearrange("(c p) f -> p c f", p=128)), s_w, 16)

    for b in range(nb):
        @blk.gpsimd
        def _(g, b=b):
            for a in range(3):
                sl = b * 3 + a
                if sl > 0:
                    g.wait_ge(ctx.sem(s_pj), 32 * sl)
                for h in range(H):
                    ctx.inc(g.dma_start(out=wslab[:, :, h * 64:(h + 1) * 64],
                                        in_=wqkv_h[h, a].rearrange("(c p) n -> p c n",
                                                                   p=128)), s_ws, 16)

        @blk.sync
        def _(s, b=b):
            for a in range(3):
                for kt in range(ntt):
                    i = (b * 3 + a) * ntt + kt
                    pb = i % 2
                    if i >= 2:
                        s.wait_ge(ctx.sem(s_pj), (i - 1) * 8)
                    ctx.inc(s.dma_start(out=xnt[:, pb],
                                        in_=xn3[:, :, (b * ntt + kt) * TT:
                                                (b * ntt + kt + 1) * TT]),
                            s_xn[pb], 16)

        @blk.tensor
        def _(t, b=b):
            if b == 0:
                ctx.wait(t, "c_const")
                ctx.wait(t, s_w, 16)
            if b > 0:
                t.wait_ge(ctx.sem(s_ot), 32 * b)
            for a in range(3):
                t.wait_ge(ctx.sem(s_ws), 16 * 16 * (b * 3 + a + 1))
                for kt in range(ntt):
                    i = (b * 3 + a) * ntt + kt
                    pb = i % 2
                    t.wait_ge(ctx.sem(s_xn[pb]), 16 * (i // 2 + 1))
                    for pr in range(H // 2):
                        gi = C["pj"]
                        gl = gi - 96 * b
                        if gl >= 2:
                            pa = 2 if (gl - 2) >= 64 else 0
                            if pa == 2:
                                t.wait_ge(ctx.sem(s_vev), 32 * b + (gl - 2) - 64 + 1)
                            else:
                                t.wait_ge(ctx.sem(s_qk), 64 * b + (gl - 2) + 1)
                        for c in range(NCHUNK):
                            mm = t.matmul(ps_A[:, gi % 2, :],
                                          wslab[:, c, pr * 128:(pr + 1) * 128],
                                          xnt[:, pb, c], start=(c == 0),
                                          stop=(c == NCHUNK - 1))
                        ctx.inc(mm, s_pj)
                        C["pj"] += 1
                        if a == 2:
                            t.wait_ge(ctx.sem(s_vev), C["vev"] + 1)
                            t.wait_ge(ctx.sem(s_ve), C["vt"])
                            for j in range(4):
                                mm = t.transpose(ps_t[:, j, :],
                                                 vtmp[:, j * 128:(j + 1) * 128],
                                                 ident[:])
                            ctx.inc(mm, s_vt)
                            C["vt"] += 1
                            C["vev"] += 1
            t.wait_ge(ctx.sem(s_vm), b + 1)
            t.wait_ge(ctx.sem(s_qk), (b + 1) * ntt * 16)
            t.wait_ge(ctx.sem(s_ve), (b + 1) * ntt * 8)
            for h in range(H):
                hb = (h % 2) * 64
                hc = h // 2
                for tqb in range(ntt):
                    io = C["pv"] // nkc
                    if io >= 2:
                        t.wait_ge(ctx.sem(s_on), io - 1)
                    for tk in range(nkc):
                        ist = C["st"]
                        if ist - b * NPV >= 2:
                            t.wait_ge(ctx.sem(s_ex), ist - 1)
                        mm = t.matmul(ps_A[:, ist % 2, :],
                                      k_fm[hb:hb + 64, hc, tk * 128:(tk + 1) * 128],
                                      q_fm[hb:hb + 64, hc, tqb * TT:(tqb + 1) * TT],
                                      start=True, stop=True,
                                      tile_position=(hb, 0))
                        ctx.inc(mm, s_st)
                        C["st"] += 1
                        t.wait_ge(ctx.sem(s_ex), C["st"])
                        mm = t.matmul(ps_o[0:65, io % 2, :],
                                      v65[:, tk, h, :], p_sb[:, ist % 2, :],
                                      start=(tk == 0), stop=(tk == nkc - 1))
                        ctx.inc(mm, s_pv)
                        C["pv"] += 1
                    t.wait_ge(ctx.sem(s_rc), 2 * io + 2)
                    if io >= 1:
                        t.wait_ge(ctx.sem(s_on), io)
                    mm = t.matmul(ps_b[0:64, :], ones_row[:1, 0:64], rrow[:1, :],
                                  start=True, stop=True)
                    ctx.inc(mm, s_bc)
                    C["bc"] += 1
            t.wait_ge(ctx.sem(s_on), (b + 1) * H * ntt)
            for kt in range(ntt):
                for dc in range(NCHUNK):
                    iw = C["wo"]
                    if iw - 32 * b >= 2:
                        t.wait_ge(ctx.sem(s_ot), iw - 1)
                    for c in range(NCHUNK):
                        mm = t.matmul(ps_A[:, iw % 2, :],
                                      woT[:, c, dc * 128:(dc + 1) * 128],
                                      o_fm[:, c, kt * TT:(kt + 1) * TT],
                                      start=(c == 0), stop=(c == NCHUNK - 1))
                    ctx.inc(mm, s_wo)
                    C["wo"] += 1

        @blk.scalar
        def _(sc, b=b):
            if b == 0:
                ctx.wait(sc, "c_const")
            for n in range(NPV):
                ie = C["ex"]
                sc.wait_ge(ctx.sem(s_st), ie + 1)
                if ie >= 2:
                    sc.wait_ge(ctx.sem(s_pv), ie - 1)
                e = sc.activation(p_sb[:, ie % 2, :], ps_A[:, ie % 2, :],
                                  AF.Exp, scale=0.125)
                ctx.inc(e, s_ex)
                C["ex"] += 1

        @blk.vector
        def _(v, b=b):
            if b > 0:
                v.wait_ge(ctx.sem(s_pv), b * NPV)
            ctx.inc(v.memset(v65[:, :, :, 64:65], 1.0), s_vm)
            for a in range(3):
                for kt in range(ntt):
                    for pr in range(H // 2):
                        gj = 96 * b + a * 32 + kt * 8 + pr
                        v.wait_ge(ctx.sem(s_pj), gj + 1)
                        if a == 2:
                            if C["vevv"] > 0:
                                v.wait_ge(ctx.sem(s_vt), C["vevv"])
                            C["vevv"] += 1
                            ctx.inc(v.tensor_copy(vtmp[:], ps_A[:, gj % 2, :]), s_vev)
                            v.wait_ge(ctx.sem(s_vt), C["ve"] + 1)
                            if b > 0 and C["ve"] % (ntt * 8) == 0:
                                v.wait_ge(ctx.sem(s_pv), b * NPV)
                            ctx.inc(v.tensor_copy(
                                v65[:, kt * 4:(kt + 1) * 4, 2 * pr:2 * pr + 2, 0:64],
                                ps_t.ap().rearrange("p a (h n) -> p a h n", h=2)),
                                s_ve)
                            C["ve"] += 1
                        else:
                            if b > 0 and C["qk"] % (ntt * 16) == 0:
                                v.wait_ge(ctx.sem(s_st), b * NPV)
                            dst = q_fm if a == 0 else k_fm
                            ctx.inc(v.tensor_copy(dst[:, pr, kt * TT:(kt + 1) * TT],
                                                  ps_A[:, gj % 2, :]), s_qk)
                            C["qk"] += 1
            for h in range(H):
                hb = (h % 2) * 64
                hc = h // 2
                for tqb in range(ntt):
                    io = C["on"]
                    v.wait_ge(ctx.sem(s_pv), (io + 1) * nkc)
                    if io >= 1:
                        v.wait_ge(ctx.sem(s_on), io)
                    if b > 0 and io == b * H * ntt:
                        v.wait_ge(ctx.sem(s_wo), b * 32)
                    vv = ctx.inc(v.tensor_copy(o_raw[:], ps_o[0:64, io % 2, :]), s_rc)
                    v.wait_ge(ctx.sem(s_rc), vv)
                    if io >= 1:
                        v.wait_ge(ctx.sem(s_bc), io)
                    ctx.inc(v.reciprocal(rrow[:1, :], ps_o[64:65, io % 2, :]), s_rc)
                    v.wait_ge(ctx.sem(s_bc), io + 1)
                    ctx.inc(v.tensor_tensor(o_fm[hb:hb + 64, hc, tqb * TT:(tqb + 1) * TT],
                                            o_raw[:], ps_b[0:64, :], ALU.mult), s_on)
                    C["on"] += 1
            for kt in range(ntt):
                for dc in range(NCHUNK):
                    it = C["ot"]
                    v.wait_ge(ctx.sem(s_wo), it + 1)
                    if it >= 2:
                        v.wait_ge(ctx.sem(s_oacc[it % 2]), 16 * (it // 2))
                    ctx.inc(v.tensor_copy(ot[:, it % 2, :], ps_A[:, it % 2, :]), s_ot)
                    C["ot"] += 1

        @blk.gpsimd
        def _(g, b=b):
            for sem, val in gates_acc:
                g.wait_ge(ctx.sem(sem), val)
            for kt in range(ntt):
                for dc in range(NCHUNK):
                    it = C["oacc"]
                    g.wait_ge(ctx.sem(s_ot), it + 1)
                    ctx.inc(g.dma_start(out=x3[:, dc, (b * ntt + kt) * TT:
                                               (b * ntt + kt + 1) * TT],
                                        in_=ot[:, it % 2, :], accum_op=ALU.add),
                            s_oacc[it % 2], 16)
                    C["oacc"] += 1

    return dict(done=[(s, ctx.val(s)) for s in s_oacc])


def emit_l1attn(ctx, tag, xn3, x3, ck_h, alog_h, wbc_h, bbc_h, dcoef_h,
                wdt_h, bdt_h, wdtp_h, bdtp_h, NTOK, gates_acc=()):
    blk = ctx.blk
    ST = 2048
    HT = 1024
    nb = NTOK // ST
    N16 = 16

    wbcT = ctx.sbuf(f"{tag}_wbc", [128, NCHUNK, 32], BF16)
    wdtT = ctx.sbuf(f"{tag}_wdt", [128, NCHUNK, 64], BF16)
    wdtpT = ctx.sbuf(f"{tag}_wdtp", [64, D], BF16)
    A_sb = ctx.sbuf(f"{tag}_A", [128, NCHUNK, N16], F32)
    xnh = ctx.sbuf(f"{tag}_xnh", [128, NCHUNK, HT + 2], BF16)
    u_bf = ctx.sbuf(f"{tag}_u", [128, NCHUNK, HT], BF16)
    dt_f = ctx.sbuf(f"{tag}_dt", [128, NCHUNK, HT], F32)
    xdt = ctx.sbuf(f"{tag}_xdt", [128, NCHUNK, HT], F32)
    y_acc = ctx.sbuf(f"{tag}_y", [128, NCHUNK, HT], F32)
    bc_sb = ctx.sbuf(f"{tag}_bc", [32, HT], F32)
    dt1_bf = ctx.sbuf(f"{tag}_dt1", [64, HT], BF16)
    Bb_sb = ctx.sbuf(f"{tag}_Bb", [128, HT], F32)
    Cb_sb = ctx.sbuf(f"{tag}_Cb", [128, HT], F32)
    alf = ctx.sbuf(f"{tag}_alf", [128, HT], F32)
    bet = ctx.sbuf(f"{tag}_bet", [128, HT], F32)
    hbuf = ctx.sbuf(f"{tag}_hb", [128, HT], F32)
    scr2 = ctx.sbuf(f"{tag}_scr2", [128, HT], F32)
    scr = ctx.sbuf(f"{tag}_scr", [128, HT], F32)
    sp1 = ctx.sbuf(f"{tag}_sp1", [128, 512], F32)
    carry = ctx.sbuf(f"{tag}_carry", [128, NCHUNK, N16], F32)
    otf = ctx.sbuf(f"{tag}_otf", [128, 2, HT], F32)
    ps_A = ctx.psum(f"{tag}_psA", [128, 2, 512], F32)
    ps_B = ctx.psum(f"{tag}_psB", [128, 2, 512], F32)
    sel32 = ctx.bufs["sel32"]

    pst32 = ps_B[:, 0, 0:32]
    srow = bc_sb[0:1, 0:HT]
    k_c, cgk = emit_colvec_load(ctx, f"{tag}_ck0", ck_h[0], D, pst32=pst32, row_buf=srow)
    k_c1, cgk1 = emit_colvec_load(ctx, f"{tag}_ck1", ck_h[1], D, pst32=pst32,
                                  prev=cgk, row_buf=srow)
    k_c2, cgk2 = emit_colvec_load(ctx, f"{tag}_ck2", ck_h[2], D, pst32=pst32,
                                  prev=cgk1, row_buf=srow)
    bbc_c, cgb = emit_colvec_load(ctx, f"{tag}_bbc", bbc_h, 32, pst32=pst32,
                                  prev=cgk2, row_buf=srow)
    bdt_c, cgd = emit_colvec_load(ctx, f"{tag}_bdt", bdt_h, 64, pst32=pst32,
                                  prev=cgb, row_buf=srow)
    bdtp_c, cgp = emit_colvec_load(ctx, f"{tag}_bdtp", bdtp_h, D, pst32=pst32,
                                   prev=cgd, row_buf=srow)
    dco_c, cgD = emit_colvec_load(ctx, f"{tag}_dco", dcoef_h, D, pst32=pst32,
                                  prev=cgp, row_buf=srow)
    s_cols = cgD

    s_w = f"{tag}_sw"
    s_A = f"{tag}_sA"
    s_xn = f"{tag}_sxn"
    s_xm = f"{tag}_sxm"
    s_cv = f"{tag}_scv"
    s_u = f"{tag}_su"
    s_bcm = f"{tag}_sbcm"
    s_bce = f"{tag}_sbce"
    s_d1m = f"{tag}_sd1m"
    s_d1e = f"{tag}_sd1e"
    s_d2m = f"{tag}_sd2m"
    s_sp3 = f"{tag}_ssp3"
    s_sp = f"{tag}_ssp"
    s_xd = f"{tag}_sxd"
    s_bb = f"{tag}_sbb"
    s_be = f"{tag}_sbe"
    s_al = f"{tag}_sal"
    s_bt = f"{tag}_sbt"
    s_h = f"{tag}_sh"
    s_cy = f"{tag}_scy"
    s_ym = f"{tag}_sym"
    s_ya = f"{tag}_sya"
    s_fin = f"{tag}_sfin"
    s_oacc = [f"{tag}_soa0", f"{tag}_soa1"]

    @blk.gpsimd
    def _(g):
        ctx.inc(g.dma_start(out=wbcT.ap(),
                            in_=wbc_h.rearrange("(c p) f -> p c f", p=128)), s_w, 16)
        ctx.inc(g.dma_start(out=wdtT.ap(),
                            in_=wdt_h.rearrange("(c p) f -> p c f", p=128)), s_w, 16)
        ctx.inc(g.dma_start(out=wdtpT.ap(), in_=wdtp_h), s_w, 16)
        ctx.inc(g.dma_start(out=A_sb.ap(),
                            in_=alog_h.rearrange("(c p) n -> p c n", p=128)), s_w, 16)

    @blk.scalar
    def _(sc):
        ctx.wait(sc, "c_const")
        ctx.wait(sc, s_w, 64)
        ctx.inc(sc.activation(A_sb.ap(), A_sb.ap(), AF.Exp), s_A)

    @blk.vector
    def _(v):
        v.wait_ge(ctx.sem(s_A), 1)
        ctx.inc(v.tensor_scalar_mul(A_sb.ap(), A_sb.ap(), -1.0), s_A)

    for b in range(nb):
        for hh in range(2):
            ih = b * 2 + hh
            t0 = b * ST + hh * HT
            J0 = ih * NCHUNK * N16

            @blk.sync
            def _(s, ih=ih, t0=t0, hh=hh):
                if ih > 0:
                    s.wait_ge(ctx.sem(s_fin), ih * NCHUNK)
                lo = t0 - 1
                hi = t0 + HT + 1
                lo_c = max(lo, b * ST)
                hi_c = min(hi, b * ST + ST)
                ctx.inc(s.dma_start(out=xnh[:, :, lo_c - lo: lo_c - lo + (hi_c - lo_c)],
                                    in_=xn3[:, :, lo_c:hi_c]), s_xn, 16)

            @blk.vector
            def _(v, ih=ih, hh=hh):
                if ih > 0:
                    v.wait_ge(ctx.sem(s_fin), ih * NCHUNK)
                if hh == 0:
                    ctx.inc(v.memset(xnh[:, :, 0:1], 0.0), s_xm)
                else:
                    ctx.inc(v.memset(xnh[:, :, HT + 1:HT + 2], 0.0), s_xm)
                v.wait_ge(ctx.sem(s_cols[0]), s_cols[1])
                v.wait_ge(ctx.sem(s_xn), 16 * (ih + 1))
                v.wait_ge(ctx.sem(s_xm), ih + 1)
                for c in range(NCHUNK):
                    iu = ih * NCHUNK + c
                    if iu > 0:
                        v.wait_ge(ctx.sem(s_u), iu)
                    vv = ctx.inc(v.tensor_scalar(scr[:], xnh[:, c, 0:HT],
                                                 k_c[:, c:c + 1], None, ALU.mult), s_cv)
                    v.wait_ge(ctx.sem(s_cv), vv)
                    vv = ctx.inc(v.scalar_tensor_tensor(scr[:], xnh[:, c, 1:HT + 1],
                                                        k_c1[:, c:c + 1], scr[:],
                                                        ALU.mult, ALU.add), s_cv)
                    v.wait_ge(ctx.sem(s_cv), vv)
                    ctx.inc(v.scalar_tensor_tensor(scr[:], xnh[:, c, 2:HT + 2],
                                                   k_c2[:, c:c + 1], scr[:],
                                                   ALU.mult, ALU.add), s_cv)
                for k in range(2):
                    ib = ih * 2 + k
                    v.wait_ge(ctx.sem(s_bcm), ib + 1)
                    ctx.inc(v.tensor_scalar(bc_sb[0:32, k * 512:(k + 1) * 512],
                                            ps_B[0:32, ib % 2, :], bbc_c[0:32, 0:1],
                                            None, ALU.add), s_bce)
                for k in range(2):
                    i1 = ih * 2 + k
                    v.wait_ge(ctx.sem(s_d1m), i1 + 1)
                    ctx.inc(v.tensor_scalar(dt1_bf[:, k * 512:(k + 1) * 512],
                                            ps_A[0:64, i1 % 2, :], bdt_c[0:64, 0:1],
                                            None, ALU.add), s_d1e)
                for dc in range(NCHUNK):
                    for k in range(2):
                        i2 = ih * 16 + dc * 2 + k
                        if ih > 0 and dc == 0 and k == 0:
                            v.wait_ge(ctx.sem(s_al), ih * 128)
                        v.wait_ge(ctx.sem(s_sp3), i2 + 1)
                        vv = ctx.inc(v.tensor_scalar(scr[:, 0:512], ps_A[:, i2 % 2, :],
                                                     bdtp_c[:, dc:dc + 1], 0.0,
                                                     ALU.add, ALU.max), f"{tag}_srl")
                        v.wait_ge(ctx.sem(f"{tag}_srl"), vv)
                        ctx.inc(v.tensor_add(dt_f[:, dc, k * 512:(k + 1) * 512],
                                             scr[:, 0:512], sp1[:]), s_sp)
                for c in range(NCHUNK):
                    if ih > 0 and c == 0:
                        v.wait_ge(ctx.sem(s_bt), J0)
                    v.wait_ge(ctx.sem(s_sp), (ih + 1) * 16)
                    ctx.inc(v.tensor_tensor(xdt[:, c, :], xnh[:, c, 1:HT + 1],
                                            dt_f[:, c, :], ALU.mult), s_xd)
                for n in range(N16):
                    jn = ih * N16 + n
                    v.wait_ge(ctx.sem(s_bb), jn + 1)
                    if jn > 0:
                        v.wait_ge(ctx.sem(s_bt), jn * NCHUNK)
                        v.wait_ge(ctx.sem(s_ym), jn * NCHUNK)
                    vv = ctx.inc(v.tensor_copy(Bb_sb.ap(), ps_B[:, :, :]
                                               .rearrange("p a f -> p (a f)")), s_be)
                    v.wait_ge(ctx.sem(s_be), vv)
                    ctx.inc(v.tensor_copy(Cb_sb.ap(), ps_A[:, :, :]
                                          .rearrange("p a f -> p (a f)")), s_be)
                    for c in range(NCHUNK):
                        j = J0 + n * NCHUNK + c
                        v.wait_ge(ctx.sem(s_al), j + 1)
                        v.wait_ge(ctx.sem(s_bt), j + 1)
                        if j > 0:
                            v.wait_ge(ctx.sem(s_ym), j)
                            v.wait_ge(ctx.sem(s_cy), j)
                        init = 0.0 if hh == 0 else carry[:, c, n:n + 1]
                        if hh == 1:
                            v.wait_ge(ctx.sem(s_cy), J0 - NCHUNK * N16 + n * NCHUNK + c + 1)
                        vv = ctx.inc(v.tensor_tensor_scan(hbuf[:], alf[:], bet[:],
                                                          init, ALU.mult, ALU.add), s_h)
                        v.wait_ge(ctx.sem(s_h), vv)
                        ctx.inc(v.tensor_copy(carry[:, c, n:n + 1], hbuf[:, HT - 1:HT]),
                                s_cy)
                        v.wait_ge(ctx.sem(s_ym), j + 1)
                        if ih > 0 and n == 0:
                            v.wait_ge(ctx.sem(s_fin), ih * NCHUNK)
                        if n == 0:
                            ctx.inc(v.tensor_copy(y_acc[:, c, :], scr2[:]), s_ya)
                        else:
                            ctx.inc(v.tensor_tensor(y_acc[:, c, :], y_acc[:, c, :],
                                                    scr2[:], ALU.add), s_ya)
                for c in range(NCHUNK):
                    v.wait_ge(ctx.sem(s_ya), J0 + N16 * NCHUNK - (NCHUNK - c - 1))
                    if ih >= 1 or c >= 2:
                        v.wait_ge(ctx.sem(s_oacc[c % 2]), 16 * (ih * 4 + c // 2))
                    ctx.inc(v.scalar_tensor_tensor(otf[:, c % 2, :], xnh[:, c, 1:HT + 1],
                                                   dco_c[:, c:c + 1], y_acc[:, c, :],
                                                   ALU.mult, ALU.add), s_fin)

            @blk.scalar
            def _(sc, ih=ih):
                for c in range(NCHUNK):
                    iu = ih * NCHUNK + c
                    sc.wait_ge(ctx.sem(s_cv), 3 * (iu + 1))
                    if ih > 0 and c == 0:
                        sc.wait_ge(ctx.sem(s_bcm), ih * 2)
                        sc.wait_ge(ctx.sem(s_d1m), ih * 2)
                    ctx.inc(sc.activation(u_bf[:, c, :], scr[:], AF.Silu), s_u)
                for dc in range(NCHUNK):
                    for k in range(2):
                        i2 = ih * 16 + dc * 2 + k
                        sc.wait_ge(ctx.sem(s_d2m), i2 + 1)
                        if i2 > 0:
                            sc.wait_ge(ctx.sem(s_sp), i2)
                        a1 = sc.activation(sp1[:], ps_A[:, i2 % 2, :], AF.Abs,
                                           bias=bdtp_c[:, dc:dc + 1])
                        ctx.seq(sc, a1, f"{tag}_spseq")
                        a2 = sc.activation(sp1[:], sp1[:], AF.Exp, scale=-1.0)
                        ctx.seq(sc, a2, f"{tag}_spseq")
                        ctx.inc(sc.activation(sp1[:], sp1[:], AF.Ln, bias=kap(ctx, 1.0)),
                                s_sp3)
                for n in range(N16):
                    for c in range(NCHUNK):
                        j = ih * N16 * NCHUNK + n * NCHUNK + c
                        if c == 0 and n == 0:
                            sc.wait_ge(ctx.sem(s_sp), (ih + 1) * 16)
                        if j > 0:
                            sc.wait_ge(ctx.sem(s_h), j)
                        ctx.inc(sc.activation(alf[:], dt_f[:, c, :], AF.Exp,
                                              scale=A_sb[:, c, n:n + 1]), s_al)

            @blk.gpsimd
            def _(gp, ih=ih):
                for n in range(N16):
                    jn = ih * N16 + n
                    gp.wait_ge(ctx.sem(s_be), 2 * (jn + 1))
                    for c in range(NCHUNK):
                        j = ih * N16 * NCHUNK + n * NCHUNK + c
                        gp.wait_ge(ctx.sem(s_xd), ih * NCHUNK + c + 1)
                        if j > 0:
                            gp.wait_ge(ctx.sem(s_h), j)
                        ctx.inc(gp.tensor_tensor(bet[:], xdt[:, c, :], Bb_sb[:],
                                                 ALU.mult), s_bt)
                        gp.wait_ge(ctx.sem(s_h), j + 1)
                        if j > 0:
                            gp.wait_ge(ctx.sem(s_ya), j)
                        ctx.inc(gp.tensor_tensor(scr2[:], hbuf[:], Cb_sb[:],
                                                 ALU.mult), s_ym)

            @blk.tensor
            def _(t, ih=ih):
                if ih == 0:
                    ctx.wait(t, "c_const")
                    ctx.wait(t, s_w, 64)
                    t.wait_ge(ctx.sem(s_A), 2)
                    t.wait_ge(ctx.sem(s_cols[0]), s_cols[1])
                t.wait_ge(ctx.sem(s_u), (ih + 1) * NCHUNK)
                if ih > 0:
                    t.wait_ge(ctx.sem(s_be), ih * 2 * N16)
                for k in range(2):
                    ib = ih * 2 + k
                    if ib >= 2:
                        t.wait_ge(ctx.sem(s_bce), ib - 1)
                    for a in range(NCHUNK):
                        mm = t.matmul(ps_B[0:32, ib % 2, :], wbcT[:, a, :],
                                      u_bf[:, a, k * 512:(k + 1) * 512],
                                      start=(a == 0), stop=(a == NCHUNK - 1))
                    ctx.inc(mm, s_bcm)
                for k in range(2):
                    i1 = ih * 2 + k
                    if i1 >= 2:
                        t.wait_ge(ctx.sem(s_d1e), i1 - 1)
                    for a in range(NCHUNK):
                        mm = t.matmul(ps_A[0:64, i1 % 2, :], wdtT[:, a, :],
                                      u_bf[:, a, k * 512:(k + 1) * 512],
                                      start=(a == 0), stop=(a == NCHUNK - 1))
                    ctx.inc(mm, s_d1m)
                t.wait_ge(ctx.sem(s_d1e), (ih + 1) * 2)
                for dc in range(NCHUNK):
                    for k in range(2):
                        i2 = ih * 16 + dc * 2 + k
                        if i2 >= 2:
                            t.wait_ge(ctx.sem(s_sp), i2 - 1)
                        mm = t.matmul(ps_A[:, i2 % 2, :],
                                      wdtpT[:, dc * 128:(dc + 1) * 128],
                                      dt1_bf[:, k * 512:(k + 1) * 512],
                                      start=True, stop=True)
                        ctx.inc(mm, s_d2m)
                t.wait_ge(ctx.sem(s_bce), (ih + 1) * 2)
                t.wait_ge(ctx.sem(s_sp), (ih + 1) * 16)
                for n in range(N16):
                    jn = ih * N16 + n
                    if jn > 0:
                        t.wait_ge(ctx.sem(s_be), jn * 2)
                    for k in range(2):
                        t.matmul(ps_B[:, k, :], sel32[:, n * 128:(n + 1) * 128],
                                 bc_sb[0:32, k * 512:(k + 1) * 512],
                                 start=True, stop=True)
                        mm2 = t.matmul(ps_A[:, k, :],
                                       sel32[:, (16 + n) * 128:(17 + n) * 128],
                                       bc_sb[0:32, k * 512:(k + 1) * 512],
                                       start=True, stop=True)
                    ctx.inc(mm2, s_bb)

            @blk.gpsimd
            def _(g_, ih=ih):
                for sem, val in gates_acc:
                    g_.wait_ge(ctx.sem(sem), val)
                for c in range(NCHUNK):
                    g_.wait_ge(ctx.sem(s_fin), ih * NCHUNK + c + 1)
                    ctx.inc(g_.dma_start(out=x3[:, c, t0:t0 + HT],
                                         in_=otf[:, c % 2, :], accum_op=ALU.add),
                            s_oacc[c % 2], 16)

    return dict(done=[(s, ctx.val(s)) for s in s_oacc])


class Stages:

    ENGS = ("sync", "gpsimd", "tensor", "vector", "scalar")

    def __init__(self, ctx, tag):
        self.ctx = ctx
        self.bar = f"{tag}_bar"
        self.bard = {"sync": f"{tag}_bardS", "gpsimd": f"{tag}_bardG"}
        ctx.sem(self.bar)
        for s in self.bard.values():
            ctx.sem(s)

    def stage(self, **engfns):
        ctx = self.ctx
        blk = ctx.blk
        target = ctx.val(self.bar)
        targetd = {k: ctx.val(s) for k, s in self.bard.items()}
        for ename in self.ENGS:
            if ename not in engfns:
                continue
            sec = getattr(blk, ename)
            fn = engfns[ename]

            def body(e, fn=fn, ename=ename):
                e.wait_ge(ctx.sem(self.bar), target)
                for k, s in self.bard.items():
                    if targetd[k]:
                        e.wait_ge(ctx.sem(s), targetd[k])
                outs = []

                def fin(instr, n=1):
                    outs.append((instr, n))
                fn(e, fin)
                got_c = False
                for instr, n in outs:
                    ctx.inc(instr, self.bard[ename] if n == 16 else self.bar, n)
                    if n != 16:
                        got_c = True
                if not got_c:
                    ctx.inc(e.nop() if ename in ("sync", "tensor", "gpsimd", "scalar")
                            else e.engine_nop(), self.bar, 1)
            sec(body)


def emit_l0attn(ctx, tag, xn3, x3, W, NTOK, gates_acc=(), phases=6):
    blk = ctx.blk
    ST = 2048
    TT = 512
    nb = NTOK // ST
    ntt = ST // TT
    H = 16
    M = 256
    EPS_K = 1e-4
    NORM = float(np.float32(64.0 ** (-0.25)))
    HNN = float(np.float32(0.5 * (NORM * NORM)))

    ps8 = ctx.psum(f"{tag}_ps8", [128, 6, TT], F32)
    pstA = ps8[:, 0, :].bitcast(BF16).rearrange("p (a f) -> p a f", f=128)
    ab = {"stg": ctx.sbuf(f"{tag}_Astg", [128, 1, 4096], BF16),
          "wtmp": ctx.sbuf(f"{tag}_Awtmp", [128, 1, 32, 128], BF16)}
    wT = {}
    prev = None
    for nm, DO_, DI_ in (("wq", D, D), ("wk", D, D), ("wv", D, D), ("wo", D, D),
                         ("w1", 4096, D), ("w2", D, 4096)):
        wTd, gate, ev = emit_wT_to_dram(ctx, f"{tag}_{nm}", W[nm], DO_, DI_,
                                        pst=pstA, prev_ev=prev, bufs=ab,
                                        sem_tag=f"{tag}_wA")
        wT[nm] = (wTd, gate)
        prev = ev
    ps_dg = ctx.psum(f"{tag}_psdg", [128, 2, TT], F32)
    pst32 = ps_dg[:, 0, 0:32]
    a1t = ctx.sbuf(f"{tag}_a1t", [128, NCHUNK, TT], F32)
    srow_t = a1t[0:1, :, :].rearrange("p a f -> p (a f)")
    cols = {}
    prevc = None
    for nm, N_ in (("bq", D), ("bk", D), ("bv", D), ("bo", D), ("b2", D),
                   ("g2", D), ("be2", D), ("g3", D), ("be3", D), ("b1", 4096)):
        col, cg = emit_colvec_load(ctx, f"{tag}_{nm}", W[nm], N_, pst32=pst32,
                                   prev=prevc, row_buf=srow_t)
        cols[nm] = col
        prevc = cg
    s_cols = prevc

    projdup = ctx.sbuf(f"{tag}_projd", [128, M], BF16)
    pstg = ctx.sbuf(f"{tag}_pstg", [128, 2, 64], BF16)
    s_prj = f"{tag}_sprj"

    @blk.gpsimd
    def _(g):
        ctx.inc(g.dma_start(out=pstg.ap(),
                            in_=W["proj"].rearrange("(a p) n -> p a n", p=128)),
                f"{tag}_sprjd", 16)

    @blk.tensor
    def _(t):
        ctx.wait(t, "c_const")
        ctx.wait(t, f"{tag}_sprjd", 16)
        t.wait_ge(ctx.sem(prev[0]), prev[1])
        for a in range(2):
            t.transpose(pstA[0:64, a, :], pstg[:, a, :], ctx.bufs["ident_bf"][:])
            mm = t.transpose(pstA[64:128, a, :], pstg[:, a, :],
                             ctx.bufs["ident_bf"][:])
        ctx.inc(mm, s_prj)

    @blk.vector
    def _(v):
        v.wait_ge(ctx.sem(s_prj), 1)
        ctx.inc(v.tensor_scalar_mul(projdup.ap().rearrange("p (a m) -> p a m", a=2),
                                    pstA[:, 0:2, :], NORM), s_prj)

    k_fm = ctx.sbuf(f"{tag}_kfm", [128, NCHUNK, ST], BF16)
    v65 = ctx.sbuf(f"{tag}_v65", [128, 16, H, 65], BF16)
    wsl = ctx.sbuf(f"{tag}_wsl", [128, 2, NCHUNK, D], BF16)
    xnt = ctx.sbuf(f"{tag}_xnt", [128, NCHUNK, TT], BF16)
    stg8 = ctx.sbuf(f"{tag}_stg8", [128, NCHUNK, TT], BF16)
    ekb = ctx.sbuf(f"{tag}_ekb", [128, 4, M], BF16)
    eqT = ctx.sbuf(f"{tag}_eqT", [128, 8, 128], BF16)
    t1s = ctx.sbuf(f"{tag}_t1s", [128, TT], F32)
    o_tm = ctx.sbuf(f"{tag}_otm", [128, 4, D], BF16)
    ofm = ctx.sbuf(f"{tag}_ofm", [128, NCHUNK, TT], BF16)
    x2b = k_fm
    hb = stg8
    ctx_sb = ctx.sbuf(f"{tag}_ctxs", [128, H, 2, 65], BF16)
    epsrow = ctx.sbuf(f"{tag}_epsr", [1, H, 65], BF16)
    epsq = ctx.sbuf(f"{tag}_epsq", [1, H, 65], BF16)
    svrow = ctx.sbuf(f"{tag}_svr", [1, H * 65], F32)
    stabn = ctx.sbuf(f"{tag}_stabn", [128, H], F32)
    rmax = ctx.sbuf(f"{tag}_rmax", [128, H], F32)
    mtmp = ctx.sbuf(f"{tag}_mtmp", [128, 4], F32)
    bias3 = ctx.sbuf(f"{tag}_bias3", [128, 4], F32)
    diagk = ctx.sbuf(f"{tag}_diagk", [128, 16, H], F32)
    bias2 = ctx.sbuf(f"{tag}_bias2", [128, 16, H], F32)
    diagq = ctx.sbuf(f"{tag}_diagq", [128, 4, H], F32)
    rows = t1s[0:1, 0:TT].rearrange("p (a f) -> p a f", a=1)[0:1, 0, :] if False         else svrow[0:1, 0:1024].rearrange("p (a f) -> p a f", a=2)
    rrow = o_tm.ap().bitcast(F32)[0:1, 0:2, :]
    a2h = ctx.dram_tmp(f"{tag}_a2h", [128, NCHUNK, NTOK], F32)
    psTv = ps8[:, 4, :].bitcast(BF16).rearrange("p (a f) -> p a f", f=128)
    psT23 = ps8[:, 2, :].bitcast(BF16).rearrange("p (a f) -> p a f", f=128)
    ps_cx = ps8[:, 4:6, :].rearrange("p a f -> p (a f)").rearrange(
        "p (t f) -> p t f", f=128)
    bones = ctx.bufs["bones"]
    ident = ctx.bufs["ident_bf"]
    ident32 = ctx.bufs["ident_f32"]
    ones_row = ctx.bufs["ones_row"]
    ones_col = ctx.bufs["ones_col"]
    ones_col_bf = ctx.bufs["ones_col_bf"]
    onesrow_bf = ctx.bufs["ones_row_bf"]

    S = Stages(ctx, tag)
    WA = ([g for _nm in ("wq", "wk", "wv", "wo", "w1", "w2") for g in wT[_nm][1]] +
          [(s_cols[0], s_cols[1]), (s_prj, 2), ("c_const", ctx.val("c_const"))] +
          list(gates_acc))

    def gate_stage(e, fin):
        for sem, val in WA:
            e.wait_ge(ctx.sem(sem), val)
    S.stage(tensor=gate_stage, vector=gate_stage, scalar=gate_stage,
            sync=gate_stage, gpsimd=gate_stage)

    def proj_tile(wslice, bcol, dst_writer, kt):
        for lo in (0, 4):
            def mmw(t, fin, lo=lo, wslice=wslice):
                for dcn in range(lo, lo + 4):
                    for a in range(NCHUNK):
                        mm = t.matmul(ps8[:, dcn - lo, :],
                                      wslice[:, a, dcn * 128:(dcn + 1) * 128],
                                      xnt[:, a, :], start=(a == 0),
                                      stop=(a == NCHUNK - 1))
                fin(mm)

            def evw(v, fin, lo=lo, bcol=bcol, dst_writer=dst_writer, kt=kt):
                for dcn in range(lo, lo + 4):
                    e = v.tensor_scalar(dst_writer(dcn), ps8[:, dcn - lo, :],
                                        bcol[:, dcn:dcn + 1], None, ALU.add)
                fin(e)
            S.stage(tensor=mmw)
            S.stage(vector=evw)

    def diag_tile(src_fm_getter, dst, sub0):
        for c in range(NCHUNK):
            def sqf(sc, fin, c=c):
                fin(sc.activation(t1s[:], src_fm_getter(c), AF.Square))

            def dgm(t, fin, c=c):
                fin(t.matmul(ps_dg[0:16, 0, :], bones[:, c * 16:(c + 1) * 16],
                             t1s[:], start=(c == 0), stop=(c == NCHUNK - 1)))
            S.stage(scalar=sqf)
            S.stage(tensor=dgm)

        def dgr(v, fin):
            fin(v.tensor_scalar_mul(t1s[0:16, :], ps_dg[0:16, 0, :], HNN))
        S.stage(vector=dgr)
        for j in range(4):
            def dgt1(t, fin, j=j):
                fin(t.transpose(ps8[:, 0, 0:16], t1s[0:16, j * 128:(j + 1) * 128],
                                ident32[0:16, 0:16]))

            def dge1(v, fin, j=j):
                fin(v.tensor_copy(dst[:, sub0 + j, :], ps8[:, 0, 0:16]))
            S.stage(tensor=dgt1)
            S.stage(vector=dge1)

    def ln_affine(gname, bname, out_writer):
        for c in range(NCHUNK):
            def sqf(sc, fin, c=c):
                fin(sc.activation(t1s[:], a1t[:, c, :], AF.Square))

            def stm(t, fin, c=c):
                t.matmul(ps_dg[0:1, 0, :], ones_col[:], a1t[:, c, :],
                         start=(c == 0), stop=(c == NCHUNK - 1))
                fin(t.matmul(ps_dg[0:1, 1, :], ones_col[:], t1s[:],
                             start=(c == 0), stop=(c == NCHUNK - 1)))
            S.stage(scalar=sqf)
            S.stage(tensor=stm)

        def rowm(v, fin):
            ctx.seq(v, v.tensor_copy(rows[:1, :], ps_dg[0:1, 0:2, :]))
            ctx.seq(v, v.tensor_scalar_mul(rows[:1, 0], rows[:1, 0], 1.0 / D))
            ctx.seq(v, v.scalar_tensor_tensor(rrow[:1, 1], rows[:1, 0], 1.0,
                                              rows[:1, 0], ALU.mult, ALU.mult))
            fin(v.scalar_tensor_tensor(rows[:1, 1], rows[:1, 1], 1.0 / D,
                                       rrow[:1, 1], ALU.mult, ALU.subtract))
        S.stage(vector=rowm)

        def lnr(sc, fin):
            a1 = sc.activation(rrow[:1, 0], rows[:1, 1], AF.Ln, bias=kap(ctx, 1e-5, 1))
            ctx.seq(sc, a1, f"{tag}_scs")
            fin(sc.activation(rrow[:1, 0], rrow[:1, 0], AF.Exp, scale=-0.5))
        S.stage(scalar=lnr)

        def mrr(v, fin):
            fin(v.scalar_tensor_tensor(rrow[:1, 1], rows[:1, 0], -1.0, rrow[:1, 0],
                                       ALU.mult, ALU.mult))
        S.stage(vector=mrr)

        def bcm(t, fin):
            t.matmul(ps_dg[:, 0, :], ones_row[:1, 0:128], rrow[:1, 0],
                     start=True, stop=True)
            fin(t.matmul(ps_dg[:, 1, :], ones_row[:1, 0:128], rrow[:1, 1],
                         start=True, stop=True))
        S.stage(tensor=bcm)

        def fini(v, fin):
            for c in range(NCHUNK):
                ctx.seq(v, v.tensor_tensor(t1s[:], a1t[:, c, :], ps_dg[:, 0, :],
                                           ALU.mult))
                ctx.seq(v, v.tensor_tensor(t1s[:], t1s[:], ps_dg[:, 1, :], ALU.add))
                e = out_writer(v, c)
                if c < NCHUNK - 1:
                    ctx.seq(v, e)
            fin(e)
        S.stage(vector=fini)

    for b in range(nb):
        tb = b * ST

        def ld_kv(g, fin):
            fin(g.dma_start(out=wsl[:, 0], in_=wT["wk"][0]), 16)
            fin(g.dma_start(out=wsl[:, 1], in_=wT["wv"][0]), 16)
        S.stage(gpsimd=ld_kv)

        for kt in range(ntt):
            def ldx(s, fin, kt=kt):
                fin(s.dma_start(out=xnt.ap(),
                                in_=xn3[:, :, tb + kt * TT: tb + (kt + 1) * TT]), 16)
            S.stage(sync=ldx)

            proj_tile(wsl[:, 0], cols["bk"],
                      lambda dcn, kt=kt: k_fm[:, dcn, kt * TT:(kt + 1) * TT], kt)
            proj_tile(wsl[:, 1], cols["bv"],
                      lambda dcn: stg8[:, dcn, :], kt)

            for j in range(4):
                def vtw(t, fin, j=j):
                    for c2 in range(2):
                        for s4 in range(4):
                            mm = t.transpose(psTv[:, c2 * 4 + s4, :],
                                             stg8[:, 2 * j + c2,
                                                  s4 * 128:(s4 + 1) * 128],
                                             ident[:])
                    fin(mm)

                def vte(v, fin, j=j, kt=kt):
                    for c2 in range(2):
                        c = 2 * j + c2
                        e = v.tensor_copy(
                            v65[:, kt * 4:(kt + 1) * 4, 2 * c:2 * c + 2, 0:64],
                            psTv[:, c2 * 4:(c2 + 1) * 4, :]
                            .rearrange("p a (h n) -> p a h n", h=2))
                    fin(e)
                S.stage(tensor=vtw)
                S.stage(vector=vte)

            diag_tile(lambda c, kt=kt: k_fm[:, c, kt * TT:(kt + 1) * TT],
                      diagk, kt * 4)

        def vmset(v, fin, b=b):
            if b == 0:
                fin(v.memset(v65[:, :, :, 64:65], 1.0))
        S.stage(vector=vmset)

        if phases < 2:
            continue
        def rmx0(v, fin):
            fin(v.memset(rmax[:], -1e30))
        S.stage(vector=rmx0)
        for kt in range(16):
            for qq in range(4):
                def xpk(t, fin, kt=kt, qq=qq):
                    for hj in range(4):
                        h = qq * 4 + hj
                        hb_ = (h % 2) * 64
                        mm = t.matmul(ps8[:, hj, 0:M],
                                      k_fm[hb_:hb_ + 64, h // 2,
                                           kt * 128:(kt + 1) * 128],
                                      projdup[hb_:hb_ + 64, :], start=True,
                                      stop=True)
                    fin(mm)

                def rmx(v, fin, kt=kt, qq=qq):
                    for hj in range(4):
                        h = qq * 4 + hj
                        ctx.seq(v, v.reduce_max(mtmp[:, 0:1], ps8[:, hj, 0:M],
                                                axis=AX.X))
                        e = v.tensor_tensor(rmax[:, h:h + 1], rmax[:, h:h + 1],
                                            mtmp[:, 0:1], ALU.max)
                        if hj < 3:
                            ctx.seq(v, e)
                    fin(e)
                S.stage(tensor=xpk)
                S.stage(vector=rmx)

        if phases == 12:
            continue

        def stb1(t, fin):
            fin(t.transpose(ps8[0:16, 0, 0:128], rmax[:], ident32[:, 0:128]))
        S.stage(tensor=stb1)

        def stb2(v, fin):
            ctx.seq(v, v.reduce_max(mtmp[0:16, 1:2], ps8[0:16, 0, 0:128], axis=AX.X))
            fin(v.tensor_copy(t1s[0:16, 0:1], mtmp[0:16, 1:2]))
        S.stage(vector=stb2)

        def stb3(t, fin):
            fin(t.transpose(ps8[0:1, 1, 0:16], t1s[0:16, 0:1], ident32[0:16, 0:16]))
        S.stage(tensor=stb3)

        def stb4(v, fin):
            fin(v.tensor_copy(t1s[0:1, 0:16], ps8[0:1, 1, 0:16]))
        S.stage(vector=stb4)

        def stb5(t, fin):
            fin(t.matmul(ps8[:, 0, 0:16], ones_row[:1, 0:128], t1s[0:1, 0:16],
                         start=True, stop=True))
        S.stage(tensor=stb5)

        def stb6(v, fin):
            ctx.seq(v, v.tensor_copy(stabn[:], ps8[:, 0, 0:16]))
            for i in range(16):
                e = v.scalar_tensor_tensor(bias2[:, i, :], diagk[:, i, :], -1.0,
                                           stabn[:], ALU.mult, ALU.subtract)
            fin(e)
        S.stage(vector=stb6)

        if phases == 13:
            continue
        def svm(t, fin):
            for kt in range(16):
                va = v65[:, kt, :, :].rearrange("p h n -> p (h n)")
                t.matmul(ps8[0:1, 0, 0:512], ones_col_bf[:], va[:, 0:512],
                         start=(kt == 0), stop=(kt == 15))
                t.matmul(ps8[0:1, 1, 0:512], ones_col_bf[:], va[:, 512:1024],
                         start=(kt == 0), stop=(kt == 15))
                mm = t.matmul(ps8[0:1, 2, 0:16], ones_col_bf[:], va[:, 1024:1040],
                              start=(kt == 0), stop=(kt == 15))
            fin(mm)
        S.stage(tensor=svm)

        def sve(v, fin):
            ctx.seq(v, v.tensor_scalar_mul(svrow[0:1, 0:512], ps8[0:1, 0, 0:512],
                                           EPS_K))
            ctx.seq(v, v.tensor_scalar_mul(svrow[0:1, 512:1024], ps8[0:1, 1, 0:512],
                                           EPS_K))
            ctx.seq(v, v.tensor_scalar_mul(svrow[0:1, 1024:1040], ps8[0:1, 2, 0:16],
                                           EPS_K))
            fin(v.tensor_copy(epsrow.ap().rearrange("p h n -> p (h n)"),
                              svrow[0:1, 0:1040]))
        S.stage(vector=sve)

        if phases < 3:
            continue
        for pp in range(8):
            for kt in range(16):
                def xpk2(t, fin, kt=kt, pp=pp):
                    for hj in range(2):
                        h = pp * 2 + hj
                        hb_ = (h % 2) * 64
                        mm = t.matmul(ps8[:, hj, 0:M],
                                      k_fm[hb_:hb_ + 64, h // 2,
                                           kt * 128:(kt + 1) * 128],
                                      projdup[hb_:hb_ + 64, :], start=True,
                                      stop=True)
                    fin(mm)

                def ek2(sc, fin, kt=kt, pp=pp):
                    for hj in range(2):
                        h = pp * 2 + hj
                        e = sc.activation(ekb[:, hj, :], ps8[:, hj, 0:M],
                                          AF.Exp, bias=bias2[:, kt, h:h + 1])
                    fin(e)

                def cxm(t, fin, kt=kt, pp=pp):
                    last = None
                    for hj in range(2):
                        h = pp * 2 + hj
                        for mc in range(2):
                            bk_ = 2 + hj * 2 + mc
                            if kt == 0:
                                t.matmul(ps8[:, bk_, 0:65], onesrow_bf[:1, 0:128],
                                         epsrow[0:1, h, :], start=True, stop=False)
                            last = t.matmul(ps8[:, bk_, 0:65],
                                            ekb[:, hj, mc * 128:(mc + 1) * 128],
                                            v65[:, kt, h, :], start=False,
                                            stop=(kt == 15))
                    fin(last)
                S.stage(tensor=xpk2)
                S.stage(scalar=ek2)
                S.stage(tensor=cxm)

            def cxe(v, fin, pp=pp):
                for hj in range(2):
                    for mc in range(2):
                        e = v.tensor_copy(ctx_sb[:, pp * 2 + hj, mc, :],
                                          ps8[:, 2 + hj * 2 + mc, 0:65])
                fin(e)
            S.stage(vector=cxe)

        for qq in range(4):
            def eqm(t, fin, qq=qq):
                for hj in range(4):
                    h = qq * 4 + hj
                    for mc in range(2):
                        mm = t.matmul(ps8[0:1, hj, 0:65], ones_col_bf[:],
                                      ctx_sb[:, h, mc, :], start=(mc == 0),
                                      stop=(mc == 1))
                fin(mm)

            def eqe(v, fin, qq=qq):
                for hj in range(4):
                    h = qq * 4 + hj
                    e = v.tensor_scalar_mul(epsq[0:1, h, :], ps8[0:1, hj, 0:65],
                                            EPS_K)
                fin(e)
            S.stage(tensor=eqm)
            S.stage(vector=eqe)

        if phases < 4:
            continue
        def ld_qo(g, fin):
            fin(g.dma_start(out=wsl[:, 0], in_=wT["wq"][0]), 16)
            fin(g.dma_start(out=wsl[:, 1], in_=wT["wo"][0]), 16)
        S.stage(gpsimd=ld_qo)

        for kt in range(ntt):
            def ldx2(s, fin, kt=kt):
                fin(s.dma_start(out=xnt.ap(),
                                in_=xn3[:, :, tb + kt * TT: tb + (kt + 1) * TT]), 16)
            S.stage(sync=ldx2)
            proj_tile(wsl[:, 0], cols["bq"], lambda dcn: stg8[:, dcn, :], kt)
            diag_tile(lambda c: stg8[:, c, :], diagq, 0)

            for sub in range(4):
                for qq in range(4):
                    def xpq(t, fin, sub=sub, qq=qq):
                        for hj in range(4):
                            h = qq * 4 + hj
                            hb_ = (h % 2) * 64
                            mm = t.matmul(ps8[:, hj, 0:M],
                                          stg8[hb_:hb_ + 64, h // 2,
                                               sub * 128:(sub + 1) * 128],
                                          projdup[hb_:hb_ + 64, :], start=True,
                                          stop=True)
                        fin(mm)

                    def qbias(v, fin, sub=sub, qq=qq):
                        for hj in range(4):
                            h = qq * 4 + hj
                            ctx.seq(v, v.reduce_max(
                                mtmp[:, 0:1], ps8[:, hj, 0:M],
                                axis=AX.X, negate=True))
                            e = v.tensor_sub(bias3[:, hj:hj + 1], mtmp[:, 0:1],
                                             diagq[:, sub, h:h + 1])
                            if hj < 3:
                                ctx.seq(v, e)
                        fin(e)

                    def eqx(sc, fin, sub=sub, qq=qq):
                        for hj in range(4):
                            e = sc.activation(ekb[:, hj, :], ps8[:, hj, 0:M],
                                              AF.Exp, bias=bias3[:, hj:hj + 1])
                        fin(e)

                    def eqt(t, fin, qq=qq):
                        for hj in range(4):
                            for mc in range(2):
                                mm = t.transpose(psT23[:, hj * 2 + mc, :],
                                                 ekb[:, hj, mc * 128:(mc + 1) * 128],
                                                 ident[:])
                        fin(mm)

                    def eqte(v, fin):
                        fin(v.tensor_copy(eqT.ap(), psT23[:, 0:8, :]))
                    S.stage(tensor=xpq)
                    S.stage(vector=qbias)
                    S.stage(scalar=eqx)
                    S.stage(tensor=eqt)
                    S.stage(vector=eqte)

                    def umm(t, fin, qq=qq):
                        last = None
                        for hj in range(4):
                            h = qq * 4 + hj
                            t.matmul(ps8[:, hj, 0:65], onesrow_bf[:1, 0:128],
                                     epsq[0:1, h, :], start=True, stop=False)
                            for mc in range(2):
                                last = t.matmul(ps8[:, hj, 0:65],
                                                eqT[:, hj * 2 + mc, :],
                                                ctx_sb[:, h, mc, :], start=False,
                                                stop=(mc == 1))
                        fin(last)

                    def uev(v, fin, sub=sub, qq=qq):
                        for hj in range(4):
                            h = qq * 4 + hj
                            ctx.seq(v, v.reciprocal(mtmp[:, 1:2],
                                                    ps8[:, hj, 64:65]))
                            e = v.tensor_scalar(o_tm[:, sub, h * 64:(h + 1) * 64],
                                                ps8[:, hj, 0:64],
                                                mtmp[:, 1:2], None, ALU.mult)
                            if hj < 3:
                                ctx.seq(v, e)
                        fin(e)
                    S.stage(tensor=umm)
                    S.stage(vector=uev)

            for j in range(4):
                def otw(t, fin, j=j):
                    for c2 in range(2):
                        for s4 in range(4):
                            mm = t.transpose(psTv[:, c2 * 4 + s4, :],
                                             o_tm[:, s4, (2 * j + c2) * 128:
                                                  (2 * j + c2 + 1) * 128],
                                             ident[:])
                    fin(mm)

                def ote(v, fin, j=j):
                    for c2 in range(2):
                        e = v.tensor_copy(
                            ofm[:, 2 * j + c2, :].rearrange("p (a f) -> p a f", f=128),
                            psTv[:, c2 * 4:(c2 + 1) * 4, :])
                    fin(e)
                S.stage(tensor=otw)
                S.stage(vector=ote)

            for lo in (0, 4):
                def womm(t, fin, lo=lo):
                    for dcn in range(lo, lo + 4):
                        for a in range(NCHUNK):
                            mm = t.matmul(ps8[:, dcn - lo, :],
                                          wsl[:, 1, a, dcn * 128:(dcn + 1) * 128],
                                          ofm[:, a, :], start=(a == 0),
                                          stop=(a == NCHUNK - 1))
                    fin(mm)

                def a1ev(v, fin, lo=lo):
                    for dcn in range(lo, lo + 4):
                        e = v.scalar_tensor_tensor(a1t[:, dcn, :], ps8[:, dcn - lo, :],
                                                   cols["bo"][:, dcn:dcn + 1],
                                                   xnt[:, dcn, :], ALU.add, ALU.add)
                    fin(e)
                S.stage(tensor=womm)
                S.stage(vector=a1ev)

            def x2w(v, c, kt=kt):
                e = v.tensor_scalar(x2b[:, c, kt * TT:(kt + 1) * TT], t1s[:],
                                    cols["g2"][:, c:c + 1], cols["be2"][:, c:c + 1],
                                    ALU.mult, ALU.add)
                ctx.seq(v, e)
                return v.tensor_scalar(a1t[:, c, :], t1s[:], cols["g2"][:, c:c + 1],
                                       cols["be2"][:, c:c + 1], ALU.mult, ALU.add)
            ln_affine("g2", "be2", x2w)

            def a2w(s, fin, kt=kt):
                fin(s.dma_start(out=a2h[:, :, tb + kt * TT: tb + (kt + 1) * TT],
                                in_=a1t.ap()), 16)
            S.stage(sync=a2w)

        if phases < 5:
            continue
        for r in range(4):
            def ldw12(g, fin, r=r):
                fin(g.dma_start(out=wsl[:, 0],
                                in_=wT["w1"][0][:, :, r * 1024:(r + 1) * 1024]), 16)
                fin(g.dma_start(out=wsl[:, 1],
                                in_=wT["w2"][0][:, r * NCHUNK:(r + 1) * NCHUNK, :]),
                    16)
            S.stage(gpsimd=ldw12)

            for kt in range(ntt):
                for lo in (0, 4):
                    def hmm(t, fin, lo=lo, kt=kt):
                        for hcn in range(lo, lo + 4):
                            for a in range(NCHUNK):
                                mm = t.matmul(ps8[:, hcn - lo, :],
                                              wsl[:, 0, a, hcn * 128:(hcn + 1) * 128],
                                              x2b[:, a, kt * TT:(kt + 1) * TT],
                                              start=(a == 0), stop=(a == NCHUNK - 1))
                        fin(mm)

                    def hev(v, fin, lo=lo, r=r):
                        for hcn in range(lo, lo + 4):
                            e = v.tensor_scalar(hb[:, hcn, :], ps8[:, hcn - lo, :],
                                                cols["b1"][:, r * 8 + hcn:
                                                           r * 8 + hcn + 1],
                                                0.0, ALU.add, ALU.max)
                        fin(e)
                    S.stage(tensor=hmm)
                    S.stage(vector=hev)
                for lo in (0, 4):
                    def f2mm(t, fin, lo=lo):
                        for dcn in range(lo, lo + 4):
                            for a in range(NCHUNK):
                                mm = t.matmul(ps8[:, dcn - lo, :],
                                              wsl[:, 1, a, dcn * 128:(dcn + 1) * 128],
                                              hb[:, a, :], start=(a == 0),
                                              stop=(a == NCHUNK - 1))
                        fin(mm)

                    def f2ev(v, fin, lo=lo, r=r):
                        for dcn in range(lo, lo + 4):
                            if r == 3:
                                e = v.tensor_scalar(a1t[:, dcn, :], ps8[:, dcn - lo, :],
                                                    cols["b2"][:, dcn:dcn + 1], None,
                                                    ALU.add)
                            else:
                                e = v.tensor_copy(a1t[:, dcn, :], ps8[:, dcn - lo, :])
                        fin(e)
                    S.stage(tensor=f2mm)
                    S.stage(vector=f2ev)

                def fac(g, fin, kt=kt):
                    fin(g.dma_start(out=a2h[:, :, tb + kt * TT: tb + (kt + 1) * TT],
                                    in_=a1t.ap(), accum_op=ALU.add), 16)
                S.stage(gpsimd=fac)

        if phases < 6:
            continue
        for kt in range(ntt):
            def lda2(s, fin, kt=kt):
                fin(s.dma_start(out=a1t.ap(),
                                in_=a2h[:, :, tb + kt * TT: tb + (kt + 1) * TT]), 16)
            S.stage(sync=lda2)

            def l3w(v, c):
                return v.tensor_scalar(a1t[:, c, :], t1s[:], cols["g3"][:, c:c + 1],
                                       cols["be3"][:, c:c + 1], ALU.mult, ALU.add)
            ln_affine("g3", "be3", l3w)

            def l3acc(g, fin, kt=kt):
                fin(g.dma_start(out=x3[:, :, tb + kt * TT: tb + (kt + 1) * TT],
                                in_=a1t.ap(), accum_op=ALU.add), 16)
            S.stage(gpsimd=l3acc)

    return dict(done=[(S.bar, ctx.val(S.bar))] +
                [(s, ctx.val(s)) for s in S.bard.values()])


def emit_l3attn(ctx, tag, xn3, x3, W, NTOK, gates_acc=(), SEQ=2048):
    blk = ctx.blk
    TT = 512
    NB = NTOK // SEQ
    ntt = SEQ // TT
    NTILE = SEQ // 128
    H = 4

    ps = ctx.psum(f"{tag}_ps", [128, 8, TT], F32)
    psT = ps[:, 0, :].bitcast(BF16).rearrange("p (a f) -> p a f", f=128)

    qn = ctx.dram_tmp(f"{tag}_qn", [128, NCHUNK, NTOK], BF16)
    kn = ctx.dram_tmp(f"{tag}_kn", [128, NCHUNK, NTOK], BF16)
    vp = ctx.dram_tmp(f"{tag}_vp", [128, NCHUNK, NTOK], BF16)
    gdr = ctx.dram_tmp(f"{tag}_gdr", [128, NTOK // 128, 20], F32)

    wsl3 = ctx.sbuf(f"{tag}_wsl3", [128, 3, NCHUNK, D], BF16)
    wcb = ctx.sbuf(f"{tag}_wcb", [128, NCHUNK, 16], BF16)
    ccol = ctx.sbuf(f"{tag}_ccol", [128, NCHUNK, 12], F32)
    bghc = ctx.sbuf(f"{tag}_bgh", [16, 1], F32)
    nwr = ctx.sbuf(f"{tag}_nwr", [1, 256], F32)
    nw_bc = ctx.sbuf(f"{tag}_nwbc", [128, 256], BF16)
    xnt3 = ctx.sbuf(f"{tag}_xnt", [128, NCHUNK, TT], BF16)
    wbuf = ctx.sbuf(f"{tag}_wbuf", [128, NCHUNK, TT + 3], BF16)
    cbuf = ctx.sbuf(f"{tag}_cbuf", [128, NCHUNK, TT], F32)
    obf = ctx.sbuf(f"{tag}_obf", [128, NCHUNK, TT], BF16)
    rnr = ctx.sbuf(f"{tag}_rnr", [1, H, TT], F32)
    car = ctx.sbuf(f"{tag}_car", [128, 3, NCHUNK, 3], BF16)
    gt32w = ctx.sbuf(f"{tag}_gt32w", [16, TT], F32)
    gtt = ctx.sbuf(f"{tag}_gtt", [128, 4, 16], F32)
    gsc = ctx.sbuf(f"{tag}_gsc", [128, 4, 4], F32)
    gden = ctx.sbuf(f"{tag}_gden", [128, 4, 4], F32)
    gout = ctx.sbuf(f"{tag}_gout", [128, 4, 20], F32)
    qkv_b = cbuf.ap().bitcast(BF16).rearrange("p a f -> p (a f)").rearrange(
        "p (c f) -> p c f", c=4)
    v_b = obf.ap().rearrange("p a f -> p (a f)").rearrange(
        "p (c f) -> p c f", c=2)
    Qf = qkv_b[:, 0:2, 0:SEQ]
    Kf = qkv_b[:, 2:4, 0:SEQ]
    Vf = v_b[:, 0:2, 0:SEQ]
    O_fm = ctx.sbuf(f"{tag}_Ofm", [128, NCHUNK, SEQ], BF16)
    gt128 = ctx.sbuf(f"{tag}_gt128", [128, NTILE, 20], F32)
    g32 = ctx.sbuf(f"{tag}_g32", [32, NTILE, 4, 20], F32)
    ktmb = ctx.sbuf(f"{tag}_ktmb", [128, 2, 128], BF16)
    vtmb = ctx.sbuf(f"{tag}_vtmb", [128, 2, 128], BF16)
    vwb = ctx.sbuf(f"{tag}_vwb", [128, 2, 128], BF16)
    vbb = ctx.sbuf(f"{tag}_vbb", [128, 2, 2, 128], BF16)
    kbb = ctx.sbuf(f"{tag}_kbb", [128, 2, 2, 128], BF16)
    utm = ctx.sbuf(f"{tag}_utm", [128, 2, 2, 128], BF16)
    wmf = ctx.sbuf(f"{tag}_wmf", [128, 2, 2, 128], BF16)
    mf = ctx.sbuf(f"{tag}_mf", [128, 2, 12, 128], BF16)
    t128 = ctx.sbuf(f"{tag}_t128", [128, 128], F32)
    S32 = ctx.sbuf(f"{tag}_S32", [128, 2, 2, 256], F32)
    Sbf = ctx.sbuf(f"{tag}_Sbf", [128, 2, 2, 256], BF16)
    u2b = ctx.sbuf(f"{tag}_u2b", [32, 2, 256], BF16)
    k32b = ctx.sbuf(f"{tag}_k32b", [32, 256], BF16)
    atb = ctx.sbuf(f"{tag}_atb", [32, 32], BF16)
    scr32 = ctx.sbuf(f"{tag}_scr32", [32, 256], F32)
    ssq = ctx.sbuf(f"{tag}_ssq", [32, 2], F32)
    o32T = ctx.sbuf(f"{tag}_o32T", [32, 4, 256], BF16)

    ident = ctx.bufs["ident_bf"]
    ident32 = ctx.bufs["ident_f32"]
    ones_row = ctx.bufs["ones_row"]
    ones_col = ctx.bufs["ones_col"]
    mlneg = ctx.bufs["mlneg"]
    mu32 = ctx.bufs["mu32"]

    S = Stages(ctx, tag)
    WA = [("c_const", ctx.val("c_const"))] + list(gates_acc)

    def gate_stage(e, fin):
        for sem, val in WA:
            e.wait_ge(ctx.sem(sem), val)
    S.stage(tensor=gate_stage, vector=gate_stage, scalar=gate_stage,
            sync=gate_stage, gpsimd=gate_stage)

    def ldw(g, fin):
        for i, nm in enumerate(("wq", "wk", "wv")):
            fin(g.dma_start(out=wsl3[:, i], in_=W[nm]), 16)
        fin(g.dma_start(out=wcb.ap(), in_=W["wcat"]), 16)

    def ldw2(s, fin):
        fin(s.dma_start(out=ccol.ap(), in_=W["conv"]), 16)
        fin(s.dma_start(out=bghc.ap(), in_=W["bgh"]), 16)
        fin(s.dma_start(out=nwr.ap(), in_=W["normw"]), 16)
    S.stage(gpsimd=ldw, sync=ldw2)

    def nw1(t, fin):
        fin(t.matmul(ps[:, 0, 0:256], ones_row[:1, 0:128], nwr[0:1, :],
                     start=True, stop=True))
    S.stage(tensor=nw1)

    def nw2(v, fin):
        fin(v.tensor_copy(nw_bc.ap(), ps[:, 0, 0:256]))
    S.stage(vector=nw2)

    for b in range(NB):
        def cz(v, fin):
            fin(v.memset(car.ap(), 0.0))
        S.stage(vector=cz)

        for kt in range(ntt):
            tb = b * SEQ + kt * TT

            def ldx(s, fin, tb=tb):
                fin(s.dma_start(out=xnt3.ap(), in_=xn3[:, :, tb:tb + TT]), 16)
            S.stage(sync=ldx)

            for pi in range(3):
                for lo in (0, 4):
                    def pm(t, fin, pi=pi, lo=lo):
                        for dcn in range(lo, lo + 4):
                            for a in range(NCHUNK):
                                mm = t.matmul(ps[:, dcn - lo, :],
                                              wsl3[:, pi, a,
                                                   dcn * 128:(dcn + 1) * 128],
                                              xnt3[:, a, :], start=(a == 0),
                                              stop=(a == NCHUNK - 1))
                        fin(mm)

                    def pe(v, fin, lo=lo):
                        for dcn in range(lo, lo + 4):
                            e = v.tensor_copy(wbuf[:, dcn, 3:3 + TT],
                                              ps[:, dcn - lo, :])
                        fin(e)
                    S.stage(tensor=pm)
                    S.stage(vector=pe)

                def cv(v, fin, pi=pi):
                    ctx.seq(v, v.tensor_copy(wbuf[:, :, 0:3], car[:, pi]))
                    for c in range(NCHUNK):
                        ctx.seq(v, v.tensor_scalar(
                            cbuf[:, c, :], wbuf[:, c, 0:TT],
                            ccol[:, c, 4 * pi:4 * pi + 1], None, ALU.mult))
                        for j in range(1, 4):
                            ctx.seq(v, v.scalar_tensor_tensor(
                                cbuf[:, c, :], wbuf[:, c, j:j + TT],
                                ccol[:, c, 4 * pi + j:4 * pi + j + 1],
                                cbuf[:, c, :], ALU.mult, ALU.add))
                    fin(v.tensor_copy(car[:, pi], wbuf[:, :, TT:TT + 3]))
                S.stage(vector=cv)

                if pi < 2:
                    def sl(sc, fin, pi=pi):
                        for c in range(NCHUNK):
                            ctx.seq(sc, sc.activation(wbuf[:, c, 3:3 + TT],
                                                      cbuf[:, c, :], AF.Silu))
                            e = sc.activation(cbuf[:, c, :],
                                              wbuf[:, c, 3:3 + TT], AF.Square)
                            if c < NCHUNK - 1:
                                ctx.seq(sc, e)
                        fin(e)
                    S.stage(scalar=sl)

                    def nq2(t, fin):
                        for h in range(H):
                            for d in range(2):
                                mm = t.matmul(ps[0:1, h, :], ones_col[:],
                                              cbuf[:, 2 * h + d, :],
                                              start=(d == 0), stop=(d == 1))
                        fin(mm)
                    S.stage(tensor=nq2)

                    def nq3(sc, fin):
                        for h in range(H):
                            ctx.seq(sc, sc.activation(rnr[0:1, h, :],
                                                      ps[0:1, h, :], AF.Ln,
                                                      bias=kap(ctx, 1e-12, 1)))
                            e = sc.activation(rnr[0:1, h, :], rnr[0:1, h, :],
                                              AF.Exp, scale=-0.5)
                            if h < H - 1:
                                ctx.seq(sc, e)
                        fin(e)
                    S.stage(scalar=nq3)

                    def nq4(t, fin):
                        for h in range(H):
                            mm = t.matmul(ps[:, 4 + h, :], ones_row[:1, 0:128],
                                          rnr[0:1, h, :], start=True, stop=True)
                        fin(mm)
                    S.stage(tensor=nq4)

                    def nq5(v, fin):
                        for h in range(H):
                            for d in range(2):
                                e = v.tensor_tensor(obf[:, 2 * h + d, :],
                                                    wbuf[:, 2 * h + d, 3:3 + TT],
                                                    ps[:, 4 + h, :], ALU.mult)
                        fin(e)
                    S.stage(vector=nq5)

                    def st1(s, fin, pi=pi, tb=tb):
                        dst = qn if pi == 0 else kn
                        fin(s.dma_start(out=dst[:, :, tb:tb + TT],
                                        in_=obf.ap()), 16)
                    S.stage(sync=st1)
                else:
                    def sl2(sc, fin):
                        for c in range(NCHUNK):
                            e = sc.activation(wbuf[:, c, 3:3 + TT],
                                              cbuf[:, c, :], AF.Silu)
                        fin(e)
                    S.stage(scalar=sl2)

                    def st2(s, fin, tb=tb):
                        fin(s.dma_start(out=vp[:, :, tb:tb + TT],
                                        in_=wbuf[:, :, 3:3 + TT]), 16)
                    S.stage(sync=st2)

            def gm(t, fin):
                for a in range(NCHUNK):
                    mm = t.matmul(ps[0:16, 0, :], wcb[:, a, :], xnt3[:, a, :],
                                  start=(a == 0), stop=(a == NCHUNK - 1))
                fin(mm)
            S.stage(tensor=gm)

            def ge(v, fin):
                fin(v.tensor_scalar(gt32w.ap(), ps[0:16, 0, :],
                                    bghc[0:16, 0:1], None, ALU.add))
            S.stage(vector=ge)

            def gt(t, fin):
                for st in range(4):
                    mm = t.transpose(ps[:, 1, st * 16:(st + 1) * 16],
                                     gt32w[0:16, st * 128:(st + 1) * 128],
                                     ident32[0:16, 0:16])
                fin(mm)
            S.stage(tensor=gt)

            def gw1(v, fin):
                fin(v.tensor_copy(gtt.ap(),
                                  ps[:, 1, 0:64].rearrange("p (a f) -> p a f",
                                                           f=16)))
            S.stage(vector=gw1)

            def gw2(sc, fin):
                for st in range(4):
                    e = sc.activation(gtt[:, st, 0:8], gtt[:, st, 0:8],
                                      AF.Sigmoid)
                fin(e)
            S.stage(scalar=gw2)

            def gw3(v, fin):
                for st in range(4):
                    ctx.seq(v, v.tensor_tensor(gsc[:, st], gtt[:, st, 8:12],
                                               gtt[:, st, 12:16], ALU.max))
                    ctx.seq(v, v.tensor_scalar(gsc[:, st], gsc[:, st], 0.0,
                                               None, ALU.max))
                    ctx.seq(v, v.tensor_sub(gtt[:, st, 8:12], gtt[:, st, 8:12],
                                            gsc[:, st]))
                    e = v.tensor_sub(gtt[:, st, 12:16], gtt[:, st, 12:16],
                                     gsc[:, st])
                    if st < 3:
                        ctx.seq(v, e)
                fin(e)
            S.stage(vector=gw3)

            def gw4(sc, fin):
                for st in range(4):
                    ctx.seq(sc, sc.activation(gtt[:, st, 8:12], gtt[:, st, 8:12],
                                              AF.Exp))
                    ctx.seq(sc, sc.activation(gtt[:, st, 12:16],
                                              gtt[:, st, 12:16], AF.Exp))
                    e = sc.activation(gsc[:, st], gsc[:, st], AF.Exp, scale=-1.0)
                    if st < 3:
                        ctx.seq(sc, e)
                fin(e)
            S.stage(scalar=gw4)

            def gw5(v, fin):
                for st in range(4):
                    ctx.seq(v, v.tensor_tensor(gden[:, st], gtt[:, st, 8:12],
                                               gtt[:, st, 12:16], ALU.add))
                    ctx.seq(v, v.tensor_tensor(gden[:, st], gden[:, st],
                                               gsc[:, st], ALU.add))
                    ctx.seq(v, v.reciprocal(gden[:, st], gden[:, st]))
                    ctx.seq(v, v.tensor_copy(gout[:, st, 0:8], gtt[:, st, 0:8]))
                    ctx.seq(v, v.tensor_tensor(gout[:, st, 8:12],
                                               gtt[:, st, 8:12], gden[:, st],
                                               ALU.mult))
                    ctx.seq(v, v.tensor_tensor(gout[:, st, 12:16],
                                               gtt[:, st, 12:16], gden[:, st],
                                               ALU.mult))
                    e = v.tensor_tensor(gout[:, st, 16:20], gsc[:, st],
                                        gden[:, st], ALU.mult)
                    if st < 3:
                        ctx.seq(v, e)
                fin(e)
            S.stage(vector=gw5)

            def gd(s, fin, tb=tb):
                fin(s.dma_start(out=gdr[:, tb // 128:tb // 128 + 4, :],
                                in_=gout.ap()), 16)
            S.stage(sync=gd)

    def ldwo(g, fin):
        fin(g.dma_start(out=wsl3[:, 0], in_=W["wo"]), 16)
    S.stage(gpsimd=ldwo)

    for b in range(NB):
        for h in range(H):
            def bl(s, fin, b=b, h=h):
                fin(s.dma_start(out=Qf, in_=qn[:, 2 * h:2 * h + 2,
                                             b * SEQ:(b + 1) * SEQ]), 16)
                fin(s.dma_start(out=Kf, in_=kn[:, 2 * h:2 * h + 2,
                                             b * SEQ:(b + 1) * SEQ]), 16)
                fin(s.dma_start(out=Vf, in_=vp[:, 2 * h:2 * h + 2,
                                             b * SEQ:(b + 1) * SEQ]), 16)
                fin(s.dma_start(out=gt128.ap(),
                                in_=gdr[:, b * NTILE:(b + 1) * NTILE, :]), 16)
                for qq in range(4):
                    fin(s.dma_start(
                        out=g32[:, :, qq, :],
                        in_=gdr[32 * qq:32 * (qq + 1),
                                b * NTILE:(b + 1) * NTILE, :]), 16)

            def bl2(v, fin):
                ctx.seq(v, v.memset(S32.ap(), 0.0))
                fin(v.memset(Sbf.ap(), 0.0))
            S.stage(sync=bl, vector=bl2)

            for it in range(NTILE):
                tok0 = it * 128

                def t1(t, fin, tok0=tok0):
                    for d in range(2):
                        t.transpose(psT[:, d, :], Kf[:, d, tok0:tok0 + 128],
                                    ident[:])
                        mm = t.transpose(psT[:, 2 + d, :],
                                         Vf[:, d, tok0:tok0 + 128], ident[:])
                    fin(mm)
                S.stage(tensor=t1)

                def t2(v, fin, it=it, h=h):
                    ctx.seq(v, v.tensor_copy(ktmb.ap(), psT[:, 0:2, :]))
                    ctx.seq(v, v.tensor_copy(vtmb.ap(), psT[:, 2:4, :]))
                    ctx.seq(v, v.tensor_scalar(vwb.ap(), vtmb.ap(),
                                               gt128[:, it, 16 + h:17 + h],
                                               None, ALU.mult))
                    for s in range(2):
                        bc = gt128[:, it, 4 * s + h:4 * s + h + 1]
                        ctx.seq(v, v.tensor_scalar(vbb[:, s], vtmb.ap(), bc,
                                                   None, ALU.mult))
                        e = v.tensor_scalar(kbb[:, s], ktmb.ap(), bc,
                                            None, ALU.mult)
                        if s < 1:
                            ctx.seq(v, e)
                    fin(e)
                S.stage(vector=t2)

                def t3(t, fin, tok0=tok0):
                    for d in range(2):
                        mm = t.matmul(ps[:, 1, 0:128], Kf[:, d, tok0:tok0 + 128],
                                      Kf[:, d, tok0:tok0 + 128],
                                      start=(d == 0), stop=(d == 1))
                    fin(mm)
                S.stage(tensor=t3)

                def t4(v, fin, it=it, h=h):
                    for s in range(2):
                        bc = gt128[:, it, 4 * s + h:4 * s + h + 1]
                        ctx.seq(v, v.tensor_scalar(t128.ap(), ps[:, 1, 0:128],
                                                   bc, None, ALU.mult))
                        e = v.tensor_tensor(mf[:, s, 0, :], t128.ap(),
                                            mlneg[:], ALU.mult)
                        if s < 1:
                            ctx.seq(v, e)
                    fin(e)
                S.stage(vector=t4)

                def t5(t, fin):
                    for s in range(2):
                        mm = t.transpose(psT[:, s, :], mf[:, s, 0, :], ident[:])
                    fin(mm)
                S.stage(tensor=t5)

                def t6(v, fin):
                    for s in range(2):
                        e = v.tensor_copy(mf[:, s, 1, :], psT[:, s, :])
                    fin(e)
                S.stage(vector=t6)

                for r in range(3):
                    def sq(t, fin, r=r):
                        for s in range(2):
                            t.matmul(ps[:, 1 + 2 * s, 0:128], mf[:, s, 2 * r + 1, :],
                                     mf[:, s, 2 * r, :], start=True, stop=True)
                            mm = t.matmul(ps[:, 2 + 2 * s, 0:128],
                                          mf[:, s, 2 * r, :],
                                          mf[:, s, 2 * r + 1, :],
                                          start=True, stop=True)
                        fin(mm)

                    def sqe(v, fin, r=r):
                        for s in range(2):
                            v.tensor_copy(mf[:, s, 2 * r + 2, :],
                                          ps[:, 1 + 2 * s, 0:128])
                            e = v.tensor_copy(mf[:, s, 2 * r + 3, :],
                                              ps[:, 2 + 2 * s, 0:128])
                        fin(e)
                    S.stage(tensor=sq)
                    S.stage(vector=sqe)

                def m16(t, fin):
                    for s in range(2):
                        mm = t.matmul(ps[:, 1 + s, 0:128], mf[:, s, 7, :],
                                      mf[:, s, 6, :], start=True, stop=True)
                    fin(mm)
                S.stage(tensor=m16)

                def m16e(v, fin):
                    for s in range(2):
                        e = v.tensor_copy(mf[:, s, 8, :], ps[:, 1 + s, 0:128])
                    fin(e)
                S.stage(vector=m16e)

                qsteps = [
                    ([(None, None), (None, 1), (None, 3), (2, 1)], 9),
                    ([(None, 9), (4, 9)], 10),
                    ([(None, 10), (6, 10)], 9),
                    ([(None, 9), (8, 9)], 11),
                ]
                for terms, dst in qsteps:
                    def qmm(t, fin, terms=terms):
                        for s in range(2):
                            for i, (lh, rh) in enumerate(terms):
                                lhs = ident[:] if lh is None else mf[:, s, lh, :]
                                rhs = ident[:] if rh is None else mf[:, s, rh, :]
                                mm = t.matmul(ps[:, 1 + s, 0:128], lhs, rhs,
                                              start=(i == 0),
                                              stop=(i == len(terms) - 1))
                        fin(mm)

                    def qev(v, fin, dst=dst):
                        for s in range(2):
                            e = v.tensor_copy(mf[:, s, dst, :],
                                              ps[:, 1 + s, 0:128])
                        fin(e)
                    S.stage(tensor=qmm)
                    S.stage(vector=qev)

                def u1(t, fin):
                    for s in range(2):
                        t.matmul(ps[:, 1 + s, 0:256], mf[:, s, 11, :],
                                 vbb[:, s].rearrange("p a f -> p (a f)"),
                                 start=True, stop=True)
                        mm = t.matmul(ps[:, 3 + s, 0:256], mf[:, s, 11, :],
                                      kbb[:, s].rearrange("p a f -> p (a f)"),
                                      start=True, stop=True)
                    fin(mm)
                S.stage(tensor=u1)

                def u2(v, fin):
                    for s in range(2):
                        v.tensor_copy(utm[:, s],
                                      ps[:, 1 + s, 0:256].rearrange(
                                          "p (a f) -> p a f", f=128))
                        e = v.tensor_scalar_mul(kbb[:, s],
                                                ps[:, 3 + s, 0:256].rearrange(
                                                    "p (a f) -> p a f", f=128),
                                                -1.0)
                    fin(e)
                S.stage(vector=u2)

                def u3(t, fin):
                    for s in range(2):
                        for d in range(2):
                            mm = t.transpose(psT[:, 2 * s + d, :],
                                             kbb[:, s, d, :], ident[:])
                    fin(mm)
                S.stage(tensor=u3)

                def u4(v, fin):
                    for s in range(2):
                        e = v.tensor_copy(wmf[:, s], psT[:, 2 * s:2 * s + 2, :])
                    fin(e)
                S.stage(vector=u4)

                for ci in range(4):
                    j32 = it * 4 + ci

                    def ca(t, fin, tok0=tok0, ci=ci):
                        for s in range(2):
                            t.matmul(ps[0:32, 1 + s, 0:256],
                                     ident[:, 32 * ci:32 * ci + 32],
                                     utm[:, s].rearrange("p a f -> p (a f)"),
                                     start=True, stop=False)
                            for d in range(2):
                                t.matmul(ps[0:32, 1 + s, 0:256],
                                         wmf[:, s, d, 32 * ci:32 * ci + 32],
                                         Sbf[:, s, d, :], start=False,
                                         stop=(d == 1))
                        t.matmul(ps[0:32, 3, 0:256],
                                 ident[:, 32 * ci:32 * ci + 32],
                                 ktmb.ap().rearrange("p a f -> p (a f)"),
                                 start=True, stop=True)
                        for d in range(2):
                            mm = t.matmul(ps[0:32, 4, 0:32],
                                          Kf[:, d, tok0 + 32 * ci:
                                             tok0 + 32 * ci + 32],
                                          Qf[:, d, tok0 + 32 * ci:
                                             tok0 + 32 * ci + 32],
                                          start=(d == 0), stop=(d == 1))
                        fin(mm)
                    S.stage(tensor=ca)

                    def cb(v, fin, it=it, ci=ci, h=h):
                        for s in range(2):
                            ctx.seq(v, v.tensor_scalar(
                                u2b[:, s, :], ps[0:32, 1 + s, 0:256],
                                g32[0:32, it, ci, 8 + 4 * s + h:9 + 4 * s + h],
                                None, ALU.mult))
                        ctx.seq(v, v.tensor_copy(k32b.ap(), ps[0:32, 3, 0:256]))
                        fin(v.tensor_tensor(atb.ap(), ps[0:32, 4, 0:32],
                                            mu32[:], ALU.mult))
                    S.stage(vector=cb)

                    def cc(t, fin, tok0=tok0, ci=ci):
                        first = True
                        for s in range(2):
                            for d in range(2):
                                t.matmul(ps[0:32, 5, 0:256],
                                         Qf[:, d, tok0 + 32 * ci:
                                            tok0 + 32 * ci + 32],
                                         Sbf[:, s, d, :], start=first,
                                         stop=False)
                                first = False
                        for s in range(2):
                            t.matmul(ps[0:32, 5, 0:256], atb[:], u2b[:, s, :],
                                     start=False, stop=False)
                        mm = t.matmul(ps[0:32, 5, 0:256],
                                      ident[:, 32 * ci:32 * ci + 32],
                                      vwb.ap().rearrange("p a f -> p (a f)"),
                                      start=False, stop=True)
                        fin(mm)
                    S.stage(tensor=cc)

                    def cd(sc, fin):
                        ctx.seq(sc, sc.activation(scr32.ap(),
                                                  ps[0:32, 5, 0:256], AF.Square,
                                                  accum_out=ssq[:, 0:1]))
                        ctx.seq(sc, sc.activation(ssq[:, 1:2], ssq[:, 0:1],
                                                  AF.Ln, scale=1.0 / 256.0,
                                                  bias=kap(ctx, 1e-5, 32)))
                        fin(sc.activation(ssq[:, 1:2], ssq[:, 1:2], AF.Exp,
                                          scale=-0.5))

                    def ce(t, fin):
                        for s in range(2):
                            for d1 in range(2):
                                mm = t.matmul(ps[:, 1 + 2 * s + d1, 0:256],
                                              k32b[:, 128 * d1:128 * (d1 + 1)],
                                              u2b[:, s, :], start=True,
                                              stop=True)
                        fin(mm)
                    S.stage(scalar=cd, tensor=ce)

                    def cf(v, fin, ci=ci):
                        ctx.seq(v, v.tensor_tensor(scr32.ap(),
                                                   ps[0:32, 5, 0:256],
                                                   nw_bc[0:32, :], ALU.mult))
                        ctx.seq(v, v.tensor_scalar(o32T[:, ci, :], scr32.ap(),
                                                   ssq[0:32, 1:2], None,
                                                   ALU.mult))
                        for s in range(2):
                            ctx.seq(v, v.tensor_tensor(
                                S32[:, s], S32[:, s],
                                ps[:, 1 + 2 * s:3 + 2 * s, 0:256], ALU.add))
                            e = v.tensor_copy(Sbf[:, s], S32[:, s])
                            if s < 1:
                                ctx.seq(v, e)
                        fin(e)
                    S.stage(vector=cf)

                def cg(t, fin):
                    for ci in range(4):
                        for d in range(2):
                            mm = t.transpose(psT[:, ci * 2 + d, 0:32],
                                             o32T[:, ci, 128 * d:128 * (d + 1)],
                                             ident[0:32, 0:32])
                    fin(mm)
                S.stage(tensor=cg)

                def ch(v, fin, tok0=tok0, h=h):
                    for ci in range(4):
                        for d in range(2):
                            e = v.tensor_copy(O_fm[:, 2 * h + d,
                                                   tok0 + 32 * ci:
                                                   tok0 + 32 * ci + 32],
                                              psT[:, ci * 2 + d, 0:32])
                    fin(e)
                S.stage(vector=ch)

        for kt in range(ntt):
            for lo in (0, 4):
                def cm(t, fin, kt=kt, lo=lo):
                    for dcn in range(lo, lo + 4):
                        for a in range(NCHUNK):
                            mm = t.matmul(ps[:, dcn - lo, :],
                                          wsl3[:, 0, a, dcn * 128:(dcn + 1) * 128],
                                          O_fm[:, a, kt * TT:(kt + 1) * TT],
                                          start=(a == 0), stop=(a == NCHUNK - 1))
                    fin(mm)

                def cev(v, fin, lo=lo):
                    for dcn in range(lo, lo + 4):
                        e = v.tensor_copy(cbuf[:, dcn, :], ps[:, dcn - lo, :])
                    fin(e)
                S.stage(tensor=cm)
                S.stage(vector=cev)

            def cdma(g, fin, b=b, kt=kt):
                fin(g.dma_start(out=x3[:, :, b * SEQ + kt * TT:
                                       b * SEQ + (kt + 1) * TT],
                                in_=cbuf.ap(), accum_op=ALU.add), 16)
            S.stage(gpsimd=cdma)

    return dict(done=[(S.bar, ctx.val(S.bar))] +
                [(s, ctx.val(s)) for s in S.bard.values()])


def emit_l3attn2(ctx, tag, xn3, x3, W, NTOK, gates_acc=(), SEQ=2048):
    blk = ctx.blk
    TT = 512
    NB = NTOK // SEQ
    ntt = SEQ // TT
    NTILE = SEQ // 128
    H = 4

    ps = ctx.psum(f"{tag}_ps", [128, 8, TT], F32)
    psT = ps[:, 0, :].bitcast(BF16).rearrange("p (a f) -> p a f", f=128)

    qn = ctx.dram_tmp(f"{tag}_qn", [128, NCHUNK, NTOK], BF16)
    kn = ctx.dram_tmp(f"{tag}_kn", [128, NCHUNK, NTOK], BF16)
    vp = ctx.dram_tmp(f"{tag}_vp", [128, NCHUNK, NTOK], BF16)
    gdr = ctx.dram_tmp(f"{tag}_gdr", [128, NTOK // 128, 20], F32)

    wsl3 = ctx.sbuf(f"{tag}_wsl3", [128, 3, NCHUNK, D], BF16)
    wcb = ctx.sbuf(f"{tag}_wcb", [128, NCHUNK, 16], BF16)
    ccol = ctx.sbuf(f"{tag}_ccol", [128, NCHUNK, 12], F32)
    bghc = ctx.sbuf(f"{tag}_bgh", [16, 1], F32)
    nwr = ctx.sbuf(f"{tag}_nwr", [1, 256], F32)
    nw_bc = ctx.sbuf(f"{tag}_nwbc", [128, 256], BF16)
    xnt3 = ctx.sbuf(f"{tag}_xnt", [128, NCHUNK, TT], BF16)
    wbuf = ctx.sbuf(f"{tag}_wbuf", [128, NCHUNK, TT + 3], BF16)
    cbuf = ctx.sbuf(f"{tag}_cbuf", [128, NCHUNK, TT], F32)
    obf = ctx.sbuf(f"{tag}_obf", [128, NCHUNK, TT], BF16)
    rnr = ctx.sbuf(f"{tag}_rnr", [1, H, TT], F32)
    car = ctx.sbuf(f"{tag}_car", [128, 3, NCHUNK, 3], BF16)
    gt32w = ctx.sbuf(f"{tag}_gt32w", [16, TT], F32)
    gtt = ctx.sbuf(f"{tag}_gtt", [128, 4, 16], F32)
    gsc = ctx.sbuf(f"{tag}_gsc", [128, 4, 4], F32)
    gden = ctx.sbuf(f"{tag}_gden", [128, 4, 4], F32)
    gout = ctx.sbuf(f"{tag}_gout", [128, 4, 20], F32)
    qkv_b = cbuf.ap().bitcast(BF16).rearrange("p a f -> p (a f)").rearrange(
        "p (c f) -> p c f", c=4)
    v_b = obf.ap().rearrange("p a f -> p (a f)").rearrange(
        "p (c f) -> p c f", c=2)
    qkv2 = ctx.sbuf(f"{tag}_qkv2", [128, 6, SEQ], BF16)
    Qf = [qkv_b[:, 0:2, 0:SEQ], qkv2[:, 0:2, :]]
    Kf = [qkv_b[:, 2:4, 0:SEQ], qkv2[:, 2:4, :]]
    Vf = [v_b[:, 0:2, 0:SEQ], qkv2[:, 4:6, :]]
    O_fm = ctx.sbuf(f"{tag}_Ofm", [128, NCHUNK, SEQ], BF16)
    gt128 = ctx.sbuf(f"{tag}_gt128", [128, NTILE, 20], F32)
    g32 = ctx.sbuf(f"{tag}_g32", [32, NTILE, 4, 20], F32)
    ktmb = ctx.sbuf(f"{tag}_ktmb", [128, 2, 2, 128], BF16)
    vtmb = ctx.sbuf(f"{tag}_vtmb", [128, 2, 2, 128], BF16)
    vwb = ctx.sbuf(f"{tag}_vwb", [128, 2, 2, 128], BF16)
    vbb = ctx.sbuf(f"{tag}_vbb", [128, 4, 2, 128], BF16)
    kbb = ctx.sbuf(f"{tag}_kbb", [128, 4, 2, 128], BF16)
    utm = ctx.sbuf(f"{tag}_utm", [128, 4, 2, 128], BF16)
    wmf = ctx.sbuf(f"{tag}_wmf", [128, 4, 2, 128], BF16)
    mf = ctx.sbuf(f"{tag}_mf", [128, 4, 12, 128], BF16)
    t128 = ctx.sbuf(f"{tag}_t128", [128, 128], F32)
    S32 = xnt3.ap().bitcast(F32).rearrange("p a f -> p (a f)").rearrange(
        "p (q d f) -> p q d f", q=4, d=2)
    Sbf = wbuf[:, 0:4, 3:515].rearrange("p a (d f) -> p a d f", d=2)
    u2b = ctx.sbuf(f"{tag}_u2b", [32, 4, 256], BF16)
    k32b = ctx.sbuf(f"{tag}_k32b", [32, 2, 256], BF16)
    atb = ctx.sbuf(f"{tag}_atb", [32, 2, 32], BF16)
    scr32 = ctx.sbuf(f"{tag}_scr32", [32, 2, 256], F32)
    ssq = ctx.sbuf(f"{tag}_ssq", [32, 2, 2], F32)
    o32T = ctx.sbuf(f"{tag}_o32T", [32, 2, 4, 256], BF16)

    ident = ctx.bufs["ident_bf"]
    ident32 = ctx.bufs["ident_f32"]
    ones_row = ctx.bufs["ones_row"]
    ones_col = ctx.bufs["ones_col"]
    mlneg = ctx.bufs["mlneg"]
    mu32 = ctx.bufs["mu32"]

    S = Stages(ctx, tag)
    WA = [("c_const", ctx.val("c_const"))] + list(gates_acc)

    def gate_stage(e, fin):
        for sem, val in WA:
            e.wait_ge(ctx.sem(sem), val)
    S.stage(tensor=gate_stage, vector=gate_stage, scalar=gate_stage,
            sync=gate_stage, gpsimd=gate_stage)

    def ldw(g, fin):
        for i, nm in enumerate(("wq", "wk", "wv")):
            fin(g.dma_start(out=wsl3[:, i], in_=W[nm]), 16)
        fin(g.dma_start(out=wcb.ap(), in_=W["wcat"]), 16)

    def ldw2(s, fin):
        fin(s.dma_start(out=ccol.ap(), in_=W["conv"]), 16)
        fin(s.dma_start(out=bghc.ap(), in_=W["bgh"]), 16)
        fin(s.dma_start(out=nwr.ap(), in_=W["normw"]), 16)
    S.stage(gpsimd=ldw, sync=ldw2)

    def nw1(t, fin):
        fin(t.matmul(ps[:, 0, 0:256], ones_row[:1, 0:128], nwr[0:1, :],
                     start=True, stop=True))
    S.stage(tensor=nw1)

    def nw2(v, fin):
        fin(v.tensor_copy(nw_bc.ap(), ps[:, 0, 0:256]))
    S.stage(vector=nw2)

    for b in range(NB):
        def cz(v, fin):
            fin(v.memset(car.ap(), 0.0))
        S.stage(vector=cz)

        for kt in range(ntt):
            tb = b * SEQ + kt * TT

            def ldx(s, fin, tb=tb):
                fin(s.dma_start(out=xnt3.ap(), in_=xn3[:, :, tb:tb + TT]), 16)
            S.stage(sync=ldx)

            for pi in range(3):
                for lo in (0, 4):
                    def pm(t, fin, pi=pi, lo=lo):
                        for dcn in range(lo, lo + 4):
                            for a in range(NCHUNK):
                                mm = t.matmul(ps[:, dcn - lo, :],
                                              wsl3[:, pi, a,
                                                   dcn * 128:(dcn + 1) * 128],
                                              xnt3[:, a, :], start=(a == 0),
                                              stop=(a == NCHUNK - 1))
                        fin(mm)

                    def pe(v, fin, lo=lo):
                        for dcn in range(lo, lo + 4):
                            e = v.tensor_copy(wbuf[:, dcn, 3:3 + TT],
                                              ps[:, dcn - lo, :])
                        fin(e)
                    S.stage(tensor=pm)
                    S.stage(vector=pe)

                def cv(v, fin, pi=pi):
                    ctx.seq(v, v.tensor_copy(wbuf[:, :, 0:3], car[:, pi]))
                    for c in range(NCHUNK):
                        ctx.seq(v, v.tensor_scalar(
                            cbuf[:, c, :], wbuf[:, c, 0:TT],
                            ccol[:, c, 4 * pi:4 * pi + 1], None, ALU.mult))
                        for j in range(1, 4):
                            ctx.seq(v, v.scalar_tensor_tensor(
                                cbuf[:, c, :], wbuf[:, c, j:j + TT],
                                ccol[:, c, 4 * pi + j:4 * pi + j + 1],
                                cbuf[:, c, :], ALU.mult, ALU.add))
                    fin(v.tensor_copy(car[:, pi], wbuf[:, :, TT:TT + 3]))
                S.stage(vector=cv)

                if pi < 2:
                    def sl(sc, fin, pi=pi):
                        for c in range(NCHUNK):
                            ctx.seq(sc, sc.activation(wbuf[:, c, 3:3 + TT],
                                                      cbuf[:, c, :], AF.Silu))
                            e = sc.activation(cbuf[:, c, :],
                                              wbuf[:, c, 3:3 + TT], AF.Square)
                            if c < NCHUNK - 1:
                                ctx.seq(sc, e)
                        fin(e)
                    S.stage(scalar=sl)

                    def nq2(t, fin):
                        for h in range(H):
                            for d in range(2):
                                mm = t.matmul(ps[0:1, h, :], ones_col[:],
                                              cbuf[:, 2 * h + d, :],
                                              start=(d == 0), stop=(d == 1))
                        fin(mm)
                    S.stage(tensor=nq2)

                    def nq3(sc, fin):
                        for h in range(H):
                            ctx.seq(sc, sc.activation(rnr[0:1, h, :],
                                                      ps[0:1, h, :], AF.Ln,
                                                      bias=kap(ctx, 1e-12, 1)))
                            e = sc.activation(rnr[0:1, h, :], rnr[0:1, h, :],
                                              AF.Exp, scale=-0.5)
                            if h < H - 1:
                                ctx.seq(sc, e)
                        fin(e)
                    S.stage(scalar=nq3)

                    def nq4(t, fin):
                        for h in range(H):
                            mm = t.matmul(ps[:, 4 + h, :], ones_row[:1, 0:128],
                                          rnr[0:1, h, :], start=True, stop=True)
                        fin(mm)
                    S.stage(tensor=nq4)

                    def nq5(v, fin):
                        for h in range(H):
                            for d in range(2):
                                e = v.tensor_tensor(obf[:, 2 * h + d, :],
                                                    wbuf[:, 2 * h + d, 3:3 + TT],
                                                    ps[:, 4 + h, :], ALU.mult)
                        fin(e)
                    S.stage(vector=nq5)

                    def st1(s, fin, pi=pi, tb=tb):
                        dst = qn if pi == 0 else kn
                        fin(s.dma_start(out=dst[:, :, tb:tb + TT],
                                        in_=obf.ap()), 16)
                    S.stage(sync=st1)
                else:
                    def sl2(sc, fin):
                        for c in range(NCHUNK):
                            e = sc.activation(wbuf[:, c, 3:3 + TT],
                                              cbuf[:, c, :], AF.Silu)
                        fin(e)
                    S.stage(scalar=sl2)

                    def st2(s, fin, tb=tb):
                        fin(s.dma_start(out=vp[:, :, tb:tb + TT],
                                        in_=wbuf[:, :, 3:3 + TT]), 16)
                    S.stage(sync=st2)

            def gm(t, fin):
                for a in range(NCHUNK):
                    mm = t.matmul(ps[0:16, 0, :], wcb[:, a, :], xnt3[:, a, :],
                                  start=(a == 0), stop=(a == NCHUNK - 1))
                fin(mm)
            S.stage(tensor=gm)

            def ge(v, fin):
                fin(v.tensor_scalar(gt32w.ap(), ps[0:16, 0, :],
                                    bghc[0:16, 0:1], None, ALU.add))
            S.stage(vector=ge)

            def gt(t, fin):
                for st in range(4):
                    mm = t.transpose(ps[:, 1, st * 16:(st + 1) * 16],
                                     gt32w[0:16, st * 128:(st + 1) * 128],
                                     ident32[0:16, 0:16])
                fin(mm)
            S.stage(tensor=gt)

            def gw1(v, fin):
                fin(v.tensor_copy(gtt.ap(),
                                  ps[:, 1, 0:64].rearrange("p (a f) -> p a f",
                                                           f=16)))
            S.stage(vector=gw1)

            def gw2(sc, fin):
                for st in range(4):
                    e = sc.activation(gtt[:, st, 0:8], gtt[:, st, 0:8],
                                      AF.Sigmoid)
                fin(e)
            S.stage(scalar=gw2)

            def gw3(v, fin):
                for st in range(4):
                    ctx.seq(v, v.tensor_tensor(gsc[:, st], gtt[:, st, 8:12],
                                               gtt[:, st, 12:16], ALU.max))
                    ctx.seq(v, v.tensor_scalar(gsc[:, st], gsc[:, st], 0.0,
                                               None, ALU.max))
                    ctx.seq(v, v.tensor_sub(gtt[:, st, 8:12], gtt[:, st, 8:12],
                                            gsc[:, st]))
                    e = v.tensor_sub(gtt[:, st, 12:16], gtt[:, st, 12:16],
                                     gsc[:, st])
                    if st < 3:
                        ctx.seq(v, e)
                fin(e)
            S.stage(vector=gw3)

            def gw4(sc, fin):
                for st in range(4):
                    ctx.seq(sc, sc.activation(gtt[:, st, 8:12], gtt[:, st, 8:12],
                                              AF.Exp))
                    ctx.seq(sc, sc.activation(gtt[:, st, 12:16],
                                              gtt[:, st, 12:16], AF.Exp))
                    e = sc.activation(gsc[:, st], gsc[:, st], AF.Exp, scale=-1.0)
                    if st < 3:
                        ctx.seq(sc, e)
                fin(e)
            S.stage(scalar=gw4)

            def gw5(v, fin):
                for st in range(4):
                    ctx.seq(v, v.tensor_tensor(gden[:, st], gtt[:, st, 8:12],
                                               gtt[:, st, 12:16], ALU.add))
                    ctx.seq(v, v.tensor_tensor(gden[:, st], gden[:, st],
                                               gsc[:, st], ALU.add))
                    ctx.seq(v, v.reciprocal(gden[:, st], gden[:, st]))
                    ctx.seq(v, v.tensor_copy(gout[:, st, 0:8], gtt[:, st, 0:8]))
                    ctx.seq(v, v.tensor_tensor(gout[:, st, 8:12],
                                               gtt[:, st, 8:12], gden[:, st],
                                               ALU.mult))
                    ctx.seq(v, v.tensor_tensor(gout[:, st, 12:16],
                                               gtt[:, st, 12:16], gden[:, st],
                                               ALU.mult))
                    e = v.tensor_tensor(gout[:, st, 16:20], gsc[:, st],
                                        gden[:, st], ALU.mult)
                    if st < 3:
                        ctx.seq(v, e)
                fin(e)
            S.stage(vector=gw5)

            def gd(s, fin, tb=tb):
                fin(s.dma_start(out=gdr[:, tb // 128:tb // 128 + 4, :],
                                in_=gout.ap()), 16)
            S.stage(sync=gd)

    def ldwo(g, fin):
        fin(g.dma_start(out=wsl3[:, 0], in_=W["wo"]), 16)
    S.stage(gpsimd=ldwo)


    for b in range(NB):
        for hp in range(2):
            def bl(s, fin, b=b, hp=hp):
                for u in range(2):
                    h = 2 * hp + u
                    fin(s.dma_start(out=Qf[u], in_=qn[:, 2 * h:2 * h + 2,
                                                     b * SEQ:(b + 1) * SEQ]), 16)
                    fin(s.dma_start(out=Kf[u], in_=kn[:, 2 * h:2 * h + 2,
                                                     b * SEQ:(b + 1) * SEQ]), 16)
                    fin(s.dma_start(out=Vf[u], in_=vp[:, 2 * h:2 * h + 2,
                                                     b * SEQ:(b + 1) * SEQ]), 16)
                if hp == 0:
                    fin(s.dma_start(out=gt128.ap(),
                                    in_=gdr[:, b * NTILE:(b + 1) * NTILE, :]),
                        16)
                    for qq in range(4):
                        fin(s.dma_start(
                            out=g32[:, :, qq, :],
                            in_=gdr[32 * qq:32 * (qq + 1),
                                    b * NTILE:(b + 1) * NTILE, :]), 16)

            def bl2(v, fin):
                ctx.seq(v, v.memset(S32, 0.0))
                fin(v.memset(Sbf, 0.0))
            S.stage(sync=bl, vector=bl2)

            for it in range(NTILE):
                tok0 = it * 128

                def t1(t, fin, tok0=tok0):
                    for u in range(2):
                        for d in range(2):
                            t.transpose(psT[:, 4 * u + d, :],
                                        Kf[u][:, d, tok0:tok0 + 128], ident[:])
                            mm = t.transpose(psT[:, 4 * u + 2 + d, :],
                                             Vf[u][:, d, tok0:tok0 + 128],
                                             ident[:])
                    fin(mm)
                S.stage(tensor=t1)

                def t2(v, fin, it=it, hp=hp):
                    for u in range(2):
                        h = 2 * hp + u
                        ctx.seq(v, v.tensor_copy(ktmb[:, u],
                                                 psT[:, 4 * u:4 * u + 2, :]))
                        ctx.seq(v, v.tensor_copy(vtmb[:, u],
                                                 psT[:, 4 * u + 2:4 * u + 4, :]))
                        ctx.seq(v, v.tensor_scalar(vwb[:, u], vtmb[:, u],
                                                   gt128[:, it, 16 + h:17 + h],
                                                   None, ALU.mult))
                        for s in range(2):
                            bc = gt128[:, it, 4 * s + h:4 * s + h + 1]
                            ctx.seq(v, v.tensor_scalar(vbb[:, 2 * u + s],
                                                       vtmb[:, u], bc, None,
                                                       ALU.mult))
                            e = v.tensor_scalar(kbb[:, 2 * u + s], ktmb[:, u],
                                                bc, None, ALU.mult)
                            if u < 1 or s < 1:
                                ctx.seq(v, e)
                    fin(e)
                S.stage(vector=t2)

                def t3(t, fin, tok0=tok0):
                    for u in range(2):
                        for d in range(2):
                            mm = t.matmul(ps[:, 1 + 4 * u, 0:128],
                                          Kf[u][:, d, tok0:tok0 + 128],
                                          Kf[u][:, d, tok0:tok0 + 128],
                                          start=(d == 0), stop=(d == 1))
                    fin(mm)
                S.stage(tensor=t3)

                def t4(v, fin, it=it, hp=hp):
                    for u in range(2):
                        h = 2 * hp + u
                        for s in range(2):
                            bc = gt128[:, it, 4 * s + h:4 * s + h + 1]
                            ctx.seq(v, v.tensor_scalar(t128.ap(),
                                                       ps[:, 1 + 4 * u, 0:128],
                                                       bc, None, ALU.mult))
                            e = v.tensor_tensor(mf[:, 2 * u + s, 0, :],
                                                t128.ap(), mlneg[:], ALU.mult)
                            if u < 1 or s < 1:
                                ctx.seq(v, e)
                    fin(e)
                S.stage(vector=t4)

                def t5(t, fin):
                    for q in range(4):
                        mm = t.transpose(psT[:, q, :], mf[:, q, 0, :], ident[:])
                    fin(mm)
                S.stage(tensor=t5)

                def t6(v, fin):
                    for q in range(4):
                        e = v.tensor_copy(mf[:, q, 1, :], psT[:, q, :])
                    fin(e)
                S.stage(vector=t6)

                for r in range(3):
                    def sq(t, fin, r=r):
                        for q in range(4):
                            t.matmul(ps[:, q, 0:128], mf[:, q, 2 * r + 1, :],
                                     mf[:, q, 2 * r, :], start=True, stop=True)
                            mm = t.matmul(ps[:, 4 + q, 0:128],
                                          mf[:, q, 2 * r, :],
                                          mf[:, q, 2 * r + 1, :], start=True,
                                          stop=True)
                        fin(mm)

                    def sqe(v, fin, r=r):
                        for q in range(4):
                            v.tensor_copy(mf[:, q, 2 * r + 2, :],
                                          ps[:, q, 0:128])
                            e = v.tensor_copy(mf[:, q, 2 * r + 3, :],
                                              ps[:, 4 + q, 0:128])
                        fin(e)
                    S.stage(tensor=sq)
                    S.stage(vector=sqe)

                def m16(t, fin):
                    for q in range(4):
                        mm = t.matmul(ps[:, q, 0:128], mf[:, q, 7, :],
                                      mf[:, q, 6, :], start=True, stop=True)
                    fin(mm)
                S.stage(tensor=m16)

                def m16e(v, fin):
                    for q in range(4):
                        e = v.tensor_copy(mf[:, q, 8, :], ps[:, q, 0:128])
                    fin(e)
                S.stage(vector=m16e)

                qsteps = [
                    ([(None, None), (None, 1), (None, 3), (2, 1)], 9),
                    ([(None, 9), (4, 9)], 10),
                    ([(None, 10), (6, 10)], 9),
                    ([(None, 9), (8, 9)], 11),
                ]
                for terms, dst in qsteps:
                    def qmm(t, fin, terms=terms):
                        for q in range(4):
                            for i, (lh, rh) in enumerate(terms):
                                lhs = ident[:] if lh is None else mf[:, q, lh, :]
                                rhs = ident[:] if rh is None else mf[:, q, rh, :]
                                mm = t.matmul(ps[:, q, 0:128], lhs, rhs,
                                              start=(i == 0),
                                              stop=(i == len(terms) - 1))
                        fin(mm)

                    def qev(v, fin, dst=dst):
                        for q in range(4):
                            e = v.tensor_copy(mf[:, q, dst, :],
                                              ps[:, q, 0:128])
                        fin(e)
                    S.stage(tensor=qmm)
                    S.stage(vector=qev)

                def u1(t, fin):
                    for q in range(4):
                        t.matmul(ps[:, q, 0:256], mf[:, q, 11, :],
                                 vbb[:, q].rearrange("p a f -> p (a f)"),
                                 start=True, stop=True)
                        mm = t.matmul(ps[:, 4 + q, 0:256], mf[:, q, 11, :],
                                      kbb[:, q].rearrange("p a f -> p (a f)"),
                                      start=True, stop=True)
                    fin(mm)
                S.stage(tensor=u1)

                def u2(v, fin):
                    for q in range(4):
                        v.tensor_copy(utm[:, q],
                                      ps[:, q, 0:256].rearrange(
                                          "p (a f) -> p a f", f=128))
                        e = v.tensor_scalar_mul(kbb[:, q],
                                                ps[:, 4 + q, 0:256].rearrange(
                                                    "p (a f) -> p a f", f=128),
                                                -1.0)
                    fin(e)
                S.stage(vector=u2)

                def u3(t, fin):
                    for q in range(4):
                        for d in range(2):
                            mm = t.transpose(psT[:, 2 * q + d, :],
                                             kbb[:, q, d, :], ident[:])
                    fin(mm)
                S.stage(tensor=u3)

                def u4(v, fin):
                    for q in range(4):
                        e = v.tensor_copy(wmf[:, q], psT[:, 2 * q:2 * q + 2, :])
                    fin(e)
                S.stage(vector=u4)

                DSB = [1, 2, 5, 6]
                for ci in range(4):
                    def ca(t, fin, tok0=tok0, ci=ci):
                        for u in range(2):
                            for s in range(2):
                                q = 2 * u + s
                                t.matmul(ps[0:32, 4 * u + s, 0:256],
                                         ident[:, 32 * ci:32 * ci + 32],
                                         utm[:, q].rearrange("p a f -> p (a f)"),
                                         start=True, stop=False)
                                for d in range(2):
                                    t.matmul(ps[0:32, 4 * u + s, 0:256],
                                             wmf[:, q, d, 32 * ci:32 * ci + 32],
                                             Sbf[:, q, d, :], start=False,
                                             stop=(d == 1))
                            t.matmul(ps[0:32, 4 * u + 2, 0:256],
                                     ident[:, 32 * ci:32 * ci + 32],
                                     ktmb[:, u].rearrange("p a f -> p (a f)"),
                                     start=True, stop=True)
                            for d in range(2):
                                mm = t.matmul(ps[0:32, 4 * u + 3, 0:32],
                                              Kf[u][:, d, tok0 + 32 * ci:
                                                   tok0 + 32 * ci + 32],
                                              Qf[u][:, d, tok0 + 32 * ci:
                                                   tok0 + 32 * ci + 32],
                                              start=(d == 0), stop=(d == 1))
                        fin(mm)
                    S.stage(tensor=ca)

                    def cb(v, fin, it=it, ci=ci, hp=hp):
                        for u in range(2):
                            h = 2 * hp + u
                            for s in range(2):
                                ctx.seq(v, v.tensor_scalar(
                                    u2b[:, 2 * u + s, :],
                                    ps[0:32, 4 * u + s, 0:256],
                                    g32[0:32, it, ci, 8 + 4 * s + h:
                                        9 + 4 * s + h], None, ALU.mult))
                            ctx.seq(v, v.tensor_copy(k32b[:, u, :],
                                                     ps[0:32, 4 * u + 2, 0:256]))
                            e = v.tensor_tensor(atb[:, u, :],
                                                ps[0:32, 4 * u + 3, 0:32],
                                                mu32[:], ALU.mult)
                            if u < 1:
                                ctx.seq(v, e)
                        fin(e)
                    S.stage(vector=cb)

                    def cc(t, fin, tok0=tok0, ci=ci):
                        for u in range(2):
                            first = True
                            for s in range(2):
                                for d in range(2):
                                    t.matmul(ps[0:32, 4 * u, 0:256],
                                             Qf[u][:, d, tok0 + 32 * ci:
                                                  tok0 + 32 * ci + 32],
                                             Sbf[:, 2 * u + s, d, :],
                                             start=first, stop=False)
                                    first = False
                            for s in range(2):
                                t.matmul(ps[0:32, 4 * u, 0:256], atb[:, u, :],
                                         u2b[:, 2 * u + s, :], start=False,
                                         stop=False)
                            mm = t.matmul(ps[0:32, 4 * u, 0:256],
                                          ident[:, 32 * ci:32 * ci + 32],
                                          vwb[:, u].rearrange("p a f -> p (a f)"),
                                          start=False, stop=True)
                        fin(mm)
                    S.stage(tensor=cc)

                    def cd(sc, fin):
                        for u in range(2):
                            ctx.seq(sc, sc.activation(scr32[:, u, :],
                                                      ps[0:32, 4 * u, 0:256],
                                                      AF.Square,
                                                      accum_out=ssq[:, u, 0:1]))
                            ctx.seq(sc, sc.activation(ssq[:, u, 1:2],
                                                      ssq[:, u, 0:1], AF.Ln,
                                                      scale=1.0 / 256.0,
                                                      bias=kap(ctx, 1e-5, 32)))
                            e = sc.activation(ssq[:, u, 1:2], ssq[:, u, 1:2],
                                              AF.Exp, scale=-0.5)
                            if u < 1:
                                ctx.seq(sc, e)
                        fin(e)

                    def ce0(t, fin):
                        for s in range(2):
                            for d1 in range(2):
                                mm = t.matmul(ps[:, DSB[2 * s + d1], 0:256],
                                              k32b[0:32, 0, 128 * d1:
                                                   128 * (d1 + 1)],
                                              u2b[:, s, :], start=True,
                                              stop=True)
                        fin(mm)
                    S.stage(scalar=cd, tensor=ce0)

                    def mk_cf(u, ci=ci):
                        def cf(v, fin, ci=ci, u=u):
                            ctx.seq(v, v.tensor_tensor(scr32[:, u, :],
                                                       ps[0:32, 4 * u, 0:256],
                                                       nw_bc[0:32, :],
                                                       ALU.mult))
                            ctx.seq(v, v.tensor_scalar(o32T[:, u, ci, :],
                                                       scr32[:, u, :],
                                                       ssq[0:32, u, 1:2], None,
                                                       ALU.mult))
                            for s in range(2):
                                q = 2 * u + s
                                ctx.seq(v, v.tensor_tensor(
                                    S32[:, q], S32[:, q],
                                    ps[:, 1 + 4 * s:3 + 4 * s, 0:256],
                                    ALU.add))
                                e = v.tensor_copy(Sbf[:, q], S32[:, q])
                                if s < 1:
                                    ctx.seq(v, e)
                            fin(e)
                        return cf

                    def ce1(t, fin):
                        for s in range(2):
                            for d1 in range(2):
                                mm = t.matmul(ps[:, DSB[2 * s + d1], 0:256],
                                              k32b[0:32, 1, 128 * d1:
                                                   128 * (d1 + 1)],
                                              u2b[:, 2 + s, :], start=True,
                                              stop=True)
                        fin(mm)
                    S.stage(vector=mk_cf(0))
                    S.stage(tensor=ce1)
                    S.stage(vector=mk_cf(1))

                for u in range(2):
                    def cg(t, fin, u=u):
                        for ci in range(4):
                            for d in range(2):
                                mm = t.transpose(psT[:, ci * 2 + d, 0:32],
                                                 o32T[:, u, ci,
                                                      128 * d:128 * (d + 1)],
                                                 ident[0:32, 0:32])
                        fin(mm)
                    S.stage(tensor=cg)

                    def ch(v, fin, tok0=tok0, hp=hp, u=u):
                        h = 2 * hp + u
                        for ci in range(4):
                            for d in range(2):
                                e = v.tensor_copy(O_fm[:, 2 * h + d,
                                                       tok0 + 32 * ci:
                                                       tok0 + 32 * ci + 32],
                                                  psT[:, ci * 2 + d, 0:32])
                        fin(e)
                    S.stage(vector=ch)

        for kt in range(ntt):
            for lo in (0, 4):
                def cm(t, fin, kt=kt, lo=lo):
                    for dcn in range(lo, lo + 4):
                        for a in range(NCHUNK):
                            mm = t.matmul(ps[:, dcn - lo, :],
                                          wsl3[:, 0, a, dcn * 128:(dcn + 1) * 128],
                                          O_fm[:, a, kt * TT:(kt + 1) * TT],
                                          start=(a == 0), stop=(a == NCHUNK - 1))
                    fin(mm)

                def cev(v, fin, lo=lo):
                    for dcn in range(lo, lo + 4):
                        e = v.tensor_copy(cbuf[:, dcn, :], ps[:, dcn - lo, :])
                    fin(e)
                S.stage(tensor=cm)
                S.stage(vector=cev)

            def cdma(g, fin, b=b, kt=kt):
                fin(g.dma_start(out=x3[:, :, b * SEQ + kt * TT:
                                       b * SEQ + (kt + 1) * TT],
                                in_=cbuf.ap(), accum_op=ALU.add), 16)
            S.stage(gpsimd=cdma)

    return dict(done=[(S.bar, ctx.val(S.bar))] +
                [(s, ctx.val(s)) for s in S.bard.values()])





def emit_xadd(ctx, tag, xn3, x3, NTOK, gates_acc=()):
    TT = 512
    nt = NTOK // TT
    xb = ctx.sbuf(f"{tag}_xb", [128, NCHUNK, TT], BF16)
    xf = ctx.sbuf(f"{tag}_xf", [128, NCHUNK, TT], F32)
    S = Stages(ctx, tag)

    def gate_stage(e, fin):
        for sem, val in gates_acc:
            e.wait_ge(ctx.sem(sem), val)
    S.stage(sync=gate_stage, vector=gate_stage, gpsimd=gate_stage)
    for kt in range(nt):
        def ld(s, fin, kt=kt):
            fin(s.dma_start(out=xb.ap(), in_=xn3[:, :, kt * TT:(kt + 1) * TT]),
                16)
        S.stage(sync=ld)

        def cp(v, fin):
            fin(v.tensor_copy(xf.ap(), xb.ap()))
        S.stage(vector=cp)

        def ac(g, fin, kt=kt):
            fin(g.dma_start(out=x3[:, :, kt * TT:(kt + 1) * TT], in_=xf.ap(),
                            accum_op=ALU.add), 16)
        S.stage(gpsimd=ac)
    return dict(done=[(S.bar, ctx.val(S.bar))] +
                [(s, ctx.val(s)) for s in S.bard.values()])


import ml_dtypes
from contextlib import ExitStack

NTOK_K = 4096
SEQ_K = 2048
USE_FUSED = True
_ALLC = ("ident_bf", "ident_f32", "ones_col", "ones_col_bf", "ones_row",
         "ones_row_bf", "konst", "sel8", "sel32", "bones", "mlneg", "mu32")
_KEEP = {
    "xn": ("ones_col", "ones_row", "konst"),
    "l0attn": ("bones", "ident_bf", "ident_f32", "ones_col", "ones_col_bf",
               "ones_row", "ones_row_bf", "konst"),
    "l0ffn": ("ident_bf", "konst"),
    "l1attn": ("sel32", "ident_f32", "konst"),
    "l1ffn": ("ident_f32", "ones_row", "sel8", "konst"),
    "l2attn": ("ident_bf", "ones_row", "konst"),
    "l2ffn": ("ident_bf", "ident_f32", "konst"),
    "l3attn": ("ident_bf", "ident_f32", "ones_col", "ones_row", "konst",
               "mlneg", "mu32"),
    "l3ffn": ("ident_bf", "ident_f32", "ones_col", "ones_col_bf", "ones_row",
              "sel8", "konst"),
    "xadd": (),
}
_SKIP = {k: tuple(c for c in _ALLC if c not in v) for k, v in _KEEP.items()}


def _fm(x):
    return np.ascontiguousarray(x.T.reshape(8, 128, -1).transpose(1, 0, 2))


def _unfm(a):
    return np.asarray(a).transpose(1, 0, 2).reshape(1024, -1).T


def _wTh(w):
    return np.ascontiguousarray(
        w.reshape(-1, 128, w.shape[1]).transpose(1, 0, 2))


def _finish(ctx, blk, r):
    @blk.vector
    def _(v):
        for s, val in r["done"]:
            ctx.wait(v, s, val)


_W_SHAPES = {
    "l0attn": [("wq", [1024, 1024], F32), ("wk", [1024, 1024], F32),
               ("wv", [1024, 1024], F32), ("wo", [1024, 1024], F32),
               ("w1", [4096, 1024], F32), ("w2", [1024, 4096], F32),
               ("proj", [256, 64], F32),
               ("bq", [1, 1024], F32), ("bk", [1, 1024], F32),
               ("bv", [1, 1024], F32), ("bo", [1, 1024], F32),
               ("b1", [1, 4096], F32), ("b2", [1, 1024], F32),
               ("g2", [1, 1024], F32), ("be2", [1, 1024], F32),
               ("g3", [1, 1024], F32), ("be3", [1, 1024], F32)],
    "l0ffn": [("wg", [3072, 1024], I32), ("wu", [3072, 1024], I32),
              ("wd", [1024, 3072], I32)],
    "l1attn": [("ck", [3, 1, 1024], F32), ("alog", [1024, 16], F32),
               ("wbc", [1024, 32], F32), ("bbc", [1, 32], F32),
               ("dco", [1, 1024], F32), ("wdt", [1024, 64], F32),
               ("bdt", [1, 64], F32), ("wdtp", [64, 1024], F32),
               ("bdtp", [1, 1024], F32)],
    "l1ffn": [("wg", [8, 1024, 1024], F32), ("wu", [8, 1024, 1024], F32),
              ("wd", [8, 1024, 1024], F32), ("wr", [1024, 8], F32),
              ("temp", [1, 1], F32)],
    "l2attn": [("wqkv", [16, 3, 1024, 64], F32), ("wo", [1024, 1024], F32)],
    "l2ffn": [("w1", [4096, 1024], F32), ("w2", [4096, 1024], F32),
              ("h1", [4096, 4096], F32), ("h2", [4096, 4096], F32),
              ("w3", [1024, 4096], F32), ("s1", [1, 4096], F32),
              ("s2", [1, 4096], F32), ("sh1", [1, 4096], F32),
              ("sh2", [1, 4096], F32), ("s3", [1, 1024], F32)],
    "l3attn": [("wq", [128, 8, 1024], F32), ("wk", [128, 8, 1024], F32),
               ("wv", [128, 8, 1024], F32), ("wo", [128, 8, 1024], F32),
               ("wcat", [128, 8, 16], F32), ("conv", [128, 8, 12], F32),
               ("bgh", [16, 1], F32), ("normw", [1, 256], F32)],
    "l3ffn": [("gam", [1, 1024], F32), ("bet", [1, 1024], F32),
              ("wr", [8, 1024], F32), ("gu", [8, 4096, 1024], F32),
              ("dn", [8, 1024, 2048], F32)],
}


def _emit_block(ctx, bnm, xn3, x3, ga):
    declare_consts(ctx, skip=_SKIP[bnm])
    W = {nm: ctx.dram_in(f"{bnm}_{nm}", sh, dt)
         for nm, sh, dt in _W_SHAPES[bnm]}
    if bnm == "l0attn":
        return emit_l0attn(ctx, "a0", xn3, x3, W, NTOK_K, gates_acc=ga)
    if bnm == "l0ffn":
        return emit_l0ffn(ctx, "f0", xn3, x3, W["wg"], W["wu"], W["wd"],
                          NTOK_K, gates_acc=ga)
    if bnm == "l1attn":
        return emit_l1attn(ctx, "m0", xn3, x3, W["ck"], W["alog"], W["wbc"],
                           W["bbc"], W["dco"], W["wdt"], W["bdt"], W["wdtp"],
                           W["bdtp"], NTOK_K, gates_acc=ga)
    if bnm == "l1ffn":
        return emit_l1ffn(ctx, "m1", xn3, x3, W["wg"], W["wu"], W["wd"],
                          W["wr"], W["temp"], NTOK_K, gates_acc=ga)
    if bnm == "l2attn":
        return emit_l2attn(ctx, "a2", xn3, x3, W["wqkv"], W["wo"], NTOK_K,
                           gates_acc=ga)
    if bnm == "l2ffn":
        return emit_l2ffn(ctx, "t2", xn3, x3, W["w1"], W["s1"], W["w2"],
                          W["s2"], W["h1"], W["sh1"], W["h2"], W["sh2"],
                          W["w3"], W["s3"], NTOK_K, gates_acc=ga)
    if bnm == "l3attn":
        return emit_l3attn2(ctx, "l3a", xn3, x3, W, NTOK_K, gates_acc=ga,
                            SEQ=SEQ_K)
    if bnm == "l3ffn":
        return emit_l3ffn(ctx, "t3", xn3, x3, W["gam"], W["bet"], W["wr"],
                          W["gu"], W["dn"], NTOK_K, gates_acc=ga)
    raise KeyError(bnm)


_LAYER_BLOCKS = [("l0attn", "l0ffn"), ("l1attn", "l1ffn"),
                 ("l2attn", "l2ffn"), ("l3attn", "l3ffn")]


def _build_fused():
    nc = bass.Bass("TRN2", target_bir_lowering=False)
    xin = nc.dram_tensor("xin", [128, NCHUNK, NTOK_K], F32,
                         kind="ExternalInput").ap()
    x3 = nc.dram_tensor("x", [128, NCHUNK, NTOK_K], F32,
                        kind="ExternalOutput").ap()
    xn3 = nc.dram_tensor("xn_state", [128, NCHUNK, NTOK_K], BF16,
                         kind="Internal").ap()
    uid = [0]

    def pass_(emitfn):
        uid[0] += 1
        with nc.reset_on_exit():
            st = ExitStack()
            ctx = Ctx(nc, st, uid=f"@{uid[0]}")
            with st:
                with nc.Block() as blk:
                    ctx.blk = blk
                    r = emitfn(ctx, blk)
                    if r is not None:
                        _finish(ctx, blk, r)

    def p_init(ctx, blk):
        @blk.sync
        def _(s):
            ctx.inc(s.dma_start(out=x3, in_=xin), "xi", 16)
            ctx.wait(s, "xi", 16)
        return None
    pass_(p_init)

    for li, blocks in enumerate(_LAYER_BLOCKS):
        def p_xn(ctx, blk):
            declare_consts(ctx, skip=_SKIP["xn"])
            return emit_xn(ctx, x3, xn3, NTOK_K, "xn0")
        pass_(p_xn)
        for bnm in blocks:
            pass_(lambda ctx, blk, bnm=bnm: _emit_block(ctx, bnm, xn3, x3, ()))
        if li == 3:
            pass_(lambda ctx, blk: emit_xadd(ctx, "xa", xn3, x3, NTOK_K))
    return nc


def _build_xn():
    nc = bass.Bass("TRN2", target_bir_lowering=False)
    st = ExitStack()
    ctx = Ctx(nc, st)
    x3 = ctx.dram_in("x", [128, NCHUNK, NTOK_K], F32)
    xn3 = ctx.dram_out("xn", [128, NCHUNK, NTOK_K], BF16)
    with st:
        with nc.Block() as blk:
            ctx.blk = blk
            declare_consts(ctx, skip=_SKIP["xn"])
            r = emit_xn(ctx, x3, xn3, NTOK_K, "xn0")
            _finish(ctx, blk, r)
    return nc


def _resid_build(emitfn):
    nc = bass.Bass("TRN2", target_bir_lowering=False)
    st = ExitStack()
    ctx = Ctx(nc, st)
    xn3 = ctx.dram_in("xn", [128, NCHUNK, NTOK_K], BF16)
    x0 = ctx.dram_in("x0", [128, NCHUNK, NTOK_K], F32)
    x3 = ctx.dram_out("x", [128, NCHUNK, NTOK_K], F32)
    with st:
        with nc.Block() as blk:
            ctx.blk = blk

            @blk.sync
            def _(s):
                ctx.inc(s.dma_start(out=x3, in_=x0), "s_xinit", 16)

            r = emitfn(ctx, xn3, x3, [("s_xinit", 16)])
            _finish(ctx, blk, r)
    return nc


def _build_block(bnm):
    return _resid_build(
        lambda ctx, xn3, x3, ga: _emit_block(ctx, bnm, xn3, x3, ga))


def _build_xadd():
    return _resid_build(
        lambda ctx, xn3, x3, ga: emit_xadd(ctx, "xa", xn3, x3, NTOK_K,
                                           gates_acc=ga))


_PROGS = {}


def _prog(name, builder):
    if name not in _PROGS:
        _PROGS[name] = builder()
    return _PROGS[name]


def _weight_maps(inp):
    r1 = lambda v: np.ascontiguousarray(np.asarray(v, np.float32))[None]
    wm = {}
    wm["l0attn"] = {
        "wq": inp["L0_attn_Wq"], "wk": inp["L0_attn_Wk"],
        "wv": inp["L0_attn_Wv"], "wo": inp["L0_attn_Wo"],
        "w1": inp["L0_attn_W1"], "w2": inp["L0_attn_W2"],
        "proj": inp["L0_attn_proj"],
        "bq": r1(inp["L0_attn_bq"]), "bk": r1(inp["L0_attn_bk"]),
        "bv": r1(inp["L0_attn_bv"]), "bo": r1(inp["L0_attn_bo"]),
        "b1": r1(inp["L0_attn_b1"]), "b2": r1(inp["L0_attn_b2"]),
        "g2": r1(inp["L0_attn_g2"]), "be2": r1(inp["L0_attn_be2"]),
        "g3": r1(inp["L0_attn_g3"]), "be3": r1(inp["L0_attn_be3"]),
    }
    wm["l0ffn"] = {
        "wg": np.asarray(inp["L0_ffn_w_gate"], np.int32),
        "wu": np.asarray(inp["L0_ffn_w_up"], np.int32),
        "wd": np.asarray(inp["L0_ffn_w_down"], np.int32),
    }
    wm["l1attn"] = {
        "ck": inp["L1_attn_conv_kernel"], "alog": inp["L1_attn_A_log"],
        "wbc": inp["L1_attn_W_bc"], "bbc": r1(inp["L1_attn_b_bc"]),
        "dco": r1(inp["L1_attn_Dcoeff"]), "wdt": inp["L1_attn_W_dt"],
        "bdt": r1(inp["L1_attn_b_dt"]), "wdtp": inp["L1_attn_W_dtp"],
        "bdtp": r1(inp["L1_attn_b_dtp"]),
    }
    wm["l1ffn"] = {
        "wg": inp["L1_ffn_Wg"], "wu": inp["L1_ffn_Wu"],
        "wd": inp["L1_ffn_Wd"], "wr": inp["L1_ffn_Wr"],
        "temp": r1(inp["L1_ffn_temperature"]),
    }
    wm["l2attn"] = {"wqkv": inp["L2_attn_w_qkv"], "wo": inp["L2_attn_w_o"]}
    wm["l2ffn"] = {
        "w1": inp["L2_ffn_w1_w"], "s1": r1(inp["L2_ffn_w1_s"]),
        "w2": inp["L2_ffn_w2_w"], "s2": r1(inp["L2_ffn_w2_s"]),
        "h1": inp["L2_ffn_h1_w"], "sh1": r1(inp["L2_ffn_h1_s"]),
        "h2": inp["L2_ffn_h2_w"], "sh2": r1(inp["L2_ffn_h2_s"]),
        "w3": inp["L2_ffn_w3_w"], "s3": r1(inp["L2_ffn_w3_s"]),
    }
    f32 = np.float32
    wm["l3attn"] = {
        "wq": _wTh(np.asarray(inp["L3_attn_Wq"], f32)),
        "wk": _wTh(np.asarray(inp["L3_attn_Wk"], f32)),
        "wv": _wTh(np.asarray(inp["L3_attn_Wv"], f32)),
        "wo": _wTh(np.asarray(inp["L3_attn_Wo"], f32)),
        "wcat": _wTh(np.concatenate(
            [np.asarray(inp["L3_attn_Wb"], f32),
             np.asarray(inp["L3_attn_Wg"], f32),
             np.asarray(inp["L3_attn_Wh"], f32)], 1)),
        "conv": _wTh(np.concatenate(
            [np.asarray(inp["L3_attn_conv_q"], f32),
             np.asarray(inp["L3_attn_conv_k"], f32),
             np.asarray(inp["L3_attn_conv_v"], f32)], 1)),
        "bgh": np.concatenate(
            [np.zeros(8, f32), np.asarray(inp["L3_attn_bg"], f32),
             np.asarray(inp["L3_attn_bh"], f32)]).astype(f32)[:, None],
        "normw": np.asarray(inp["L3_attn_norm_w"], f32)[None, :],
    }
    wm["l3ffn"] = {
        "gam": r1(inp["L3_ffn_ln_gamma"]), "bet": r1(inp["L3_ffn_ln_beta"]),
        "wr": inp["L3_ffn_router_w"], "gu": inp["L3_ffn_gate_up_w"],
        "dn": inp["L3_ffn_down_w"],
    }
    return wm


def kernel(**inp):
    from concourse.bass_utils import run_bass_kernel_spmd
    cores = list(range(8))
    x_full = np.asarray(inp["L0_attn_video"], np.float32)
    x_state = [_fm(np.ascontiguousarray(
        x_full[2 * i:2 * i + 2].reshape(NTOK_K, 1024))) for i in range(8)]
    wm = _weight_maps(inp)

    if USE_FUSED:
        allconsts = const_inputs()
        winp = {f"{b}_{k}": v for b, m in wm.items() for k, v in m.items()}
        nc = _prog("fused", _build_fused)
        res = run_bass_kernel_spmd(
            nc, [{**allconsts, **winp, "xin": x_state[i]} for i in cores],
            core_ids=cores)
        x_state = [res.results[i]["x"] for i in cores]
    else:
        cmaps = {k: const_inputs_for(v) for k, v in _SKIP.items()}
        for li, (anm, fnm) in enumerate(_LAYER_BLOCKS):
            nc = _prog("xn", _build_xn)
            res = run_bass_kernel_spmd(
                nc, [{**cmaps["xn"], "x": x_state[i]} for i in cores],
                core_ids=cores)
            xn_state = [res.results[i]["xn"] for i in cores]
            for bnm in (anm, fnm):
                pin = {f"{bnm}_{k}": v for k, v in wm[bnm].items()}
                nc = _prog(bnm, lambda bnm=bnm: _build_block(bnm))
                res = run_bass_kernel_spmd(
                    nc, [{**cmaps[bnm], **pin, "xn": xn_state[i],
                          "x0": x_state[i]} for i in cores], core_ids=cores)
                x_state = [res.results[i]["x"] for i in cores]
            if li == 3:
                nc = _prog("xadd", _build_xadd)
                res = run_bass_kernel_spmd(
                    nc, [{"xn": xn_state[i], "x0": x_state[i]}
                         for i in cores], core_ids=cores)
                x_state = [res.results[i]["x"] for i in cores]

    out = np.concatenate(
        [_unfm(np.asarray(x_state[i], np.float32)).reshape(2, SEQ_K, 1024)
         for i in cores], 0)
    return np.ascontiguousarray(out.astype(np.float32))
